# version 08
# baseline (speedup 1.0000x reference)
import numpy as np
import ml_dtypes
from contextlib import ExitStack

import concourse.bass as bass
import concourse.mybir as mybir
from concourse.bass_utils import run_bass_kernel_spmd

F32 = mybir.dt.float32
BF16 = mybir.dt.bfloat16
F16 = mybir.dt.float16
I32 = mybir.dt.int32
AF = mybir.ActivationFunctionType
ALU = mybir.AluOpType
AX = mybir.AxisListType
BFNP = ml_dtypes.bfloat16

T = 2048
D = 2048
NT = T // 128
ND = D // 128
NH = 16
HD = 128
S = 4096
EPS = 1e-5

L0_SCALE = float(1.0 / np.sqrt(np.float32(128.0)).astype(np.float32))
L1_SCALE = 1.0 / 64.0


class Prog:

    def __init__(self, name="prog"):
        self.nc = bass.Bass("TRN2", target_bir_lowering=False, name=name)
        self.ctx = ExitStack()
        self.sems = {}
        self.cnt = {}
        self.block = None

    def sb(self, name, shape, dtype, at=None):
        key = ("sb", name)
        if not hasattr(self, "_bufs"):
            self._bufs = {}
        if key not in self._bufs:
            if at is None:
                self._bufs[key] = self.nc.alloc_sbuf_tensor(name, shape, dtype)
            else:
                self._bufs[key] = self.nc.alloc_sbuf_tensor_at(
                    name, shape, dtype, offset=at)
        return self._bufs[key]

    def addr(self, name):
        return self.nc.lookup_mloc(self._bufs[("sb", name)]).addr

    def ps(self, name, shape, dtype=F32):
        return self.ctx.enter_context(self.nc.psum_tensor(name, shape, dtype))

    def sem(self, name):
        if name not in self.sems:
            self.sems[name] = self.ctx.enter_context(self.nc.semaphore(name))
            self.cnt[name] = 0
        return self.sems[name]

    def start(self):
        for s in ("te", "sc", "ve", "gp", "ev"):
            self.sem(s)
        self.pfd = [self.ps(f"pfd{i}", [128, 1024], F32) for i in range(2)]
        self.pf = [self.pfd[0][:, 0:512], self.pfd[0][:, 512:1024],
                   self.pfd[1][:, 0:512], self.pfd[1][:, 512:1024],
                   self.ps("pf4", [128, 512], F32),
                   self.ps("pf5", [128, 512], F32)]
        self.pb = [self.ps(f"pbh{i}", [128, 1024], BF16)[:, 0:512]
                   for i in range(2)]
        self.block = self.ctx.enter_context(self.nc.Block())
        return self.block

    def inc(self, bi, name, k=1):
        self.cnt[name] += k
        bi.then_inc(self.sems[name], k)
        return (self.sems[name], self.cnt[name])

    def mark(self, name):
        return (self.sems[name], self.cnt[name])

    def finish(self):
        self.ctx.close()
        return self.nc


def wge(eng, mk):
    eng.wait_ge(mk[0], mk[1])


def apx(x):
    a = getattr(x, "ap", None)
    return x.ap() if callable(a) else x


def emit_consts(p):
    ident = p.sb("ident", [128, 128], BF16)
    ones_row = p.sb("ones_row", [1, 512], BF16)
    ones128 = p.sb("ones128", [128, 128], BF16)
    identf = p.sb("identf", [128, 128], F32)
    b = p.block
    sem = p.sems
    GP = p.cnt["gp"]

    @b.gpsimd
    def _(g):
        g.memset(ident[:], 0.0).then_inc(sem["gp"], 1)
        g.wait_ge(sem["gp"], GP + 1)
        g.affine_select(out=ident[:], in_=ident[:], compare_op=ALU.not_equal,
                        fill=1.0, base=0, pattern=[[-1, 128]],
                        channel_multiplier=1).then_inc(sem["gp"], 1)
        g.memset(ones_row[:], 1.0).then_inc(sem["gp"], 1)
        g.memset(ones128[:], 1.0).then_inc(sem["gp"], 1)
        g.memset(identf[:], 0.0).then_inc(sem["gp"], 1)
        g.wait_ge(sem["gp"], GP + 5)
        g.affine_select(out=identf[:], in_=identf[:], compare_op=ALU.not_equal,
                        fill=1.0, base=0, pattern=[[-1, 128]],
                        channel_multiplier=1).then_inc(sem["gp"], 1)
    p.cnt["gp"] += 6
    ready = (sem["gp"], p.cnt["gp"])
    p._identf = identf
    return ident, ones_row, ones128, ready


class LNOut:
    pass


def emit_ln_uT(p, x_hbm, uT, t0, n_tiles, consts, tspan, name="",
               guard_marks=()):
    ident = consts[0]
    cready = consts[3]
    b = p.block
    sem = p.sems
    xt = [p.sb(f"lnxt{name}{i}", [128, D], F32) for i in range(2)]
    for i in range(2):
        p.sem(f"lx{name}{i}")
    trash_bf = p.sb(f"lntrash{name}", [128, D], BF16)
    trash_sq = p.sb(f"lnsq{name}", [128, D], F32)
    sums = p.sb(f"lnsums{name}", [128, 1], F32)
    ssq = p.sb(f"lnssq{name}", [128, 1], F32)
    mean = p.sb(f"lnmean{name}", [128, 1], F32)
    msq = p.sb(f"lnmsq{name}", [128, 1], F32)
    var = p.sb(f"lnvar{name}", [128, 1], F32)
    std = p.sb(f"lnstd{name}", [128, 1], F32)
    rstd = p.sb(f"lnrstd{name}", [128, 1], F32)
    negmr = p.sb(f"lnnegmr{name}", [128, 1], F32)
    u_t = [p.sb(f"lnu{name}{i}", [128, D], BF16) for i in range(2)]
    pb = p.pb

    SC = p.cnt["sc"]
    VE = p.cnt["ve"]
    TE = p.cnt["te"]
    EV = p.cnt["ev"]
    xmark = []

    @b.sync
    def _(sync):
        for mk in guard_marks:
            wge(sync, mk)
        for tt in range(n_tiles):
            if tt >= 2:
                sync.wait_ge(sem["sc"], SC + 3 * (tt - 2) + 3)
                sync.wait_ge(sem["ve"], VE + 7 * (tt - 2) + 1)
            else:
                sync.wait_ge(sem["sc"], SC)
                sync.wait_ge(sem["ve"], VE)
            bi = sync.dma_start(out=xt[tt % 2][:],
                                in_=x_hbm[(t0 + tt) * 128:(t0 + tt + 1) * 128, :])
            xmark.append(p.inc(bi, f"lx{name}{tt % 2}", 16))
    assert len(xmark) == n_tiles

    @b.scalar
    def _(s):
        for tt in range(n_tiles):
            wge(s, xmark[tt])
            if tt >= 1:
                s.wait_ge(sem["ve"], VE + 7 * (tt - 1) + 2)
            s.activation(trash_bf[:], xt[tt % 2][:], AF.Identity,
                         accum_out=sums[:]).then_inc(sem["sc"], 1)
            s.wait_ge(sem["ve"], VE + 7 * tt + 5)
            s.activation(std[:], var[:], AF.Sqrt).then_inc(sem["sc"], 1)
            s.wait_ge(sem["ve"], VE + 7 * tt + 7)
            if tt >= 2:
                s.wait_ge(sem["te"], TE + 4 * (tt - 2) + 4)
            else:
                s.wait_ge(sem["te"], TE)
            s.activation(u_t[tt % 2][:], xt[tt % 2][:], AF.Identity,
                         bias=negmr[:], scale=rstd[:]).then_inc(sem["sc"], 1)
    p.cnt["sc"] += 3 * n_tiles

    def emit_evac(v, tt):
        for g in range(4):
            G = 4 * tt + g
            v.wait_ge(sem["te"], TE + G + 1)
            dst = apx(uT).rearrange("p (dc t) -> p dc t", dc=ND, t=tspan)[
                :, 4 * g:4 * g + 4, tt * 128:(tt + 1) * 128]
            src = pb[G % 2].rearrange("p (j f) -> p j f", j=4, f=128)
            v.tensor_copy(dst, src).then_inc(sem["ev"], 1)

    @b.vector
    def _(v):
        for mk in guard_marks:
            wge(v, mk)
        for tt in range(n_tiles):
            wge(v, xmark[tt])
            v.scalar_tensor_tensor(trash_sq[:], xt[tt % 2][:], 0.0, xt[tt % 2][:],
                                   op0=ALU.add, op1=ALU.mult,
                                   accum_out=ssq[:]).then_inc(sem["ve"], 1)
            v.wait_ge(sem["sc"], SC + 3 * tt + 1)
            v.tensor_scalar_mul(mean[:], sums[:], 1.0 / D).then_inc(sem["ve"], 1)
            v.wait_ge(sem["ve"], VE + 7 * tt + 2)
            v.scalar_tensor_tensor(msq[:], mean[:], 0.0, mean[:],
                                   op0=ALU.add, op1=ALU.mult).then_inc(sem["ve"], 1)
            v.wait_ge(sem["ve"], VE + 7 * tt + 3)
            v.scalar_tensor_tensor(var[:], ssq[:], 1.0 / D, msq[:],
                                   op0=ALU.mult, op1=ALU.subtract).then_inc(sem["ve"], 1)
            v.wait_ge(sem["ve"], VE + 7 * tt + 4)
            v.tensor_scalar_add(var[:], var[:], EPS).then_inc(sem["ve"], 1)
            v.wait_ge(sem["sc"], SC + 3 * tt + 2)
            v.reciprocal(rstd[:], std[:]).then_inc(sem["ve"], 1)
            v.wait_ge(sem["ve"], VE + 7 * tt + 6)
            v.scalar_tensor_tensor(negmr[:], mean[:], -1.0, rstd[:],
                                   op0=ALU.mult, op1=ALU.mult).then_inc(sem["ve"], 1)
            if tt >= 1:
                emit_evac(v, tt - 1)
        emit_evac(v, n_tiles - 1)
    p.cnt["ve"] += 7 * n_tiles
    p.cnt["ev"] += 4 * n_tiles

    @b.tensor
    def _(t):
        wge(t, cready)
        for tt in range(n_tiles):
            t.wait_ge(sem["sc"], SC + 3 * tt + 3)
            for g in range(4):
                G = 4 * tt + g
                if G >= 2:
                    t.wait_ge(sem["ev"], EV + (G - 2) + 1)
                else:
                    t.wait_ge(sem["ev"], EV)
                for j in range(4):
                    dc = 4 * g + j
                    bi = t.transpose(pb[G % 2][:, j * 128:(j + 1) * 128],
                                     u_t[tt % 2][:, dc * 128:(dc + 1) * 128],
                                     ident[:])
                bi.then_inc(sem["te"], 1)
    p.cnt["te"] += 4 * n_tiles


    out = LNOut()
    out.tile_done = lambda tt: (sem["ev"], EV + 4 * (tt + 1))
    out.all_done = (sem["ev"], EV + 4 * n_tiles)
    return out


def emit_proj_fm(p, uT, w_hbm, w_cols, b_hbm, b_off, out_hbm, consts, ln_mark,
                 nfc, tspan, name=""):
    _, ones_row, _, cready = consts
    b = p.block
    sem = p.sems
    ntg = tspan // 512
    wbuf = [p.sb(f"pfw{name}{i}", [128, ND * 128], BF16) for i in range(2)]
    stage = [p.sb(f"pfs{name}{i}", [128, 512], BF16) for i in range(2)]
    for i in range(2):
        p.sem(f"pfl{name}{i}")
        p.sem(f"pfo{name}{i}")
    p.sem(f"pflb{name}")
    pf = p.pf[0:2]
    brow = p.sb(f"pfb{name}", [1, nfc * 128], BF16) if b_hbm is not None else None

    TE = p.cnt["te"]
    SC = p.cnt["sc"]
    SO = [p.cnt[f"pfo{name}{i}"] for i in range(2)]
    wmark = []
    bmark = []

    @b.gpsimd
    def _(g):
        if b_hbm is not None:
            g.wait_ge(sem["te"], TE)
            bi = g.dma_start(out=brow[:], in_=b_hbm[:, b_off:b_off + nfc * 128])
            bmark.append(p.inc(bi, f"pflb{name}", 16))
        for fc in range(nfc):
            if fc >= 2:
                g.wait_ge(sem["te"], TE + ntg * (fc - 1))
            else:
                g.wait_ge(sem["te"], TE)
            src = w_hbm[:, w_cols + fc * 128:w_cols + (fc + 1) * 128]\
                .rearrange("(dc p) f -> p dc f", p=128)
            dst = wbuf[fc % 2].ap().rearrange("p (dc f) -> p dc f", dc=ND, f=128)
            bi = g.dma_start(out=dst, in_=src)
            wmark.append(p.inc(bi, f"pfl{name}{fc % 2}", 16))

    @b.tensor
    def _(t):
        wge(t, cready)
        for fc in range(nfc):
            wge(t, wmark[fc])
            if fc == 0 and bmark:
                wge(t, bmark[0])
            for tg in range(ntg):
                K = fc * ntg + tg
                if K >= 2:
                    t.wait_ge(sem["sc"], SC + (K - 2) + 1)
                else:
                    t.wait_ge(sem["sc"], SC)
                wge(t, ln_mark(tg))
                first = True
                if b_hbm is not None:
                    t.matmul(pf[K % 2][:], brow[0:1, fc * 128:(fc + 1) * 128],
                             ones_row[0:1, :], start=True, stop=False)
                    first = False
                for dc in range(ND):
                    bi = t.matmul(
                        pf[K % 2][:], wbuf[fc % 2][:, dc * 128:(dc + 1) * 128],
                        apx(uT)[:, dc * tspan + tg * 512: dc * tspan + tg * 512 + 512],
                        start=first, stop=(dc == ND - 1))
                    first = False
                bi.then_inc(sem["te"], 1)
    p.cnt["te"] += nfc * ntg

    @b.scalar
    def _(s):
        for fc in range(nfc):
            for tg in range(ntg):
                K = fc * ntg + tg
                s.wait_ge(sem["te"], TE + K + 1)
                if K >= 2:
                    s.wait_ge(p.sems[f"pfo{name}{K % 2}"],
                              SO[K % 2] + 16 * (K // 2))
                s.activation(stage[K % 2][:], pf[K % 2][:], AF.Identity)\
                    .then_inc(sem["sc"], 1)
    p.cnt["sc"] += nfc * ntg

    @b.sync
    def _(sync):
        for fc in range(nfc):
            for tg in range(ntg):
                K = fc * ntg + tg
                sync.wait_ge(sem["sc"], SC + K + 1)
                bi = sync.dma_start(out=out_hbm[fc * 128:(fc + 1) * 128,
                                                tg * 512:tg * 512 + 512],
                                    in_=stage[K % 2][:])
                p.inc(bi, f"pfo{name}{K % 2}", 16)
    return [p.mark(f"pfo{name}0"), p.mark(f"pfo{name}1")]


def emit_proj_tm(p, uT, w_hbm, w_cols, b_hbm, b_off, out_hbm, consts, ln_mark,
                 tspan, t0=0, nfg=4, name=""):
    _, ones_row, _, cready = consts
    b = p.block
    sem = p.sems
    ntl = tspan // 128
    GR = 4
    ngr = ntl // GR
    wtile = [p.sb(f"ptw{name}{i}", [128, 512], BF16) for i in range(2)]
    stage = [p.sb(f"pts{name}{i}", [128, 512], BF16) for i in range(2)]
    for i in range(2):
        p.sem(f"ptl{name}{i}")
        p.sem(f"pto{name}{i}")
    p.sem(f"ptlb{name}")
    pt = p.pf[0:4]
    brow = p.sb(f"ptb{name}", [1, nfg * 512], BF16) if b_hbm is not None else None

    TE = p.cnt["te"]
    SC = p.cnt["sc"]
    SO = [p.cnt[f"pto{name}{i}"] for i in range(2)]
    wmark = []
    bmark = []

    @b.gpsimd
    def _(g):
        if b_hbm is not None:
            bi = g.dma_start(out=brow[:], in_=b_hbm[:, b_off:b_off + nfg * 512])
            bmark.append(p.inc(bi, f"ptlb{name}", 16))
        for fg in range(nfg):
            for ttg in range(ngr):
                for dc in range(ND):
                    L = (fg * ngr + ttg) * ND + dc
                    if L >= 2:
                        g.wait_ge(sem["te"], TE + L - 1)
                    else:
                        g.wait_ge(sem["te"], TE)
                    bi = g.dma_start(
                        out=wtile[L % 2][:],
                        in_=w_hbm[dc * 128:(dc + 1) * 128,
                                  w_cols + fg * 512:w_cols + (fg + 1) * 512])
                    wmark.append(p.inc(bi, f"ptl{name}{L % 2}", 16))

    @b.tensor
    def _(t):
        wge(t, cready)
        for fg in range(nfg):
            for ttg in range(ngr):
                grp = fg * ngr + ttg
                for dc in range(ND):
                    L = grp * ND + dc
                    wge(t, wmark[L])
                    if L == 0 and bmark:
                        wge(t, bmark[0])
                    for i in range(GR):
                        tt = ttg * GR + i
                        if dc == 0:
                            if grp >= 1:
                                t.wait_ge(sem["sc"], SC + (grp - 1) * GR + i + 1)
                            else:
                                t.wait_ge(sem["sc"], SC)
                            wge(t, ln_mark(tt))
                            if b_hbm is not None:
                                t.matmul(pt[i][:], ones_row[0:1, 0:128],
                                         brow[0:1, fg * 512:(fg + 1) * 512],
                                         start=True, stop=False)
                        bi = t.matmul(
                            pt[i][:],
                            apx(uT)[:, dc * tspan + tt * 128: dc * tspan + tt * 128 + 128],
                            wtile[L % 2][:],
                            start=(dc == 0 and b_hbm is None),
                            stop=(dc == ND - 1))
                    bi.then_inc(sem["te"], 1)
    p.cnt["te"] += nfg * ngr * ND

    @b.scalar
    def _(s):
        for fg in range(nfg):
            for ttg in range(ngr):
                grp = fg * ngr + ttg
                s.wait_ge(sem["te"], TE + (grp + 1) * ND)
                for i in range(GR):
                    E = grp * GR + i
                    if E >= 2:
                        s.wait_ge(p.sems[f"pto{name}{E % 2}"],
                                  SO[E % 2] + 16 * (E // 2))
                    s.activation(stage[E % 2][:], pt[i][:], AF.Identity)\
                        .then_inc(sem["sc"], 1)
    p.cnt["sc"] += nfg * ngr * GR

    @b.sync
    def _(sync):
        for fg in range(nfg):
            for ttg in range(ngr):
                grp = fg * ngr + ttg
                for i in range(GR):
                    tt = ttg * GR + i
                    E = grp * GR + i
                    sync.wait_ge(sem["sc"], SC + E + 1)
                    bi = sync.dma_start(
                        out=out_hbm[(t0 + tt) * 128:(t0 + tt + 1) * 128,
                                    fg * 512:(fg + 1) * 512],
                        in_=stage[E % 2][:])
                    p.inc(bi, f"pto{name}{E % 2}", 16)
    return [p.mark(f"pto{name}0"), p.mark(f"pto{name}1")]


def emit_attn_core(p, qTh, kv_own, kv_all, alpha_in, consts, scale, l1,
                   AVTn, name="", guard_marks=()):
    ident, ones_row, ones128, cready = consts
    b = p.block
    sem = p.sems
    kT_h = [p.sb(f"at_k{name}{i}", [128, S], BF16) for i in range(2)]
    v_h = [p.sb(f"at_v{name}{i}", [128, 32 * 128], BF16) for i in range(2)]
    qT_h = [p.sb(f"at_q{name}{i}", [128, T], BF16) for i in range(2)]
    pA = p.sb(f"at_pa{name}", [128, T], BF16)
    pC = p.sb(f"at_pc{name}", [128, T], BF16)
    expT = p.sb(f"at_e{name}", [128, 16 * 512], BF16)
    r_s = p.sb(f"at_r{name}", [128, 512], F32)
    a_sb = p.sb(f"at_al{name}", [1, 2], BF16)
    lnA = p.sb(f"at_lna{name}", [128, 1], F32)
    betac = p.sb(f"at_bc{name}", [128, 1], F32)
    B_t = p.sb(f"at_bt{name}", [128, 512], BF16)
    for i in range(2):
        p.sem(f"athd{name}{i}")
    p.sem(f"atal{name}")
    ps_sd = p.pfd
    ps_l = p.pf[4]
    ps_av = p.pf[5]
    ps_b = p.pf[4]

    NQG = T // 512

    def classify(h, qg, ktc):
        if ktc >= 16:
            return "blend" if l1 else "full"
        if not l1:
            return "full"
        lo = 128 * ktc
        if lo >= 512 * (qg + 1):
            return "skip"
        if lo + 127 <= 512 * qg:
            return "full"
        return "diag"

    TE0 = p.cnt["te"]
    SC0 = p.cnt["sc"]
    GP0 = p.cnt["gp"]
    VE0 = p.cnt["ve"]
    def pdone(h):
        return VE0 + 1 + 4 * h + 8 * max(0, h - 1) + 4

    def grp_base(H):
        if H <= NH - 3:
            return VE0 + 9 + 12 * H
        return VE0 + 1 + 4 * NH + 8 * H

    rc_mark = {}
    mu_mark = {}
    for H in range(NH):
        for qg in range(NH * 0 + 4):
            rc_mark[(H, qg)] = grp_base(H) + 2 * qg + 1
            mu_mark[(H, qg)] = grp_base(H) + 2 * qg + 2
    VE_END = VE0 + 1 + 4 * NH + 8 * NH
    TE1 = TE0 + 1
    SC1 = SC0 + 2

    te_n, sc_n = TE1, SC1
    gp_n = GP0 + (4 if l1 else 0)
    score_mark = {}
    exp_mark = {}
    gp_mark = {}
    lp0_mark = {}
    avp0_mark = {}
    l_mark = {}
    av_mark = {}
    exp_done_own = {}
    exp_done = {}
    n_own_of = {}
    for h in range(NH):
        for qg in range(NQG):
            own = [k for k in range(16) if classify(h, qg, k) != "skip"]
            assert own == list(range(len(own))) and len(own) % 2 == 0
            n_own_of[(h, qg)] = len(own)
            for jp in range(len(own) // 2):
                te_n += 2
                score_mark[(h, qg, 2 * jp)] = te_n - 1
                score_mark[(h, qg, 2 * jp + 1)] = te_n
                sc_n += 1
                exp_mark[(h, qg, 2 * jp)] = sc_n
                exp_mark[(h, qg, 2 * jp + 1)] = sc_n
            exp_done_own[(h, qg)] = sc_n
            te_n += 1
            lp0_mark[(h, qg)] = te_n
            te_n += 1
            avp0_mark[(h, qg)] = te_n
            for jp in range(8):
                te_n += 2
                score_mark[(h, qg, 16 + 2 * jp)] = te_n - 1
                score_mark[(h, qg, 17 + 2 * jp)] = te_n
                sc_n += 1
                exp_mark[(h, qg, 16 + 2 * jp)] = sc_n
                exp_mark[(h, qg, 17 + 2 * jp)] = sc_n
            if l1:
                k0 = 4 * qg
                if k0 + 4 <= 15:
                    gp_n += 1
                    for k in range(k0 + 4, 16):
                        gp_mark[(h, qg, k)] = gp_n
                gp_n += 2
                for k in range(k0, k0 + 4):
                    gp_mark[(h, qg, k)] = gp_n
                gp_n += 1
                for k in range(16, 32):
                    gp_mark[(h, qg, k)] = gp_n
            exp_done[(h, qg)] = sc_n
            te_n += 1
            l_mark[(h, qg)] = te_n
            te_n += 1
            av_mark[(h, qg)] = te_n

    def prev_group(h, qg):
        if qg > 0:
            return (h, qg - 1)
        if h > 0:
            return (h - 1, NQG - 1)
        return None

    amark = []
    hmark = []

    @b.sync
    def _(sync):
        for mk in guard_marks:
            wge(sync, mk)
        sync.wait_ge(sem["te"], TE0)
        sync.wait_ge(sem["ve"], VE0)
        bi = sync.dma_start(out=a_sb[:], in_=alpha_in)
        amark.append(p.inc(bi, f"atal{name}", 16))
        for h in range(NH):
            if h >= 2:
                sync.wait_ge(sem["te"], av_mark[(h - 2, NQG - 1)])
                sync.wait_ge(sem["ve"], pdone(h - 2))
            if h >= 1:
                sync.wait_ge(sem["ve"], pdone(h - 1))
            sn = f"athd{name}{h % 2}"
            bi = sync.dma_start(out=kT_h[h % 2][:, 0:T],
                                in_=kv_own[h * 128:(h + 1) * 128, :])
            p.inc(bi, sn, 16)
            bi = sync.dma_start(out=pA[:],
                                in_=kv_all[h * 128:(h + 1) * 128, :])
            p.inc(bi, sn, 16)
            bi = sync.dma_start(
                out=kT_h[h % 2][:, T:2 * T],
                in_=kv_all[2 * T + h * 128:2 * T + (h + 1) * 128, :])
            p.inc(bi, sn, 16)
            src = kv_own[T:2 * T, h * 128:(h + 1) * 128]\
                .rearrange("(c p) f -> p c f", p=128)
            dst = v_h[h % 2].ap().rearrange("p (c f) -> p c f",
                                            c=32, f=128)[:, 0:16, :]
            bi = sync.dma_start(out=dst, in_=src)
            p.inc(bi, sn, 16)
            src = kv_all[T:2 * T, h * 128:(h + 1) * 128]\
                .rearrange("(c p) f -> p c f", p=128)
            bi = sync.dma_start(
                out=pC.ap().rearrange("p (c f) -> p c f", c=16, f=128),
                in_=src)
            p.inc(bi, sn, 16)
            src = kv_all[3 * T:4 * T, h * 128:(h + 1) * 128]\
                .rearrange("(c p) f -> p c f", p=128)
            bi = sync.dma_start(
                out=v_h[h % 2].ap().rearrange("p (c f) -> p c f",
                                              c=32, f=128)[:, 16:32, :],
                in_=src)
            p.inc(bi, sn, 16)
            bi = sync.dma_start(out=qT_h[h % 2][:],
                                in_=qTh[h * 128:(h + 1) * 128, :])
            hmark.append(p.inc(bi, sn, 16))

    def emit_partner(v, h):
        wge(v, hmark[h])
        if h >= 2:
            v.wait_ge(sem["te"], av_mark[(h - 2, NQG - 1)])
        ve_b = pdone(h) - 4
        v.tensor_add(pA[:], pA[:], kT_h[h % 2][:, T:2 * T])\
            .then_inc(sem["ve"], 1)
        v.wait_ge(sem["ve"], ve_b + 1)
        v.tensor_sub(kT_h[h % 2][:, T:2 * T], pA[:],
                     kT_h[h % 2][:, 0:T]).then_inc(sem["ve"], 1)
        v.tensor_add(pC[:], pC[:],
                     v_h[h % 2][:, 16 * 128:32 * 128]).then_inc(sem["ve"], 1)
        v.wait_ge(sem["ve"], ve_b + 3)
        v.tensor_sub(v_h[h % 2][:, 16 * 128:32 * 128], pC[:],
                     v_h[h % 2][:, 0:16 * 128]).then_inc(sem["ve"], 1)

    def emit_grp(v, H):
        for qg in range(NQG):
            v.wait_ge(sem["te"], l_mark[(H, qg)])
            if not (H == 0 and qg == 0):
                pm = mu_mark[(H, qg - 1)] if qg > 0 else mu_mark[(H - 1, 3)]
                v.wait_ge(sem["ve"], pm)
            v.reciprocal(r_s[:], ps_l[:]).then_inc(sem["ve"], 1)
            v.wait_ge(sem["ve"], rc_mark[(H, qg)])
            v.wait_ge(sem["te"], av_mark[(H, qg)])
            v.tensor_mul(apx(AVTn)[:, H * T + qg * 512: H * T + qg * 512 + 512],
                         ps_av[:], r_s[:]).then_inc(sem["ve"], 1)

    @b.vector
    def _(v):
        wge(v, (sem["sc"], SC1))
        v.tensor_copy(B_t.ap().rearrange("p (a f) -> p a f", a=1, f=512),
                      betac.ap().broadcast_to([128, 1, 512]))\
            .then_inc(sem["ve"], 1)
        emit_partner(v, 0)
        emit_partner(v, 1)
        for h in range(2, NH):
            emit_grp(v, h - 2)
            emit_partner(v, h)
        emit_grp(v, NH - 2)
        emit_grp(v, NH - 1)
    p.cnt["ve"] = VE_END

    @b.tensor
    def _(t):
        wge(t, cready)
        wge(t, amark[0])
        t.wait_ge(sem["sc"], SC0)
        t.matmul(ps_b[:, 0:2], ones_row[0:1, 0:128], a_sb[0:1, :],
                 start=True, stop=True).then_inc(sem["te"], 1)
    p.cnt["te"] = TE1

    @b.scalar
    def _(s):
        s.wait_ge(sem["te"], TE1)
        s.activation(lnA[:], ps_b[:, 0:1], AF.Identity).then_inc(sem["sc"], 1)
        s.activation(betac[:], ps_b[:, 1:2], AF.Identity).then_inc(sem["sc"], 1)
    p.cnt["sc"] = SC1

    @b.tensor
    def _(t):
        w_sc = [0]
        w_gp = [0]

        def wsc(val):
            if val > w_sc[0]:
                t.wait_ge(sem["sc"], val)
                w_sc[0] = val

        def wgp(val):
            if val > w_gp[0]:
                t.wait_ge(sem["gp"], val)
                w_gp[0] = val

        for h in range(NH):
            wge(t, hmark[h])
            t.wait_ge(sem["ve"], pdone(h))
            for qg in range(NQG):
                pg = prev_group(h, qg)
                n_own = n_own_of[(h, qg)]
                for jp in range(n_own // 2):
                    if jp >= 2:
                        wsc(exp_mark[(h, qg, 2 * (jp - 2))])
                    elif pg is not None:
                        wsc(exp_done[pg])
                    else:
                        wsc(SC1)
                    for kk in (2 * jp, 2 * jp + 1):
                        bi = t.matmul(
                            ps_sd[jp % 2][:, (kk % 2) * 512:(kk % 2) * 512 + 512],
                            kT_h[h % 2][:, kk * 128:(kk + 1) * 128],
                            qT_h[h % 2][:, qg * 512:(qg + 1) * 512],
                            start=True, stop=True)
                        bi.then_inc(sem["te"], 1)
                for ktc in range(16):
                    c = classify(h, qg, ktc)
                    if c == "full":
                        wsc(exp_mark[(h, qg, ktc)])
                    else:
                        wgp(gp_mark[(h, qg, ktc)])
                    if ktc == 0 and pg is not None:
                        t.wait_ge(sem["ve"], rc_mark[pg])
                    bi = t.matmul(ps_l[:], ones128[:],
                                  expT[:, ktc * 512:(ktc + 1) * 512],
                                  start=(ktc == 0), stop=False)
                bi.then_inc(sem["te"], 1)
                for ktc in range(16):
                    if ktc == 0 and pg is not None:
                        t.wait_ge(sem["ve"], mu_mark[pg])
                    bi = t.matmul(ps_av[:],
                                  v_h[h % 2][:, ktc * 128:(ktc + 1) * 128],
                                  expT[:, ktc * 512:(ktc + 1) * 512],
                                  start=(ktc == 0), stop=False)
                bi.then_inc(sem["te"], 1)
                for jp in range(8):
                    if jp >= 2:
                        wsc(exp_mark[(h, qg, 16 + 2 * (jp - 2))])
                    else:
                        wsc(exp_done_own[(h, qg)])
                    for kk in (16 + 2 * jp, 17 + 2 * jp):
                        bi = t.matmul(
                            ps_sd[jp % 2][:, (kk % 2) * 512:(kk % 2) * 512 + 512],
                            kT_h[h % 2][:, kk * 128:(kk + 1) * 128],
                            qT_h[h % 2][:, qg * 512:(qg + 1) * 512],
                            start=True, stop=True)
                        bi.then_inc(sem["te"], 1)
                for ktc in range(16, 32):
                    if l1:
                        wgp(gp_mark[(h, qg, ktc)])
                    else:
                        wsc(exp_mark[(h, qg, ktc)])
                    bi = t.matmul(ps_l[:], ones128[:],
                                  expT[:, (ktc - 16) * 512:(ktc - 15) * 512],
                                  start=False, stop=(ktc == 31))
                bi.then_inc(sem["te"], 1)
                for ktc in range(16, 32):
                    bi = t.matmul(ps_av[:],
                                  v_h[h % 2][:, ktc * 128:(ktc + 1) * 128],
                                  expT[:, (ktc - 16) * 512:(ktc - 15) * 512],
                                  start=False, stop=(ktc == 31))
                bi.then_inc(sem["te"], 1)
    p.cnt["te"] = te_n

    @b.scalar
    def _(s):
        for h in range(NH):
            for qg in range(NQG):
                pg = prev_group(h, qg)
                n_own = n_own_of[(h, qg)]
                for jp in range(n_own // 2):
                    if jp == 0 and pg is not None:
                        s.wait_ge(sem["te"], av_mark[pg])
                    s.wait_ge(sem["te"], score_mark[(h, qg, 2 * jp + 1)])
                    s.activation(expT[:, 2 * jp * 512:(2 * jp + 2) * 512],
                                 ps_sd[jp % 2][:], AF.Exp, scale=scale)\
                        .then_inc(sem["sc"], 1)
                for jp in range(8):
                    if jp == 0:
                        s.wait_ge(sem["te"], avp0_mark[(h, qg)])
                    s.wait_ge(sem["te"], score_mark[(h, qg, 17 + 2 * jp)])
                    if l1:
                        s.activation(expT[:, 2 * jp * 512:(2 * jp + 2) * 512],
                                     ps_sd[jp % 2][:], AF.Exp, scale=scale,
                                     bias=lnA[:]).then_inc(sem["sc"], 1)
                    else:
                        s.activation(expT[:, 2 * jp * 512:(2 * jp + 2) * 512],
                                     ps_sd[jp % 2][:], AF.Exp, scale=scale)\
                            .then_inc(sem["sc"], 1)
    p.cnt["sc"] = sc_n

    if l1:
        msk1 = p.sb(f"at_m1{name}", [128, 4 * 512], BF16)
        msk0 = p.sb(f"at_m0{name}", [128, 4 * 512], BF16)

        @b.gpsimd
        def _(g):
            g.memset(msk1[:], 1.0).then_inc(sem["gp"], 1)
            g.wait_ge(sem["gp"], GP0 + 1)
            m3 = msk1.ap().rearrange("p (c f) -> p c f", c=4, f=512)
            g.affine_select(out=m3, in_=m3, compare_op=ALU.is_ge, fill=0.0,
                            base=0, pattern=[[-128, 4], [1, 512]],
                            channel_multiplier=-1).then_inc(sem["gp"], 1)
            g.memset(msk0[:], 0.0).then_inc(sem["gp"], 1)
            g.wait_ge(sem["gp"], GP0 + 3)
            m3 = msk0.ap().rearrange("p (c f) -> p c f", c=4, f=512)
            g.affine_select(out=m3, in_=m3, compare_op=ALU.is_ge, fill=1.0,
                            base=0, pattern=[[-128, 4], [1, 512]],
                            channel_multiplier=-1).then_inc(sem["gp"], 1)
            gpc = GP0 + 4
            g.wait_ge(sem["ve"], VE0 + 1)
            for h in range(NH):
                for qg in range(NQG):
                    pg = prev_group(h, qg)
                    k0 = 4 * qg
                    if pg is not None:
                        g.wait_ge(sem["te"], av_mark[pg])
                    if k0 + 4 <= 15:
                        g.memset(expT[:, (k0 + 4) * 512:16 * 512], 1.0)\
                            .then_inc(sem["gp"], 1)
                        gpc += 1
                    g.wait_ge(sem["sc"], exp_mark[(h, qg, k0 + 3)])
                    dg = expT[:, k0 * 512:(k0 + 4) * 512]
                    g.tensor_mul(dg, dg, msk1[:]).then_inc(sem["gp"], 1)
                    gpc += 1
                    g.wait_ge(sem["gp"], gpc)
                    g.tensor_add(dg, dg, msk0[:]).then_inc(sem["gp"], 1)
                    gpc += 1
                    g.wait_ge(sem["sc"], exp_mark[(h, qg, 31)])
                    src3 = expT[:, 0:16 * 512]\
                        .rearrange("p (c f) -> p c f", c=16, f=512)
                    bt3 = B_t.ap().broadcast_to([128, 512, 16])\
                        .rearrange("p f c -> p c f")
                    g.tensor_add(src3, src3, bt3).then_inc(sem["gp"], 1)
                    gpc += 1
            assert gpc == gp_n, (gpc, gp_n)
    p.cnt["gp"] = gp_n

    return p.mark("ve")


def build_attn(l1):
    p = Prog("attn")
    nc = p.nc
    qTh = nc.dram_tensor("qT", [D, T], BF16, kind="ExternalInput").ap()
    kv_own = nc.dram_tensor("kv_own", [2 * T, D], BF16, kind="ExternalInput").ap()
    kv_all = nc.dram_tensor("kv_all", [4 * T, D], BF16, kind="ExternalInput").ap()
    alpha = nc.dram_tensor("alpha", [1, 2], BF16, kind="ExternalInput").ap()
    x = nc.dram_tensor("x", [T, D], F32, kind="ExternalInput").ap()
    w_out = nc.dram_tensor("w_out", [D, D], F32, kind="ExternalInput").ap()
    b_out = None if l1 else \
        nc.dram_tensor("b_out", [1, D], F32, kind="ExternalInput").ap()
    x1 = nc.dram_tensor("x1", [T, D], F32, kind="ExternalOutput").ap()

    AVTn = p.sb("AVTn", [128, NH * T], BF16)
    p.start()
    consts = emit_consts(p)
    scale = L1_SCALE if l1 else L0_SCALE
    avm = emit_attn_core(p, qTh, kv_own, kv_all, alpha, consts, scale, l1, AVTn)
    mk = emit_mm_ln_resid(
        p, lambda ck, tt: AVTn[:, ck * T + tt * 128: ck * T + tt * 128 + 128],
        ND, w_out, True, b_out, x, x1, consts, [avm], 0, NT, name="o")
    emit_final_wait(p, mk)
    return p.finish()


def emit_gu(p, uT, w_guT, h_actT, consts, ready_mark, name="", guard_marks=()):
    _, ones_row, _, cready = consts
    b = p.block
    sem = p.sems
    NI = 44
    wg = [p.sb(f"guwg{name}{i}", [128, ND * 128], BF16) for i in range(2)]
    wu = [p.sb(f"guwu{name}{i}", [128, ND * 128], BF16) for i in range(2)]
    gsil = [p.sb(f"gusil{name}{i}", [128, 512], BF16) for i in range(2)]
    for i in range(2):
        p.sem(f"gug{name}{i}")
        p.sem(f"guu{name}{i}")
    psg = [p.pf[0], p.pf[1]]
    psu = [p.pf[2], p.pf[3]]

    TE = p.cnt["te"]
    SC = p.cnt["sc"]
    VE = p.cnt["ve"]
    gmark, umark = [], []

    @b.gpsimd
    def _(g):
        for mk in guard_marks:
            wge(g, mk)
        for ic in range(NI):
            if ic >= 2:
                g.wait_ge(sem["te"], TE + 2 * (ic - 1))
            else:
                g.wait_ge(sem["te"], TE)
            for (wbuf, marks, semn, col) in (
                    (wg, gmark, f"gug{name}{ic % 2}", ic * 128),
                    (wu, umark, f"guu{name}{ic % 2}", 5632 + ic * 128)):
                srcw = w_guT[:, col:col + 128]\
                    .rearrange("(dc p) f -> p dc f", p=128)
                dstw = wbuf[ic % 2].ap().rearrange("p (dc f) -> p dc f",
                                                   dc=ND, f=128)
                bi = g.dma_start(out=dstw, in_=srcw)
                marks.append(p.inc(bi, semn, 16))

    @b.tensor
    def _(t):
        wge(t, cready)
        wge(t, ready_mark)
        for ic in range(NI):
            wge(t, gmark[ic])
            wge(t, umark[ic])
            if ic >= 2:
                t.wait_ge(sem["sc"], SC + (ic - 2) + 1)
                t.wait_ge(sem["ve"], VE + (ic - 2) + 1)
            else:
                t.wait_ge(sem["sc"], SC)
                t.wait_ge(sem["ve"], VE)
            for dc in range(ND):
                bi = t.matmul(psg[ic % 2][:],
                              wg[ic % 2][:, dc * 128:(dc + 1) * 128],
                              apx(uT)[:, dc * 512:dc * 512 + 512],
                              start=(dc == 0), stop=(dc == ND - 1))
            bi.then_inc(sem["te"], 1)
            for dc in range(ND):
                bi = t.matmul(psu[ic % 2][:],
                              wu[ic % 2][:, dc * 128:(dc + 1) * 128],
                              apx(uT)[:, dc * 512:dc * 512 + 512],
                              start=(dc == 0), stop=(dc == ND - 1))
            bi.then_inc(sem["te"], 1)
    p.cnt["te"] += 2 * NI

    @b.scalar
    def _(s):
        for ic in range(NI):
            s.wait_ge(sem["te"], TE + 2 * ic + 1)
            if ic >= 2:
                s.wait_ge(sem["ve"], VE + (ic - 2) + 1)
            else:
                s.wait_ge(sem["ve"], VE)
            s.activation(gsil[ic % 2][:], psg[ic % 2][:], AF.Silu)\
                .then_inc(sem["sc"], 1)
    p.cnt["sc"] += NI

    @b.vector
    def _(v):
        for mk in guard_marks:
            wge(v, mk)
        v.wait_ge(sem["te"], TE)
        for ic in range(NI):
            v.wait_ge(sem["sc"], SC + ic + 1)
            v.wait_ge(sem["te"], TE + 2 * ic + 2)
            v.tensor_mul(apx(h_actT)[:, ic * 512:(ic + 1) * 512],
                         gsil[ic % 2][:], psu[ic % 2][:]).then_inc(sem["ve"], 1)
    p.cnt["ve"] += NI

    return p.mark("ve")


def emit_dq(p, gup, gus, dnp, dns, guT, dnT, consts):
    W = 1536
    pk = p.sb("dq_pk", [128, W], I32)
    iscr = p.sb("dq_i", [128, W], I32)
    kf = p.sb("dq_kf", [128, W], F32)
    sf = p.sb("dq_sf", [128, W], F32)
    r4 = p.sb("dq_r4", [128, W], F32)
    r6 = p.sb("dq_r6", [128, W], F32)
    m1 = p.sb("dq_m1", [128, W], F32)
    sful = p.sb("dq_s", [128, W], F32)
    gusT = p.sb("dq_gusT", [128, 176], F32)
    dnsT = p.sb("dq_dnsT", [128, 96], F32)
    cm2 = p.sb("dq_cm2", [128, 1], F32)
    cm6 = p.sb("dq_cm6", [128, 1], F32)
    idf = p.sb("dq_idf", [128, 128], F32)
    deq_gu = p.sb("dq_dgu", [128, 88 * 256], BF16)
    deq_dn = p.sb("dq_ddn", [128, 16 * 768], BF16)
    gusf = deq_gu.ap().bitcast(F32)[0:2, 0:11264]
    dnsf = deq_dn.ap().bitcast(F32)[0:6, 0:2048]
    stg = [p.sb(f"dq_stg{i}", [128, 512], BF16) for i in range(2)]
    for s_ in ("dqpk", "dqpk2", "dqsc", "dqsc2"):
        p.sem(s_)

    b = p.block
    sem = p.sems
    psf = p.pf[4]
    pb = p.pb

    lm = {}

    @b.gpsimd
    def _(g):
        bi = g.dma_start(
            out=pk[:, 0:1408].rearrange("p (a w) -> p a w", a=44, w=32),
            in_=gup[0:5632, :].rearrange("(a p) w -> p a w", p=128))
        lm["gu0"] = p.inc(bi, "dqpk", 16)
        bi = g.dma_start(out=gusf, in_=gus)
        lm["gus"] = p.inc(bi, "dqsc", 16)
        bi = g.dma_start(out=dnsf, in_=dns)
        lm["dns"] = p.inc(bi, "dqsc2", 16)
        g.memset(cm2[:], -2.0).then_inc(sem["gp"], 1)
        g.memset(cm6[:], -6.0).then_inc(sem["gp"], 1)
        g.memset(idf[:], 0.0).then_inc(sem["gp"], 1)
        g.wait_ge(sem["gp"], p.cnt["gp"] + 3)
        g.affine_select(out=idf[:], in_=idf[:], compare_op=ALU.not_equal,
                        fill=1.0, base=0, pattern=[[-1, 128]],
                        channel_multiplier=1).then_inc(sem["gp"], 1)
    p.cnt["gp"] += 4
    gp_done = p.mark("gp")

    @b.tensor
    def _(t):
        wge(t, gp_done)
        wge(t, lm["gus"])
        for a in range(88):
            bi = t.transpose(psf[:, 2 * a:2 * a + 2],
                             gusf[:, a * 128:(a + 1) * 128], idf[0:2, 0:2])
        bi.then_inc(sem["te"], 1)
    p.cnt["te"] += 1
    tm_gus = p.mark("te")

    @b.vector
    def _(v):
        wge(v, tm_gus)
        v.tensor_copy(gusT[:], psf[:, 0:176]).then_inc(sem["ve"], 1)
    p.cnt["ve"] += 1
    gusT_done = p.mark("ve")

    @b.tensor
    def _(t):
        wge(t, gusT_done)
        wge(t, lm["dns"])
        for a in range(16):
            bi = t.transpose(psf[:, 6 * a:6 * a + 6],
                             dnsf[:, a * 128:(a + 1) * 128], idf[0:6, 0:6])
        bi.then_inc(sem["te"], 1)
    p.cnt["te"] += 1
    tm_dns = p.mark("te")

    @b.vector
    def _(v):
        wge(v, tm_dns)
        v.tensor_copy(dnsT[:], psf[:, 0:96]).then_inc(sem["ve"], 1)
    p.cnt["ve"] += 1

    def emit_dq_planes(nwords, na, wper, scT, spere, deq, hl_of_w, lm_key):
        wsub = wper // spere
        VE0 = p.cnt["ve"]

        @b.vector
        def _(v):
            v.wait_ge(sem["ve"], VE0)
            srcS = apx(scT).rearrange("p (a s) -> p a s", a=na, s=spere)\
                .broadcast_to([128, na, spere, wsub])
            dstS = sful[:, 0:nwords].rearrange("p (a s w) -> p a s w",
                                               a=na, s=spere, w=wsub)
            v.tensor_copy(dstS, srcS).then_inc(sem["ve"], 1)
        p.cnt["ve"] += 1

        for j in range(8):
            VE = p.cnt["ve"]
            SC = p.cnt["sc"]

            @b.vector
            def _(v):
                wge(v, lm[lm_key])
                v.wait_ge(sem["ve"], VE)
                v.tensor_scalar(iscr[:, 0:nwords], pk[:, 0:nwords], 4 * j, 7,
                                ALU.logical_shift_right, ALU.bitwise_and)\
                    .then_inc(sem["ve"], 1)
                v.wait_ge(sem["ve"], VE + 1)
                v.tensor_copy(kf[:, 0:nwords], iscr[:, 0:nwords])\
                    .then_inc(sem["ve"], 1)
                v.wait_ge(sem["ve"], VE + 2)
                v.tensor_scalar(iscr[:, 0:nwords], pk[:, 0:nwords], 4 * j, 8,
                                ALU.logical_shift_right, ALU.bitwise_and)\
                    .then_inc(sem["ve"], 1)
                v.wait_ge(sem["ve"], VE + 3)
                v.tensor_copy(sf[:, 0:nwords], iscr[:, 0:nwords])\
                    .then_inc(sem["ve"], 1)
                v.wait_ge(sem["ve"], VE + 4)
                v.tensor_scalar(m1[:, 0:nwords], sf[:, 0:nwords], -0.25, 1.0,
                                ALU.mult, ALU.add).then_inc(sem["ve"], 1)
                v.wait_ge(sem["ve"], VE + 5)
                v.wait_ge(sem["sc"], SC + 2)
                v.scalar_tensor_tensor(sf[:, 0:nwords], kf[:, 0:nwords], 0.5,
                                       r4[:, 0:nwords], op0=ALU.mult,
                                       op1=ALU.add).then_inc(sem["ve"], 1)
                v.wait_ge(sem["ve"], VE + 6)
                v.tensor_add(kf[:, 0:nwords], sf[:, 0:nwords], r6[:, 0:nwords])\
                    .then_inc(sem["ve"], 1)
                v.wait_ge(sem["ve"], VE + 7)
                v.tensor_mul(r4[:, 0:nwords], kf[:, 0:nwords], m1[:, 0:nwords])\
                    .then_inc(sem["ve"], 1)
                v.wait_ge(sem["ve"], VE + 8)
                dstp = apx(deq).rearrange("p (a w) -> p a w", a=na, w=8 * wper)[
                    :, :, hl_of_w(j)::8]
                srcp = r4[:, 0:nwords].rearrange("p (a w) -> p a w", a=na, w=wper)
                srcs = sful[:, 0:nwords].rearrange("p (a w) -> p a w", a=na, w=wper)
                v.tensor_tensor(dstp, srcp, srcs, op=ALU.mult)\
                    .then_inc(sem["ve"], 1)
            p.cnt["ve"] += 9

            @b.scalar
            def _(s):
                s.wait_ge(sem["ve"], VE + 2)
                if j > 0:
                    s.wait_ge(sem["ve"], VE)
                wge(s, gp_done)
                s.activation(r4[:, 0:nwords], kf[:, 0:nwords], AF.Relu,
                             scale=0.5, bias=cm2[:]).then_inc(sem["sc"], 1)
                s.activation(r6[:, 0:nwords], kf[:, 0:nwords], AF.Relu,
                             bias=cm6[:]).then_inc(sem["sc"], 1)
            p.cnt["sc"] += 2

    emit_dq_planes(1408, 44, 32, gusT[:, 0:88],
                   2, deq_gu[:, 0:44 * 256], lambda j: j, "gu0")
    gu0_done = p.mark("ve")

    @b.gpsimd
    def _(g):
        wge(g, gu0_done)
        bi = g.dma_start(
            out=pk[:, 0:1408].rearrange("p (a w) -> p a w", a=44, w=32),
            in_=gup[5632:11264, :].rearrange("(a p) w -> p a w", p=128))
        lm["gu1"] = p.inc(bi, "dqpk", 16)
    emit_dq_planes(1408, 44, 32, gusT[:, 88:176],
                   2, deq_gu[:, 44 * 256:88 * 256], lambda j: j, "gu1")
    guplanes_done = p.mark("ve")

    @b.gpsimd
    def _(g):
        wge(g, guplanes_done)
        bi = g.dma_start(out=pk[:, 0:1536].rearrange("p (a w) -> p a w",
                                                     a=16, w=96),
                         in_=dnp.rearrange("(a p) w -> p a w", p=128))
        lm["dn"] = p.inc(bi, "dqpk2", 16)
    emit_dq_planes(1536, 16, 96, dnsT, 6, deq_dn, lambda j: j, "dn")

    def emit_store_T(deq, ntile_a, per_a, nsub, out_hbm, stsem, base_mark,
                     guard_marks=()):
        TE = p.cnt["te"]
        VE = p.cnt["ve"]
        ssl = [f"{stsem}a", f"{stsem}b"]
        for s_ in ssl:
            p.sem(s_)
        ST0 = [p.cnt[s_] for s_ in ssl]
        nbatch = ntile_a // 4
        ops = []
        for sub in range(nsub):
            for ba in range(nbatch):
                ops.append((sub, ba))

        @b.tensor
        def _(t):
            wge(t, base_mark)
            for n_, (sub, ba) in enumerate(ops):
                if n_ >= 2:
                    t.wait_ge(sem["ve"], VE + (n_ - 2) + 1)
                for q in range(4):
                    a = 4 * ba + q
                    bi = t.transpose(
                        pb[n_ % 2][:, q * 128:(q + 1) * 128],
                        apx(deq)[:, a * per_a + sub * 128: a * per_a + sub * 128 + 128],
                        consts[0][:])
                bi.then_inc(sem["te"], 1)
        p.cnt["te"] += len(ops)

        @b.vector
        def _(v):
            for n_, (sub, ba) in enumerate(ops):
                v.wait_ge(sem["te"], TE + n_ + 1)
                if n_ >= 2:
                    v.wait_ge(p.sems[ssl[n_ % 2]], ST0[n_ % 2] + 16 * (n_ // 2))
                else:
                    for mk in guard_marks:
                        wge(v, mk)
                v.tensor_copy(stg[n_ % 2][:], pb[n_ % 2][:]).then_inc(sem["ve"], 1)
        p.cnt["ve"] += len(ops)

        @b.sync
        def _(sync):
            for n_, (sub, ba) in enumerate(ops):
                sync.wait_ge(sem["ve"], VE + n_ + 1)
                bi = sync.dma_start(
                    out=out_hbm[sub * 128:(sub + 1) * 128,
                                ba * 512:(ba + 1) * 512],
                    in_=stg[n_ % 2][:])
                p.inc(bi, ssl[n_ % 2], 16)

    emit_store_T(deq_gu, 88, 256, 2, guT, "dqst0", p.mark("ve"))
    emit_store_T(deq_dn, 16, 768, 6, dnT, "dqst1", p.mark("ve"),
                 guard_marks=[p.mark("dqst0a"), p.mark("dqst0b")])
    return [p.mark("dqst0a"), p.mark("dqst0b"),
            p.mark("dqst1a"), p.mark("dqst1b")]




def build_dq():
    p = Prog("dq")
    nc = p.nc
    gup = nc.dram_tensor("gup_sh", [11264, 32], I32, kind="ExternalInput").ap()
    gus = nc.dram_tensor("gus_sh", [2, 11264], F16, kind="ExternalInput").ap()
    dnp = nc.dram_tensor("dnp_sh", [2048, 96], I32, kind="ExternalInput").ap()
    dns = nc.dram_tensor("dns_sh", [6, 2048], F16, kind="ExternalInput").ap()
    guT = nc.dram_tensor("guT_sh", [256, 11264], BF16, kind="ExternalOutput").ap()
    dnT = nc.dram_tensor("dnT_sh", [768, 2048], BF16, kind="ExternalOutput").ap()
    p.start()
    consts = emit_consts(p)
    mks = emit_dq(p, gup, gus, dnp, dns, guT, dnT, consts)
    emit_final_wait(p, mks)
    return p.finish()


def build_ffn():
    p = Prog("ffn")
    nc = p.nc
    x = nc.dram_tensor("x", [T, D], F32, kind="ExternalInput").ap()
    w_guT = nc.dram_tensor("w_guT", [D, 11264], BF16, kind="ExternalInput").ap()
    w_downT = nc.dram_tensor("w_downT", [5632, D], BF16, kind="ExternalInput").ap()
    x2 = nc.dram_tensor("x2", [T, D], F32, kind="ExternalOutput").ap()

    uT = p.sb("uTf", [128, ND * 512], BF16)
    h_actT = p.sb("hactT", [128, 44 * 512], BF16)
    p.start()
    consts = emit_consts(p)
    for tg in range(4):
        ln = emit_ln_uT(p, x, uT, 4 * tg, 4, consts, 512, name="f")
        gum = emit_gu(p, uT, w_guT, h_actT, consts, ln.all_done, name="f")
        emit_mm_ln_resid(
            p, lambda ck, tt: h_actT[:, ck * 512 + tt * 128: ck * 512 + tt * 128 + 128],
            44, w_downT, False, None, x, x2, consts, [gum], 4 * tg, 4, name="d")
    emit_final_wait(p, [p.mark("mlstd0"), p.mark("mlstd1")])
    return p.finish()


def emit_mm_ln_resid(p, lhsT_ap, nck, w_hbm, cast_w, b_hbm, x_hbm, out_hbm,
                     consts, ready_marks, t0, ntl, name="", guard_marks=(),
                     extra_mms=None):
    _, ones_row, _, cready = consts
    b = p.block
    sem = p.sems
    ngr = ntl // 2
    wtile = [p.sb(f"mlw{name}{i}", [128, 512], BF16) for i in range(2)]
    y_s = [p.sb(f"mly{name}{i}", [128, D], F32) for i in range(2)]
    o_s = [p.sb(f"mlo{name}{i}", [128, D], F32) for i in range(2)]
    xt = [p.sb(f"mlx{name}{i}", [128, D], F32) for i in range(2)]
    sums4 = [p.sb(f"mls4{name}{i}", [128, 4], F32) for i in range(2)]
    tot = [p.sb(f"mltot{name}{i}", [128, 1], F32) for i in range(2)]
    ssq = [p.sb(f"mlssq{name}{i}", [128, 1], F32) for i in range(2)]
    mean = [p.sb(f"mlmean{name}{i}", [128, 1], F32) for i in range(2)]
    msq = [p.sb(f"mlmsq{name}{i}", [128, 1], F32) for i in range(2)]
    var = [p.sb(f"mlvar{name}{i}", [128, 1], F32) for i in range(2)]
    std = [p.sb(f"mlstd{name}{i}", [128, 1], F32) for i in range(2)]
    rstd = [p.sb(f"mlrstd{name}{i}", [128, 1], F32) for i in range(2)]
    negmr = [p.sb(f"mlnegmr{name}{i}", [128, 1], F32) for i in range(2)]
    inst = getattr(p, "_ml_inst", 0)
    p._ml_inst = inst + 1
    wl = [f"mll{name}{inst}{i}" for i in range(2)]
    for i in range(2):
        p.sem(wl[i])
        p.sem(f"mlxl{name}{i}")
        p.sem(f"mlst{name}{i}")
    p.sem(f"mlb{name}")
    brow = p.sb(f"mlb{name}", [1, D], BF16) if b_hbm is not None else None
    pt = p.pf[0:2]

    TE = p.cnt["te"]
    SC = p.cnt["sc"]
    VE = p.cnt["ve"]
    STB = [p.cnt[f"mlst{name}{i}"] for i in range(2)]
    XLB = [p.cnt[f"mlxl{name}{i}"] for i in range(2)]

    def evac_sc(ttg, fg, i):
        return SC + 12 * ttg + 2 * fg + i + 1

    def sqrt_sc(ttg, i):
        return SC + 12 * ttg + 8 + 2 * i + 1

    def norm_sc(ttg, i):
        return SC + 12 * ttg + 8 + 2 * i + 2

    def ve_tile(ttg, i, k):
        return VE + 18 * ttg + 9 * i + k

    wmark = []
    bmark = []
    xmark = []

    @b.gpsimd
    def _(g):
        if b_hbm is not None:
            g.wait_ge(sem["te"], TE)
            bi = g.dma_start(out=brow[:], in_=b_hbm[:, 0:D])
            bmark.append(p.inc(bi, f"mlb{name}", 16))
        for ttg in range(ngr):
            for fg in range(4):
                for ck in range(nck):
                    L = (ttg * 4 + fg) * nck + ck
                    if L >= 2:
                        g.wait_ge(sem["te"], TE + L - 1)
                    else:
                        g.wait_ge(sem["te"], TE)
                    bi = g.dma_start(out=wtile[L % 2][:],
                                     in_=w_hbm[ck * 128:(ck + 1) * 128,
                                               fg * 512:(fg + 1) * 512])
                    wmark.append(p.inc(bi, wl[L % 2], 16))

    @b.tensor
    def _(t):
        wge(t, cready)
        for mk in ready_marks:
            wge(t, mk)
        if bmark:
            wge(t, bmark[0])
        for ttg in range(ngr):
            for fg in range(4):
                for ck in range(nck):
                    L = (ttg * 4 + fg) * nck + ck
                    wge(t, wmark[L])
                    for i in range(2):
                        tt = 2 * ttg + i
                        if ck == 0:
                            if fg > 0:
                                t.wait_ge(sem["sc"], evac_sc(ttg, fg - 1, i))
                            elif ttg > 0:
                                t.wait_ge(sem["sc"], evac_sc(ttg - 1, 3, i))
                            else:
                                t.wait_ge(sem["sc"], SC)
                            if b_hbm is not None:
                                t.matmul(pt[i][:], ones_row[0:1, 0:128],
                                         brow[0:1, fg * 512:(fg + 1) * 512],
                                         start=True, stop=False)
                        bi = t.matmul(pt[i][:], lhsT_ap(ck, tt), wtile[L % 2][:],
                                      start=(ck == 0 and b_hbm is None),
                                      stop=(ck == nck - 1 and extra_mms is None))
                        if ck == nck - 1 and extra_mms is not None:
                            bi = extra_mms(t, pt[i], tt, fg)
                    bi.then_inc(sem["te"], 1)
    p.cnt["te"] += ngr * 4 * nck

    @b.scalar
    def _(s):
        for ttg in range(ngr):
            for fg in range(4):
                s.wait_ge(sem["te"], TE + (ttg * 4 + fg + 1) * nck)
                for i in range(2):
                    if ttg >= 1 and fg == 0:
                        s.wait_ge(sem["ve"], ve_tile(ttg - 1, i, 9))
                    elif fg == 0:
                        s.wait_ge(sem["ve"], VE)
                    s.activation(y_s[i][:, fg * 512:(fg + 1) * 512], pt[i][:],
                                 AF.Identity,
                                 accum_out=sums4[i][:, fg:fg + 1])\
                        .then_inc(sem["sc"], 1)
            for i in range(2):
                s.wait_ge(sem["ve"], ve_tile(ttg, i, 6))
                s.activation(std[i][:], var[i][:], AF.Sqrt).then_inc(sem["sc"], 1)
                s.wait_ge(sem["ve"], ve_tile(ttg, i, 8))
                if ttg >= 1:
                    s.wait_ge(p.sems[f"mlst{name}{i}"], STB[i] + 16 * ttg)
                else:
                    s.wait_ge(p.sems[f"mlst{name}{i}"], STB[i])
                s.activation(o_s[i][:], y_s[i][:], AF.Identity,
                             bias=negmr[i][:], scale=rstd[i][:])\
                    .then_inc(sem["sc"], 1)
    p.cnt["sc"] += ngr * 12

    @b.vector
    def _(v):
        for ttg in range(ngr):
            for i in range(2):
                tt = 2 * ttg + i
                v.wait_ge(sem["sc"], evac_sc(ttg, 3, i))
                v.tensor_reduce(tot[i][:], sums4[i][:], axis=AX.X, op=ALU.add)\
                    .then_inc(sem["ve"], 1)
                if ttg >= 1:
                    v.wait_ge(p.sems[f"mlst{name}{i}"], STB[i] + 16 * ttg)
                else:
                    v.wait_ge(p.sems[f"mlst{name}{i}"], STB[i])
                v.scalar_tensor_tensor(o_s[i][:], y_s[i][:], 0.0, y_s[i][:],
                                       op0=ALU.add, op1=ALU.mult,
                                       accum_out=ssq[i][:]).then_inc(sem["ve"], 1)
                v.wait_ge(sem["ve"], ve_tile(ttg, i, 2))
                v.tensor_scalar_mul(mean[i][:], tot[i][:], 1.0 / D)\
                    .then_inc(sem["ve"], 1)
                v.wait_ge(sem["ve"], ve_tile(ttg, i, 3))
                v.scalar_tensor_tensor(msq[i][:], mean[i][:], 0.0, mean[i][:],
                                       op0=ALU.add, op1=ALU.mult)\
                    .then_inc(sem["ve"], 1)
                v.wait_ge(sem["ve"], ve_tile(ttg, i, 4))
                v.scalar_tensor_tensor(var[i][:], ssq[i][:], 1.0 / D, msq[i][:],
                                       op0=ALU.mult, op1=ALU.subtract)\
                    .then_inc(sem["ve"], 1)
                v.wait_ge(sem["ve"], ve_tile(ttg, i, 5))
                v.tensor_scalar_add(var[i][:], var[i][:], EPS)\
                    .then_inc(sem["ve"], 1)
                v.wait_ge(sem["sc"], sqrt_sc(ttg, i))
                v.reciprocal(rstd[i][:], std[i][:]).then_inc(sem["ve"], 1)
                v.wait_ge(sem["ve"], ve_tile(ttg, i, 7))
                v.scalar_tensor_tensor(negmr[i][:], mean[i][:], -1.0, rstd[i][:],
                                       op0=ALU.mult, op1=ALU.mult)\
                    .then_inc(sem["ve"], 1)
                v.wait_ge(sem["sc"], norm_sc(ttg, i))
                v.wait_ge(p.sems[f"mlxl{name}{i}"], XLB[i] + 16 * (ttg + 1))
                v.tensor_add(o_s[i][:], o_s[i][:], xt[i][:]).then_inc(sem["ve"], 1)
    p.cnt["ve"] += ngr * 18

    @b.sync
    def _(sync):
        for mk in guard_marks:
            wge(sync, mk)

        def store(ttg):
            for i in range(2):
                tt = 2 * ttg + i
                sync.wait_ge(sem["ve"], ve_tile(ttg, i, 9))
                bi = sync.dma_start(
                    out=out_hbm[(t0 + tt) * 128:(t0 + tt + 1) * 128, :],
                    in_=o_s[i][:])
                p.inc(bi, f"mlst{name}{i}", 16)

        for ttg in range(ngr):
            for i in range(2):
                tt = 2 * ttg + i
                if ttg >= 1:
                    sync.wait_ge(sem["ve"], ve_tile(ttg - 1, i, 9))
                else:
                    sync.wait_ge(sem["ve"], VE)
                bi = sync.dma_start(
                    out=xt[i][:],
                    in_=x_hbm[(t0 + tt) * 128:(t0 + tt + 1) * 128, :])
                xmark.append(p.inc(bi, f"mlxl{name}{i}", 16))
            if ttg >= 1:
                store(ttg - 1)
        store(ngr - 1)
    return [p.mark(f"mlst{name}0"), p.mark(f"mlst{name}1")]


def build_moe():
    p = Prog("moe")
    nc = p.nc
    x = nc.dram_tensor("x", [T, D], F32, kind="ExternalInput").ap()
    Wb = nc.dram_tensor("W_base", [D, D], F32, kind="ExternalInput").ap()
    bb = nc.dram_tensor("b_base", [1, D], F32, kind="ExternalInput").ap()
    As = nc.dram_tensor("A_s", [D, 16], F32, kind="ExternalInput").ap()
    Bs = nc.dram_tensor("B_s", [16, D], F32, kind="ExternalInput").ap()
    Ar = nc.dram_tensor("A_r_flat", [D, 96], F32, kind="ExternalInput").ap()
    Br = nc.dram_tensor("B_r_flat", [96, D], F32, kind="ExternalInput").ap()
    Wr = nc.dram_tensor("W_router", [D, 6], F32, kind="ExternalInput").ap()
    br = nc.dram_tensor("b_router", [1, 6], F32, kind="ExternalInput").ap()
    x4 = nc.dram_tensor("x4", [T, D], F32, kind="ExternalOutput").ap()

    HT = 8
    TSP = HT * 128
    uT = p.sb("uTm", [128, ND * TSP], BF16)
    Wb_bf = p.sb("m_wb", [128, ND * D], BF16)
    Ar_bf = p.sb("m_ar", [128, ND * 96], BF16)
    As_bf = p.sb("m_as", [128, ND * 16], BF16)
    Wr_bf = p.sb("m_wr", [128, ND * 6], BF16)
    Bs_bf = p.sb("m_bs", [16, D], BF16)
    Br_bf = p.sb("m_br", [96, D], BF16)
    bb_bf = p.sb("m_bb", [1, D], BF16)
    br_bf = p.sb("m_brr", [1, 6], BF16)
    E16 = p.sb("m_e16", [6, 96], BF16)
    xaT = p.sb("m_xa", [16, 128], BF16)
    hT = p.sb("m_h", [96, 128], BF16)
    hwT = p.sb("m_hw", [96, 128], BF16)
    wtT_s = p.sb("m_wtt", [6, 128], BF16)
    e_s = p.sb("m_es", [128, 6], F32)
    g_s = p.sb("m_gs", [128, 6], F32)
    g2_s = p.sb("m_g2", [128, 6], F32)
    kp_s = p.sb("m_kp", [128, 6], F32)
    wt_s = p.sb("m_wt", [128, 6], F32)
    wt_bf = p.sb("m_wtb", [128, 6], BF16)
    gmax = p.sb("m_gm", [128, 1], F32)
    negmax = p.sb("m_ngm", [128, 1], F32)
    esum = p.sb("m_esm", [128, 1], F32)
    er = p.sb("m_er", [128, 1], F32)
    m1s = p.sb("m_m1", [128, 1], F32)
    m2s = p.sb("m_m2", [128, 1], F32)
    y_s = [p.sb(f"m_y{i}", [128, D], F32) for i in range(2)]
    o_s = [p.sb(f"m_o{i}", [128, D], F32) for i in range(2)]
    xt = [p.sb(f"m_x{i}", [128, D], F32) for i in range(2)]
    sums4 = [p.sb(f"m_s4{i}", [128, 4], F32) for i in range(2)]
    tot = [p.sb(f"m_tot{i}", [128, 1], F32) for i in range(2)]
    ssq = [p.sb(f"m_ssq{i}", [128, 1], F32) for i in range(2)]
    meanb = [p.sb(f"m_mean{i}", [128, 1], F32) for i in range(2)]
    msqb = [p.sb(f"m_msq{i}", [128, 1], F32) for i in range(2)]
    varb = [p.sb(f"m_var{i}", [128, 1], F32) for i in range(2)]
    stdb = [p.sb(f"m_std{i}", [128, 1], F32) for i in range(2)]
    rstdb = [p.sb(f"m_rstd{i}", [128, 1], F32) for i in range(2)]
    negmrb = [p.sb(f"m_negmr{i}", [128, 1], F32) for i in range(2)]
    for s_ in ("mwl", "mxl0", "mxl1", "mst0", "mst1"):
        p.sem(s_)

    p.start()
    consts = emit_consts(p)
    ident, ones_row, ones128, cready = consts
    b = p.block
    sem = p.sems
    GP = p.cnt["gp"]

    lm = {}

    @b.gpsimd
    def _(g):
        def ld(dst3, src3, key):
            bi = g.dma_start(out=dst3, in_=src3)
            lm[key] = p.inc(bi, "mwl", 16)
        ld(Wb_bf.ap().rearrange("p (dc f) -> p dc f", dc=ND, f=D),
           Wb.rearrange("(dc p) f -> p dc f", p=128), "wb")
        ld(Ar_bf.ap().rearrange("p (dc f) -> p dc f", dc=ND, f=96),
           Ar.rearrange("(dc p) f -> p dc f", p=128), "ar")
        ld(As_bf.ap().rearrange("p (dc f) -> p dc f", dc=ND, f=16),
           As.rearrange("(dc p) f -> p dc f", p=128), "as")
        ld(Wr_bf.ap().rearrange("p (dc f) -> p dc f", dc=ND, f=6),
           Wr.rearrange("(dc p) f -> p dc f", p=128), "wr")
        ld(Bs_bf[:], Bs, "bs")
        ld(Br_bf[:], Br, "br")
        ld(bb_bf[:], bb, "bb")
        ld(br_bf[:], br, "brr")
        g.memset(E16[:], 1.0).then_inc(sem["gp"], 1)
        g.wait_ge(sem["gp"], GP + 1)
        g.affine_select(out=E16[:], in_=E16[:], compare_op=ALU.is_ge, fill=0.0,
                        base=0, pattern=[[1, 96]], channel_multiplier=-16)\
            .then_inc(sem["gp"], 1)
        g.wait_ge(sem["gp"], GP + 2)
        g.affine_select(out=E16[:], in_=E16[:], compare_op=ALU.is_ge, fill=0.0,
                        base=15, pattern=[[-1, 96]], channel_multiplier=16)\
            .then_inc(sem["gp"], 1)
    p.cnt["gp"] += 3
    wready = p.mark("mwl")
    e16_done = p.mark("gp")

    ps_xa = p.pf[4][0:16, 0:128]
    ps_h = p.pf[5][0:96, 0:128]
    ps_g = p.pf[2][0:128, 0:6]
    ps_wb = p.pf[3][0:96, 0:128]
    ps_wt = p.pb[0][0:6, 0:128]
    ps_y = [p.pf[0], p.pf[1]]

    def emit_half(half):
        ln = emit_ln_uT(p, x, uT, HT * half, HT, consts, TSP, name="m")
        TE = p.cnt["te"]
        SC = p.cnt["sc"]
        VE = p.cnt["ve"]
        MSTB = [p.cnt[f"mst{i}"] for i in range(2)]
        MXLB = [p.cnt[f"mxl{i}"] for i in range(2)]

        def te_m(tt, k):
            return TE + 9 * tt + k

        def sc_m(tt, k):
            return SC + 10 * tt + k

        def ve_m(tt, k):
            return VE + 21 * tt + k

        @b.tensor
        def _(t):
            wge(t, cready)
            wge(t, wready)
            wge(t, e16_done)
            for tt in range(HT):
                wge(t, ln.tile_done(tt))
                ut = lambda dc: uT[:, dc * TSP + tt * 128: dc * TSP + tt * 128 + 128]
                if tt >= 1:
                    t.wait_ge(sem["sc"], sc_m(tt - 1, 1))
                else:
                    t.wait_ge(sem["sc"], SC)
                for dc in range(ND):
                    bi = t.matmul(ps_xa, As_bf[:, dc * 16:(dc + 1) * 16], ut(dc),
                                  start=(dc == 0), stop=(dc == ND - 1))
                bi.then_inc(sem["te"], 1)
                if tt >= 1:
                    t.wait_ge(sem["sc"], sc_m(tt - 1, 2))
                for dc in range(ND):
                    bi = t.matmul(ps_h, Ar_bf[:, dc * 96:(dc + 1) * 96], ut(dc),
                                  start=(dc == 0), stop=(dc == ND - 1))
                bi.then_inc(sem["te"], 1)
                if tt >= 1:
                    t.wait_ge(sem["sc"], sc_m(tt - 1, 3))
                    t.wait_ge(sem["ve"], ve_m(tt - 1, 4))
                else:
                    t.wait_ge(sem["ve"], VE)
                t.matmul(ps_g, ones_row[0:1, 0:128], br_bf[0:1, :],
                         start=True, stop=False)
                for dc in range(ND):
                    bi = t.matmul(ps_g, ut(dc), Wr_bf[:, dc * 6:(dc + 1) * 6],
                                  start=False, stop=(dc == ND - 1))
                bi.then_inc(sem["te"], 1)
                t.wait_ge(sem["ve"], ve_m(tt, 11))
                if tt >= 1:
                    t.wait_ge(sem["sc"], sc_m(tt - 1, 4))
                t.transpose(ps_wt, wt_bf[:], ident[:]).then_inc(sem["te"], 1)
                t.wait_ge(sem["sc"], sc_m(tt, 4))
                if tt >= 1:
                    t.wait_ge(sem["ve"], ve_m(tt - 1, 12))
                t.matmul(ps_wb, E16[:], wtT_s[:], start=True, stop=True)\
                    .then_inc(sem["te"], 1)
                t.wait_ge(sem["sc"], sc_m(tt, 1))
                t.wait_ge(sem["ve"], ve_m(tt, 12))
                for fg in range(4):
                    if fg >= 1:
                        t.wait_ge(sem["sc"], sc_m(tt, 4 + fg))
                    elif tt >= 2:
                        t.wait_ge(sem["sc"], sc_m(tt - 2, 8))
                    else:
                        t.wait_ge(sem["sc"], SC)
                    t.matmul(ps_y[tt % 2][:], ones_row[0:1, 0:128],
                             bb_bf[0:1, fg * 512:(fg + 1) * 512],
                             start=True, stop=False)
                    for dc in range(ND):
                        t.matmul(ps_y[tt % 2][:], ut(dc),
                                 Wb_bf[:, dc * D + fg * 512: dc * D + fg * 512 + 512],
                                 start=False, stop=False)
                    t.matmul(ps_y[tt % 2][:], xaT[:],
                             Bs_bf[:, fg * 512:(fg + 1) * 512],
                             start=False, stop=False)
                    bi = t.matmul(ps_y[tt % 2][:], hwT[:],
                                  Br_bf[:, fg * 512:(fg + 1) * 512],
                                  start=False, stop=True)
                    bi.then_inc(sem["te"], 1)
        p.cnt["te"] += 9 * HT

        @b.scalar
        def _(s):
            for tt in range(HT):
                i = tt % 2
                s.wait_ge(sem["te"], te_m(tt, 1))
                if tt >= 1:
                    s.wait_ge(sem["te"], te_m(tt - 1, 9))
                s.activation(xaT[:], ps_xa, AF.Identity).then_inc(sem["sc"], 1)
                s.wait_ge(sem["te"], te_m(tt, 2))
                if tt >= 1:
                    s.wait_ge(sem["ve"], ve_m(tt - 1, 12))
                s.activation(hT[:], ps_h, AF.Identity).then_inc(sem["sc"], 1)
                s.wait_ge(sem["te"], te_m(tt, 3))
                s.wait_ge(sem["ve"], ve_m(tt, 2))
                if tt >= 1:
                    s.wait_ge(sem["ve"], ve_m(tt - 1, 4))
                s.activation(e_s[:], ps_g, AF.Exp, bias=negmax[:],
                             accum_out=esum[:]).then_inc(sem["sc"], 1)
                s.wait_ge(sem["te"], te_m(tt, 4))
                if tt >= 1:
                    s.wait_ge(sem["te"], te_m(tt - 1, 5))
                s.activation(wtT_s[:], ps_wt, AF.Identity).then_inc(sem["sc"], 1)
                for fg in range(4):
                    s.wait_ge(sem["te"], te_m(tt, 6 + fg))
                    if tt >= 2 and fg == 0:
                        s.wait_ge(sem["ve"], ve_m(tt - 2, 14))
                    elif fg == 0:
                        s.wait_ge(sem["ve"], VE)
                    s.activation(y_s[i][:, fg * 512:(fg + 1) * 512], ps_y[i][:],
                                 AF.Identity, accum_out=sums4[i][:, fg:fg + 1])\
                        .then_inc(sem["sc"], 1)
                s.wait_ge(sem["ve"], ve_m(tt, 18))
                s.activation(stdb[i][:], varb[i][:], AF.Sqrt).then_inc(sem["sc"], 1)
                s.wait_ge(sem["ve"], ve_m(tt, 20))
                if tt >= 2:
                    s.wait_ge(p.sems[f"mst{i}"], MSTB[i] + 16 * (tt // 2))
                else:
                    s.wait_ge(p.sems[f"mst{i}"], MSTB[i])
                s.activation(o_s[i][:], y_s[i][:], AF.Identity,
                             bias=negmrb[i][:], scale=rstdb[i][:])\
                    .then_inc(sem["sc"], 1)
        p.cnt["sc"] += 10 * HT

        @b.vector
        def _(v):
            for tt in range(HT):
                i = tt % 2
                v.wait_ge(sem["te"], te_m(tt, 3))
                if tt >= 1:
                    v.wait_ge(sem["ve"], ve_m(tt - 1, 2))
                else:
                    v.wait_ge(sem["ve"], VE)
                v.reduce_max(gmax[:], ps_g, axis=AX.X).then_inc(sem["ve"], 1)
                v.wait_ge(sem["ve"], ve_m(tt, 1))
                if tt >= 1:
                    v.wait_ge(sem["sc"], sc_m(tt - 1, 3))
                v.tensor_scalar_mul(negmax[:], gmax[:], -1.0).then_inc(sem["ve"], 1)
                v.wait_ge(sem["sc"], sc_m(tt, 3))
                v.reciprocal(er[:], esum[:]).then_inc(sem["ve"], 1)
                v.wait_ge(sem["ve"], ve_m(tt, 3))
                v.tensor_scalar_mul(g_s[:], e_s[:], er[:, 0:1]).then_inc(sem["ve"], 1)
                v.wait_ge(sem["ve"], ve_m(tt, 4))
                v.reduce_max(m1s[:], g_s[:], axis=AX.X).then_inc(sem["ve"], 1)
                v.wait_ge(sem["ve"], ve_m(tt, 5))
                v.tensor_single_scalar(g2_s[:], g_s[:], m1s[:, 0:1], ALU.is_equal)\
                    .then_inc(sem["ve"], 1)
                v.wait_ge(sem["ve"], ve_m(tt, 6))
                v.scalar_tensor_tensor(g2_s[:], g2_s[:], -1e30, g_s[:],
                                       op0=ALU.mult, op1=ALU.add)\
                    .then_inc(sem["ve"], 1)
                v.wait_ge(sem["ve"], ve_m(tt, 7))
                v.reduce_max(m2s[:], g2_s[:], axis=AX.X).then_inc(sem["ve"], 1)
                v.wait_ge(sem["ve"], ve_m(tt, 8))
                v.tensor_single_scalar(kp_s[:], g_s[:], m2s[:, 0:1], ALU.is_ge)\
                    .then_inc(sem["ve"], 1)
                v.wait_ge(sem["ve"], ve_m(tt, 9))
                v.tensor_mul(wt_s[:], g_s[:], kp_s[:]).then_inc(sem["ve"], 1)
                v.wait_ge(sem["ve"], ve_m(tt, 10))
                if tt >= 1:
                    v.wait_ge(sem["te"], te_m(tt - 1, 4))
                else:
                    v.wait_ge(sem["te"], TE)
                v.tensor_copy(wt_bf[:], wt_s[:]).then_inc(sem["ve"], 1)
                v.wait_ge(sem["te"], te_m(tt, 5))
                v.wait_ge(sem["sc"], sc_m(tt, 2))
                if tt >= 1:
                    v.wait_ge(sem["te"], te_m(tt - 1, 9))
                v.tensor_mul(hwT[:], hT[:], ps_wb).then_inc(sem["ve"], 1)
                v.wait_ge(sem["sc"], sc_m(tt, 8))
                v.tensor_reduce(tot[i][:], sums4[i][:], axis=AX.X, op=ALU.add)\
                    .then_inc(sem["ve"], 1)
                if tt >= 2:
                    v.wait_ge(p.sems[f"mst{i}"], MSTB[i] + 16 * (tt // 2))
                else:
                    v.wait_ge(p.sems[f"mst{i}"], MSTB[i])
                v.scalar_tensor_tensor(o_s[i][:], y_s[i][:], 0.0, y_s[i][:],
                                       op0=ALU.add, op1=ALU.mult,
                                       accum_out=ssq[i][:]).then_inc(sem["ve"], 1)
                v.wait_ge(sem["ve"], ve_m(tt, 14))
                v.tensor_scalar_mul(meanb[i][:], tot[i][:], 1.0 / D)\
                    .then_inc(sem["ve"], 1)
                v.wait_ge(sem["ve"], ve_m(tt, 15))
                v.scalar_tensor_tensor(msqb[i][:], meanb[i][:], 0.0, meanb[i][:],
                                       op0=ALU.add, op1=ALU.mult)\
                    .then_inc(sem["ve"], 1)
                v.wait_ge(sem["ve"], ve_m(tt, 16))
                v.scalar_tensor_tensor(varb[i][:], ssq[i][:], 1.0 / D, msqb[i][:],
                                       op0=ALU.mult, op1=ALU.subtract)\
                    .then_inc(sem["ve"], 1)
                v.wait_ge(sem["ve"], ve_m(tt, 17))
                v.tensor_scalar_add(varb[i][:], varb[i][:], EPS)\
                    .then_inc(sem["ve"], 1)
                v.wait_ge(sem["sc"], sc_m(tt, 9))
                v.reciprocal(rstdb[i][:], stdb[i][:]).then_inc(sem["ve"], 1)
                v.wait_ge(sem["ve"], ve_m(tt, 19))
                v.scalar_tensor_tensor(negmrb[i][:], meanb[i][:], -1.0,
                                       rstdb[i][:], op0=ALU.mult, op1=ALU.mult)\
                    .then_inc(sem["ve"], 1)
                v.wait_ge(sem["sc"], sc_m(tt, 10))
                v.wait_ge(p.sems[f"mxl{i}"], MXLB[i] + 16 * (tt // 2 + 1))
                v.tensor_add(o_s[i][:], o_s[i][:], xt[i][:]).then_inc(sem["ve"], 1)
        p.cnt["ve"] += 21 * HT

        @b.sync
        def _(sync):
            for tt in range(HT):
                i = tt % 2
                gt = HT * half + tt
                if tt >= 2:
                    sync.wait_ge(sem["ve"], ve_m(tt - 2, 21))
                else:
                    sync.wait_ge(sem["ve"], VE)
                bi = sync.dma_start(out=xt[i][:],
                                    in_=x[gt * 128:(gt + 1) * 128, :])
                p.inc(bi, f"mxl{i}", 16)
                if tt >= 1:
                    j = (tt - 1) % 2
                    gtp = HT * half + tt - 1
                    sync.wait_ge(sem["ve"], ve_m(tt - 1, 21))
                    bi = sync.dma_start(out=x4[gtp * 128:(gtp + 1) * 128, :],
                                        in_=o_s[j][:])
                    p.inc(bi, f"mst{j}", 16)
            j = (HT - 1) % 2
            gtp = HT * half + HT - 1
            sync.wait_ge(sem["ve"], ve_m(HT - 1, 21))
            bi = sync.dma_start(out=x4[gtp * 128:(gtp + 1) * 128, :],
                                in_=o_s[j][:])
            p.inc(bi, f"mst{j}", 16)

    emit_half(0)
    emit_half(1)
    emit_final_wait(p, [p.mark("mst0"), p.mark("mst1")])
    return p.finish()


def emit_final_wait(p, store_marks):
    b = p.block

    @b.sync
    def _(sync):
        for mk in store_marks:
            wge(sync, mk)



def emit_moe_pre(p, uT, As, Ar, Wr, br, consts, ln, name=""):
    ident, ones_row, ones128, cready = consts
    b = p.block
    sem = p.sems
    As_bf = p.sb(f"mp_as{name}", [128, ND * 16], BF16)
    Ar_bf = p.sb(f"mp_ar{name}", [128, ND * 96], BF16)
    Wr_bf = p.sb(f"mp_wr{name}", [128, ND * 6], BF16)
    br_bf = p.sb(f"mp_brr{name}", [1, 6], BF16)
    E16 = p.sb(f"mp_e16{name}", [6, 96], BF16)
    xaT_all = p.sb(f"mp_xa{name}", [16, T], BF16)
    hwT_all = p.sb(f"mp_hw{name}", [96, T], BF16)
    hT = p.sb(f"mp_h{name}", [96, 128], BF16)
    wtT_s = p.sb(f"mp_wtt{name}", [6, 128], BF16)
    e_s = p.sb(f"mp_es{name}", [128, 6], F32)
    g_s = p.sb(f"mp_gs{name}", [128, 6], F32)
    g2_s = p.sb(f"mp_g2{name}", [128, 6], F32)
    kp_s = p.sb(f"mp_kp{name}", [128, 6], F32)
    wt_s = p.sb(f"mp_wt{name}", [128, 6], F32)
    gmax = p.sb(f"mp_gm{name}", [128, 1], F32)
    negmax = p.sb(f"mp_ngm{name}", [128, 1], F32)
    esum = p.sb(f"mp_esm{name}", [128, 1], F32)
    er = p.sb(f"mp_er{name}", [128, 1], F32)
    m1s = p.sb(f"mp_m1{name}", [128, 1], F32)
    m2s = p.sb(f"mp_m2{name}", [128, 1], F32)
    p.sem(f"mpw{name}")

    GP = p.cnt["gp"]
    lm = []

    @b.gpsimd
    def _(g):
        g.wait_ge(sem["te"], p.cnt["te"])
        def ld(dst3, src3):
            bi = g.dma_start(out=dst3, in_=src3)
            lm.append(p.inc(bi, f"mpw{name}", 16))
        ld(As_bf.ap().rearrange("p (dc f) -> p dc f", dc=ND, f=16),
           As.rearrange("(dc p) f -> p dc f", p=128))
        ld(Ar_bf.ap().rearrange("p (dc f) -> p dc f", dc=ND, f=96),
           Ar.rearrange("(dc p) f -> p dc f", p=128))
        ld(Wr_bf.ap().rearrange("p (dc f) -> p dc f", dc=ND, f=6),
           Wr.rearrange("(dc p) f -> p dc f", p=128))
        ld(br_bf[:], br)
        g.memset(E16[:], 1.0).then_inc(sem["gp"], 1)
        g.wait_ge(sem["gp"], GP + 1)
        g.affine_select(out=E16[:], in_=E16[:], compare_op=ALU.is_ge, fill=0.0,
                        base=0, pattern=[[1, 96]], channel_multiplier=-16)\
            .then_inc(sem["gp"], 1)
        g.wait_ge(sem["gp"], GP + 2)
        g.affine_select(out=E16[:], in_=E16[:], compare_op=ALU.is_ge, fill=0.0,
                        base=15, pattern=[[-1, 96]], channel_multiplier=16)\
            .then_inc(sem["gp"], 1)
    p.cnt["gp"] += 3
    e16_done = p.mark("gp")

    TE = p.cnt["te"]
    SC = p.cnt["sc"]
    VE = p.cnt["ve"]

    def te_m(tt, k):
        return TE + 5 * tt + k

    def sc_m(tt, k):
        return SC + 4 * tt + k

    def ve_m(tt, k):
        return VE + 11 * tt + k

    ps_xa = p.pf[4][0:16, 0:128]
    ps_h = p.pf[5][0:96, 0:128]
    ps_g = p.pf[2][0:128, 0:6]
    ps_wb = p.pf[3][0:96, 0:128]
    ps_wt = p.pf[3][0:6, 0:128]
    wtT_f = p.sb(f"mp_wttf{name}", [6, 128], F32)

    @b.tensor
    def _(t):
        wge(t, cready)
        wge(t, e16_done)
        for mk in lm:
            wge(t, mk)
        for tt in range(NT):
            wge(t, ln.tile_done(tt))
            ut = lambda dc: apx(uT)[:, dc * T + tt * 128: dc * T + tt * 128 + 128]
            if tt >= 1:
                t.wait_ge(sem["sc"], sc_m(tt - 1, 1))
            else:
                t.wait_ge(sem["sc"], SC)
                t.wait_ge(sem["ve"], VE)
            for dc in range(ND):
                bi = t.matmul(ps_xa, As_bf[:, dc * 16:(dc + 1) * 16], ut(dc),
                              start=(dc == 0), stop=(dc == ND - 1))
            bi.then_inc(sem["te"], 1)
            if tt >= 1:
                t.wait_ge(sem["sc"], sc_m(tt - 1, 2))
            for dc in range(ND):
                bi = t.matmul(ps_h, Ar_bf[:, dc * 96:(dc + 1) * 96], ut(dc),
                              start=(dc == 0), stop=(dc == ND - 1))
            bi.then_inc(sem["te"], 1)
            if tt >= 1:
                t.wait_ge(sem["sc"], sc_m(tt - 1, 3))
                t.wait_ge(sem["ve"], ve_m(tt - 1, 4))
            t.matmul(ps_g, ones_row[0:1, 0:128], br_bf[0:1, :],
                     start=True, stop=False)
            for dc in range(ND):
                bi = t.matmul(ps_g, ut(dc), Wr_bf[:, dc * 6:(dc + 1) * 6],
                              start=False, stop=(dc == ND - 1))
            bi.then_inc(sem["te"], 1)
            t.wait_ge(sem["ve"], ve_m(tt, 10))
            if tt >= 1:
                t.wait_ge(sem["sc"], sc_m(tt - 1, 4))
                t.wait_ge(sem["ve"], ve_m(tt - 1, 11))
            t.transpose(ps_wt, wt_s[:], p._identf[:]).then_inc(sem["te"], 1)
            t.wait_ge(sem["sc"], sc_m(tt, 4))
            t.matmul(ps_wb, E16[:], wtT_s[:], start=True, stop=True)\
                .then_inc(sem["te"], 1)
    p.cnt["te"] += 5 * NT

    @b.scalar
    def _(s):
        for tt in range(NT):
            s.wait_ge(sem["te"], te_m(tt, 1))
            s.activation(xaT_all[:, tt * 128:(tt + 1) * 128], ps_xa, AF.Identity)\
                .then_inc(sem["sc"], 1)
            s.wait_ge(sem["te"], te_m(tt, 2))
            if tt >= 1:
                s.wait_ge(sem["ve"], ve_m(tt - 1, 11))
            s.activation(hT[:], ps_h, AF.Identity).then_inc(sem["sc"], 1)
            s.wait_ge(sem["te"], te_m(tt, 3))
            s.wait_ge(sem["ve"], ve_m(tt, 2))
            if tt >= 1:
                s.wait_ge(sem["ve"], ve_m(tt - 1, 4))
            s.activation(e_s[:], ps_g, AF.Exp, bias=negmax[:],
                         accum_out=esum[:]).then_inc(sem["sc"], 1)
            s.wait_ge(sem["te"], te_m(tt, 4))
            if tt >= 1:
                s.wait_ge(sem["te"], te_m(tt - 1, 5))
            s.activation(wtT_s[:], ps_wt, AF.Identity).then_inc(sem["sc"], 1)
    p.cnt["sc"] += 4 * NT

    @b.vector
    def _(v):
        for tt in range(NT):
            v.wait_ge(sem["te"], te_m(tt, 3))
            if tt >= 1:
                v.wait_ge(sem["ve"], ve_m(tt - 1, 2))
            else:
                v.wait_ge(sem["ve"], VE)
            v.reduce_max(gmax[:], ps_g, axis=AX.X).then_inc(sem["ve"], 1)
            v.wait_ge(sem["ve"], ve_m(tt, 1))
            if tt >= 1:
                v.wait_ge(sem["sc"], sc_m(tt - 1, 3))
            v.tensor_scalar_mul(negmax[:], gmax[:], -1.0).then_inc(sem["ve"], 1)
            v.wait_ge(sem["sc"], sc_m(tt, 3))
            v.reciprocal(er[:], esum[:]).then_inc(sem["ve"], 1)
            v.wait_ge(sem["ve"], ve_m(tt, 3))
            v.tensor_scalar_mul(g_s[:], e_s[:], er[:, 0:1]).then_inc(sem["ve"], 1)
            v.wait_ge(sem["ve"], ve_m(tt, 4))
            v.reduce_max(m1s[:], g_s[:], axis=AX.X).then_inc(sem["ve"], 1)
            v.wait_ge(sem["ve"], ve_m(tt, 5))
            v.tensor_single_scalar(g2_s[:], g_s[:], m1s[:, 0:1], ALU.is_equal)\
                .then_inc(sem["ve"], 1)
            v.wait_ge(sem["ve"], ve_m(tt, 6))
            v.scalar_tensor_tensor(g2_s[:], g2_s[:], -1e30, g_s[:],
                                   op0=ALU.mult, op1=ALU.add).then_inc(sem["ve"], 1)
            v.wait_ge(sem["ve"], ve_m(tt, 7))
            v.reduce_max(m2s[:], g2_s[:], axis=AX.X).then_inc(sem["ve"], 1)
            v.wait_ge(sem["ve"], ve_m(tt, 8))
            v.tensor_single_scalar(kp_s[:], g_s[:], m2s[:, 0:1], ALU.is_ge)\
                .then_inc(sem["ve"], 1)
            v.wait_ge(sem["ve"], ve_m(tt, 9))
            if tt >= 1:
                v.wait_ge(sem["te"], te_m(tt - 1, 4))
            v.tensor_mul(wt_s[:], g_s[:], kp_s[:]).then_inc(sem["ve"], 1)
            v.wait_ge(sem["te"], te_m(tt, 5))
            v.wait_ge(sem["sc"], sc_m(tt, 2))
            v.tensor_mul(hwT_all[:, tt * 128:(tt + 1) * 128], hT[:], ps_wb)\
                .then_inc(sem["ve"], 1)
    p.cnt["ve"] += 11 * NT

    def xa_mark(tt):
        return (sem["sc"], sc_m(tt, 1))

    def hw_mark(tt):
        return (sem["ve"], ve_m(tt, 11))

    return xaT_all, hwT_all, xa_mark, hw_mark


def build_qkv(sep_weights):
    p = Prog("qkv")
    nc = p.nc
    x = nc.dram_tensor("x", [T, D], F32, kind="ExternalInput").ap()
    if sep_weights:
        wq = nc.dram_tensor("wq", [D, D], F32, kind="ExternalInput").ap()
        wk = nc.dram_tensor("wk", [D, D], F32, kind="ExternalInput").ap()
        wv = nc.dram_tensor("wv", [D, D], F32, kind="ExternalInput").ap()
        bq = None
    else:
        wq = nc.dram_tensor("w_qkv", [D, 3 * D], F32, kind="ExternalInput").ap()
        wk = wv = wq
        bq = nc.dram_tensor("b_qkv", [1, 3 * D], F32, kind="ExternalInput").ap()
    qT = nc.dram_tensor("qT", [D, T], BF16, kind="ExternalOutput").ap()
    kT = nc.dram_tensor("kT", [D, T], BF16, kind="ExternalOutput").ap()
    v = nc.dram_tensor("v", [T, D], BF16, kind="ExternalOutput").ap()

    uT = p.sb("uT", [128, ND * T], BF16)
    p.start()
    consts = emit_consts(p)
    ln = emit_ln_uT(p, x, uT, 0, NT, consts, T)
    koff = 0 if sep_weights else D
    voff = 0 if sep_weights else 2 * D
    lnm_fm = lambda tg: ln.tile_done(4 * tg + 3)
    mk = []
    mk += emit_proj_fm(p, uT, wq, 0, bq, 0, qT, consts, lnm_fm, ND, T, name="q")
    mk += emit_proj_fm(p, uT, wk, koff, bq, D, kT, consts, lnm_fm, ND, T, name="k")
    mk += emit_proj_tm(p, uT, wv, voff, bq, 2 * D, v, consts, ln.tile_done, T, name="v")
    emit_final_wait(p, mk)
    return p.finish()


def build_fused(debug=False, upto="all", skip_dq=False):
    p = Prog("mk")
    nc = p.nc
    f = lambda n, s, d=F32: nc.dram_tensor(n, s, d, kind="ExternalInput").ap()
    x0 = f("x", [T, D])
    w_qkv = f("w_qkv", [D, 3 * D])
    b_qkv = f("b_qkv", [1, 3 * D])
    w_out0 = f("w_out0", [D, D])
    b_out0 = f("b_out0", [1, D])
    gup = f("gup_sh", [11264, 32], I32)
    gus = f("gus_sh", [2, 11264], F16)
    dnp = f("dnp_sh", [2048, 96], I32)
    dns = f("dns_sh", [6, 2048], F16)
    wq1 = f("wq1", [D, D])
    wk1 = f("wk1", [D, D])
    wv1 = f("wv1", [D, D])
    wo1 = f("wo1", [D, D])
    Wb = f("W_base", [D, D])
    bb = f("b_base", [1, D])
    As = f("A_s", [D, 16])
    Bs = f("B_s", [16, D])
    Ar = f("A_r_flat", [D, 96])
    Br = f("B_r_flat", [96, D])
    Wr = f("W_router", [D, 6])
    br = f("b_router", [1, 6])
    alpha = f("alpha", [1, 2], BF16)
    out = nc.dram_tensor("out", [T, D], F32, kind="ExternalOutput").ap()

    ik = "ExternalOutput" if debug else "Internal"
    g = lambda n, s, d: nc.dram_tensor(n, s, d, kind="Internal").ap()
    gd = lambda n, s, d: nc.dram_tensor(n, s, d, kind=ik).ap()
    x1 = gd("x1d", [T, D], F32)
    x2 = gd("x2d", [T, D], F32)
    x3 = gd("x3d", [T, D], F32)
    qTd = gd("qTd", [D, T], BF16)
    kv_own = g("kv_own", [2 * T, D], BF16)
    kv_all = g("kv_all", [4 * T, D], BF16)
    guT_sh = g("guT_shd", [256, 11264], BF16)
    guT_all = g("guT_alld", [2048, 11264], BF16)
    dnT_sh = g("dnT_shd", [768, 2048], BF16)
    dnT_all = g("dnT_alld", [6144, 2048], BF16)

    uT = p.sb("uT", [128, ND * T], BF16)
    atk = [p.sb(f"at_k{i}", [128, S], BF16) for i in range(2)]
    atv = [p.sb(f"at_v{i}", [128, 32 * 128], BF16) for i in range(2)]
    atq = [p.sb(f"at_q{i}", [128, T], BF16) for i in range(2)]
    ate = p.sb("at_e", [128, 16 * 512], BF16)
    mly = [p.sb(f"mlyml{i}", [128, D], F32) for i in range(2)]
    mlo = [p.sb(f"mloml{i}", [128, D], F32) for i in range(2)]
    mlx = [p.sb(f"mlxml{i}", [128, D], F32) for i in range(2)]
    uT_a = p.addr("uT")
    atk_a = p.addr("at_k0")
    assert p.addr("at_k1") == atk_a + 8192
    assert p.addr("at_v0") == atk_a + 16384
    assert p.addr("at_v1") == atk_a + 24576
    assert p.addr("at_q0") == atk_a + 32768
    mly_a = p.addr("mlyml0")
    assert p.addr("mlxml1") == mly_a + 5 * 8192
    ate_a = p.addr("at_e")
    p.sb("dq_kf", [128, 1536], F32, at=atk_a)
    p.sb("dq_sf", [128, 1536], F32, at=atk_a + 6144)
    p.sb("dq_r4", [128, 1536], F32, at=atk_a + 12288)
    p.sb("dq_r6", [128, 1536], F32, at=atk_a + 18432)
    p.sb("dq_m1", [128, 1536], F32, at=atk_a + 24576)
    p.sb("dq_s", [128, 1536], F32, at=atk_a + 30720)
    p.sb("dq_pk", [128, 1536], I32, at=ate_a)
    p.sb("dq_i", [128, 1536], I32, at=ate_a + 6144)
    p.sb("dq_dgu", [128, 88 * 256], BF16, at=uT_a)
    p.sb("dq_ddn", [128, 16 * 768], BF16, at=mly_a)
    p.sb("lnxtg0", [128, D], F32, at=mly_a)
    p.sb("lnxtg1", [128, D], F32, at=mly_a + 8192)
    p.sb("lnsqg", [128, D], F32, at=mly_a + 16384)
    p.sb("lntrashg", [128, D], BF16, at=mly_a + 24576)
    p.sb("lnug0", [128, D], BF16, at=mly_a + 28672)
    p.sb("lnug1", [128, D], BF16, at=mly_a + 32768)
    p.sb("pfwfm0", [128, ND * 128], BF16, at=ate_a)
    p.sb("pfwfm1", [128, ND * 128], BF16, at=ate_a + 4096)
    p.sb("pfsfm0", [128, 512], BF16, at=ate_a + 8192)
    p.sb("pfsfm1", [128, 512], BF16, at=ate_a + 9216)
    p.sb("ptwtm0", [128, 512], BF16, at=ate_a + 10240)
    p.sb("ptwtm1", [128, 512], BF16, at=ate_a + 11264)
    p.sb("ptstm0", [128, 512], BF16, at=ate_a + 12288)
    p.sb("ptstm1", [128, 512], BF16, at=ate_a + 13312)
    p.sb("pfbfm", [1, ND * 128], BF16, at=atk_a + 32768)
    p.sb("ptbtm", [1, 4 * 512], BF16, at=atk_a + 36864)
    p.sb("guwgf0", [128, ND * 128], BF16, at=atk_a)
    p.sb("guwgf1", [128, ND * 128], BF16, at=atk_a + 4096)
    p.sb("guwuf0", [128, ND * 128], BF16, at=atk_a + 8192)
    p.sb("guwuf1", [128, ND * 128], BF16, at=atk_a + 12288)
    p.sb("gusilf0", [128, 512], BF16, at=atk_a + 16384)
    p.sb("gusilf1", [128, 512], BF16, at=atk_a + 17408)
    hactT = p.sb("hactTf", [128, 44 * 512], BF16, at=uT_a + ND * 512 * 2)
    p.sb("mp_as", [128, ND * 16], BF16, at=atk_a)
    p.sb("mp_ar", [128, ND * 96], BF16, at=atk_a + 1024)
    p.sb("mp_wr", [128, ND * 6], BF16, at=atk_a + 4096)
    p.sb("mp_xa", [16, T], BF16, at=atk_a + 8192)
    p.sb("mp_hw", [96, T], BF16, at=atk_a + 12288)

    p.start()
    consts = emit_consts(p)
    p.sem("cc")

    if skip_dq:
        dq_mks = []
    else:
        dq_mks = emit_dq(p, gup, gus, dnp, dns, guT_sh, dnT_sh, consts)

    ln0 = emit_ln_uT(p, x0, uT, 0, NT, consts, T, name="g",
                     guard_marks=dq_mks)
    lnm_fm = lambda tg: ln0.tile_done(4 * tg + 3)
    mkq = emit_proj_fm(p, uT, w_qkv, 0, b_qkv, 0, qTd, consts, lnm_fm,
                       ND, T, name="fm")
    mkk = emit_proj_fm(p, uT, w_qkv, D, b_qkv, D, kv_own[0:T, :], consts,
                       lnm_fm, ND, T, name="fm")
    mkv = emit_proj_tm(p, uT, w_qkv, 2 * D, b_qkv, 2 * D, kv_own[T:2 * T, :],
                       consts, ln0.tile_done, T, name="tm")

    if not skip_dq:
        @p.block.gpsimd
        def _(gq):
            for mk in dq_mks:
                wge(gq, mk)
            gq.collective_compute(
                "AllGather", ALU.bypass, replica_groups=[list(range(8))],
                ins=[guT_sh.opt()], outs=[guT_all.opt()]).then_inc(p.sems["cc"], 1)
            gq.collective_compute(
                "AllGather", ALU.bypass, replica_groups=[list(range(8))],
                ins=[dnT_sh.opt()], outs=[dnT_all.opt()]).then_inc(p.sems["cc"], 1)
        p.cnt["cc"] += 2
    ccdq = p.mark("cc")
    if upto == "qkv":
        emit_final_wait(p, mkq + mkk + mkv)
        return p.finish()

    @p.block.gpsimd
    def _(gq):
        for mk in mkk + mkv:
            wge(gq, mk)
        gq.collective_compute(
            "AllGather", ALU.bypass,
            replica_groups=[[0, 1], [2, 3], [4, 5], [6, 7]],
            ins=[kv_own.opt()], outs=[kv_all.opt()]).then_inc(p.sems["cc"], 1)
    p.cnt["cc"] += 1
    cc0 = p.mark("cc")

    avm0 = emit_attn_core(p, qTd, kv_own, kv_all, alpha, consts, L0_SCALE,
                          False, uT, guard_marks=[cc0] + mkq)
    mko0 = emit_mm_ln_resid(
        p, lambda ck, tt: apx(uT)[:, ck * T + tt * 128: ck * T + tt * 128 + 128],
        ND, w_out0, True, b_out0, x0, x1, consts, [avm0], 0, NT, name="ml",
        guard_marks=[avm0])
    if upto == "l0attn":
        emit_final_wait(p, mko0)
        return p.finish()

    for tg in range(4):
        lnf = emit_ln_uT(p, x1, uT[:, 0:ND * 512], 4 * tg, 4, consts, 512,
                         name="g", guard_marks=mko0 if tg == 0 else ())
        gum = emit_gu(p, uT[:, 0:ND * 512], guT_all, hactT, consts,
                      lnf.all_done, name="f",
                      guard_marks=[ccdq] + (mko0 if tg == 0 else []))
        mkf = emit_mm_ln_resid(
            p, lambda ck, tt: apx(hactT)[:, ck * 512 + tt * 128:
                                         ck * 512 + tt * 128 + 128],
            44, dnT_all[0:5632, :], False, None, x1, x2, consts, [gum],
            4 * tg, 4, name="ml", guard_marks=[gum])
    if upto == "ffn":
        emit_final_wait(p, mkf)
        return p.finish()

    ln1 = emit_ln_uT(p, x2, uT, 0, NT, consts, T, name="g", guard_marks=mkf)
    lnm1 = lambda tg: ln1.tile_done(4 * tg + 3)
    mkq1 = emit_proj_fm(p, uT, wq1, 0, None, 0, qTd, consts, lnm1,
                        ND, T, name="fm")
    mkk1 = emit_proj_fm(p, uT, wk1, 0, None, 0, kv_own[0:T, :], consts,
                        lnm1, ND, T, name="fm")
    mkv1 = emit_proj_tm(p, uT, wv1, 0, None, 0, kv_own[T:2 * T, :],
                        consts, ln1.tile_done, T, name="tm")

    @p.block.gpsimd
    def _(gq):
        for mk in mkk1 + mkv1:
            wge(gq, mk)
        gq.collective_compute(
            "AllGather", ALU.bypass,
            replica_groups=[[0, 1], [2, 3], [4, 5], [6, 7]],
            ins=[kv_own.opt()], outs=[kv_all.opt()]).then_inc(p.sems["cc"], 1)
    p.cnt["cc"] += 1
    cc1 = p.mark("cc")

    if upto == "l1qkv":
        emit_final_wait(p, mkq1 + mkk1 + mkv1)
        return p.finish()

    avm1 = emit_attn_core(p, qTd, kv_own, kv_all, alpha, consts, L1_SCALE,
                          True, uT, guard_marks=[cc1] + mkq1)
    mko1 = emit_mm_ln_resid(
        p, lambda ck, tt: apx(uT)[:, ck * T + tt * 128: ck * T + tt * 128 + 128],
        ND, wo1, True, None, x2, x3, consts, [avm1], 0, NT, name="ml",
        guard_marks=[avm1])

    if upto == "l1attn":
        emit_final_wait(p, mko1)
        return p.finish()

    lnm_ = emit_ln_uT(p, x3, uT, 0, NT, consts, T, name="g", guard_marks=mko1)
    xaT_all, hwT_all, xa_mark, hw_mark = emit_moe_pre(
        p, uT, As, Ar, Wr, br, consts, lnm_, name="")
    Bs_bf = p.sb("fz_bs", [16, D], BF16, at=p.addr("at_v0"))
    Br_bf = p.sb("fz_br", [96, D], BF16, at=p.addr("at_v0") + 4096)
    p.sem("fzw")
    fzm = []

    @p.block.gpsimd
    def _(gq):
        gq.wait_ge(p.sems["te"], p.cnt["te"])
        bi = gq.dma_start(out=Bs_bf[:], in_=Bs)
        fzm.append(p.inc(bi, "fzw", 16))
        bi = gq.dma_start(out=Br_bf[:], in_=Br)
        fzm.append(p.inc(bi, "fzw", 16))

    def extra(t, psum, tt, fg):
        for mk in fzm:
            wge(t, mk)
        wge(t, xa_mark(tt))
        wge(t, hw_mark(tt))
        t.matmul(psum[:], xaT_all[:, tt * 128:(tt + 1) * 128],
                 Bs_bf[:, fg * 512:(fg + 1) * 512], start=False, stop=False)
        return t.matmul(psum[:], hwT_all[:, tt * 128:(tt + 1) * 128],
                        Br_bf[:, fg * 512:(fg + 1) * 512],
                        start=False, stop=True)

    mkfin = emit_mm_ln_resid(
        p, lambda ck, tt: apx(uT)[:, ck * T + tt * 128: ck * T + tt * 128 + 128],
        ND, Wb, True, bb, x3, out, consts, [hw_mark(NT - 1)], 0, NT,
        name="ml", guard_marks=[hw_mark(0)], extra_mms=extra)

    emit_final_wait(p, mkfin)
    return p.finish()


_PROGS = {}


def _prog(key, builder, *args):
    if key not in _PROGS:
        _PROGS[key] = builder(*args)
    return _PROGS[key]


def kernel(**inputs):
    f32 = np.float32
    x = np.ascontiguousarray(inputs["L0_attn_x"], dtype=f32)
    nc = _prog("fused", build_fused)
    gup = inputs["L0_ffn_gate_up_packed"]
    gus = inputs["L0_ffn_gate_up_scales"]
    dnp = inputs["L0_ffn_down_packed"]
    dns = inputs["L0_ffn_down_scales"]
    Ar_flat = np.ascontiguousarray(
        inputs["L1_ffn_A_r"].transpose(1, 0, 2).reshape(D, 96))
    Br_flat = np.ascontiguousarray(inputs["L1_ffn_B_r"].reshape(96, D))
    row = lambda a: np.ascontiguousarray(a.reshape(1, -1), f32)
    maps = []
    for c in range(8):
        dnp_sh = np.zeros((2048, 96), np.int32)
        dns_sh = np.zeros((6, 2048), np.float16)
        w0, w1 = 96 * c, min(96 * (c + 1), 704)
        dnp_sh[:, 0:w1 - w0] = dnp[:, w0:w1]
        g0, g1 = 6 * c, min(6 * (c + 1), 44)
        dns_sh[0:g1 - g0, :] = dns[g0:g1, :]
        maps.append({
            "x": np.ascontiguousarray(x[c // 2, (c % 2) * T:(c % 2 + 1) * T]),
            "w_qkv": inputs["L0_attn_w_qkv"],
            "b_qkv": row(inputs["L0_attn_b_qkv"]),
            "w_out0": inputs["L0_attn_w_out"],
            "b_out0": row(inputs["L0_attn_b_out"]),
            "gup_sh": np.ascontiguousarray(gup[:, 32 * c:32 * (c + 1)]),
            "gus_sh": np.ascontiguousarray(gus[2 * c:2 * c + 2, :]),
            "dnp_sh": dnp_sh, "dns_sh": dns_sh,
            "wq1": inputs["L1_attn_Wq"], "wk1": inputs["L1_attn_Wk"],
            "wv1": inputs["L1_attn_Wv"], "wo1": inputs["L1_attn_Wo"],
            "W_base": inputs["L1_ffn_W_base"],
            "b_base": row(inputs["L1_ffn_b_base"]),
            "A_s": inputs["L1_ffn_A_s"], "B_s": inputs["L1_ffn_B_s"],
            "A_r_flat": Ar_flat, "B_r_flat": Br_flat,
            "W_router": inputs["L1_ffn_W_router"],
            "b_router": row(inputs["L1_ffn_b_router"]),
            "alpha": np.array([[-1e30, 1.0]] if c % 2 == 0 else [[0.0, 0.0]],
                              np.float32).astype(BFNP),
        })
    res = run_bass_kernel_spmd(nc, maps, core_ids=list(range(8))).results
    out = np.empty((4, S, D), np.float32)
    for c in range(8):
        out[c // 2, (c % 2) * T:(c % 2 + 1) * T] = res[c]["out"]
    return out


def kernel_multilaunch(**inputs):
    f32 = np.float32
    x = np.ascontiguousarray(inputs["L0_attn_x"], dtype=f32)
    xs = [np.ascontiguousarray(x[c // 2, (c % 2) * T:(c % 2 + 1) * T])
          for c in range(8)]

    def run(nc, maps, cores=None):
        cores = list(range(8)) if cores is None else cores
        return run_bass_kernel_spmd(nc, maps, core_ids=cores).results

    w_qkv = inputs["L0_attn_w_qkv"]
    b_qkv = np.ascontiguousarray(inputs["L0_attn_b_qkv"].reshape(1, -1), f32)
    r1 = run(_prog("qkv0", build_qkv, False),
             [{"x": xs[c], "w_qkv": w_qkv, "b_qkv": b_qkv} for c in range(8)])

    w_out = inputs["L0_attn_w_out"]
    b_out = np.ascontiguousarray(inputs["L0_attn_b_out"].reshape(1, -1), f32)
    kv_own = [np.concatenate([r1[c]["kT"], r1[c]["v"]], axis=0) for c in range(8)]
    kv_all = [np.concatenate([kv_own[2 * b], kv_own[2 * b + 1]], axis=0)
              for b in range(4)]
    al0 = np.zeros((1, 2), np.float32).astype(BFNP)
    r2 = run(_prog("attn0", build_attn, False),
             [{"qT": r1[c]["qT"], "kv_own": kv_own[c], "kv_all": kv_all[c // 2],
               "alpha": al0, "x": xs[c], "w_out": w_out, "b_out": b_out}
              for c in range(8)])
    x1 = [r2[c]["x1"] for c in range(8)]

    gup = inputs["L0_ffn_gate_up_packed"]
    gus = inputs["L0_ffn_gate_up_scales"]
    dnp = inputs["L0_ffn_down_packed"]
    dns = inputs["L0_ffn_down_scales"]
    dq_maps = []
    for c in range(8):
        dnp_sh = np.zeros((2048, 96), np.int32)
        dns_sh = np.zeros((6, 2048), np.float16)
        w0, w1 = 96 * c, min(96 * (c + 1), 704)
        dnp_sh[:, 0:w1 - w0] = dnp[:, w0:w1]
        g0, g1 = 6 * c, min(6 * (c + 1), 44)
        dns_sh[0:g1 - g0, :] = dns[g0:g1, :]
        dq_maps.append({
            "gup_sh": np.ascontiguousarray(gup[:, 32 * c:32 * (c + 1)]),
            "gus_sh": np.ascontiguousarray(gus[2 * c:2 * c + 2, :]),
            "dnp_sh": dnp_sh, "dns_sh": dns_sh})
    r3 = run(_prog("dq", build_dq), dq_maps)
    w_guT = np.concatenate([r3[c]["guT_sh"] for c in range(8)], axis=0)
    w_downT = np.ascontiguousarray(
        np.concatenate([r3[c]["dnT_sh"] for c in range(8)], axis=0)[0:5632])

    r4 = run(_prog("ffn", build_ffn),
             [{"x": x1[c], "w_guT": w_guT, "w_downT": w_downT}
              for c in range(8)])
    x2 = [r4[c]["x2"] for c in range(8)]

    r5 = run(_prog("qkv1", build_qkv, True),
             [{"x": x2[c], "wq": inputs["L1_attn_Wq"],
               "wk": inputs["L1_attn_Wk"], "wv": inputs["L1_attn_Wv"]}
              for c in range(8)])
    kv_own1 = [np.concatenate([r5[c]["kT"], r5[c]["v"]], axis=0) for c in range(8)]
    kv_all1 = [np.concatenate([kv_own1[2 * b], kv_own1[2 * b + 1]], axis=0)
               for b in range(4)]

    Wo = inputs["L1_attn_Wo"]
    alph = [np.array([[-1e30, 1.0]] if c % 2 == 0 else [[0.0, 0.0]],
                     np.float32).astype(BFNP) for c in range(8)]
    r6 = run(_prog("attn1", build_attn, True),
             [{"qT": r5[c]["qT"], "kv_own": kv_own1[c],
               "kv_all": kv_all1[c // 2], "alpha": alph[c],
               "x": x2[c], "w_out": Wo} for c in range(8)])
    x3 = [r6[c]["x1"] for c in range(8)]

    Ar_flat = np.ascontiguousarray(
        inputs["L1_ffn_A_r"].transpose(1, 0, 2).reshape(D, 96))
    Br_flat = np.ascontiguousarray(inputs["L1_ffn_B_r"].reshape(96, D))
    r8 = run(_prog("moe", build_moe),
             [{"x": x3[c], "W_base": inputs["L1_ffn_W_base"],
               "b_base": np.ascontiguousarray(
                   inputs["L1_ffn_b_base"].reshape(1, -1), f32),
               "A_s": inputs["L1_ffn_A_s"], "B_s": inputs["L1_ffn_B_s"],
               "A_r_flat": Ar_flat, "B_r_flat": Br_flat,
               "W_router": inputs["L1_ffn_W_router"],
               "b_router": np.ascontiguousarray(
                   inputs["L1_ffn_b_router"].reshape(1, -1), f32)}
              for c in range(8)])

    out = np.empty((4, S, D), np.float32)
    for c in range(8):
        out[c // 2, (c % 2) * T:(c % 2 + 1) * T] = r8[c]["x4"]
    return out


# version 09
# speedup vs baseline: 1.0678x; 1.0678x over previous
import numpy as np
import ml_dtypes
from contextlib import ExitStack

import concourse.bass as bass
import concourse.mybir as mybir
from concourse.bass_utils import run_bass_kernel_spmd

F32 = mybir.dt.float32
BF16 = mybir.dt.bfloat16
F16 = mybir.dt.float16
I32 = mybir.dt.int32
AF = mybir.ActivationFunctionType
ALU = mybir.AluOpType
AX = mybir.AxisListType
BFNP = ml_dtypes.bfloat16

T = 2048
D = 2048
NT = T // 128
ND = D // 128
NH = 16
HD = 128
S = 4096
EPS = 1e-5

L0_SCALE = float(1.0 / np.sqrt(np.float32(128.0)).astype(np.float32))
L1_SCALE = 1.0 / 64.0


class Prog:

    def __init__(self, name="prog"):
        self.nc = bass.Bass("TRN2", target_bir_lowering=False, name=name)
        self.ctx = ExitStack()
        self.sems = {}
        self.cnt = {}
        self.block = None

    def sb(self, name, shape, dtype, at=None):
        key = ("sb", name)
        if not hasattr(self, "_bufs"):
            self._bufs = {}
        if key not in self._bufs:
            if at is None:
                self._bufs[key] = self.nc.alloc_sbuf_tensor(name, shape, dtype)
            else:
                self._bufs[key] = self.nc.alloc_sbuf_tensor_at(
                    name, shape, dtype, offset=at)
        return self._bufs[key]

    def addr(self, name):
        return self.nc.lookup_mloc(self._bufs[("sb", name)]).addr

    def ps(self, name, shape, dtype=F32):
        return self.ctx.enter_context(self.nc.psum_tensor(name, shape, dtype))

    def sem(self, name):
        if name not in self.sems:
            self.sems[name] = self.ctx.enter_context(self.nc.semaphore(name))
            self.cnt[name] = 0
        return self.sems[name]

    def start(self):
        for s in ("te", "sc", "ve", "gp", "ev"):
            self.sem(s)
        self.pfd = [self.ps(f"pfd{i}", [128, 1024], F32) for i in range(2)]
        self.pf = [self.pfd[0][:, 0:512], self.pfd[0][:, 512:1024],
                   self.pfd[1][:, 0:512], self.pfd[1][:, 512:1024],
                   self.ps("pf4", [128, 512], F32),
                   self.ps("pf5", [128, 512], F32)]
        self.pb = [self.ps(f"pbh{i}", [128, 1024], BF16)[:, 0:512]
                   for i in range(2)]
        self.block = self.ctx.enter_context(self.nc.Block())
        return self.block

    def inc(self, bi, name, k=1):
        self.cnt[name] += k
        bi.then_inc(self.sems[name], k)
        return (self.sems[name], self.cnt[name])

    def mark(self, name):
        return (self.sems[name], self.cnt[name])

    def finish(self):
        self.ctx.close()
        return self.nc


def wge(eng, mk):
    eng.wait_ge(mk[0], mk[1])


def apx(x):
    a = getattr(x, "ap", None)
    return x.ap() if callable(a) else x


def emit_consts(p):
    ident = p.sb("ident", [128, 128], BF16)
    ones_row = p.sb("ones_row", [1, 512], BF16)
    ones128 = p.sb("ones128", [128, 128], BF16)
    sixteens = p.sb("sixteens", [128, 128], BF16)
    identf = p.sb("identf", [128, 128], F32)
    b = p.block
    sem = p.sems
    GP = p.cnt["gp"]

    @b.gpsimd
    def _(g):
        g.memset(ident[:], 0.0).then_inc(sem["gp"], 1)
        g.wait_ge(sem["gp"], GP + 1)
        g.affine_select(out=ident[:], in_=ident[:], compare_op=ALU.not_equal,
                        fill=1.0, base=0, pattern=[[-1, 128]],
                        channel_multiplier=1).then_inc(sem["gp"], 1)
        g.memset(ones_row[:], 1.0).then_inc(sem["gp"], 1)
        g.memset(ones128[:], 1.0).then_inc(sem["gp"], 1)
        g.memset(sixteens[:], 16.0).then_inc(sem["gp"], 1)
        g.memset(identf[:], 0.0).then_inc(sem["gp"], 1)
        g.wait_ge(sem["gp"], GP + 6)
        g.affine_select(out=identf[:], in_=identf[:], compare_op=ALU.not_equal,
                        fill=1.0, base=0, pattern=[[-1, 128]],
                        channel_multiplier=1).then_inc(sem["gp"], 1)
    p.cnt["gp"] += 7
    ready = (sem["gp"], p.cnt["gp"])
    p._identf = identf
    p._sixteens = sixteens
    return ident, ones_row, ones128, ready


class LNOut:
    pass


def emit_ln_uT(p, x_hbm, uT, t0, n_tiles, consts, tspan, name="",
               guard_marks=()):
    ident = consts[0]
    cready = consts[3]
    b = p.block
    sem = p.sems
    xt = [p.sb(f"lnxt{name}{i}", [128, D], F32) for i in range(2)]
    for i in range(2):
        p.sem(f"lx{name}{i}")
    trash_bf = p.sb(f"lntrash{name}", [128, D], BF16)
    trash_sq = p.sb(f"lnsq{name}", [128, D], F32)
    sums = p.sb(f"lnsums{name}", [128, 1], F32)
    ssq = p.sb(f"lnssq{name}", [128, 1], F32)
    mean = p.sb(f"lnmean{name}", [128, 1], F32)
    msq = p.sb(f"lnmsq{name}", [128, 1], F32)
    var = p.sb(f"lnvar{name}", [128, 1], F32)
    std = p.sb(f"lnstd{name}", [128, 1], F32)
    rstd = p.sb(f"lnrstd{name}", [128, 1], F32)
    negmr = p.sb(f"lnnegmr{name}", [128, 1], F32)
    u_t = [p.sb(f"lnu{name}{i}", [128, D], BF16) for i in range(2)]
    pb = p.pb

    SC = p.cnt["sc"]
    VE = p.cnt["ve"]
    TE = p.cnt["te"]
    EV = p.cnt["ev"]
    xmark = []

    @b.sync
    def _(sync):
        for mk in guard_marks:
            wge(sync, mk)
        for tt in range(n_tiles):
            if tt >= 2:
                sync.wait_ge(sem["sc"], SC + 3 * (tt - 2) + 3)
                sync.wait_ge(sem["ve"], VE + 7 * (tt - 2) + 1)
            else:
                sync.wait_ge(sem["sc"], SC)
                sync.wait_ge(sem["ve"], VE)
            bi = sync.dma_start(out=xt[tt % 2][:],
                                in_=x_hbm[(t0 + tt) * 128:(t0 + tt + 1) * 128, :])
            xmark.append(p.inc(bi, f"lx{name}{tt % 2}", 16))
    assert len(xmark) == n_tiles

    @b.scalar
    def _(s):
        for tt in range(n_tiles):
            wge(s, xmark[tt])
            if tt >= 1:
                s.wait_ge(sem["ve"], VE + 7 * (tt - 1) + 2)
            s.activation(trash_bf[:], xt[tt % 2][:], AF.Identity,
                         accum_out=sums[:]).then_inc(sem["sc"], 1)
            s.wait_ge(sem["ve"], VE + 7 * tt + 5)
            s.activation(std[:], var[:], AF.Sqrt).then_inc(sem["sc"], 1)
            s.wait_ge(sem["ve"], VE + 7 * tt + 7)
            if tt >= 2:
                s.wait_ge(sem["te"], TE + 4 * (tt - 2) + 4)
            else:
                s.wait_ge(sem["te"], TE)
            s.activation(u_t[tt % 2][:], xt[tt % 2][:], AF.Identity,
                         bias=negmr[:], scale=rstd[:]).then_inc(sem["sc"], 1)
    p.cnt["sc"] += 3 * n_tiles

    def emit_evac(v, tt):
        for g in range(4):
            G = 4 * tt + g
            v.wait_ge(sem["te"], TE + G + 1)
            dst = apx(uT).rearrange("p (dc t) -> p dc t", dc=ND, t=tspan)[
                :, 4 * g:4 * g + 4, tt * 128:(tt + 1) * 128]
            src = pb[G % 2].rearrange("p (j f) -> p j f", j=4, f=128)
            v.tensor_copy(dst, src).then_inc(sem["ev"], 1)

    @b.vector
    def _(v):
        for mk in guard_marks:
            wge(v, mk)
        for tt in range(n_tiles):
            wge(v, xmark[tt])
            v.scalar_tensor_tensor(trash_sq[:], xt[tt % 2][:], 0.0, xt[tt % 2][:],
                                   op0=ALU.add, op1=ALU.mult,
                                   accum_out=ssq[:]).then_inc(sem["ve"], 1)
            v.wait_ge(sem["sc"], SC + 3 * tt + 1)
            v.tensor_scalar_mul(mean[:], sums[:], 1.0 / D).then_inc(sem["ve"], 1)
            v.wait_ge(sem["ve"], VE + 7 * tt + 2)
            v.scalar_tensor_tensor(msq[:], mean[:], 0.0, mean[:],
                                   op0=ALU.add, op1=ALU.mult).then_inc(sem["ve"], 1)
            v.wait_ge(sem["ve"], VE + 7 * tt + 3)
            v.scalar_tensor_tensor(var[:], ssq[:], 1.0 / D, msq[:],
                                   op0=ALU.mult, op1=ALU.subtract).then_inc(sem["ve"], 1)
            v.wait_ge(sem["ve"], VE + 7 * tt + 4)
            v.tensor_scalar_add(var[:], var[:], EPS).then_inc(sem["ve"], 1)
            v.wait_ge(sem["sc"], SC + 3 * tt + 2)
            v.reciprocal(rstd[:], std[:]).then_inc(sem["ve"], 1)
            v.wait_ge(sem["ve"], VE + 7 * tt + 6)
            v.scalar_tensor_tensor(negmr[:], mean[:], -1.0, rstd[:],
                                   op0=ALU.mult, op1=ALU.mult).then_inc(sem["ve"], 1)
            if tt >= 1:
                emit_evac(v, tt - 1)
        emit_evac(v, n_tiles - 1)
    p.cnt["ve"] += 7 * n_tiles
    p.cnt["ev"] += 4 * n_tiles

    @b.tensor
    def _(t):
        wge(t, cready)
        for tt in range(n_tiles):
            t.wait_ge(sem["sc"], SC + 3 * tt + 3)
            for g in range(4):
                G = 4 * tt + g
                if G >= 2:
                    t.wait_ge(sem["ev"], EV + (G - 2) + 1)
                else:
                    t.wait_ge(sem["ev"], EV)
                for j in range(4):
                    dc = 4 * g + j
                    bi = t.transpose(pb[G % 2][:, j * 128:(j + 1) * 128],
                                     u_t[tt % 2][:, dc * 128:(dc + 1) * 128],
                                     ident[:])
                bi.then_inc(sem["te"], 1)
    p.cnt["te"] += 4 * n_tiles


    out = LNOut()
    out.tile_done = lambda tt: (sem["ev"], EV + 4 * (tt + 1))
    out.all_done = (sem["ev"], EV + 4 * n_tiles)
    return out


def emit_proj_fm(p, uT, w_hbm, w_cols, b_hbm, b_off, out_hbm, consts, ln_mark,
                 nfc, tspan, name=""):
    _, ones_row, _, cready = consts
    b = p.block
    sem = p.sems
    ntg = tspan // 512
    wbuf = [p.sb(f"pfw{name}{i}", [128, ND * 128], BF16) for i in range(2)]
    stage = [p.sb(f"pfs{name}{i}", [128, 512], BF16) for i in range(2)]
    for i in range(2):
        p.sem(f"pfl{name}{i}")
        p.sem(f"pfo{name}{i}")
    p.sem(f"pflb{name}")
    pf = p.pf[0:2]
    brow = p.sb(f"pfb{name}", [1, nfc * 128], BF16) if b_hbm is not None else None

    TE = p.cnt["te"]
    SC = p.cnt["sc"]
    SO = [p.cnt[f"pfo{name}{i}"] for i in range(2)]
    wmark = []
    bmark = []

    @b.gpsimd
    def _(g):
        if b_hbm is not None:
            g.wait_ge(sem["te"], TE)
            bi = g.dma_start(out=brow[:], in_=b_hbm[:, b_off:b_off + nfc * 128])
            bmark.append(p.inc(bi, f"pflb{name}", 16))
        for fc in range(nfc):
            if fc >= 2:
                g.wait_ge(sem["te"], TE + ntg * (fc - 1))
            else:
                g.wait_ge(sem["te"], TE)
            src = w_hbm[:, w_cols + fc * 128:w_cols + (fc + 1) * 128]\
                .rearrange("(dc p) f -> p dc f", p=128)
            dst = wbuf[fc % 2].ap().rearrange("p (dc f) -> p dc f", dc=ND, f=128)
            bi = g.dma_start(out=dst, in_=src)
            wmark.append(p.inc(bi, f"pfl{name}{fc % 2}", 16))

    @b.tensor
    def _(t):
        wge(t, cready)
        for fc in range(nfc):
            wge(t, wmark[fc])
            if fc == 0 and bmark:
                wge(t, bmark[0])
            for tg in range(ntg):
                K = fc * ntg + tg
                if K >= 2:
                    t.wait_ge(sem["sc"], SC + (K - 2) + 1)
                else:
                    t.wait_ge(sem["sc"], SC)
                wge(t, ln_mark(tg))
                first = True
                if b_hbm is not None:
                    t.matmul(pf[K % 2][:], brow[0:1, fc * 128:(fc + 1) * 128],
                             ones_row[0:1, :], start=True, stop=False)
                    first = False
                for dc in range(ND):
                    bi = t.matmul(
                        pf[K % 2][:], wbuf[fc % 2][:, dc * 128:(dc + 1) * 128],
                        apx(uT)[:, dc * tspan + tg * 512: dc * tspan + tg * 512 + 512],
                        start=first, stop=(dc == ND - 1))
                    first = False
                bi.then_inc(sem["te"], 1)
    p.cnt["te"] += nfc * ntg

    @b.scalar
    def _(s):
        for fc in range(nfc):
            for tg in range(ntg):
                K = fc * ntg + tg
                s.wait_ge(sem["te"], TE + K + 1)
                if K >= 2:
                    s.wait_ge(p.sems[f"pfo{name}{K % 2}"],
                              SO[K % 2] + 16 * (K // 2))
                s.activation(stage[K % 2][:], pf[K % 2][:], AF.Identity)\
                    .then_inc(sem["sc"], 1)
    p.cnt["sc"] += nfc * ntg

    @b.sync
    def _(sync):
        for fc in range(nfc):
            for tg in range(ntg):
                K = fc * ntg + tg
                sync.wait_ge(sem["sc"], SC + K + 1)
                bi = sync.dma_start(out=out_hbm[fc * 128:(fc + 1) * 128,
                                                tg * 512:tg * 512 + 512],
                                    in_=stage[K % 2][:])
                p.inc(bi, f"pfo{name}{K % 2}", 16)
    return [p.mark(f"pfo{name}0"), p.mark(f"pfo{name}1")]


def emit_proj_tm(p, uT, w_hbm, w_cols, b_hbm, b_off, out_hbm, consts, ln_mark,
                 tspan, t0=0, nfg=4, name=""):
    _, ones_row, _, cready = consts
    b = p.block
    sem = p.sems
    ntl = tspan // 128
    GR = 4
    ngr = ntl // GR
    wtile = [p.sb(f"ptw{name}{i}", [128, 512], BF16) for i in range(2)]
    stage = [p.sb(f"pts{name}{i}", [128, 512], BF16) for i in range(2)]
    for i in range(2):
        p.sem(f"ptl{name}{i}")
        p.sem(f"pto{name}{i}")
    p.sem(f"ptlb{name}")
    pt = p.pf[0:4]
    brow = p.sb(f"ptb{name}", [1, nfg * 512], BF16) if b_hbm is not None else None

    TE = p.cnt["te"]
    SC = p.cnt["sc"]
    SO = [p.cnt[f"pto{name}{i}"] for i in range(2)]
    wmark = []
    bmark = []

    @b.gpsimd
    def _(g):
        if b_hbm is not None:
            bi = g.dma_start(out=brow[:], in_=b_hbm[:, b_off:b_off + nfg * 512])
            bmark.append(p.inc(bi, f"ptlb{name}", 16))
        for fg in range(nfg):
            for ttg in range(ngr):
                for dc in range(ND):
                    L = (fg * ngr + ttg) * ND + dc
                    if L >= 2:
                        g.wait_ge(sem["te"], TE + L - 1)
                    else:
                        g.wait_ge(sem["te"], TE)
                    bi = g.dma_start(
                        out=wtile[L % 2][:],
                        in_=w_hbm[dc * 128:(dc + 1) * 128,
                                  w_cols + fg * 512:w_cols + (fg + 1) * 512])
                    wmark.append(p.inc(bi, f"ptl{name}{L % 2}", 16))

    @b.tensor
    def _(t):
        wge(t, cready)
        for fg in range(nfg):
            for ttg in range(ngr):
                grp = fg * ngr + ttg
                for dc in range(ND):
                    L = grp * ND + dc
                    wge(t, wmark[L])
                    if L == 0 and bmark:
                        wge(t, bmark[0])
                    for i in range(GR):
                        tt = ttg * GR + i
                        if dc == 0:
                            if grp >= 1:
                                t.wait_ge(sem["sc"], SC + (grp - 1) * GR + i + 1)
                            else:
                                t.wait_ge(sem["sc"], SC)
                            wge(t, ln_mark(tt))
                            if b_hbm is not None:
                                t.matmul(pt[i][:], ones_row[0:1, 0:128],
                                         brow[0:1, fg * 512:(fg + 1) * 512],
                                         start=True, stop=False)
                        bi = t.matmul(
                            pt[i][:],
                            apx(uT)[:, dc * tspan + tt * 128: dc * tspan + tt * 128 + 128],
                            wtile[L % 2][:],
                            start=(dc == 0 and b_hbm is None),
                            stop=(dc == ND - 1))
                    bi.then_inc(sem["te"], 1)
    p.cnt["te"] += nfg * ngr * ND

    @b.scalar
    def _(s):
        for fg in range(nfg):
            for ttg in range(ngr):
                grp = fg * ngr + ttg
                s.wait_ge(sem["te"], TE + (grp + 1) * ND)
                for i in range(GR):
                    E = grp * GR + i
                    if E >= 2:
                        s.wait_ge(p.sems[f"pto{name}{E % 2}"],
                                  SO[E % 2] + 16 * (E // 2))
                    s.activation(stage[E % 2][:], pt[i][:], AF.Identity)\
                        .then_inc(sem["sc"], 1)
    p.cnt["sc"] += nfg * ngr * GR

    @b.sync
    def _(sync):
        for fg in range(nfg):
            for ttg in range(ngr):
                grp = fg * ngr + ttg
                for i in range(GR):
                    tt = ttg * GR + i
                    E = grp * GR + i
                    sync.wait_ge(sem["sc"], SC + E + 1)
                    bi = sync.dma_start(
                        out=out_hbm[(t0 + tt) * 128:(t0 + tt + 1) * 128,
                                    fg * 512:(fg + 1) * 512],
                        in_=stage[E % 2][:])
                    p.inc(bi, f"pto{name}{E % 2}", 16)
    return [p.mark(f"pto{name}0"), p.mark(f"pto{name}1")]


def emit_attn_core(p, qTh, kv_own, kv_all, alpha_in, consts, scale, l1,
                   AVTn, name="", guard_marks=()):
    ident, ones_row, ones128, cready = consts
    b = p.block
    sem = p.sems
    kT_h = [p.sb(f"at_k{name}{i}", [128, S], BF16) for i in range(2)]
    v_h = [p.sb(f"at_v{name}{i}", [128, 32 * 128], BF16) for i in range(2)]
    qT_h = [p.sb(f"at_q{name}{i}", [128, T], BF16) for i in range(2)]
    pA = p.sb(f"at_pa{name}", [128, T], BF16)
    pC = p.sb(f"at_pc{name}", [128, T], BF16)
    expT = p.sb(f"at_e{name}", [128, 16 * 512], BF16)
    r_s = p.sb(f"at_r{name}", [128, 512], F32)
    a_sb = p.sb(f"at_al{name}", [1, 2], BF16)
    lnA = p.sb(f"at_lna{name}", [128, 1], F32)
    betac = p.sb(f"at_bc{name}", [128, 1], F32)
    B_t = p.sb(f"at_bt{name}", [128, 512], BF16)
    for i in range(2):
        p.sem(f"athd{name}{i}")
    p.sem(f"atal{name}")
    ps_sd = p.pfd
    ps_l = p.pf[4]
    ps_av = p.pf[5]
    ps_b = p.pf[4]

    NQG = T // 512

    def classify(h, qg, ktc):
        if ktc >= 16:
            return "blend" if l1 else "full"
        if not l1:
            return "full"
        lo = 128 * ktc
        if lo >= 512 * (qg + 1):
            return "skip"
        if lo + 127 <= 512 * qg:
            return "full"
        return "diag"

    TE0 = p.cnt["te"]
    SC0 = p.cnt["sc"]
    GP0 = p.cnt["gp"]
    VE0 = p.cnt["ve"]
    def pdone(h):
        return VE0 + 1 + 4 * h + 8 * max(0, h - 1) + 4

    def grp_base(H):
        if H <= NH - 3:
            return VE0 + 9 + 12 * H
        return VE0 + 1 + 4 * NH + 8 * H

    rc_mark = {}
    mu_mark = {}
    for H in range(NH):
        for qg in range(NH * 0 + 4):
            rc_mark[(H, qg)] = grp_base(H) + 2 * qg + 1
            mu_mark[(H, qg)] = grp_base(H) + 2 * qg + 2
    VE_END = VE0 + 1 + 4 * NH + 8 * NH
    TE1 = TE0 + 1
    SC1 = SC0 + 2

    te_n, sc_n = TE1, SC1
    gp_n = GP0 + (4 if l1 else 0)
    score_mark = {}
    exp_mark = {}
    gp_mark = {}
    lp0_mark = {}
    avp0_mark = {}
    l_mark = {}
    av_mark = {}
    exp_done_own = {}
    exp_done = {}
    n_own_of = {}
    for h in range(NH):
        for qg in range(NQG):
            own = [k for k in range(16) if classify(h, qg, k) != "skip"]
            assert own == list(range(len(own))) and len(own) % 2 == 0
            n_own_of[(h, qg)] = len(own)
            for jp in range(len(own) // 2):
                te_n += 2
                score_mark[(h, qg, 2 * jp)] = te_n - 1
                score_mark[(h, qg, 2 * jp + 1)] = te_n
                sc_n += 1
                exp_mark[(h, qg, 2 * jp)] = sc_n
                exp_mark[(h, qg, 2 * jp + 1)] = sc_n
            exp_done_own[(h, qg)] = sc_n
            te_n += 1
            lp0_mark[(h, qg)] = te_n
            te_n += 1
            avp0_mark[(h, qg)] = te_n
            for jp in range(8):
                te_n += 2
                score_mark[(h, qg, 16 + 2 * jp)] = te_n - 1
                score_mark[(h, qg, 17 + 2 * jp)] = te_n
                sc_n += 1
                exp_mark[(h, qg, 16 + 2 * jp)] = sc_n
                exp_mark[(h, qg, 17 + 2 * jp)] = sc_n
            if l1:
                k0 = 4 * qg
                if k0 + 4 <= 15:
                    gp_n += 1
                    for k in range(k0 + 4, 16):
                        gp_mark[(h, qg, k)] = gp_n
                gp_n += 2
                for k in range(k0, k0 + 4):
                    gp_mark[(h, qg, k)] = gp_n
            exp_done[(h, qg)] = sc_n
            te_n += 1
            l_mark[(h, qg)] = te_n
            te_n += 1
            av_mark[(h, qg)] = te_n

    def prev_group(h, qg):
        if qg > 0:
            return (h, qg - 1)
        if h > 0:
            return (h - 1, NQG - 1)
        return None

    amark = []
    hmark = []

    @b.sync
    def _(sync):
        for mk in guard_marks:
            wge(sync, mk)
        sync.wait_ge(sem["te"], TE0)
        sync.wait_ge(sem["ve"], VE0)
        bi = sync.dma_start(out=a_sb[:], in_=alpha_in)
        amark.append(p.inc(bi, f"atal{name}", 16))
        for h in range(NH):
            if h >= 2:
                sync.wait_ge(sem["te"], av_mark[(h - 2, NQG - 1)])
                sync.wait_ge(sem["ve"], pdone(h - 2))
            if h >= 1:
                sync.wait_ge(sem["ve"], pdone(h - 1))
            sn = f"athd{name}{h % 2}"
            bi = sync.dma_start(out=kT_h[h % 2][:, 0:T],
                                in_=kv_own[h * 128:(h + 1) * 128, :])
            p.inc(bi, sn, 16)
            bi = sync.dma_start(out=pA[:],
                                in_=kv_all[h * 128:(h + 1) * 128, :])
            p.inc(bi, sn, 16)
            bi = sync.dma_start(
                out=kT_h[h % 2][:, T:2 * T],
                in_=kv_all[2 * T + h * 128:2 * T + (h + 1) * 128, :])
            p.inc(bi, sn, 16)
            src = kv_own[T:2 * T, h * 128:(h + 1) * 128]\
                .rearrange("(c p) f -> p c f", p=128)
            dst = v_h[h % 2].ap().rearrange("p (c f) -> p c f",
                                            c=32, f=128)[:, 0:16, :]
            bi = sync.dma_start(out=dst, in_=src)
            p.inc(bi, sn, 16)
            src = kv_all[T:2 * T, h * 128:(h + 1) * 128]\
                .rearrange("(c p) f -> p c f", p=128)
            bi = sync.dma_start(
                out=pC.ap().rearrange("p (c f) -> p c f", c=16, f=128),
                in_=src)
            p.inc(bi, sn, 16)
            src = kv_all[3 * T:4 * T, h * 128:(h + 1) * 128]\
                .rearrange("(c p) f -> p c f", p=128)
            bi = sync.dma_start(
                out=v_h[h % 2].ap().rearrange("p (c f) -> p c f",
                                              c=32, f=128)[:, 16:32, :],
                in_=src)
            p.inc(bi, sn, 16)
            bi = sync.dma_start(out=qT_h[h % 2][:],
                                in_=qTh[h * 128:(h + 1) * 128, :])
            hmark.append(p.inc(bi, sn, 16))

    def emit_partner(v, h):
        wge(v, hmark[h])
        if h >= 2:
            v.wait_ge(sem["te"], av_mark[(h - 2, NQG - 1)])
        ve_b = pdone(h) - 4
        v.tensor_add(pA[:], pA[:], kT_h[h % 2][:, T:2 * T])\
            .then_inc(sem["ve"], 1)
        v.wait_ge(sem["ve"], ve_b + 1)
        v.tensor_sub(kT_h[h % 2][:, T:2 * T], pA[:],
                     kT_h[h % 2][:, 0:T]).then_inc(sem["ve"], 1)
        v.tensor_add(pC[:], pC[:],
                     v_h[h % 2][:, 16 * 128:32 * 128]).then_inc(sem["ve"], 1)
        v.wait_ge(sem["ve"], ve_b + 3)
        v.tensor_sub(v_h[h % 2][:, 16 * 128:32 * 128], pC[:],
                     v_h[h % 2][:, 0:16 * 128]).then_inc(sem["ve"], 1)

    def emit_grp(v, H):
        for qg in range(NQG):
            v.wait_ge(sem["te"], l_mark[(H, qg)])
            if not (H == 0 and qg == 0):
                pm = mu_mark[(H, qg - 1)] if qg > 0 else mu_mark[(H - 1, 3)]
                v.wait_ge(sem["ve"], pm)
            v.reciprocal(r_s[:], ps_l[:]).then_inc(sem["ve"], 1)
            v.wait_ge(sem["ve"], rc_mark[(H, qg)])
            v.wait_ge(sem["te"], av_mark[(H, qg)])
            v.tensor_mul(apx(AVTn)[:, H * T + qg * 512: H * T + qg * 512 + 512],
                         ps_av[:], r_s[:]).then_inc(sem["ve"], 1)

    @b.vector
    def _(v):
        wge(v, (sem["sc"], SC1))
        v.tensor_copy(B_t.ap().rearrange("p (a f) -> p a f", a=1, f=512),
                      betac.ap().broadcast_to([128, 1, 512]))\
            .then_inc(sem["ve"], 1)
        emit_partner(v, 0)
        emit_partner(v, 1)
        for h in range(2, NH):
            emit_grp(v, h - 2)
            emit_partner(v, h)
        emit_grp(v, NH - 2)
        emit_grp(v, NH - 1)
    p.cnt["ve"] = VE_END

    @b.tensor
    def _(t):
        wge(t, cready)
        wge(t, amark[0])
        t.wait_ge(sem["sc"], SC0)
        t.matmul(ps_b[:, 0:2], ones_row[0:1, 0:128], a_sb[0:1, :],
                 start=True, stop=True).then_inc(sem["te"], 1)
    p.cnt["te"] = TE1

    @b.scalar
    def _(s):
        s.wait_ge(sem["te"], TE1)
        s.activation(lnA[:], ps_b[:, 0:1], AF.Identity).then_inc(sem["sc"], 1)
        s.activation(betac[:], ps_b[:, 1:2], AF.Identity).then_inc(sem["sc"], 1)
    p.cnt["sc"] = SC1

    @b.tensor
    def _(t):
        w_sc = [0]
        w_gp = [0]

        def wsc(val):
            if val > w_sc[0]:
                t.wait_ge(sem["sc"], val)
                w_sc[0] = val

        def wgp(val):
            if val > w_gp[0]:
                t.wait_ge(sem["gp"], val)
                w_gp[0] = val

        for h in range(NH):
            wge(t, hmark[h])
            t.wait_ge(sem["ve"], pdone(h))
            for qg in range(NQG):
                pg = prev_group(h, qg)
                n_own = n_own_of[(h, qg)]
                for jp in range(n_own // 2):
                    if jp >= 2:
                        wsc(exp_mark[(h, qg, 2 * (jp - 2))])
                    elif pg is not None:
                        wsc(exp_done[pg])
                    else:
                        wsc(SC1)
                    for kk in (2 * jp, 2 * jp + 1):
                        bi = t.matmul(
                            ps_sd[jp % 2][:, (kk % 2) * 512:(kk % 2) * 512 + 512],
                            kT_h[h % 2][:, kk * 128:(kk + 1) * 128],
                            qT_h[h % 2][:, qg * 512:(qg + 1) * 512],
                            start=True, stop=True)
                        bi.then_inc(sem["te"], 1)
                for ktc in range(16):
                    c = classify(h, qg, ktc)
                    if c == "full":
                        wsc(exp_mark[(h, qg, ktc)])
                    else:
                        wgp(gp_mark[(h, qg, ktc)])
                    if ktc == 0 and pg is not None:
                        t.wait_ge(sem["ve"], rc_mark[pg])
                    bi = t.matmul(ps_l[:], ones128[:],
                                  expT[:, ktc * 512:(ktc + 1) * 512],
                                  start=(ktc == 0), stop=False)
                bi.then_inc(sem["te"], 1)
                for ktc in range(16):
                    if ktc == 0 and pg is not None:
                        t.wait_ge(sem["ve"], mu_mark[pg])
                    bi = t.matmul(ps_av[:],
                                  v_h[h % 2][:, ktc * 128:(ktc + 1) * 128],
                                  expT[:, ktc * 512:(ktc + 1) * 512],
                                  start=(ktc == 0), stop=False)
                bi.then_inc(sem["te"], 1)
                for jp in range(8):
                    if jp >= 2:
                        wsc(exp_mark[(h, qg, 16 + 2 * (jp - 2))])
                    else:
                        wsc(exp_done_own[(h, qg)])
                    for kk in (16 + 2 * jp, 17 + 2 * jp):
                        bi = t.matmul(
                            ps_sd[jp % 2][:, (kk % 2) * 512:(kk % 2) * 512 + 512],
                            kT_h[h % 2][:, kk * 128:(kk + 1) * 128],
                            qT_h[h % 2][:, qg * 512:(qg + 1) * 512],
                            start=True, stop=True)
                        bi.then_inc(sem["te"], 1)
                for ktc in range(16, 32):
                    wsc(exp_mark[(h, qg, ktc)])
                    bi = t.matmul(ps_l[:], ones128[:],
                                  expT[:, (ktc - 16) * 512:(ktc - 15) * 512],
                                  start=False, stop=(not l1 and ktc == 31))
                if l1:
                    bi = t.matmul(ps_l[:], p._sixteens[:], B_t[:],
                                  start=False, stop=True)
                bi.then_inc(sem["te"], 1)
                for ktc in range(16, 32):
                    bi = t.matmul(ps_av[:],
                                  v_h[h % 2][:, ktc * 128:(ktc + 1) * 128],
                                  expT[:, (ktc - 16) * 512:(ktc - 15) * 512],
                                  start=False, stop=(not l1 and ktc == 31))
                if l1:
                    for ktc in range(16, 32):
                        bi = t.matmul(ps_av[:],
                                      v_h[h % 2][:, ktc * 128:(ktc + 1) * 128],
                                      B_t[:], start=False, stop=(ktc == 31))
                bi.then_inc(sem["te"], 1)
    p.cnt["te"] = te_n

    @b.scalar
    def _(s):
        for h in range(NH):
            for qg in range(NQG):
                pg = prev_group(h, qg)
                n_own = n_own_of[(h, qg)]
                for jp in range(n_own // 2):
                    if jp == 0 and pg is not None:
                        s.wait_ge(sem["te"], av_mark[pg])
                    s.wait_ge(sem["te"], score_mark[(h, qg, 2 * jp + 1)])
                    s.activation(expT[:, 2 * jp * 512:(2 * jp + 2) * 512],
                                 ps_sd[jp % 2][:], AF.Exp, scale=scale)\
                        .then_inc(sem["sc"], 1)
                for jp in range(8):
                    if jp == 0:
                        s.wait_ge(sem["te"], avp0_mark[(h, qg)])
                    s.wait_ge(sem["te"], score_mark[(h, qg, 17 + 2 * jp)])
                    if l1:
                        s.activation(expT[:, 2 * jp * 512:(2 * jp + 2) * 512],
                                     ps_sd[jp % 2][:], AF.Exp, scale=scale,
                                     bias=lnA[:]).then_inc(sem["sc"], 1)
                    else:
                        s.activation(expT[:, 2 * jp * 512:(2 * jp + 2) * 512],
                                     ps_sd[jp % 2][:], AF.Exp, scale=scale)\
                            .then_inc(sem["sc"], 1)
    p.cnt["sc"] = sc_n

    if l1:
        msk1 = p.sb(f"at_m1{name}", [128, 4 * 512], BF16)
        msk0 = p.sb(f"at_m0{name}", [128, 4 * 512], BF16)

        @b.gpsimd
        def _(g):
            g.memset(msk1[:], 1.0).then_inc(sem["gp"], 1)
            g.wait_ge(sem["gp"], GP0 + 1)
            m3 = msk1.ap().rearrange("p (c f) -> p c f", c=4, f=512)
            g.affine_select(out=m3, in_=m3, compare_op=ALU.is_ge, fill=0.0,
                            base=0, pattern=[[-128, 4], [1, 512]],
                            channel_multiplier=-1).then_inc(sem["gp"], 1)
            g.memset(msk0[:], 0.0).then_inc(sem["gp"], 1)
            g.wait_ge(sem["gp"], GP0 + 3)
            m3 = msk0.ap().rearrange("p (c f) -> p c f", c=4, f=512)
            g.affine_select(out=m3, in_=m3, compare_op=ALU.is_ge, fill=1.0,
                            base=0, pattern=[[-128, 4], [1, 512]],
                            channel_multiplier=-1).then_inc(sem["gp"], 1)
            gpc = GP0 + 4
            g.wait_ge(sem["ve"], VE0 + 1)
            for h in range(NH):
                for qg in range(NQG):
                    pg = prev_group(h, qg)
                    k0 = 4 * qg
                    if pg is not None:
                        g.wait_ge(sem["te"], av_mark[pg])
                    if k0 + 4 <= 15:
                        g.memset(expT[:, (k0 + 4) * 512:16 * 512], 1.0)\
                            .then_inc(sem["gp"], 1)
                        gpc += 1
                    g.wait_ge(sem["sc"], exp_mark[(h, qg, k0 + 3)])
                    dg = expT[:, k0 * 512:(k0 + 4) * 512]
                    g.tensor_mul(dg, dg, msk1[:]).then_inc(sem["gp"], 1)
                    gpc += 1
                    g.wait_ge(sem["gp"], gpc)
                    g.tensor_add(dg, dg, msk0[:]).then_inc(sem["gp"], 1)
                    gpc += 1
            assert gpc == gp_n, (gpc, gp_n)
    p.cnt["gp"] = gp_n

    return p.mark("ve")


def build_attn(l1):
    p = Prog("attn")
    nc = p.nc
    qTh = nc.dram_tensor("qT", [D, T], BF16, kind="ExternalInput").ap()
    kv_own = nc.dram_tensor("kv_own", [2 * T, D], BF16, kind="ExternalInput").ap()
    kv_all = nc.dram_tensor("kv_all", [4 * T, D], BF16, kind="ExternalInput").ap()
    alpha = nc.dram_tensor("alpha", [1, 2], BF16, kind="ExternalInput").ap()
    x = nc.dram_tensor("x", [T, D], F32, kind="ExternalInput").ap()
    w_out = nc.dram_tensor("w_out", [D, D], F32, kind="ExternalInput").ap()
    b_out = None if l1 else \
        nc.dram_tensor("b_out", [1, D], F32, kind="ExternalInput").ap()
    x1 = nc.dram_tensor("x1", [T, D], F32, kind="ExternalOutput").ap()

    AVTn = p.sb("AVTn", [128, NH * T], BF16)
    p.start()
    consts = emit_consts(p)
    scale = L1_SCALE if l1 else L0_SCALE
    avm = emit_attn_core(p, qTh, kv_own, kv_all, alpha, consts, scale, l1, AVTn)
    mk = emit_mm_ln_resid(
        p, lambda ck, tt: AVTn[:, ck * T + tt * 128: ck * T + tt * 128 + 128],
        ND, w_out, True, b_out, x, x1, consts, [avm], 0, NT, name="o")
    emit_final_wait(p, mk)
    return p.finish()


def emit_gu(p, uT, w_guT, h_actT, consts, ready_mark, name="", guard_marks=()):
    _, ones_row, _, cready = consts
    b = p.block
    sem = p.sems
    NI = 44
    wg = [p.sb(f"guwg{name}{i}", [128, ND * 128], BF16) for i in range(2)]
    wu = [p.sb(f"guwu{name}{i}", [128, ND * 128], BF16) for i in range(2)]
    gsil = [p.sb(f"gusil{name}{i}", [128, 512], BF16) for i in range(2)]
    for i in range(2):
        p.sem(f"gug{name}{i}")
        p.sem(f"guu{name}{i}")
    psg = [p.pf[0], p.pf[1]]
    psu = [p.pf[2], p.pf[3]]

    TE = p.cnt["te"]
    SC = p.cnt["sc"]
    VE = p.cnt["ve"]
    gmark, umark = [], []

    @b.gpsimd
    def _(g):
        for mk in guard_marks:
            wge(g, mk)
        for ic in range(NI):
            if ic >= 2:
                g.wait_ge(sem["te"], TE + 2 * (ic - 1))
            else:
                g.wait_ge(sem["te"], TE)
            for (wbuf, marks, semn, col) in (
                    (wg, gmark, f"gug{name}{ic % 2}", ic * 128),
                    (wu, umark, f"guu{name}{ic % 2}", 5632 + ic * 128)):
                srcw = w_guT[:, col:col + 128]\
                    .rearrange("(dc p) f -> p dc f", p=128)
                dstw = wbuf[ic % 2].ap().rearrange("p (dc f) -> p dc f",
                                                   dc=ND, f=128)
                bi = g.dma_start(out=dstw, in_=srcw)
                marks.append(p.inc(bi, semn, 16))

    @b.tensor
    def _(t):
        wge(t, cready)
        wge(t, ready_mark)
        for ic in range(NI):
            wge(t, gmark[ic])
            wge(t, umark[ic])
            if ic >= 2:
                t.wait_ge(sem["sc"], SC + (ic - 2) + 1)
                t.wait_ge(sem["ve"], VE + (ic - 2) + 1)
            else:
                t.wait_ge(sem["sc"], SC)
                t.wait_ge(sem["ve"], VE)
            for dc in range(ND):
                bi = t.matmul(psg[ic % 2][:],
                              wg[ic % 2][:, dc * 128:(dc + 1) * 128],
                              apx(uT)[:, dc * 512:dc * 512 + 512],
                              start=(dc == 0), stop=(dc == ND - 1))
            bi.then_inc(sem["te"], 1)
            for dc in range(ND):
                bi = t.matmul(psu[ic % 2][:],
                              wu[ic % 2][:, dc * 128:(dc + 1) * 128],
                              apx(uT)[:, dc * 512:dc * 512 + 512],
                              start=(dc == 0), stop=(dc == ND - 1))
            bi.then_inc(sem["te"], 1)
    p.cnt["te"] += 2 * NI

    @b.scalar
    def _(s):
        for ic in range(NI):
            s.wait_ge(sem["te"], TE + 2 * ic + 1)
            if ic >= 2:
                s.wait_ge(sem["ve"], VE + (ic - 2) + 1)
            else:
                s.wait_ge(sem["ve"], VE)
            s.activation(gsil[ic % 2][:], psg[ic % 2][:], AF.Silu)\
                .then_inc(sem["sc"], 1)
    p.cnt["sc"] += NI

    @b.vector
    def _(v):
        for mk in guard_marks:
            wge(v, mk)
        v.wait_ge(sem["te"], TE)
        for ic in range(NI):
            v.wait_ge(sem["sc"], SC + ic + 1)
            v.wait_ge(sem["te"], TE + 2 * ic + 2)
            v.tensor_mul(apx(h_actT)[:, ic * 512:(ic + 1) * 512],
                         gsil[ic % 2][:], psu[ic % 2][:]).then_inc(sem["ve"], 1)
    p.cnt["ve"] += NI

    return p.mark("ve")


def emit_dq(p, gup, gus, dnp, dns, guT, dnT, consts):
    W = 1536
    pk = p.sb("dq_pk", [128, W], I32)
    iscr = p.sb("dq_i", [128, W], I32)
    kf = p.sb("dq_kf", [128, W], F32)
    sf = p.sb("dq_sf", [128, W], F32)
    r4 = p.sb("dq_r4", [128, W], F32)
    r6 = p.sb("dq_r6", [128, W], F32)
    m1 = p.sb("dq_m1", [128, W], F32)
    sful = p.sb("dq_s", [128, W], F32)
    gusT = p.sb("dq_gusT", [128, 176], F32)
    dnsT = p.sb("dq_dnsT", [128, 96], F32)
    cm2 = p.sb("dq_cm2", [128, 1], F32)
    cm6 = p.sb("dq_cm6", [128, 1], F32)
    idf = p.sb("dq_idf", [128, 128], F32)
    deq_gu = p.sb("dq_dgu", [128, 88 * 256], BF16)
    deq_dn = p.sb("dq_ddn", [128, 16 * 768], BF16)
    gusf = deq_gu.ap().bitcast(F32)[0:2, 0:11264]
    dnsf = deq_dn.ap().bitcast(F32)[0:6, 0:2048]
    stg = [p.sb(f"dq_stg{i}", [128, 512], BF16) for i in range(2)]
    for s_ in ("dqpk", "dqpk2", "dqsc", "dqsc2"):
        p.sem(s_)

    b = p.block
    sem = p.sems
    psf = p.pf[4]
    pb = p.pb

    lm = {}

    @b.gpsimd
    def _(g):
        bi = g.dma_start(
            out=pk[:, 0:1408].rearrange("p (a w) -> p a w", a=44, w=32),
            in_=gup[0:5632, :].rearrange("(a p) w -> p a w", p=128))
        lm["gu0"] = p.inc(bi, "dqpk", 16)
        bi = g.dma_start(out=gusf, in_=gus)
        lm["gus"] = p.inc(bi, "dqsc", 16)
        bi = g.dma_start(out=dnsf, in_=dns)
        lm["dns"] = p.inc(bi, "dqsc2", 16)
        g.memset(cm2[:], -2.0).then_inc(sem["gp"], 1)
        g.memset(cm6[:], -6.0).then_inc(sem["gp"], 1)
        g.memset(idf[:], 0.0).then_inc(sem["gp"], 1)
        g.wait_ge(sem["gp"], p.cnt["gp"] + 3)
        g.affine_select(out=idf[:], in_=idf[:], compare_op=ALU.not_equal,
                        fill=1.0, base=0, pattern=[[-1, 128]],
                        channel_multiplier=1).then_inc(sem["gp"], 1)
    p.cnt["gp"] += 4
    gp_done = p.mark("gp")

    @b.tensor
    def _(t):
        wge(t, gp_done)
        wge(t, lm["gus"])
        for a in range(88):
            bi = t.transpose(psf[:, 2 * a:2 * a + 2],
                             gusf[:, a * 128:(a + 1) * 128], idf[0:2, 0:2])
        bi.then_inc(sem["te"], 1)
    p.cnt["te"] += 1
    tm_gus = p.mark("te")

    @b.vector
    def _(v):
        wge(v, tm_gus)
        v.tensor_copy(gusT[:], psf[:, 0:176]).then_inc(sem["ve"], 1)
    p.cnt["ve"] += 1
    gusT_done = p.mark("ve")

    @b.tensor
    def _(t):
        wge(t, gusT_done)
        wge(t, lm["dns"])
        for a in range(16):
            bi = t.transpose(psf[:, 6 * a:6 * a + 6],
                             dnsf[:, a * 128:(a + 1) * 128], idf[0:6, 0:6])
        bi.then_inc(sem["te"], 1)
    p.cnt["te"] += 1
    tm_dns = p.mark("te")

    @b.vector
    def _(v):
        wge(v, tm_dns)
        v.tensor_copy(dnsT[:], psf[:, 0:96]).then_inc(sem["ve"], 1)
    p.cnt["ve"] += 1

    def emit_dq_planes(nwords, na, wper, scT, spere, deq, hl_of_w, lm_key):
        wsub = wper // spere
        VE0 = p.cnt["ve"]

        @b.vector
        def _(v):
            v.wait_ge(sem["ve"], VE0)
            srcS = apx(scT).rearrange("p (a s) -> p a s", a=na, s=spere)\
                .broadcast_to([128, na, spere, wsub])
            dstS = sful[:, 0:nwords].rearrange("p (a s w) -> p a s w",
                                               a=na, s=spere, w=wsub)
            v.tensor_copy(dstS, srcS).then_inc(sem["ve"], 1)
        p.cnt["ve"] += 1

        for j in range(8):
            VE = p.cnt["ve"]
            SC = p.cnt["sc"]

            @b.vector
            def _(v):
                wge(v, lm[lm_key])
                v.wait_ge(sem["ve"], VE)
                v.tensor_scalar(iscr[:, 0:nwords], pk[:, 0:nwords], 4 * j, 7,
                                ALU.logical_shift_right, ALU.bitwise_and)\
                    .then_inc(sem["ve"], 1)
                v.wait_ge(sem["ve"], VE + 1)
                v.tensor_copy(kf[:, 0:nwords], iscr[:, 0:nwords])\
                    .then_inc(sem["ve"], 1)
                v.wait_ge(sem["ve"], VE + 2)
                v.tensor_scalar(iscr[:, 0:nwords], pk[:, 0:nwords], 4 * j, 8,
                                ALU.logical_shift_right, ALU.bitwise_and)\
                    .then_inc(sem["ve"], 1)
                v.wait_ge(sem["ve"], VE + 3)
                v.tensor_copy(sf[:, 0:nwords], iscr[:, 0:nwords])\
                    .then_inc(sem["ve"], 1)
                v.wait_ge(sem["ve"], VE + 4)
                v.tensor_scalar(m1[:, 0:nwords], sf[:, 0:nwords], -0.25, 1.0,
                                ALU.mult, ALU.add).then_inc(sem["ve"], 1)
                v.wait_ge(sem["ve"], VE + 5)
                v.wait_ge(sem["sc"], SC + 2)
                v.scalar_tensor_tensor(sf[:, 0:nwords], kf[:, 0:nwords], 0.5,
                                       r4[:, 0:nwords], op0=ALU.mult,
                                       op1=ALU.add).then_inc(sem["ve"], 1)
                v.wait_ge(sem["ve"], VE + 6)
                v.tensor_add(kf[:, 0:nwords], sf[:, 0:nwords], r6[:, 0:nwords])\
                    .then_inc(sem["ve"], 1)
                v.wait_ge(sem["ve"], VE + 7)
                v.tensor_mul(r4[:, 0:nwords], kf[:, 0:nwords], m1[:, 0:nwords])\
                    .then_inc(sem["ve"], 1)
                v.wait_ge(sem["ve"], VE + 8)
                dstp = apx(deq).rearrange("p (a w) -> p a w", a=na, w=8 * wper)[
                    :, :, hl_of_w(j)::8]
                srcp = r4[:, 0:nwords].rearrange("p (a w) -> p a w", a=na, w=wper)
                srcs = sful[:, 0:nwords].rearrange("p (a w) -> p a w", a=na, w=wper)
                v.tensor_tensor(dstp, srcp, srcs, op=ALU.mult)\
                    .then_inc(sem["ve"], 1)
            p.cnt["ve"] += 9

            @b.scalar
            def _(s):
                s.wait_ge(sem["ve"], VE + 2)
                if j > 0:
                    s.wait_ge(sem["ve"], VE)
                wge(s, gp_done)
                s.activation(r4[:, 0:nwords], kf[:, 0:nwords], AF.Relu,
                             scale=0.5, bias=cm2[:]).then_inc(sem["sc"], 1)
                s.activation(r6[:, 0:nwords], kf[:, 0:nwords], AF.Relu,
                             bias=cm6[:]).then_inc(sem["sc"], 1)
            p.cnt["sc"] += 2

    emit_dq_planes(1408, 44, 32, gusT[:, 0:88],
                   2, deq_gu[:, 0:44 * 256], lambda j: j, "gu0")
    gu0_done = p.mark("ve")

    @b.gpsimd
    def _(g):
        wge(g, gu0_done)
        bi = g.dma_start(
            out=pk[:, 0:1408].rearrange("p (a w) -> p a w", a=44, w=32),
            in_=gup[5632:11264, :].rearrange("(a p) w -> p a w", p=128))
        lm["gu1"] = p.inc(bi, "dqpk", 16)
    emit_dq_planes(1408, 44, 32, gusT[:, 88:176],
                   2, deq_gu[:, 44 * 256:88 * 256], lambda j: j, "gu1")
    guplanes_done = p.mark("ve")

    @b.gpsimd
    def _(g):
        wge(g, guplanes_done)
        bi = g.dma_start(out=pk[:, 0:1536].rearrange("p (a w) -> p a w",
                                                     a=16, w=96),
                         in_=dnp.rearrange("(a p) w -> p a w", p=128))
        lm["dn"] = p.inc(bi, "dqpk2", 16)
    emit_dq_planes(1536, 16, 96, dnsT, 6, deq_dn, lambda j: j, "dn")

    def emit_store_T(deq, ntile_a, per_a, nsub, out_hbm, stsem, base_mark,
                     guard_marks=()):
        TE = p.cnt["te"]
        VE = p.cnt["ve"]
        ssl = [f"{stsem}a", f"{stsem}b"]
        for s_ in ssl:
            p.sem(s_)
        ST0 = [p.cnt[s_] for s_ in ssl]
        nbatch = ntile_a // 4
        ops = []
        for sub in range(nsub):
            for ba in range(nbatch):
                ops.append((sub, ba))

        @b.tensor
        def _(t):
            wge(t, base_mark)
            for n_, (sub, ba) in enumerate(ops):
                if n_ >= 2:
                    t.wait_ge(sem["ve"], VE + (n_ - 2) + 1)
                for q in range(4):
                    a = 4 * ba + q
                    bi = t.transpose(
                        pb[n_ % 2][:, q * 128:(q + 1) * 128],
                        apx(deq)[:, a * per_a + sub * 128: a * per_a + sub * 128 + 128],
                        consts[0][:])
                bi.then_inc(sem["te"], 1)
        p.cnt["te"] += len(ops)

        @b.vector
        def _(v):
            for n_, (sub, ba) in enumerate(ops):
                v.wait_ge(sem["te"], TE + n_ + 1)
                if n_ >= 2:
                    v.wait_ge(p.sems[ssl[n_ % 2]], ST0[n_ % 2] + 16 * (n_ // 2))
                else:
                    for mk in guard_marks:
                        wge(v, mk)
                v.tensor_copy(stg[n_ % 2][:], pb[n_ % 2][:]).then_inc(sem["ve"], 1)
        p.cnt["ve"] += len(ops)

        @b.sync
        def _(sync):
            for n_, (sub, ba) in enumerate(ops):
                sync.wait_ge(sem["ve"], VE + n_ + 1)
                bi = sync.dma_start(
                    out=out_hbm[sub * 128:(sub + 1) * 128,
                                ba * 512:(ba + 1) * 512],
                    in_=stg[n_ % 2][:])
                p.inc(bi, ssl[n_ % 2], 16)

    emit_store_T(deq_gu, 88, 256, 2, guT, "dqst0", p.mark("ve"))
    emit_store_T(deq_dn, 16, 768, 6, dnT, "dqst1", p.mark("ve"),
                 guard_marks=[p.mark("dqst0a"), p.mark("dqst0b")])
    return [p.mark("dqst0a"), p.mark("dqst0b"),
            p.mark("dqst1a"), p.mark("dqst1b")]




def build_dq():
    p = Prog("dq")
    nc = p.nc
    gup = nc.dram_tensor("gup_sh", [11264, 32], I32, kind="ExternalInput").ap()
    gus = nc.dram_tensor("gus_sh", [2, 11264], F16, kind="ExternalInput").ap()
    dnp = nc.dram_tensor("dnp_sh", [2048, 96], I32, kind="ExternalInput").ap()
    dns = nc.dram_tensor("dns_sh", [6, 2048], F16, kind="ExternalInput").ap()
    guT = nc.dram_tensor("guT_sh", [256, 11264], BF16, kind="ExternalOutput").ap()
    dnT = nc.dram_tensor("dnT_sh", [768, 2048], BF16, kind="ExternalOutput").ap()
    p.start()
    consts = emit_consts(p)
    mks = emit_dq(p, gup, gus, dnp, dns, guT, dnT, consts)
    emit_final_wait(p, mks)
    return p.finish()


def build_ffn():
    p = Prog("ffn")
    nc = p.nc
    x = nc.dram_tensor("x", [T, D], F32, kind="ExternalInput").ap()
    w_guT = nc.dram_tensor("w_guT", [D, 11264], BF16, kind="ExternalInput").ap()
    w_downT = nc.dram_tensor("w_downT", [5632, D], BF16, kind="ExternalInput").ap()
    x2 = nc.dram_tensor("x2", [T, D], F32, kind="ExternalOutput").ap()

    uT = p.sb("uTf", [128, ND * 512], BF16)
    h_actT = p.sb("hactT", [128, 44 * 512], BF16)
    p.start()
    consts = emit_consts(p)
    for tg in range(4):
        ln = emit_ln_uT(p, x, uT, 4 * tg, 4, consts, 512, name="f")
        gum = emit_gu(p, uT, w_guT, h_actT, consts, ln.all_done, name="f")
        emit_mm_ln_resid(
            p, lambda ck, tt: h_actT[:, ck * 512 + tt * 128: ck * 512 + tt * 128 + 128],
            44, w_downT, False, None, x, x2, consts, [gum], 4 * tg, 4, name="d")
    emit_final_wait(p, [p.mark("mlstd0"), p.mark("mlstd1")])
    return p.finish()


def emit_mm_ln_resid(p, lhsT_ap, nck, w_hbm, cast_w, b_hbm, x_hbm, out_hbm,
                     consts, ready_marks, t0, ntl, name="", guard_marks=(),
                     extra_mms=None):
    _, ones_row, _, cready = consts
    b = p.block
    sem = p.sems
    ngr = ntl // 2
    wtile = [p.sb(f"mlw{name}{i}", [128, 512], BF16) for i in range(2)]
    y_s = [p.sb(f"mly{name}{i}", [128, D], F32) for i in range(2)]
    o_s = [p.sb(f"mlo{name}{i}", [128, D], F32) for i in range(2)]
    xt = [p.sb(f"mlx{name}{i}", [128, D], F32) for i in range(2)]
    sums4 = [p.sb(f"mls4{name}{i}", [128, 4], F32) for i in range(2)]
    tot = [p.sb(f"mltot{name}{i}", [128, 1], F32) for i in range(2)]
    ssq = [p.sb(f"mlssq{name}{i}", [128, 1], F32) for i in range(2)]
    mean = [p.sb(f"mlmean{name}{i}", [128, 1], F32) for i in range(2)]
    msq = [p.sb(f"mlmsq{name}{i}", [128, 1], F32) for i in range(2)]
    var = [p.sb(f"mlvar{name}{i}", [128, 1], F32) for i in range(2)]
    std = [p.sb(f"mlstd{name}{i}", [128, 1], F32) for i in range(2)]
    rstd = [p.sb(f"mlrstd{name}{i}", [128, 1], F32) for i in range(2)]
    negmr = [p.sb(f"mlnegmr{name}{i}", [128, 1], F32) for i in range(2)]
    inst = getattr(p, "_ml_inst", 0)
    p._ml_inst = inst + 1
    wl = [f"mll{name}{inst}{i}" for i in range(2)]
    for i in range(2):
        p.sem(wl[i])
        p.sem(f"mlxl{name}{i}")
        p.sem(f"mlst{name}{i}")
    p.sem(f"mlb{name}")
    brow = p.sb(f"mlb{name}", [1, D], BF16) if b_hbm is not None else None
    pt = p.pf[0:2]

    TE = p.cnt["te"]
    SC = p.cnt["sc"]
    VE = p.cnt["ve"]
    STB = [p.cnt[f"mlst{name}{i}"] for i in range(2)]
    XLB = [p.cnt[f"mlxl{name}{i}"] for i in range(2)]

    def evac_sc(ttg, fg, i):
        return SC + 12 * ttg + 2 * fg + i + 1

    def sqrt_sc(ttg, i):
        return SC + 12 * ttg + 8 + 2 * i + 1

    def norm_sc(ttg, i):
        return SC + 12 * ttg + 8 + 2 * i + 2

    def ve_tile(ttg, i, k):
        return VE + 18 * ttg + 9 * i + k

    wmark = []
    bmark = []
    xmark = []

    @b.gpsimd
    def _(g):
        if b_hbm is not None:
            g.wait_ge(sem["te"], TE)
            bi = g.dma_start(out=brow[:], in_=b_hbm[:, 0:D])
            bmark.append(p.inc(bi, f"mlb{name}", 16))
        for ttg in range(ngr):
            for fg in range(4):
                for ck in range(nck):
                    L = (ttg * 4 + fg) * nck + ck
                    if L >= 2:
                        g.wait_ge(sem["te"], TE + L - 1)
                    else:
                        g.wait_ge(sem["te"], TE)
                    bi = g.dma_start(out=wtile[L % 2][:],
                                     in_=w_hbm[ck * 128:(ck + 1) * 128,
                                               fg * 512:(fg + 1) * 512])
                    wmark.append(p.inc(bi, wl[L % 2], 16))

    @b.tensor
    def _(t):
        wge(t, cready)
        for mk in ready_marks:
            wge(t, mk)
        if bmark:
            wge(t, bmark[0])
        for ttg in range(ngr):
            for fg in range(4):
                for ck in range(nck):
                    L = (ttg * 4 + fg) * nck + ck
                    wge(t, wmark[L])
                    for i in range(2):
                        tt = 2 * ttg + i
                        if ck == 0:
                            if fg > 0:
                                t.wait_ge(sem["sc"], evac_sc(ttg, fg - 1, i))
                            elif ttg > 0:
                                t.wait_ge(sem["sc"], evac_sc(ttg - 1, 3, i))
                            else:
                                t.wait_ge(sem["sc"], SC)
                            if b_hbm is not None:
                                t.matmul(pt[i][:], ones_row[0:1, 0:128],
                                         brow[0:1, fg * 512:(fg + 1) * 512],
                                         start=True, stop=False)
                        bi = t.matmul(pt[i][:], lhsT_ap(ck, tt), wtile[L % 2][:],
                                      start=(ck == 0 and b_hbm is None),
                                      stop=(ck == nck - 1 and extra_mms is None))
                        if ck == nck - 1 and extra_mms is not None:
                            bi = extra_mms(t, pt[i], tt, fg)
                    bi.then_inc(sem["te"], 1)
    p.cnt["te"] += ngr * 4 * nck

    @b.scalar
    def _(s):
        for ttg in range(ngr):
            for fg in range(4):
                s.wait_ge(sem["te"], TE + (ttg * 4 + fg + 1) * nck)
                for i in range(2):
                    if ttg >= 1 and fg == 0:
                        s.wait_ge(sem["ve"], ve_tile(ttg - 1, i, 9))
                    elif fg == 0:
                        s.wait_ge(sem["ve"], VE)
                    s.activation(y_s[i][:, fg * 512:(fg + 1) * 512], pt[i][:],
                                 AF.Identity,
                                 accum_out=sums4[i][:, fg:fg + 1])\
                        .then_inc(sem["sc"], 1)
            for i in range(2):
                s.wait_ge(sem["ve"], ve_tile(ttg, i, 6))
                s.activation(std[i][:], var[i][:], AF.Sqrt).then_inc(sem["sc"], 1)
                s.wait_ge(sem["ve"], ve_tile(ttg, i, 8))
                if ttg >= 1:
                    s.wait_ge(p.sems[f"mlst{name}{i}"], STB[i] + 16 * ttg)
                else:
                    s.wait_ge(p.sems[f"mlst{name}{i}"], STB[i])
                s.activation(o_s[i][:], y_s[i][:], AF.Identity,
                             bias=negmr[i][:], scale=rstd[i][:])\
                    .then_inc(sem["sc"], 1)
    p.cnt["sc"] += ngr * 12

    @b.vector
    def _(v):
        for ttg in range(ngr):
            for i in range(2):
                tt = 2 * ttg + i
                v.wait_ge(sem["sc"], evac_sc(ttg, 3, i))
                v.tensor_reduce(tot[i][:], sums4[i][:], axis=AX.X, op=ALU.add)\
                    .then_inc(sem["ve"], 1)
                if ttg >= 1:
                    v.wait_ge(p.sems[f"mlst{name}{i}"], STB[i] + 16 * ttg)
                else:
                    v.wait_ge(p.sems[f"mlst{name}{i}"], STB[i])
                v.scalar_tensor_tensor(o_s[i][:], y_s[i][:], 0.0, y_s[i][:],
                                       op0=ALU.add, op1=ALU.mult,
                                       accum_out=ssq[i][:]).then_inc(sem["ve"], 1)
                v.wait_ge(sem["ve"], ve_tile(ttg, i, 2))
                v.tensor_scalar_mul(mean[i][:], tot[i][:], 1.0 / D)\
                    .then_inc(sem["ve"], 1)
                v.wait_ge(sem["ve"], ve_tile(ttg, i, 3))
                v.scalar_tensor_tensor(msq[i][:], mean[i][:], 0.0, mean[i][:],
                                       op0=ALU.add, op1=ALU.mult)\
                    .then_inc(sem["ve"], 1)
                v.wait_ge(sem["ve"], ve_tile(ttg, i, 4))
                v.scalar_tensor_tensor(var[i][:], ssq[i][:], 1.0 / D, msq[i][:],
                                       op0=ALU.mult, op1=ALU.subtract)\
                    .then_inc(sem["ve"], 1)
                v.wait_ge(sem["ve"], ve_tile(ttg, i, 5))
                v.tensor_scalar_add(var[i][:], var[i][:], EPS)\
                    .then_inc(sem["ve"], 1)
                v.wait_ge(sem["sc"], sqrt_sc(ttg, i))
                v.reciprocal(rstd[i][:], std[i][:]).then_inc(sem["ve"], 1)
                v.wait_ge(sem["ve"], ve_tile(ttg, i, 7))
                v.scalar_tensor_tensor(negmr[i][:], mean[i][:], -1.0, rstd[i][:],
                                       op0=ALU.mult, op1=ALU.mult)\
                    .then_inc(sem["ve"], 1)
                v.wait_ge(sem["sc"], norm_sc(ttg, i))
                v.wait_ge(p.sems[f"mlxl{name}{i}"], XLB[i] + 16 * (ttg + 1))
                v.tensor_add(o_s[i][:], o_s[i][:], xt[i][:]).then_inc(sem["ve"], 1)
    p.cnt["ve"] += ngr * 18

    @b.sync
    def _(sync):
        for mk in guard_marks:
            wge(sync, mk)

        def store(ttg):
            for i in range(2):
                tt = 2 * ttg + i
                sync.wait_ge(sem["ve"], ve_tile(ttg, i, 9))
                bi = sync.dma_start(
                    out=out_hbm[(t0 + tt) * 128:(t0 + tt + 1) * 128, :],
                    in_=o_s[i][:])
                p.inc(bi, f"mlst{name}{i}", 16)

        for ttg in range(ngr):
            for i in range(2):
                tt = 2 * ttg + i
                if ttg >= 1:
                    sync.wait_ge(sem["ve"], ve_tile(ttg - 1, i, 9))
                else:
                    sync.wait_ge(sem["ve"], VE)
                bi = sync.dma_start(
                    out=xt[i][:],
                    in_=x_hbm[(t0 + tt) * 128:(t0 + tt + 1) * 128, :])
                xmark.append(p.inc(bi, f"mlxl{name}{i}", 16))
            if ttg >= 1:
                store(ttg - 1)
        store(ngr - 1)
    return [p.mark(f"mlst{name}0"), p.mark(f"mlst{name}1")]


def build_moe():
    p = Prog("moe")
    nc = p.nc
    x = nc.dram_tensor("x", [T, D], F32, kind="ExternalInput").ap()
    Wb = nc.dram_tensor("W_base", [D, D], F32, kind="ExternalInput").ap()
    bb = nc.dram_tensor("b_base", [1, D], F32, kind="ExternalInput").ap()
    As = nc.dram_tensor("A_s", [D, 16], F32, kind="ExternalInput").ap()
    Bs = nc.dram_tensor("B_s", [16, D], F32, kind="ExternalInput").ap()
    Ar = nc.dram_tensor("A_r_flat", [D, 96], F32, kind="ExternalInput").ap()
    Br = nc.dram_tensor("B_r_flat", [96, D], F32, kind="ExternalInput").ap()
    Wr = nc.dram_tensor("W_router", [D, 6], F32, kind="ExternalInput").ap()
    br = nc.dram_tensor("b_router", [1, 6], F32, kind="ExternalInput").ap()
    x4 = nc.dram_tensor("x4", [T, D], F32, kind="ExternalOutput").ap()

    HT = 8
    TSP = HT * 128
    uT = p.sb("uTm", [128, ND * TSP], BF16)
    Wb_bf = p.sb("m_wb", [128, ND * D], BF16)
    Ar_bf = p.sb("m_ar", [128, ND * 96], BF16)
    As_bf = p.sb("m_as", [128, ND * 16], BF16)
    Wr_bf = p.sb("m_wr", [128, ND * 6], BF16)
    Bs_bf = p.sb("m_bs", [16, D], BF16)
    Br_bf = p.sb("m_br", [96, D], BF16)
    bb_bf = p.sb("m_bb", [1, D], BF16)
    br_bf = p.sb("m_brr", [1, 6], BF16)
    E16 = p.sb("m_e16", [6, 96], BF16)
    xaT = p.sb("m_xa", [16, 128], BF16)
    hT = p.sb("m_h", [96, 128], BF16)
    hwT = p.sb("m_hw", [96, 128], BF16)
    wtT_s = p.sb("m_wtt", [6, 128], BF16)
    e_s = p.sb("m_es", [128, 6], F32)
    g_s = p.sb("m_gs", [128, 6], F32)
    g2_s = p.sb("m_g2", [128, 6], F32)
    kp_s = p.sb("m_kp", [128, 6], F32)
    wt_s = p.sb("m_wt", [128, 6], F32)
    wt_bf = p.sb("m_wtb", [128, 6], BF16)
    gmax = p.sb("m_gm", [128, 1], F32)
    negmax = p.sb("m_ngm", [128, 1], F32)
    esum = p.sb("m_esm", [128, 1], F32)
    er = p.sb("m_er", [128, 1], F32)
    m1s = p.sb("m_m1", [128, 1], F32)
    m2s = p.sb("m_m2", [128, 1], F32)
    y_s = [p.sb(f"m_y{i}", [128, D], F32) for i in range(2)]
    o_s = [p.sb(f"m_o{i}", [128, D], F32) for i in range(2)]
    xt = [p.sb(f"m_x{i}", [128, D], F32) for i in range(2)]
    sums4 = [p.sb(f"m_s4{i}", [128, 4], F32) for i in range(2)]
    tot = [p.sb(f"m_tot{i}", [128, 1], F32) for i in range(2)]
    ssq = [p.sb(f"m_ssq{i}", [128, 1], F32) for i in range(2)]
    meanb = [p.sb(f"m_mean{i}", [128, 1], F32) for i in range(2)]
    msqb = [p.sb(f"m_msq{i}", [128, 1], F32) for i in range(2)]
    varb = [p.sb(f"m_var{i}", [128, 1], F32) for i in range(2)]
    stdb = [p.sb(f"m_std{i}", [128, 1], F32) for i in range(2)]
    rstdb = [p.sb(f"m_rstd{i}", [128, 1], F32) for i in range(2)]
    negmrb = [p.sb(f"m_negmr{i}", [128, 1], F32) for i in range(2)]
    for s_ in ("mwl", "mxl0", "mxl1", "mst0", "mst1"):
        p.sem(s_)

    p.start()
    consts = emit_consts(p)
    ident, ones_row, ones128, cready = consts
    b = p.block
    sem = p.sems
    GP = p.cnt["gp"]

    lm = {}

    @b.gpsimd
    def _(g):
        def ld(dst3, src3, key):
            bi = g.dma_start(out=dst3, in_=src3)
            lm[key] = p.inc(bi, "mwl", 16)
        ld(Wb_bf.ap().rearrange("p (dc f) -> p dc f", dc=ND, f=D),
           Wb.rearrange("(dc p) f -> p dc f", p=128), "wb")
        ld(Ar_bf.ap().rearrange("p (dc f) -> p dc f", dc=ND, f=96),
           Ar.rearrange("(dc p) f -> p dc f", p=128), "ar")
        ld(As_bf.ap().rearrange("p (dc f) -> p dc f", dc=ND, f=16),
           As.rearrange("(dc p) f -> p dc f", p=128), "as")
        ld(Wr_bf.ap().rearrange("p (dc f) -> p dc f", dc=ND, f=6),
           Wr.rearrange("(dc p) f -> p dc f", p=128), "wr")
        ld(Bs_bf[:], Bs, "bs")
        ld(Br_bf[:], Br, "br")
        ld(bb_bf[:], bb, "bb")
        ld(br_bf[:], br, "brr")
        g.memset(E16[:], 1.0).then_inc(sem["gp"], 1)
        g.wait_ge(sem["gp"], GP + 1)
        g.affine_select(out=E16[:], in_=E16[:], compare_op=ALU.is_ge, fill=0.0,
                        base=0, pattern=[[1, 96]], channel_multiplier=-16)\
            .then_inc(sem["gp"], 1)
        g.wait_ge(sem["gp"], GP + 2)
        g.affine_select(out=E16[:], in_=E16[:], compare_op=ALU.is_ge, fill=0.0,
                        base=15, pattern=[[-1, 96]], channel_multiplier=16)\
            .then_inc(sem["gp"], 1)
    p.cnt["gp"] += 3
    wready = p.mark("mwl")
    e16_done = p.mark("gp")

    ps_xa = p.pf[4][0:16, 0:128]
    ps_h = p.pf[5][0:96, 0:128]
    ps_g = p.pf[2][0:128, 0:6]
    ps_wb = p.pf[3][0:96, 0:128]
    ps_wt = p.pb[0][0:6, 0:128]
    ps_y = [p.pf[0], p.pf[1]]

    def emit_half(half):
        ln = emit_ln_uT(p, x, uT, HT * half, HT, consts, TSP, name="m")
        TE = p.cnt["te"]
        SC = p.cnt["sc"]
        VE = p.cnt["ve"]
        MSTB = [p.cnt[f"mst{i}"] for i in range(2)]
        MXLB = [p.cnt[f"mxl{i}"] for i in range(2)]

        def te_m(tt, k):
            return TE + 9 * tt + k

        def sc_m(tt, k):
            return SC + 10 * tt + k

        def ve_m(tt, k):
            return VE + 21 * tt + k

        @b.tensor
        def _(t):
            wge(t, cready)
            wge(t, wready)
            wge(t, e16_done)
            for tt in range(HT):
                wge(t, ln.tile_done(tt))
                ut = lambda dc: uT[:, dc * TSP + tt * 128: dc * TSP + tt * 128 + 128]
                if tt >= 1:
                    t.wait_ge(sem["sc"], sc_m(tt - 1, 1))
                else:
                    t.wait_ge(sem["sc"], SC)
                for dc in range(ND):
                    bi = t.matmul(ps_xa, As_bf[:, dc * 16:(dc + 1) * 16], ut(dc),
                                  start=(dc == 0), stop=(dc == ND - 1))
                bi.then_inc(sem["te"], 1)
                if tt >= 1:
                    t.wait_ge(sem["sc"], sc_m(tt - 1, 2))
                for dc in range(ND):
                    bi = t.matmul(ps_h, Ar_bf[:, dc * 96:(dc + 1) * 96], ut(dc),
                                  start=(dc == 0), stop=(dc == ND - 1))
                bi.then_inc(sem["te"], 1)
                if tt >= 1:
                    t.wait_ge(sem["sc"], sc_m(tt - 1, 3))
                    t.wait_ge(sem["ve"], ve_m(tt - 1, 4))
                else:
                    t.wait_ge(sem["ve"], VE)
                t.matmul(ps_g, ones_row[0:1, 0:128], br_bf[0:1, :],
                         start=True, stop=False)
                for dc in range(ND):
                    bi = t.matmul(ps_g, ut(dc), Wr_bf[:, dc * 6:(dc + 1) * 6],
                                  start=False, stop=(dc == ND - 1))
                bi.then_inc(sem["te"], 1)
                t.wait_ge(sem["ve"], ve_m(tt, 11))
                if tt >= 1:
                    t.wait_ge(sem["sc"], sc_m(tt - 1, 4))
                t.transpose(ps_wt, wt_bf[:], ident[:]).then_inc(sem["te"], 1)
                t.wait_ge(sem["sc"], sc_m(tt, 4))
                if tt >= 1:
                    t.wait_ge(sem["ve"], ve_m(tt - 1, 12))
                t.matmul(ps_wb, E16[:], wtT_s[:], start=True, stop=True)\
                    .then_inc(sem["te"], 1)
                t.wait_ge(sem["sc"], sc_m(tt, 1))
                t.wait_ge(sem["ve"], ve_m(tt, 12))
                for fg in range(4):
                    if fg >= 1:
                        t.wait_ge(sem["sc"], sc_m(tt, 4 + fg))
                    elif tt >= 2:
                        t.wait_ge(sem["sc"], sc_m(tt - 2, 8))
                    else:
                        t.wait_ge(sem["sc"], SC)
                    t.matmul(ps_y[tt % 2][:], ones_row[0:1, 0:128],
                             bb_bf[0:1, fg * 512:(fg + 1) * 512],
                             start=True, stop=False)
                    for dc in range(ND):
                        t.matmul(ps_y[tt % 2][:], ut(dc),
                                 Wb_bf[:, dc * D + fg * 512: dc * D + fg * 512 + 512],
                                 start=False, stop=False)
                    t.matmul(ps_y[tt % 2][:], xaT[:],
                             Bs_bf[:, fg * 512:(fg + 1) * 512],
                             start=False, stop=False)
                    bi = t.matmul(ps_y[tt % 2][:], hwT[:],
                                  Br_bf[:, fg * 512:(fg + 1) * 512],
                                  start=False, stop=True)
                    bi.then_inc(sem["te"], 1)
        p.cnt["te"] += 9 * HT

        @b.scalar
        def _(s):
            for tt in range(HT):
                i = tt % 2
                s.wait_ge(sem["te"], te_m(tt, 1))
                if tt >= 1:
                    s.wait_ge(sem["te"], te_m(tt - 1, 9))
                s.activation(xaT[:], ps_xa, AF.Identity).then_inc(sem["sc"], 1)
                s.wait_ge(sem["te"], te_m(tt, 2))
                if tt >= 1:
                    s.wait_ge(sem["ve"], ve_m(tt - 1, 12))
                s.activation(hT[:], ps_h, AF.Identity).then_inc(sem["sc"], 1)
                s.wait_ge(sem["te"], te_m(tt, 3))
                s.wait_ge(sem["ve"], ve_m(tt, 2))
                if tt >= 1:
                    s.wait_ge(sem["ve"], ve_m(tt - 1, 4))
                s.activation(e_s[:], ps_g, AF.Exp, bias=negmax[:],
                             accum_out=esum[:]).then_inc(sem["sc"], 1)
                s.wait_ge(sem["te"], te_m(tt, 4))
                if tt >= 1:
                    s.wait_ge(sem["te"], te_m(tt - 1, 5))
                s.activation(wtT_s[:], ps_wt, AF.Identity).then_inc(sem["sc"], 1)
                for fg in range(4):
                    s.wait_ge(sem["te"], te_m(tt, 6 + fg))
                    if tt >= 2 and fg == 0:
                        s.wait_ge(sem["ve"], ve_m(tt - 2, 14))
                    elif fg == 0:
                        s.wait_ge(sem["ve"], VE)
                    s.activation(y_s[i][:, fg * 512:(fg + 1) * 512], ps_y[i][:],
                                 AF.Identity, accum_out=sums4[i][:, fg:fg + 1])\
                        .then_inc(sem["sc"], 1)
                s.wait_ge(sem["ve"], ve_m(tt, 18))
                s.activation(stdb[i][:], varb[i][:], AF.Sqrt).then_inc(sem["sc"], 1)
                s.wait_ge(sem["ve"], ve_m(tt, 20))
                if tt >= 2:
                    s.wait_ge(p.sems[f"mst{i}"], MSTB[i] + 16 * (tt // 2))
                else:
                    s.wait_ge(p.sems[f"mst{i}"], MSTB[i])
                s.activation(o_s[i][:], y_s[i][:], AF.Identity,
                             bias=negmrb[i][:], scale=rstdb[i][:])\
                    .then_inc(sem["sc"], 1)
        p.cnt["sc"] += 10 * HT

        @b.vector
        def _(v):
            for tt in range(HT):
                i = tt % 2
                v.wait_ge(sem["te"], te_m(tt, 3))
                if tt >= 1:
                    v.wait_ge(sem["ve"], ve_m(tt - 1, 2))
                else:
                    v.wait_ge(sem["ve"], VE)
                v.reduce_max(gmax[:], ps_g, axis=AX.X).then_inc(sem["ve"], 1)
                v.wait_ge(sem["ve"], ve_m(tt, 1))
                if tt >= 1:
                    v.wait_ge(sem["sc"], sc_m(tt - 1, 3))
                v.tensor_scalar_mul(negmax[:], gmax[:], -1.0).then_inc(sem["ve"], 1)
                v.wait_ge(sem["sc"], sc_m(tt, 3))
                v.reciprocal(er[:], esum[:]).then_inc(sem["ve"], 1)
                v.wait_ge(sem["ve"], ve_m(tt, 3))
                v.tensor_scalar_mul(g_s[:], e_s[:], er[:, 0:1]).then_inc(sem["ve"], 1)
                v.wait_ge(sem["ve"], ve_m(tt, 4))
                v.reduce_max(m1s[:], g_s[:], axis=AX.X).then_inc(sem["ve"], 1)
                v.wait_ge(sem["ve"], ve_m(tt, 5))
                v.tensor_single_scalar(g2_s[:], g_s[:], m1s[:, 0:1], ALU.is_equal)\
                    .then_inc(sem["ve"], 1)
                v.wait_ge(sem["ve"], ve_m(tt, 6))
                v.scalar_tensor_tensor(g2_s[:], g2_s[:], -1e30, g_s[:],
                                       op0=ALU.mult, op1=ALU.add)\
                    .then_inc(sem["ve"], 1)
                v.wait_ge(sem["ve"], ve_m(tt, 7))
                v.reduce_max(m2s[:], g2_s[:], axis=AX.X).then_inc(sem["ve"], 1)
                v.wait_ge(sem["ve"], ve_m(tt, 8))
                v.tensor_single_scalar(kp_s[:], g_s[:], m2s[:, 0:1], ALU.is_ge)\
                    .then_inc(sem["ve"], 1)
                v.wait_ge(sem["ve"], ve_m(tt, 9))
                v.tensor_mul(wt_s[:], g_s[:], kp_s[:]).then_inc(sem["ve"], 1)
                v.wait_ge(sem["ve"], ve_m(tt, 10))
                if tt >= 1:
                    v.wait_ge(sem["te"], te_m(tt - 1, 4))
                else:
                    v.wait_ge(sem["te"], TE)
                v.tensor_copy(wt_bf[:], wt_s[:]).then_inc(sem["ve"], 1)
                v.wait_ge(sem["te"], te_m(tt, 5))
                v.wait_ge(sem["sc"], sc_m(tt, 2))
                if tt >= 1:
                    v.wait_ge(sem["te"], te_m(tt - 1, 9))
                v.tensor_mul(hwT[:], hT[:], ps_wb).then_inc(sem["ve"], 1)
                v.wait_ge(sem["sc"], sc_m(tt, 8))
                v.tensor_reduce(tot[i][:], sums4[i][:], axis=AX.X, op=ALU.add)\
                    .then_inc(sem["ve"], 1)
                if tt >= 2:
                    v.wait_ge(p.sems[f"mst{i}"], MSTB[i] + 16 * (tt // 2))
                else:
                    v.wait_ge(p.sems[f"mst{i}"], MSTB[i])
                v.scalar_tensor_tensor(o_s[i][:], y_s[i][:], 0.0, y_s[i][:],
                                       op0=ALU.add, op1=ALU.mult,
                                       accum_out=ssq[i][:]).then_inc(sem["ve"], 1)
                v.wait_ge(sem["ve"], ve_m(tt, 14))
                v.tensor_scalar_mul(meanb[i][:], tot[i][:], 1.0 / D)\
                    .then_inc(sem["ve"], 1)
                v.wait_ge(sem["ve"], ve_m(tt, 15))
                v.scalar_tensor_tensor(msqb[i][:], meanb[i][:], 0.0, meanb[i][:],
                                       op0=ALU.add, op1=ALU.mult)\
                    .then_inc(sem["ve"], 1)
                v.wait_ge(sem["ve"], ve_m(tt, 16))
                v.scalar_tensor_tensor(varb[i][:], ssq[i][:], 1.0 / D, msqb[i][:],
                                       op0=ALU.mult, op1=ALU.subtract)\
                    .then_inc(sem["ve"], 1)
                v.wait_ge(sem["ve"], ve_m(tt, 17))
                v.tensor_scalar_add(varb[i][:], varb[i][:], EPS)\
                    .then_inc(sem["ve"], 1)
                v.wait_ge(sem["sc"], sc_m(tt, 9))
                v.reciprocal(rstdb[i][:], stdb[i][:]).then_inc(sem["ve"], 1)
                v.wait_ge(sem["ve"], ve_m(tt, 19))
                v.scalar_tensor_tensor(negmrb[i][:], meanb[i][:], -1.0,
                                       rstdb[i][:], op0=ALU.mult, op1=ALU.mult)\
                    .then_inc(sem["ve"], 1)
                v.wait_ge(sem["sc"], sc_m(tt, 10))
                v.wait_ge(p.sems[f"mxl{i}"], MXLB[i] + 16 * (tt // 2 + 1))
                v.tensor_add(o_s[i][:], o_s[i][:], xt[i][:]).then_inc(sem["ve"], 1)
        p.cnt["ve"] += 21 * HT

        @b.sync
        def _(sync):
            for tt in range(HT):
                i = tt % 2
                gt = HT * half + tt
                if tt >= 2:
                    sync.wait_ge(sem["ve"], ve_m(tt - 2, 21))
                else:
                    sync.wait_ge(sem["ve"], VE)
                bi = sync.dma_start(out=xt[i][:],
                                    in_=x[gt * 128:(gt + 1) * 128, :])
                p.inc(bi, f"mxl{i}", 16)
                if tt >= 1:
                    j = (tt - 1) % 2
                    gtp = HT * half + tt - 1
                    sync.wait_ge(sem["ve"], ve_m(tt - 1, 21))
                    bi = sync.dma_start(out=x4[gtp * 128:(gtp + 1) * 128, :],
                                        in_=o_s[j][:])
                    p.inc(bi, f"mst{j}", 16)
            j = (HT - 1) % 2
            gtp = HT * half + HT - 1
            sync.wait_ge(sem["ve"], ve_m(HT - 1, 21))
            bi = sync.dma_start(out=x4[gtp * 128:(gtp + 1) * 128, :],
                                in_=o_s[j][:])
            p.inc(bi, f"mst{j}", 16)

    emit_half(0)
    emit_half(1)
    emit_final_wait(p, [p.mark("mst0"), p.mark("mst1")])
    return p.finish()


def emit_final_wait(p, store_marks):
    b = p.block

    @b.sync
    def _(sync):
        for mk in store_marks:
            wge(sync, mk)



def emit_moe_pre(p, uT, As, Ar, Wr, br, consts, ln, name=""):
    ident, ones_row, ones128, cready = consts
    b = p.block
    sem = p.sems
    As_bf = p.sb(f"mp_as{name}", [128, ND * 16], BF16)
    Ar_bf = p.sb(f"mp_ar{name}", [128, ND * 96], BF16)
    Wr_bf = p.sb(f"mp_wr{name}", [128, ND * 6], BF16)
    br_bf = p.sb(f"mp_brr{name}", [1, 6], BF16)
    E16 = p.sb(f"mp_e16{name}", [6, 96], BF16)
    xaT_all = p.sb(f"mp_xa{name}", [16, T], BF16)
    hwT_all = p.sb(f"mp_hw{name}", [96, T], BF16)
    hT = p.sb(f"mp_h{name}", [96, 128], BF16)
    wtT_s = p.sb(f"mp_wtt{name}", [6, 128], BF16)
    e_s = p.sb(f"mp_es{name}", [128, 6], F32)
    g_s = p.sb(f"mp_gs{name}", [128, 6], F32)
    g2_s = p.sb(f"mp_g2{name}", [128, 6], F32)
    kp_s = p.sb(f"mp_kp{name}", [128, 6], F32)
    wt_s = p.sb(f"mp_wt{name}", [128, 6], F32)
    gmax = p.sb(f"mp_gm{name}", [128, 1], F32)
    negmax = p.sb(f"mp_ngm{name}", [128, 1], F32)
    esum = p.sb(f"mp_esm{name}", [128, 1], F32)
    er = p.sb(f"mp_er{name}", [128, 1], F32)
    m1s = p.sb(f"mp_m1{name}", [128, 1], F32)
    m2s = p.sb(f"mp_m2{name}", [128, 1], F32)
    p.sem(f"mpw{name}")

    GP = p.cnt["gp"]
    lm = []

    @b.gpsimd
    def _(g):
        g.wait_ge(sem["te"], p.cnt["te"])
        def ld(dst3, src3):
            bi = g.dma_start(out=dst3, in_=src3)
            lm.append(p.inc(bi, f"mpw{name}", 16))
        ld(As_bf.ap().rearrange("p (dc f) -> p dc f", dc=ND, f=16),
           As.rearrange("(dc p) f -> p dc f", p=128))
        ld(Ar_bf.ap().rearrange("p (dc f) -> p dc f", dc=ND, f=96),
           Ar.rearrange("(dc p) f -> p dc f", p=128))
        ld(Wr_bf.ap().rearrange("p (dc f) -> p dc f", dc=ND, f=6),
           Wr.rearrange("(dc p) f -> p dc f", p=128))
        ld(br_bf[:], br)
        g.memset(E16[:], 1.0).then_inc(sem["gp"], 1)
        g.wait_ge(sem["gp"], GP + 1)
        g.affine_select(out=E16[:], in_=E16[:], compare_op=ALU.is_ge, fill=0.0,
                        base=0, pattern=[[1, 96]], channel_multiplier=-16)\
            .then_inc(sem["gp"], 1)
        g.wait_ge(sem["gp"], GP + 2)
        g.affine_select(out=E16[:], in_=E16[:], compare_op=ALU.is_ge, fill=0.0,
                        base=15, pattern=[[-1, 96]], channel_multiplier=16)\
            .then_inc(sem["gp"], 1)
    p.cnt["gp"] += 3
    e16_done = p.mark("gp")

    TE = p.cnt["te"]
    SC = p.cnt["sc"]
    VE = p.cnt["ve"]

    def te_m(tt, k):
        return TE + 5 * tt + k

    def sc_m(tt, k):
        return SC + 4 * tt + k

    def ve_m(tt, k):
        return VE + 11 * tt + k

    ps_xa = p.pf[4][0:16, 0:128]
    ps_h = p.pf[5][0:96, 0:128]
    ps_g = p.pf[2][0:128, 0:6]
    ps_wb = p.pf[3][0:96, 0:128]
    ps_wt = p.pf[3][0:6, 0:128]
    wtT_f = p.sb(f"mp_wttf{name}", [6, 128], F32)

    @b.tensor
    def _(t):
        wge(t, cready)
        wge(t, e16_done)
        for mk in lm:
            wge(t, mk)
        for tt in range(NT):
            wge(t, ln.tile_done(tt))
            ut = lambda dc: apx(uT)[:, dc * T + tt * 128: dc * T + tt * 128 + 128]
            if tt >= 1:
                t.wait_ge(sem["sc"], sc_m(tt - 1, 1))
            else:
                t.wait_ge(sem["sc"], SC)
                t.wait_ge(sem["ve"], VE)
            for dc in range(ND):
                bi = t.matmul(ps_xa, As_bf[:, dc * 16:(dc + 1) * 16], ut(dc),
                              start=(dc == 0), stop=(dc == ND - 1))
            bi.then_inc(sem["te"], 1)
            if tt >= 1:
                t.wait_ge(sem["sc"], sc_m(tt - 1, 2))
            for dc in range(ND):
                bi = t.matmul(ps_h, Ar_bf[:, dc * 96:(dc + 1) * 96], ut(dc),
                              start=(dc == 0), stop=(dc == ND - 1))
            bi.then_inc(sem["te"], 1)
            if tt >= 1:
                t.wait_ge(sem["sc"], sc_m(tt - 1, 3))
                t.wait_ge(sem["ve"], ve_m(tt - 1, 4))
            t.matmul(ps_g, ones_row[0:1, 0:128], br_bf[0:1, :],
                     start=True, stop=False)
            for dc in range(ND):
                bi = t.matmul(ps_g, ut(dc), Wr_bf[:, dc * 6:(dc + 1) * 6],
                              start=False, stop=(dc == ND - 1))
            bi.then_inc(sem["te"], 1)
            t.wait_ge(sem["ve"], ve_m(tt, 10))
            if tt >= 1:
                t.wait_ge(sem["sc"], sc_m(tt - 1, 4))
                t.wait_ge(sem["ve"], ve_m(tt - 1, 11))
            t.transpose(ps_wt, wt_s[:], p._identf[:]).then_inc(sem["te"], 1)
            t.wait_ge(sem["sc"], sc_m(tt, 4))
            t.matmul(ps_wb, E16[:], wtT_s[:], start=True, stop=True)\
                .then_inc(sem["te"], 1)
    p.cnt["te"] += 5 * NT

    @b.scalar
    def _(s):
        for tt in range(NT):
            s.wait_ge(sem["te"], te_m(tt, 1))
            s.activation(xaT_all[:, tt * 128:(tt + 1) * 128], ps_xa, AF.Identity)\
                .then_inc(sem["sc"], 1)
            s.wait_ge(sem["te"], te_m(tt, 2))
            if tt >= 1:
                s.wait_ge(sem["ve"], ve_m(tt - 1, 11))
            s.activation(hT[:], ps_h, AF.Identity).then_inc(sem["sc"], 1)
            s.wait_ge(sem["te"], te_m(tt, 3))
            s.wait_ge(sem["ve"], ve_m(tt, 2))
            if tt >= 1:
                s.wait_ge(sem["ve"], ve_m(tt - 1, 4))
            s.activation(e_s[:], ps_g, AF.Exp, bias=negmax[:],
                         accum_out=esum[:]).then_inc(sem["sc"], 1)
            s.wait_ge(sem["te"], te_m(tt, 4))
            if tt >= 1:
                s.wait_ge(sem["te"], te_m(tt - 1, 5))
            s.activation(wtT_s[:], ps_wt, AF.Identity).then_inc(sem["sc"], 1)
    p.cnt["sc"] += 4 * NT

    @b.vector
    def _(v):
        for tt in range(NT):
            v.wait_ge(sem["te"], te_m(tt, 3))
            if tt >= 1:
                v.wait_ge(sem["ve"], ve_m(tt - 1, 2))
            else:
                v.wait_ge(sem["ve"], VE)
            v.reduce_max(gmax[:], ps_g, axis=AX.X).then_inc(sem["ve"], 1)
            v.wait_ge(sem["ve"], ve_m(tt, 1))
            if tt >= 1:
                v.wait_ge(sem["sc"], sc_m(tt - 1, 3))
            v.tensor_scalar_mul(negmax[:], gmax[:], -1.0).then_inc(sem["ve"], 1)
            v.wait_ge(sem["sc"], sc_m(tt, 3))
            v.reciprocal(er[:], esum[:]).then_inc(sem["ve"], 1)
            v.wait_ge(sem["ve"], ve_m(tt, 3))
            v.tensor_scalar_mul(g_s[:], e_s[:], er[:, 0:1]).then_inc(sem["ve"], 1)
            v.wait_ge(sem["ve"], ve_m(tt, 4))
            v.reduce_max(m1s[:], g_s[:], axis=AX.X).then_inc(sem["ve"], 1)
            v.wait_ge(sem["ve"], ve_m(tt, 5))
            v.tensor_single_scalar(g2_s[:], g_s[:], m1s[:, 0:1], ALU.is_equal)\
                .then_inc(sem["ve"], 1)
            v.wait_ge(sem["ve"], ve_m(tt, 6))
            v.scalar_tensor_tensor(g2_s[:], g2_s[:], -1e30, g_s[:],
                                   op0=ALU.mult, op1=ALU.add).then_inc(sem["ve"], 1)
            v.wait_ge(sem["ve"], ve_m(tt, 7))
            v.reduce_max(m2s[:], g2_s[:], axis=AX.X).then_inc(sem["ve"], 1)
            v.wait_ge(sem["ve"], ve_m(tt, 8))
            v.tensor_single_scalar(kp_s[:], g_s[:], m2s[:, 0:1], ALU.is_ge)\
                .then_inc(sem["ve"], 1)
            v.wait_ge(sem["ve"], ve_m(tt, 9))
            if tt >= 1:
                v.wait_ge(sem["te"], te_m(tt - 1, 4))
            v.tensor_mul(wt_s[:], g_s[:], kp_s[:]).then_inc(sem["ve"], 1)
            v.wait_ge(sem["te"], te_m(tt, 5))
            v.wait_ge(sem["sc"], sc_m(tt, 2))
            v.tensor_mul(hwT_all[:, tt * 128:(tt + 1) * 128], hT[:], ps_wb)\
                .then_inc(sem["ve"], 1)
    p.cnt["ve"] += 11 * NT

    def xa_mark(tt):
        return (sem["sc"], sc_m(tt, 1))

    def hw_mark(tt):
        return (sem["ve"], ve_m(tt, 11))

    return xaT_all, hwT_all, xa_mark, hw_mark


def build_qkv(sep_weights):
    p = Prog("qkv")
    nc = p.nc
    x = nc.dram_tensor("x", [T, D], F32, kind="ExternalInput").ap()
    if sep_weights:
        wq = nc.dram_tensor("wq", [D, D], F32, kind="ExternalInput").ap()
        wk = nc.dram_tensor("wk", [D, D], F32, kind="ExternalInput").ap()
        wv = nc.dram_tensor("wv", [D, D], F32, kind="ExternalInput").ap()
        bq = None
    else:
        wq = nc.dram_tensor("w_qkv", [D, 3 * D], F32, kind="ExternalInput").ap()
        wk = wv = wq
        bq = nc.dram_tensor("b_qkv", [1, 3 * D], F32, kind="ExternalInput").ap()
    qT = nc.dram_tensor("qT", [D, T], BF16, kind="ExternalOutput").ap()
    kT = nc.dram_tensor("kT", [D, T], BF16, kind="ExternalOutput").ap()
    v = nc.dram_tensor("v", [T, D], BF16, kind="ExternalOutput").ap()

    uT = p.sb("uT", [128, ND * T], BF16)
    p.start()
    consts = emit_consts(p)
    ln = emit_ln_uT(p, x, uT, 0, NT, consts, T)
    koff = 0 if sep_weights else D
    voff = 0 if sep_weights else 2 * D
    lnm_fm = lambda tg: ln.tile_done(4 * tg + 3)
    mk = []
    mk += emit_proj_fm(p, uT, wq, 0, bq, 0, qT, consts, lnm_fm, ND, T, name="q")
    mk += emit_proj_fm(p, uT, wk, koff, bq, D, kT, consts, lnm_fm, ND, T, name="k")
    mk += emit_proj_tm(p, uT, wv, voff, bq, 2 * D, v, consts, ln.tile_done, T, name="v")
    emit_final_wait(p, mk)
    return p.finish()


def build_fused(debug=False, upto="all", skip_dq=False):
    p = Prog("mk")
    nc = p.nc
    f = lambda n, s, d=F32: nc.dram_tensor(n, s, d, kind="ExternalInput").ap()
    x0 = f("x", [T, D])
    w_qkv = f("w_qkv", [D, 3 * D])
    b_qkv = f("b_qkv", [1, 3 * D])
    w_out0 = f("w_out0", [D, D])
    b_out0 = f("b_out0", [1, D])
    gup = f("gup_sh", [11264, 32], I32)
    gus = f("gus_sh", [2, 11264], F16)
    dnp = f("dnp_sh", [2048, 96], I32)
    dns = f("dns_sh", [6, 2048], F16)
    wq1 = f("wq1", [D, D])
    wk1 = f("wk1", [D, D])
    wv1 = f("wv1", [D, D])
    wo1 = f("wo1", [D, D])
    Wb = f("W_base", [D, D])
    bb = f("b_base", [1, D])
    As = f("A_s", [D, 16])
    Bs = f("B_s", [16, D])
    Ar = f("A_r_flat", [D, 96])
    Br = f("B_r_flat", [96, D])
    Wr = f("W_router", [D, 6])
    br = f("b_router", [1, 6])
    alpha = f("alpha", [1, 2], BF16)
    out = nc.dram_tensor("out", [T, D], F32, kind="ExternalOutput").ap()

    ik = "ExternalOutput" if debug else "Internal"
    g = lambda n, s, d: nc.dram_tensor(n, s, d, kind="Internal").ap()
    gd = lambda n, s, d: nc.dram_tensor(n, s, d, kind=ik).ap()
    x1 = gd("x1d", [T, D], F32)
    x2 = gd("x2d", [T, D], F32)
    x3 = gd("x3d", [T, D], F32)
    qTd = gd("qTd", [D, T], BF16)
    kv_own = g("kv_own", [2 * T, D], BF16)
    kv_all = g("kv_all", [4 * T, D], BF16)
    guT_sh = g("guT_shd", [256, 11264], BF16)
    guT_all = g("guT_alld", [2048, 11264], BF16)
    dnT_sh = g("dnT_shd", [768, 2048], BF16)
    dnT_all = g("dnT_alld", [6144, 2048], BF16)

    uT = p.sb("uT", [128, ND * T], BF16)
    atk = [p.sb(f"at_k{i}", [128, S], BF16) for i in range(2)]
    atv = [p.sb(f"at_v{i}", [128, 32 * 128], BF16) for i in range(2)]
    atq = [p.sb(f"at_q{i}", [128, T], BF16) for i in range(2)]
    ate = p.sb("at_e", [128, 16 * 512], BF16)
    mly = [p.sb(f"mlyml{i}", [128, D], F32) for i in range(2)]
    mlo = [p.sb(f"mloml{i}", [128, D], F32) for i in range(2)]
    mlx = [p.sb(f"mlxml{i}", [128, D], F32) for i in range(2)]
    uT_a = p.addr("uT")
    atk_a = p.addr("at_k0")
    assert p.addr("at_k1") == atk_a + 8192
    assert p.addr("at_v0") == atk_a + 16384
    assert p.addr("at_v1") == atk_a + 24576
    assert p.addr("at_q0") == atk_a + 32768
    mly_a = p.addr("mlyml0")
    assert p.addr("mlxml1") == mly_a + 5 * 8192
    ate_a = p.addr("at_e")
    p.sb("dq_kf", [128, 1536], F32, at=atk_a)
    p.sb("dq_sf", [128, 1536], F32, at=atk_a + 6144)
    p.sb("dq_r4", [128, 1536], F32, at=atk_a + 12288)
    p.sb("dq_r6", [128, 1536], F32, at=atk_a + 18432)
    p.sb("dq_m1", [128, 1536], F32, at=atk_a + 24576)
    p.sb("dq_s", [128, 1536], F32, at=atk_a + 30720)
    p.sb("dq_pk", [128, 1536], I32, at=ate_a)
    p.sb("dq_i", [128, 1536], I32, at=ate_a + 6144)
    p.sb("dq_dgu", [128, 88 * 256], BF16, at=uT_a)
    p.sb("dq_ddn", [128, 16 * 768], BF16, at=mly_a)
    p.sb("lnxtg0", [128, D], F32, at=mly_a)
    p.sb("lnxtg1", [128, D], F32, at=mly_a + 8192)
    p.sb("lnsqg", [128, D], F32, at=mly_a + 16384)
    p.sb("lntrashg", [128, D], BF16, at=mly_a + 24576)
    p.sb("lnug0", [128, D], BF16, at=mly_a + 28672)
    p.sb("lnug1", [128, D], BF16, at=mly_a + 32768)
    p.sb("pfwfm0", [128, ND * 128], BF16, at=ate_a)
    p.sb("pfwfm1", [128, ND * 128], BF16, at=ate_a + 4096)
    p.sb("pfsfm0", [128, 512], BF16, at=ate_a + 8192)
    p.sb("pfsfm1", [128, 512], BF16, at=ate_a + 9216)
    p.sb("ptwtm0", [128, 512], BF16, at=ate_a + 10240)
    p.sb("ptwtm1", [128, 512], BF16, at=ate_a + 11264)
    p.sb("ptstm0", [128, 512], BF16, at=ate_a + 12288)
    p.sb("ptstm1", [128, 512], BF16, at=ate_a + 13312)
    p.sb("pfbfm", [1, ND * 128], BF16, at=atk_a + 32768)
    p.sb("ptbtm", [1, 4 * 512], BF16, at=atk_a + 36864)
    p.sb("guwgf0", [128, ND * 128], BF16, at=atk_a)
    p.sb("guwgf1", [128, ND * 128], BF16, at=atk_a + 4096)
    p.sb("guwuf0", [128, ND * 128], BF16, at=atk_a + 8192)
    p.sb("guwuf1", [128, ND * 128], BF16, at=atk_a + 12288)
    p.sb("gusilf0", [128, 512], BF16, at=atk_a + 16384)
    p.sb("gusilf1", [128, 512], BF16, at=atk_a + 17408)
    hactT = p.sb("hactTf", [128, 44 * 512], BF16, at=uT_a + ND * 512 * 2)
    p.sb("mp_as", [128, ND * 16], BF16, at=atk_a)
    p.sb("mp_ar", [128, ND * 96], BF16, at=atk_a + 1024)
    p.sb("mp_wr", [128, ND * 6], BF16, at=atk_a + 4096)
    p.sb("mp_xa", [16, T], BF16, at=atk_a + 8192)
    p.sb("mp_hw", [96, T], BF16, at=atk_a + 12288)

    p.start()
    consts = emit_consts(p)
    p.sem("cc")

    if skip_dq:
        dq_mks = []
    else:
        dq_mks = emit_dq(p, gup, gus, dnp, dns, guT_sh, dnT_sh, consts)

    ln0 = emit_ln_uT(p, x0, uT, 0, NT, consts, T, name="g",
                     guard_marks=dq_mks)
    lnm_fm = lambda tg: ln0.tile_done(4 * tg + 3)
    mkk = emit_proj_fm(p, uT, w_qkv, D, b_qkv, D, kv_own[0:T, :], consts,
                       lnm_fm, ND, T, name="fm")
    mkv = emit_proj_tm(p, uT, w_qkv, 2 * D, b_qkv, 2 * D, kv_own[T:2 * T, :],
                       consts, ln0.tile_done, T, name="tm")
    mkq = emit_proj_fm(p, uT, w_qkv, 0, b_qkv, 0, qTd, consts, lnm_fm,
                       ND, T, name="fm")

    @p.block.gpsimd
    def _(gq):
        for mk in mkk + mkv:
            wge(gq, mk)
        gq.collective_compute(
            "AllGather", ALU.bypass,
            replica_groups=[[0, 1], [2, 3], [4, 5], [6, 7]],
            ins=[kv_own.opt()], outs=[kv_all.opt()]).then_inc(p.sems["cc"], 1)
    p.cnt["cc"] += 1
    cc0 = p.mark("cc")

    if not skip_dq:
        @p.block.gpsimd
        def _(gq):
            for mk in dq_mks:
                wge(gq, mk)
            gq.collective_compute(
                "AllGather", ALU.bypass, replica_groups=[list(range(8))],
                ins=[guT_sh.opt()], outs=[guT_all.opt()]).then_inc(p.sems["cc"], 1)
            gq.collective_compute(
                "AllGather", ALU.bypass, replica_groups=[list(range(8))],
                ins=[dnT_sh.opt()], outs=[dnT_all.opt()]).then_inc(p.sems["cc"], 1)
        p.cnt["cc"] += 2
    ccdq = p.mark("cc")
    if upto == "qkv":
        emit_final_wait(p, mkq + mkk + mkv)
        return p.finish()

    avm0 = emit_attn_core(p, qTd, kv_own, kv_all, alpha, consts, L0_SCALE,
                          False, uT, guard_marks=[cc0] + mkq)
    mko0 = emit_mm_ln_resid(
        p, lambda ck, tt: apx(uT)[:, ck * T + tt * 128: ck * T + tt * 128 + 128],
        ND, w_out0, True, b_out0, x0, x1, consts, [avm0], 0, NT, name="ml",
        guard_marks=[avm0])
    if upto == "l0attn":
        emit_final_wait(p, mko0)
        return p.finish()

    for tg in range(4):
        lnf = emit_ln_uT(p, x1, uT[:, 0:ND * 512], 4 * tg, 4, consts, 512,
                         name="g", guard_marks=mko0 if tg == 0 else ())
        gum = emit_gu(p, uT[:, 0:ND * 512], guT_all, hactT, consts,
                      lnf.all_done, name="f",
                      guard_marks=[ccdq] + (mko0 if tg == 0 else []))
        mkf = emit_mm_ln_resid(
            p, lambda ck, tt: apx(hactT)[:, ck * 512 + tt * 128:
                                         ck * 512 + tt * 128 + 128],
            44, dnT_all[0:5632, :], False, None, x1, x2, consts, [gum],
            4 * tg, 4, name="ml", guard_marks=[gum])
    if upto == "ffn":
        emit_final_wait(p, mkf)
        return p.finish()

    ln1 = emit_ln_uT(p, x2, uT, 0, NT, consts, T, name="g", guard_marks=mkf)
    lnm1 = lambda tg: ln1.tile_done(4 * tg + 3)
    mkk1 = emit_proj_fm(p, uT, wk1, 0, None, 0, kv_own[0:T, :], consts,
                        lnm1, ND, T, name="fm")
    mkv1 = emit_proj_tm(p, uT, wv1, 0, None, 0, kv_own[T:2 * T, :],
                        consts, ln1.tile_done, T, name="tm")
    mkq1 = emit_proj_fm(p, uT, wq1, 0, None, 0, qTd, consts, lnm1,
                        ND, T, name="fm")

    @p.block.gpsimd
    def _(gq):
        for mk in mkk1 + mkv1:
            wge(gq, mk)
        gq.collective_compute(
            "AllGather", ALU.bypass,
            replica_groups=[[0, 1], [2, 3], [4, 5], [6, 7]],
            ins=[kv_own.opt()], outs=[kv_all.opt()]).then_inc(p.sems["cc"], 1)
    p.cnt["cc"] += 1
    cc1 = p.mark("cc")

    if upto == "l1qkv":
        emit_final_wait(p, mkq1 + mkk1 + mkv1)
        return p.finish()

    avm1 = emit_attn_core(p, qTd, kv_own, kv_all, alpha, consts, L1_SCALE,
                          True, uT, guard_marks=[cc1] + mkq1)
    mko1 = emit_mm_ln_resid(
        p, lambda ck, tt: apx(uT)[:, ck * T + tt * 128: ck * T + tt * 128 + 128],
        ND, wo1, True, None, x2, x3, consts, [avm1], 0, NT, name="ml",
        guard_marks=[avm1])

    if upto == "l1attn":
        emit_final_wait(p, mko1)
        return p.finish()

    lnm_ = emit_ln_uT(p, x3, uT, 0, NT, consts, T, name="g", guard_marks=mko1)
    xaT_all, hwT_all, xa_mark, hw_mark = emit_moe_pre(
        p, uT, As, Ar, Wr, br, consts, lnm_, name="")
    Bs_bf = p.sb("fz_bs", [16, D], BF16, at=p.addr("at_v0"))
    Br_bf = p.sb("fz_br", [96, D], BF16, at=p.addr("at_v0") + 4096)
    p.sem("fzw")
    fzm = []

    @p.block.gpsimd
    def _(gq):
        gq.wait_ge(p.sems["te"], p.cnt["te"])
        bi = gq.dma_start(out=Bs_bf[:], in_=Bs)
        fzm.append(p.inc(bi, "fzw", 16))
        bi = gq.dma_start(out=Br_bf[:], in_=Br)
        fzm.append(p.inc(bi, "fzw", 16))

    def extra(t, psum, tt, fg):
        for mk in fzm:
            wge(t, mk)
        wge(t, xa_mark(tt))
        wge(t, hw_mark(tt))
        t.matmul(psum[:], xaT_all[:, tt * 128:(tt + 1) * 128],
                 Bs_bf[:, fg * 512:(fg + 1) * 512], start=False, stop=False)
        return t.matmul(psum[:], hwT_all[:, tt * 128:(tt + 1) * 128],
                        Br_bf[:, fg * 512:(fg + 1) * 512],
                        start=False, stop=True)

    mkfin = emit_mm_ln_resid(
        p, lambda ck, tt: apx(uT)[:, ck * T + tt * 128: ck * T + tt * 128 + 128],
        ND, Wb, True, bb, x3, out, consts, [hw_mark(NT - 1)], 0, NT,
        name="ml", guard_marks=[hw_mark(0)], extra_mms=extra)

    emit_final_wait(p, mkfin)
    return p.finish()


_PROGS = {}


def _prog(key, builder, *args):
    if key not in _PROGS:
        _PROGS[key] = builder(*args)
    return _PROGS[key]


def kernel(**inputs):
    f32 = np.float32
    x = np.ascontiguousarray(inputs["L0_attn_x"], dtype=f32)
    nc = _prog("fused", build_fused)
    gup = inputs["L0_ffn_gate_up_packed"]
    gus = inputs["L0_ffn_gate_up_scales"]
    dnp = inputs["L0_ffn_down_packed"]
    dns = inputs["L0_ffn_down_scales"]
    Ar_flat = np.ascontiguousarray(
        inputs["L1_ffn_A_r"].transpose(1, 0, 2).reshape(D, 96))
    Br_flat = np.ascontiguousarray(inputs["L1_ffn_B_r"].reshape(96, D))
    row = lambda a: np.ascontiguousarray(a.reshape(1, -1), f32)
    maps = []
    for c in range(8):
        dnp_sh = np.zeros((2048, 96), np.int32)
        dns_sh = np.zeros((6, 2048), np.float16)
        w0, w1 = 96 * c, min(96 * (c + 1), 704)
        dnp_sh[:, 0:w1 - w0] = dnp[:, w0:w1]
        g0, g1 = 6 * c, min(6 * (c + 1), 44)
        dns_sh[0:g1 - g0, :] = dns[g0:g1, :]
        maps.append({
            "x": np.ascontiguousarray(x[c // 2, (c % 2) * T:(c % 2 + 1) * T]),
            "w_qkv": inputs["L0_attn_w_qkv"],
            "b_qkv": row(inputs["L0_attn_b_qkv"]),
            "w_out0": inputs["L0_attn_w_out"],
            "b_out0": row(inputs["L0_attn_b_out"]),
            "gup_sh": np.ascontiguousarray(gup[:, 32 * c:32 * (c + 1)]),
            "gus_sh": np.ascontiguousarray(gus[2 * c:2 * c + 2, :]),
            "dnp_sh": dnp_sh, "dns_sh": dns_sh,
            "wq1": inputs["L1_attn_Wq"], "wk1": inputs["L1_attn_Wk"],
            "wv1": inputs["L1_attn_Wv"], "wo1": inputs["L1_attn_Wo"],
            "W_base": inputs["L1_ffn_W_base"],
            "b_base": row(inputs["L1_ffn_b_base"]),
            "A_s": inputs["L1_ffn_A_s"], "B_s": inputs["L1_ffn_B_s"],
            "A_r_flat": Ar_flat, "B_r_flat": Br_flat,
            "W_router": inputs["L1_ffn_W_router"],
            "b_router": row(inputs["L1_ffn_b_router"]),
            "alpha": np.array([[-1e30, 1.0]] if c % 2 == 0 else [[0.0, 0.0]],
                              np.float32).astype(BFNP),
        })
    res = run_bass_kernel_spmd(nc, maps, core_ids=list(range(8))).results
    out = np.empty((4, S, D), np.float32)
    for c in range(8):
        out[c // 2, (c % 2) * T:(c % 2 + 1) * T] = res[c]["out"]
    return out


def kernel_multilaunch(**inputs):
    f32 = np.float32
    x = np.ascontiguousarray(inputs["L0_attn_x"], dtype=f32)
    xs = [np.ascontiguousarray(x[c // 2, (c % 2) * T:(c % 2 + 1) * T])
          for c in range(8)]

    def run(nc, maps, cores=None):
        cores = list(range(8)) if cores is None else cores
        return run_bass_kernel_spmd(nc, maps, core_ids=cores).results

    w_qkv = inputs["L0_attn_w_qkv"]
    b_qkv = np.ascontiguousarray(inputs["L0_attn_b_qkv"].reshape(1, -1), f32)
    r1 = run(_prog("qkv0", build_qkv, False),
             [{"x": xs[c], "w_qkv": w_qkv, "b_qkv": b_qkv} for c in range(8)])

    w_out = inputs["L0_attn_w_out"]
    b_out = np.ascontiguousarray(inputs["L0_attn_b_out"].reshape(1, -1), f32)
    kv_own = [np.concatenate([r1[c]["kT"], r1[c]["v"]], axis=0) for c in range(8)]
    kv_all = [np.concatenate([kv_own[2 * b], kv_own[2 * b + 1]], axis=0)
              for b in range(4)]
    al0 = np.zeros((1, 2), np.float32).astype(BFNP)
    r2 = run(_prog("attn0", build_attn, False),
             [{"qT": r1[c]["qT"], "kv_own": kv_own[c], "kv_all": kv_all[c // 2],
               "alpha": al0, "x": xs[c], "w_out": w_out, "b_out": b_out}
              for c in range(8)])
    x1 = [r2[c]["x1"] for c in range(8)]

    gup = inputs["L0_ffn_gate_up_packed"]
    gus = inputs["L0_ffn_gate_up_scales"]
    dnp = inputs["L0_ffn_down_packed"]
    dns = inputs["L0_ffn_down_scales"]
    dq_maps = []
    for c in range(8):
        dnp_sh = np.zeros((2048, 96), np.int32)
        dns_sh = np.zeros((6, 2048), np.float16)
        w0, w1 = 96 * c, min(96 * (c + 1), 704)
        dnp_sh[:, 0:w1 - w0] = dnp[:, w0:w1]
        g0, g1 = 6 * c, min(6 * (c + 1), 44)
        dns_sh[0:g1 - g0, :] = dns[g0:g1, :]
        dq_maps.append({
            "gup_sh": np.ascontiguousarray(gup[:, 32 * c:32 * (c + 1)]),
            "gus_sh": np.ascontiguousarray(gus[2 * c:2 * c + 2, :]),
            "dnp_sh": dnp_sh, "dns_sh": dns_sh})
    r3 = run(_prog("dq", build_dq), dq_maps)
    w_guT = np.concatenate([r3[c]["guT_sh"] for c in range(8)], axis=0)
    w_downT = np.ascontiguousarray(
        np.concatenate([r3[c]["dnT_sh"] for c in range(8)], axis=0)[0:5632])

    r4 = run(_prog("ffn", build_ffn),
             [{"x": x1[c], "w_guT": w_guT, "w_downT": w_downT}
              for c in range(8)])
    x2 = [r4[c]["x2"] for c in range(8)]

    r5 = run(_prog("qkv1", build_qkv, True),
             [{"x": x2[c], "wq": inputs["L1_attn_Wq"],
               "wk": inputs["L1_attn_Wk"], "wv": inputs["L1_attn_Wv"]}
              for c in range(8)])
    kv_own1 = [np.concatenate([r5[c]["kT"], r5[c]["v"]], axis=0) for c in range(8)]
    kv_all1 = [np.concatenate([kv_own1[2 * b], kv_own1[2 * b + 1]], axis=0)
               for b in range(4)]

    Wo = inputs["L1_attn_Wo"]
    alph = [np.array([[-1e30, 1.0]] if c % 2 == 0 else [[0.0, 0.0]],
                     np.float32).astype(BFNP) for c in range(8)]
    r6 = run(_prog("attn1", build_attn, True),
             [{"qT": r5[c]["qT"], "kv_own": kv_own1[c],
               "kv_all": kv_all1[c // 2], "alpha": alph[c],
               "x": x2[c], "w_out": Wo} for c in range(8)])
    x3 = [r6[c]["x1"] for c in range(8)]

    Ar_flat = np.ascontiguousarray(
        inputs["L1_ffn_A_r"].transpose(1, 0, 2).reshape(D, 96))
    Br_flat = np.ascontiguousarray(inputs["L1_ffn_B_r"].reshape(96, D))
    r8 = run(_prog("moe", build_moe),
             [{"x": x3[c], "W_base": inputs["L1_ffn_W_base"],
               "b_base": np.ascontiguousarray(
                   inputs["L1_ffn_b_base"].reshape(1, -1), f32),
               "A_s": inputs["L1_ffn_A_s"], "B_s": inputs["L1_ffn_B_s"],
               "A_r_flat": Ar_flat, "B_r_flat": Br_flat,
               "W_router": inputs["L1_ffn_W_router"],
               "b_router": np.ascontiguousarray(
                   inputs["L1_ffn_b_router"].reshape(1, -1), f32)}
              for c in range(8)])

    out = np.empty((4, S, D), np.float32)
    for c in range(8):
        out[c // 2, (c % 2) * T:(c % 2 + 1) * T] = r8[c]["x4"]
    return out


# version 10
# speedup vs baseline: 1.4843x; 1.3900x over previous
import numpy as np
import ml_dtypes
from contextlib import ExitStack

import concourse.bass as bass
import concourse.mybir as mybir
from concourse.bass_utils import run_bass_kernel_spmd

F32 = mybir.dt.float32
BF16 = mybir.dt.bfloat16
F16 = mybir.dt.float16
I32 = mybir.dt.int32
AF = mybir.ActivationFunctionType
ALU = mybir.AluOpType
AX = mybir.AxisListType
BFNP = ml_dtypes.bfloat16

T = 2048
D = 2048
NT = T // 128
ND = D // 128
NH = 16
HD = 128
S = 4096
EPS = 1e-5

L0_SCALE = float(1.0 / np.sqrt(np.float32(128.0)).astype(np.float32))
L1_SCALE = 1.0 / 64.0


class Prog:

    def __init__(self, name="prog"):
        self.nc = bass.Bass("TRN2", target_bir_lowering=False, name=name)
        self.ctx = ExitStack()
        self.sems = {}
        self.cnt = {}
        self.block = None

    def sb(self, name, shape, dtype, at=None):
        key = ("sb", name)
        if not hasattr(self, "_bufs"):
            self._bufs = {}
        if key not in self._bufs:
            if at is None:
                self._bufs[key] = self.nc.alloc_sbuf_tensor(name, shape, dtype)
            else:
                self._bufs[key] = self.nc.alloc_sbuf_tensor_at(
                    name, shape, dtype, offset=at)
        return self._bufs[key]

    def addr(self, name):
        return self.nc.lookup_mloc(self._bufs[("sb", name)]).addr

    def ps(self, name, shape, dtype=F32):
        return self.ctx.enter_context(self.nc.psum_tensor(name, shape, dtype))

    def sem(self, name):
        if name not in self.sems:
            self.sems[name] = self.ctx.enter_context(self.nc.semaphore(name))
            self.cnt[name] = 0
        return self.sems[name]

    def start(self):
        for s in ("te", "sc", "ve", "gp", "ev"):
            self.sem(s)
        self.pfd = [self.ps(f"pfd{i}", [128, 1024], F32) for i in range(2)]
        self.pf = [self.pfd[0][:, 0:512], self.pfd[0][:, 512:1024],
                   self.pfd[1][:, 0:512], self.pfd[1][:, 512:1024],
                   self.ps("pf4", [128, 512], F32),
                   self.ps("pf5", [128, 512], F32)]
        self.pb = [self.ps(f"pbh{i}", [128, 1024], BF16)[:, 0:512]
                   for i in range(2)]
        self.block = self.ctx.enter_context(self.nc.Block())
        return self.block

    def inc(self, bi, name, k=1):
        self.cnt[name] += k
        bi.then_inc(self.sems[name], k)
        return (self.sems[name], self.cnt[name])

    def mark(self, name):
        return (self.sems[name], self.cnt[name])

    def finish(self):
        self.ctx.close()
        return self.nc


def wge(eng, mk):
    eng.wait_ge(mk[0], mk[1])


def apx(x):
    a = getattr(x, "ap", None)
    return x.ap() if callable(a) else x


def emit_consts(p):
    ident = p.sb("ident", [128, 128], BF16)
    ones_row = p.sb("ones_row", [1, 512], BF16)
    ones128 = p.sb("ones128", [128, 128], BF16)
    sixteens = p.sb("sixteens", [128, 128], BF16)
    identf = p.sb("identf", [128, 128], F32)
    b = p.block
    sem = p.sems
    GP = p.cnt["gp"]

    @b.gpsimd
    def _(g):
        g.memset(ident[:], 0.0).then_inc(sem["gp"], 1)
        g.wait_ge(sem["gp"], GP + 1)
        g.affine_select(out=ident[:], in_=ident[:], compare_op=ALU.not_equal,
                        fill=1.0, base=0, pattern=[[-1, 128]],
                        channel_multiplier=1).then_inc(sem["gp"], 1)
        g.memset(ones_row[:], 1.0).then_inc(sem["gp"], 1)
        g.memset(ones128[:], 1.0).then_inc(sem["gp"], 1)
        g.memset(sixteens[:], 16.0).then_inc(sem["gp"], 1)
        g.memset(identf[:], 0.0).then_inc(sem["gp"], 1)
        g.wait_ge(sem["gp"], GP + 6)
        g.affine_select(out=identf[:], in_=identf[:], compare_op=ALU.not_equal,
                        fill=1.0, base=0, pattern=[[-1, 128]],
                        channel_multiplier=1).then_inc(sem["gp"], 1)
    p.cnt["gp"] += 7
    ready = (sem["gp"], p.cnt["gp"])
    p._identf = identf
    p._sixteens = sixteens
    return ident, ones_row, ones128, ready


class LNOut:
    pass


def emit_ln_uT(p, x_hbm, uT, t0, n_tiles, consts, tspan, name="",
               guard_marks=()):
    ident = consts[0]
    cready = consts[3]
    b = p.block
    sem = p.sems
    xt = [p.sb(f"lnxt{name}{i}", [128, D], F32) for i in range(2)]
    for i in range(2):
        p.sem(f"lx{name}{i}")
    trash_bf = p.sb(f"lntrash{name}", [128, D], BF16)
    trash_sq = p.sb(f"lnsq{name}", [128, D], F32)
    sums = p.sb(f"lnsums{name}", [128, 1], F32)
    ssq = p.sb(f"lnssq{name}", [128, 1], F32)
    mean = p.sb(f"lnmean{name}", [128, 1], F32)
    msq = p.sb(f"lnmsq{name}", [128, 1], F32)
    var = p.sb(f"lnvar{name}", [128, 1], F32)
    std = p.sb(f"lnstd{name}", [128, 1], F32)
    rstd = p.sb(f"lnrstd{name}", [128, 1], F32)
    negmr = p.sb(f"lnnegmr{name}", [128, 1], F32)
    u_t = [p.sb(f"lnu{name}{i}", [128, D], BF16) for i in range(2)]
    pb = p.pb

    SC = p.cnt["sc"]
    VE = p.cnt["ve"]
    TE = p.cnt["te"]
    EV = p.cnt["ev"]
    xmark = []

    @b.sync
    def _(sync):
        for mk in guard_marks:
            wge(sync, mk)
        for tt in range(n_tiles):
            if tt >= 2:
                sync.wait_ge(sem["sc"], SC + 3 * (tt - 2) + 3)
                sync.wait_ge(sem["ve"], VE + 7 * (tt - 2) + 1)
            else:
                sync.wait_ge(sem["sc"], SC)
                sync.wait_ge(sem["ve"], VE)
            bi = sync.dma_start(out=xt[tt % 2][:],
                                in_=x_hbm[(t0 + tt) * 128:(t0 + tt + 1) * 128, :])
            xmark.append(p.inc(bi, f"lx{name}{tt % 2}", 16))
    assert len(xmark) == n_tiles

    @b.scalar
    def _(s):
        for tt in range(n_tiles):
            wge(s, xmark[tt])
            if tt >= 1:
                s.wait_ge(sem["ve"], VE + 7 * (tt - 1) + 2)
            s.activation(trash_bf[:], xt[tt % 2][:], AF.Identity,
                         accum_out=sums[:]).then_inc(sem["sc"], 1)
            s.wait_ge(sem["ve"], VE + 7 * tt + 5)
            s.activation(std[:], var[:], AF.Sqrt).then_inc(sem["sc"], 1)
            s.wait_ge(sem["ve"], VE + 7 * tt + 7)
            if tt >= 2:
                s.wait_ge(sem["te"], TE + 4 * (tt - 2) + 4)
            else:
                s.wait_ge(sem["te"], TE)
            s.activation(u_t[tt % 2][:], xt[tt % 2][:], AF.Identity,
                         bias=negmr[:], scale=rstd[:]).then_inc(sem["sc"], 1)
    p.cnt["sc"] += 3 * n_tiles

    def emit_evac(v, tt):
        for g in range(4):
            G = 4 * tt + g
            v.wait_ge(sem["te"], TE + G + 1)
            dst = apx(uT).rearrange("p (dc t) -> p dc t", dc=ND, t=tspan)[
                :, 4 * g:4 * g + 4, tt * 128:(tt + 1) * 128]
            src = pb[G % 2].rearrange("p (j f) -> p j f", j=4, f=128)
            v.tensor_copy(dst, src).then_inc(sem["ev"], 1)

    @b.vector
    def _(v):
        for mk in guard_marks:
            wge(v, mk)
        for tt in range(n_tiles):
            wge(v, xmark[tt])
            v.scalar_tensor_tensor(trash_sq[:], xt[tt % 2][:], 0.0, xt[tt % 2][:],
                                   op0=ALU.add, op1=ALU.mult,
                                   accum_out=ssq[:]).then_inc(sem["ve"], 1)
            v.wait_ge(sem["sc"], SC + 3 * tt + 1)
            v.tensor_scalar_mul(mean[:], sums[:], 1.0 / D).then_inc(sem["ve"], 1)
            v.wait_ge(sem["ve"], VE + 7 * tt + 2)
            v.scalar_tensor_tensor(msq[:], mean[:], 0.0, mean[:],
                                   op0=ALU.add, op1=ALU.mult).then_inc(sem["ve"], 1)
            v.wait_ge(sem["ve"], VE + 7 * tt + 3)
            v.scalar_tensor_tensor(var[:], ssq[:], 1.0 / D, msq[:],
                                   op0=ALU.mult, op1=ALU.subtract).then_inc(sem["ve"], 1)
            v.wait_ge(sem["ve"], VE + 7 * tt + 4)
            v.tensor_scalar_add(var[:], var[:], EPS).then_inc(sem["ve"], 1)
            v.wait_ge(sem["sc"], SC + 3 * tt + 2)
            v.reciprocal(rstd[:], std[:]).then_inc(sem["ve"], 1)
            v.wait_ge(sem["ve"], VE + 7 * tt + 6)
            v.scalar_tensor_tensor(negmr[:], mean[:], -1.0, rstd[:],
                                   op0=ALU.mult, op1=ALU.mult).then_inc(sem["ve"], 1)
            if tt >= 1:
                emit_evac(v, tt - 1)
        emit_evac(v, n_tiles - 1)
    p.cnt["ve"] += 7 * n_tiles
    p.cnt["ev"] += 4 * n_tiles

    @b.tensor
    def _(t):
        wge(t, cready)
        for tt in range(n_tiles):
            t.wait_ge(sem["sc"], SC + 3 * tt + 3)
            for g in range(4):
                G = 4 * tt + g
                if G >= 2:
                    t.wait_ge(sem["ev"], EV + (G - 2) + 1)
                else:
                    t.wait_ge(sem["ev"], EV)
                for j in range(4):
                    dc = 4 * g + j
                    bi = t.transpose(pb[G % 2][:, j * 128:(j + 1) * 128],
                                     u_t[tt % 2][:, dc * 128:(dc + 1) * 128],
                                     ident[:])
                bi.then_inc(sem["te"], 1)
    p.cnt["te"] += 4 * n_tiles


    out = LNOut()
    out.tile_done = lambda tt: (sem["ev"], EV + 4 * (tt + 1))
    out.all_done = (sem["ev"], EV + 4 * n_tiles)
    return out


def emit_proj_fm(p, uT, w_hbm, w_cols, b_hbm, b_off, out_hbm, consts, ln_mark,
                 nfc, tspan, name=""):
    _, ones_row, _, cready = consts
    b = p.block
    sem = p.sems
    ntg = tspan // 512
    wbuf = [p.sb(f"pfw{name}{i}", [128, ND * 128], BF16) for i in range(2)]
    stage = [p.sb(f"pfs{name}{i}", [128, 512], BF16) for i in range(2)]
    for i in range(2):
        p.sem(f"pfl{name}{i}")
        p.sem(f"pfo{name}{i}")
    p.sem(f"pflb{name}")
    pf = p.pf[0:2]
    brow = p.sb(f"pfb{name}", [1, nfc * 128], BF16) if b_hbm is not None else None

    TE = p.cnt["te"]
    SC = p.cnt["sc"]
    SO = [p.cnt[f"pfo{name}{i}"] for i in range(2)]
    wmark = []
    bmark = []

    @b.gpsimd
    def _(g):
        if b_hbm is not None:
            g.wait_ge(sem["te"], TE)
            bi = g.dma_start(out=brow[:], in_=b_hbm[:, b_off:b_off + nfc * 128])
            bmark.append(p.inc(bi, f"pflb{name}", 16))
        for fc in range(nfc):
            if fc >= 2:
                g.wait_ge(sem["te"], TE + ntg * (fc - 1))
            else:
                g.wait_ge(sem["te"], TE)
            src = w_hbm[:, w_cols + fc * 128:w_cols + (fc + 1) * 128]\
                .rearrange("(dc p) f -> p dc f", p=128)
            dst = wbuf[fc % 2].ap().rearrange("p (dc f) -> p dc f", dc=ND, f=128)
            bi = g.dma_start(out=dst, in_=src)
            wmark.append(p.inc(bi, f"pfl{name}{fc % 2}", 16))

    @b.tensor
    def _(t):
        wge(t, cready)
        for fc in range(nfc):
            wge(t, wmark[fc])
            if fc == 0 and bmark:
                wge(t, bmark[0])
            for tg in range(ntg):
                K = fc * ntg + tg
                if K >= 2:
                    t.wait_ge(sem["sc"], SC + (K - 2) + 1)
                else:
                    t.wait_ge(sem["sc"], SC)
                wge(t, ln_mark(tg))
                first = True
                if b_hbm is not None:
                    t.matmul(pf[K % 2][:], brow[0:1, fc * 128:(fc + 1) * 128],
                             ones_row[0:1, :], start=True, stop=False)
                    first = False
                for dc in range(ND):
                    bi = t.matmul(
                        pf[K % 2][:], wbuf[fc % 2][:, dc * 128:(dc + 1) * 128],
                        apx(uT)[:, dc * tspan + tg * 512: dc * tspan + tg * 512 + 512],
                        start=first, stop=(dc == ND - 1))
                    first = False
                bi.then_inc(sem["te"], 1)
    p.cnt["te"] += nfc * ntg

    @b.scalar
    def _(s):
        for fc in range(nfc):
            for tg in range(ntg):
                K = fc * ntg + tg
                s.wait_ge(sem["te"], TE + K + 1)
                if K >= 2:
                    s.wait_ge(p.sems[f"pfo{name}{K % 2}"],
                              SO[K % 2] + 16 * (K // 2))
                s.activation(stage[K % 2][:], pf[K % 2][:], AF.Identity)\
                    .then_inc(sem["sc"], 1)
    p.cnt["sc"] += nfc * ntg

    @b.sync
    def _(sync):
        for fc in range(nfc):
            for tg in range(ntg):
                K = fc * ntg + tg
                sync.wait_ge(sem["sc"], SC + K + 1)
                bi = sync.dma_start(out=out_hbm[fc * 128:(fc + 1) * 128,
                                                tg * 512:tg * 512 + 512],
                                    in_=stage[K % 2][:])
                p.inc(bi, f"pfo{name}{K % 2}", 16)
    return [p.mark(f"pfo{name}0"), p.mark(f"pfo{name}1")]


def emit_proj_tm(p, uT, w_hbm, w_cols, b_hbm, b_off, out_hbm, consts, ln_mark,
                 tspan, t0=0, nfg=4, name=""):
    _, ones_row, _, cready = consts
    b = p.block
    sem = p.sems
    ntl = tspan // 128
    GR = 4
    ngr = ntl // GR
    wtile = [p.sb(f"ptw{name}{i}", [128, 512], BF16) for i in range(2)]
    stage = [p.sb(f"pts{name}{i}", [128, 512], BF16) for i in range(2)]
    for i in range(2):
        p.sem(f"ptl{name}{i}")
        p.sem(f"pto{name}{i}")
    p.sem(f"ptlb{name}")
    pt = p.pf[0:4]
    brow = p.sb(f"ptb{name}", [1, nfg * 512], BF16) if b_hbm is not None else None

    TE = p.cnt["te"]
    SC = p.cnt["sc"]
    SO = [p.cnt[f"pto{name}{i}"] for i in range(2)]
    wmark = []
    bmark = []

    @b.gpsimd
    def _(g):
        if b_hbm is not None:
            bi = g.dma_start(out=brow[:], in_=b_hbm[:, b_off:b_off + nfg * 512])
            bmark.append(p.inc(bi, f"ptlb{name}", 16))
        for fg in range(nfg):
            for ttg in range(ngr):
                for dc in range(ND):
                    L = (fg * ngr + ttg) * ND + dc
                    if L >= 2:
                        g.wait_ge(sem["te"], TE + L - 1)
                    else:
                        g.wait_ge(sem["te"], TE)
                    bi = g.dma_start(
                        out=wtile[L % 2][:],
                        in_=w_hbm[dc * 128:(dc + 1) * 128,
                                  w_cols + fg * 512:w_cols + (fg + 1) * 512])
                    wmark.append(p.inc(bi, f"ptl{name}{L % 2}", 16))

    @b.tensor
    def _(t):
        wge(t, cready)
        for fg in range(nfg):
            for ttg in range(ngr):
                grp = fg * ngr + ttg
                for dc in range(ND):
                    L = grp * ND + dc
                    wge(t, wmark[L])
                    if L == 0 and bmark:
                        wge(t, bmark[0])
                    for i in range(GR):
                        tt = ttg * GR + i
                        if dc == 0:
                            if grp >= 1:
                                t.wait_ge(sem["sc"], SC + (grp - 1) * GR + i + 1)
                            else:
                                t.wait_ge(sem["sc"], SC)
                            wge(t, ln_mark(tt))
                            if b_hbm is not None:
                                t.matmul(pt[i][:], ones_row[0:1, 0:128],
                                         brow[0:1, fg * 512:(fg + 1) * 512],
                                         start=True, stop=False)
                        bi = t.matmul(
                            pt[i][:],
                            apx(uT)[:, dc * tspan + tt * 128: dc * tspan + tt * 128 + 128],
                            wtile[L % 2][:],
                            start=(dc == 0 and b_hbm is None),
                            stop=(dc == ND - 1))
                    bi.then_inc(sem["te"], 1)
    p.cnt["te"] += nfg * ngr * ND

    @b.scalar
    def _(s):
        for fg in range(nfg):
            for ttg in range(ngr):
                grp = fg * ngr + ttg
                s.wait_ge(sem["te"], TE + (grp + 1) * ND)
                for i in range(GR):
                    E = grp * GR + i
                    if E >= 2:
                        s.wait_ge(p.sems[f"pto{name}{E % 2}"],
                                  SO[E % 2] + 16 * (E // 2))
                    s.activation(stage[E % 2][:], pt[i][:], AF.Identity)\
                        .then_inc(sem["sc"], 1)
    p.cnt["sc"] += nfg * ngr * GR

    @b.sync
    def _(sync):
        for fg in range(nfg):
            for ttg in range(ngr):
                grp = fg * ngr + ttg
                for i in range(GR):
                    tt = ttg * GR + i
                    E = grp * GR + i
                    sync.wait_ge(sem["sc"], SC + E + 1)
                    bi = sync.dma_start(
                        out=out_hbm[(t0 + tt) * 128:(t0 + tt + 1) * 128,
                                    fg * 512:(fg + 1) * 512],
                        in_=stage[E % 2][:])
                    p.inc(bi, f"pto{name}{E % 2}", 16)
    return [p.mark(f"pto{name}0"), p.mark(f"pto{name}1")]


def emit_attn_core(p, qTh, kv_own, kv_all, alpha_in, consts, scale, l1,
                   AVTn, name="", guard_marks=()):
    ident, ones_row, ones128, cready = consts
    b = p.block
    sem = p.sems
    kT_h = [p.sb(f"at_k{name}{i}", [128, S], BF16) for i in range(2)]
    v_h = [p.sb(f"at_v{name}{i}", [128, 32 * 128], BF16) for i in range(2)]
    qT_h = [p.sb(f"at_q{name}{i}", [128, T], BF16) for i in range(2)]
    pA = p.sb(f"at_pa{name}", [128, T], BF16)
    pC = p.sb(f"at_pc{name}", [128, T], BF16)
    expT = p.sb(f"at_e{name}", [128, 16 * 512], BF16)
    r_s = p.sb(f"at_r{name}", [128, 512], F32)
    a_sb = p.sb(f"at_al{name}", [1, 2], BF16)
    lnA = p.sb(f"at_lna{name}", [128, 1], F32)
    betac = p.sb(f"at_bc{name}", [128, 1], F32)
    B_t = p.sb(f"at_bt{name}", [128, 512], BF16)
    for i in range(2):
        p.sem(f"athd{name}{i}")
    p.sem(f"atal{name}")
    ps_sd = p.pfd
    ps_l = p.pf[4]
    ps_av = p.pf[5]
    ps_b = p.pf[4]

    NQG = T // 512

    def classify(h, qg, ktc):
        if ktc >= 16:
            return "blend" if l1 else "full"
        if not l1:
            return "full"
        lo = 128 * ktc
        if lo >= 512 * (qg + 1):
            return "skip"
        if lo + 127 <= 512 * qg:
            return "full"
        return "diag"

    TE0 = p.cnt["te"]
    SC0 = p.cnt["sc"]
    GP0 = p.cnt["gp"]
    VE0 = p.cnt["ve"]
    def pdone(h):
        return VE0 + 1 + 4 * h + 8 * max(0, h - 1) + 4

    def grp_base(H):
        if H <= NH - 3:
            return VE0 + 9 + 12 * H
        return VE0 + 1 + 4 * NH + 8 * H

    rc_mark = {}
    mu_mark = {}
    for H in range(NH):
        for qg in range(NH * 0 + 4):
            rc_mark[(H, qg)] = grp_base(H) + 2 * qg + 1
            mu_mark[(H, qg)] = grp_base(H) + 2 * qg + 2
    VE_END = VE0 + 1 + 4 * NH + 8 * NH
    TE1 = TE0 + 1
    SC1 = SC0 + 2

    te_n, sc_n = TE1, SC1
    gp_n = GP0 + (4 if l1 else 0)
    score_mark = {}
    exp_mark = {}
    gp_mark = {}
    lp0_mark = {}
    avp0_mark = {}
    l_mark = {}
    av_mark = {}
    exp_done_own = {}
    exp_done = {}
    n_own_of = {}
    for h in range(NH):
        for qg in range(NQG):
            own = [k for k in range(16) if classify(h, qg, k) != "skip"]
            assert own == list(range(len(own))) and len(own) % 2 == 0
            n_own_of[(h, qg)] = len(own)
            for jp in range(len(own) // 2):
                te_n += 2
                score_mark[(h, qg, 2 * jp)] = te_n - 1
                score_mark[(h, qg, 2 * jp + 1)] = te_n
                sc_n += 1
                exp_mark[(h, qg, 2 * jp)] = sc_n
                exp_mark[(h, qg, 2 * jp + 1)] = sc_n
            exp_done_own[(h, qg)] = sc_n
            te_n += 1
            lp0_mark[(h, qg)] = te_n
            te_n += 1
            avp0_mark[(h, qg)] = te_n
            for jp in range(8):
                te_n += 2
                score_mark[(h, qg, 16 + 2 * jp)] = te_n - 1
                score_mark[(h, qg, 17 + 2 * jp)] = te_n
                sc_n += 1
                exp_mark[(h, qg, 16 + 2 * jp)] = sc_n
                exp_mark[(h, qg, 17 + 2 * jp)] = sc_n
            if l1:
                k0 = 4 * qg
                if k0 + 4 <= 15:
                    gp_n += 1
                    for k in range(k0 + 4, 16):
                        gp_mark[(h, qg, k)] = gp_n
                gp_n += 2
                for k in range(k0, k0 + 4):
                    gp_mark[(h, qg, k)] = gp_n
            exp_done[(h, qg)] = sc_n
            te_n += 1
            l_mark[(h, qg)] = te_n
            te_n += 1
            av_mark[(h, qg)] = te_n

    def prev_group(h, qg):
        if qg > 0:
            return (h, qg - 1)
        if h > 0:
            return (h - 1, NQG - 1)
        return None

    amark = []
    hmark = []

    @b.sync
    def _(sync):
        for mk in guard_marks:
            wge(sync, mk)
        sync.wait_ge(sem["te"], TE0)
        sync.wait_ge(sem["ve"], VE0)
        bi = sync.dma_start(out=a_sb[:], in_=alpha_in)
        amark.append(p.inc(bi, f"atal{name}", 16))
        for h in range(NH):
            if h >= 2:
                sync.wait_ge(sem["te"], av_mark[(h - 2, NQG - 1)])
                sync.wait_ge(sem["ve"], pdone(h - 2))
            if h >= 1:
                sync.wait_ge(sem["ve"], pdone(h - 1))
            sn = f"athd{name}{h % 2}"
            bi = sync.dma_start(out=kT_h[h % 2][:, 0:T],
                                in_=kv_own[h * 128:(h + 1) * 128, :])
            p.inc(bi, sn, 16)
            bi = sync.dma_start(out=pA[:],
                                in_=kv_all[h * 128:(h + 1) * 128, :])
            p.inc(bi, sn, 16)
            bi = sync.dma_start(
                out=kT_h[h % 2][:, T:2 * T],
                in_=kv_all[2 * T + h * 128:2 * T + (h + 1) * 128, :])
            p.inc(bi, sn, 16)
            src = kv_own[T:2 * T, h * 128:(h + 1) * 128]\
                .rearrange("(c p) f -> p c f", p=128)
            dst = v_h[h % 2].ap().rearrange("p (c f) -> p c f",
                                            c=32, f=128)[:, 0:16, :]
            bi = sync.dma_start(out=dst, in_=src)
            p.inc(bi, sn, 16)
            src = kv_all[T:2 * T, h * 128:(h + 1) * 128]\
                .rearrange("(c p) f -> p c f", p=128)
            bi = sync.dma_start(
                out=pC.ap().rearrange("p (c f) -> p c f", c=16, f=128),
                in_=src)
            p.inc(bi, sn, 16)
            src = kv_all[3 * T:4 * T, h * 128:(h + 1) * 128]\
                .rearrange("(c p) f -> p c f", p=128)
            bi = sync.dma_start(
                out=v_h[h % 2].ap().rearrange("p (c f) -> p c f",
                                              c=32, f=128)[:, 16:32, :],
                in_=src)
            p.inc(bi, sn, 16)
            bi = sync.dma_start(out=qT_h[h % 2][:],
                                in_=qTh[h * 128:(h + 1) * 128, :])
            hmark.append(p.inc(bi, sn, 16))

    def emit_partner(v, h):
        wge(v, hmark[h])
        if h >= 2:
            v.wait_ge(sem["te"], av_mark[(h - 2, NQG - 1)])
        ve_b = pdone(h) - 4
        v.tensor_add(pA[:], pA[:], kT_h[h % 2][:, T:2 * T])\
            .then_inc(sem["ve"], 1)
        v.wait_ge(sem["ve"], ve_b + 1)
        v.tensor_sub(kT_h[h % 2][:, T:2 * T], pA[:],
                     kT_h[h % 2][:, 0:T]).then_inc(sem["ve"], 1)
        v.tensor_add(pC[:], pC[:],
                     v_h[h % 2][:, 16 * 128:32 * 128]).then_inc(sem["ve"], 1)
        v.wait_ge(sem["ve"], ve_b + 3)
        v.tensor_sub(v_h[h % 2][:, 16 * 128:32 * 128], pC[:],
                     v_h[h % 2][:, 0:16 * 128]).then_inc(sem["ve"], 1)

    def emit_grp(v, H):
        for qg in range(NQG):
            v.wait_ge(sem["te"], l_mark[(H, qg)])
            if not (H == 0 and qg == 0):
                pm = mu_mark[(H, qg - 1)] if qg > 0 else mu_mark[(H - 1, 3)]
                v.wait_ge(sem["ve"], pm)
            v.reciprocal(r_s[:], ps_l[:]).then_inc(sem["ve"], 1)
            v.wait_ge(sem["ve"], rc_mark[(H, qg)])
            v.wait_ge(sem["te"], av_mark[(H, qg)])
            v.tensor_mul(apx(AVTn)[:, H * T + qg * 512: H * T + qg * 512 + 512],
                         ps_av[:], r_s[:]).then_inc(sem["ve"], 1)

    @b.vector
    def _(v):
        wge(v, (sem["sc"], SC1))
        v.tensor_copy(B_t.ap().rearrange("p (a f) -> p a f", a=1, f=512),
                      betac.ap().broadcast_to([128, 1, 512]))\
            .then_inc(sem["ve"], 1)
        emit_partner(v, 0)
        emit_partner(v, 1)
        for h in range(2, NH):
            emit_grp(v, h - 2)
            emit_partner(v, h)
        emit_grp(v, NH - 2)
        emit_grp(v, NH - 1)
    p.cnt["ve"] = VE_END

    @b.tensor
    def _(t):
        wge(t, cready)
        wge(t, amark[0])
        t.wait_ge(sem["sc"], SC0)
        t.matmul(ps_b[:, 0:2], ones_row[0:1, 0:128], a_sb[0:1, :],
                 start=True, stop=True).then_inc(sem["te"], 1)
    p.cnt["te"] = TE1

    @b.scalar
    def _(s):
        s.wait_ge(sem["te"], TE1)
        s.activation(lnA[:], ps_b[:, 0:1], AF.Identity).then_inc(sem["sc"], 1)
        s.activation(betac[:], ps_b[:, 1:2], AF.Identity).then_inc(sem["sc"], 1)
    p.cnt["sc"] = SC1

    @b.tensor
    def _(t):
        w_sc = [0]
        w_gp = [0]

        def wsc(val):
            if val > w_sc[0]:
                t.wait_ge(sem["sc"], val)
                w_sc[0] = val

        def wgp(val):
            if val > w_gp[0]:
                t.wait_ge(sem["gp"], val)
                w_gp[0] = val

        for h in range(NH):
            wge(t, hmark[h])
            t.wait_ge(sem["ve"], pdone(h))
            for qg in range(NQG):
                pg = prev_group(h, qg)
                n_own = n_own_of[(h, qg)]
                for jp in range(n_own // 2):
                    if jp >= 2:
                        wsc(exp_mark[(h, qg, 2 * (jp - 2))])
                    elif pg is not None:
                        wsc(exp_done[pg])
                    else:
                        wsc(SC1)
                    for kk in (2 * jp, 2 * jp + 1):
                        bi = t.matmul(
                            ps_sd[jp % 2][:, (kk % 2) * 512:(kk % 2) * 512 + 512],
                            kT_h[h % 2][:, kk * 128:(kk + 1) * 128],
                            qT_h[h % 2][:, qg * 512:(qg + 1) * 512],
                            start=True, stop=True)
                        bi.then_inc(sem["te"], 1)
                for ktc in range(16):
                    c = classify(h, qg, ktc)
                    if c == "full":
                        wsc(exp_mark[(h, qg, ktc)])
                    else:
                        wgp(gp_mark[(h, qg, ktc)])
                    if ktc == 0 and pg is not None:
                        t.wait_ge(sem["ve"], rc_mark[pg])
                    bi = t.matmul(ps_l[:], ones128[:],
                                  expT[:, ktc * 512:(ktc + 1) * 512],
                                  start=(ktc == 0), stop=False)
                bi.then_inc(sem["te"], 1)
                for ktc in range(16):
                    if ktc == 0 and pg is not None:
                        t.wait_ge(sem["ve"], mu_mark[pg])
                    bi = t.matmul(ps_av[:],
                                  v_h[h % 2][:, ktc * 128:(ktc + 1) * 128],
                                  expT[:, ktc * 512:(ktc + 1) * 512],
                                  start=(ktc == 0), stop=False)
                bi.then_inc(sem["te"], 1)
                for jp in range(8):
                    if jp >= 2:
                        wsc(exp_mark[(h, qg, 16 + 2 * (jp - 2))])
                    else:
                        wsc(exp_done_own[(h, qg)])
                    for kk in (16 + 2 * jp, 17 + 2 * jp):
                        bi = t.matmul(
                            ps_sd[jp % 2][:, (kk % 2) * 512:(kk % 2) * 512 + 512],
                            kT_h[h % 2][:, kk * 128:(kk + 1) * 128],
                            qT_h[h % 2][:, qg * 512:(qg + 1) * 512],
                            start=True, stop=True)
                        bi.then_inc(sem["te"], 1)
                for ktc in range(16, 32):
                    wsc(exp_mark[(h, qg, ktc)])
                    bi = t.matmul(ps_l[:], ones128[:],
                                  expT[:, (ktc - 16) * 512:(ktc - 15) * 512],
                                  start=False, stop=(not l1 and ktc == 31))
                if l1:
                    bi = t.matmul(ps_l[:], p._sixteens[:], B_t[:],
                                  start=False, stop=True)
                bi.then_inc(sem["te"], 1)
                for ktc in range(16, 32):
                    bi = t.matmul(ps_av[:],
                                  v_h[h % 2][:, ktc * 128:(ktc + 1) * 128],
                                  expT[:, (ktc - 16) * 512:(ktc - 15) * 512],
                                  start=False, stop=(not l1 and ktc == 31))
                if l1:
                    for ktc in range(16, 32):
                        bi = t.matmul(ps_av[:],
                                      v_h[h % 2][:, ktc * 128:(ktc + 1) * 128],
                                      B_t[:], start=False, stop=(ktc == 31))
                bi.then_inc(sem["te"], 1)
    p.cnt["te"] = te_n

    @b.scalar
    def _(s):
        for h in range(NH):
            for qg in range(NQG):
                pg = prev_group(h, qg)
                n_own = n_own_of[(h, qg)]
                for jp in range(n_own // 2):
                    if jp == 0 and pg is not None:
                        s.wait_ge(sem["te"], av_mark[pg])
                    s.wait_ge(sem["te"], score_mark[(h, qg, 2 * jp + 1)])
                    s.activation(expT[:, 2 * jp * 512:(2 * jp + 2) * 512],
                                 ps_sd[jp % 2][:], AF.Exp, scale=scale)\
                        .then_inc(sem["sc"], 1)
                for jp in range(8):
                    if jp == 0:
                        s.wait_ge(sem["te"], avp0_mark[(h, qg)])
                    s.wait_ge(sem["te"], score_mark[(h, qg, 17 + 2 * jp)])
                    if l1:
                        s.activation(expT[:, 2 * jp * 512:(2 * jp + 2) * 512],
                                     ps_sd[jp % 2][:], AF.Exp, scale=scale,
                                     bias=lnA[:]).then_inc(sem["sc"], 1)
                    else:
                        s.activation(expT[:, 2 * jp * 512:(2 * jp + 2) * 512],
                                     ps_sd[jp % 2][:], AF.Exp, scale=scale)\
                            .then_inc(sem["sc"], 1)
    p.cnt["sc"] = sc_n

    if l1:
        msk1 = p.sb(f"at_m1{name}", [128, 4 * 512], BF16)
        msk0 = p.sb(f"at_m0{name}", [128, 4 * 512], BF16)

        @b.gpsimd
        def _(g):
            g.memset(msk1[:], 1.0).then_inc(sem["gp"], 1)
            g.wait_ge(sem["gp"], GP0 + 1)
            m3 = msk1.ap().rearrange("p (c f) -> p c f", c=4, f=512)
            g.affine_select(out=m3, in_=m3, compare_op=ALU.is_ge, fill=0.0,
                            base=0, pattern=[[-128, 4], [1, 512]],
                            channel_multiplier=-1).then_inc(sem["gp"], 1)
            g.memset(msk0[:], 0.0).then_inc(sem["gp"], 1)
            g.wait_ge(sem["gp"], GP0 + 3)
            m3 = msk0.ap().rearrange("p (c f) -> p c f", c=4, f=512)
            g.affine_select(out=m3, in_=m3, compare_op=ALU.is_ge, fill=1.0,
                            base=0, pattern=[[-128, 4], [1, 512]],
                            channel_multiplier=-1).then_inc(sem["gp"], 1)
            gpc = GP0 + 4
            g.wait_ge(sem["ve"], VE0 + 1)
            for h in range(NH):
                for qg in range(NQG):
                    pg = prev_group(h, qg)
                    k0 = 4 * qg
                    if pg is not None:
                        g.wait_ge(sem["te"], av_mark[pg])
                    if k0 + 4 <= 15:
                        g.memset(expT[:, (k0 + 4) * 512:16 * 512], 1.0)\
                            .then_inc(sem["gp"], 1)
                        gpc += 1
                    g.wait_ge(sem["sc"], exp_mark[(h, qg, k0 + 3)])
                    dg = expT[:, k0 * 512:(k0 + 4) * 512]
                    g.tensor_mul(dg, dg, msk1[:]).then_inc(sem["gp"], 1)
                    gpc += 1
                    g.wait_ge(sem["gp"], gpc)
                    g.tensor_add(dg, dg, msk0[:]).then_inc(sem["gp"], 1)
                    gpc += 1
            assert gpc == gp_n, (gpc, gp_n)
    p.cnt["gp"] = gp_n

    return p.mark("ve")


def build_attn(l1):
    p = Prog("attn")
    nc = p.nc
    qTh = nc.dram_tensor("qT", [D, T], BF16, kind="ExternalInput").ap()
    kv_own = nc.dram_tensor("kv_own", [2 * T, D], BF16, kind="ExternalInput").ap()
    kv_all = nc.dram_tensor("kv_all", [4 * T, D], BF16, kind="ExternalInput").ap()
    alpha = nc.dram_tensor("alpha", [1, 2], BF16, kind="ExternalInput").ap()
    x = nc.dram_tensor("x", [T, D], F32, kind="ExternalInput").ap()
    w_out = nc.dram_tensor("w_out", [D, D], F32, kind="ExternalInput").ap()
    b_out = None if l1 else \
        nc.dram_tensor("b_out", [1, D], F32, kind="ExternalInput").ap()
    x1 = nc.dram_tensor("x1", [T, D], F32, kind="ExternalOutput").ap()

    AVTn = p.sb("AVTn", [128, NH * T], BF16)
    p.start()
    consts = emit_consts(p)
    scale = L1_SCALE if l1 else L0_SCALE
    avm = emit_attn_core(p, qTh, kv_own, kv_all, alpha, consts, scale, l1, AVTn)
    mk = emit_mm_ln_resid(
        p, lambda ck, tt: AVTn[:, ck * T + tt * 128: ck * T + tt * 128 + 128],
        ND, w_out, True, b_out, x, x1, consts, [avm], 0, NT, name="o")
    emit_final_wait(p, mk)
    return p.finish()


def emit_gu(p, uT, w_guT, h_actT, consts, ready_mark, name="", guard_marks=()):
    _, ones_row, _, cready = consts
    b = p.block
    sem = p.sems
    NI = 44
    wg = [p.sb(f"guwg{name}{i}", [128, ND * 128], BF16) for i in range(2)]
    wu = [p.sb(f"guwu{name}{i}", [128, ND * 128], BF16) for i in range(2)]
    gsil = [p.sb(f"gusil{name}{i}", [128, 512], BF16) for i in range(2)]
    for i in range(2):
        p.sem(f"gug{name}{i}")
        p.sem(f"guu{name}{i}")
    psg = [p.pf[0], p.pf[1]]
    psu = [p.pf[2], p.pf[3]]

    TE = p.cnt["te"]
    SC = p.cnt["sc"]
    VE = p.cnt["ve"]
    gmark, umark = [], []

    @b.gpsimd
    def _(g):
        for mk in guard_marks:
            wge(g, mk)
        for ic in range(NI):
            if ic >= 2:
                g.wait_ge(sem["te"], TE + 2 * (ic - 1))
            else:
                g.wait_ge(sem["te"], TE)
            for (wbuf, marks, semn, col) in (
                    (wg, gmark, f"gug{name}{ic % 2}", ic * 128),
                    (wu, umark, f"guu{name}{ic % 2}", 5632 + ic * 128)):
                srcw = w_guT[:, col:col + 128]\
                    .rearrange("(dc p) f -> p dc f", p=128)
                dstw = wbuf[ic % 2].ap().rearrange("p (dc f) -> p dc f",
                                                   dc=ND, f=128)
                bi = g.dma_start(out=dstw, in_=srcw)
                marks.append(p.inc(bi, semn, 16))

    @b.tensor
    def _(t):
        wge(t, cready)
        wge(t, ready_mark)
        for ic in range(NI):
            wge(t, gmark[ic])
            wge(t, umark[ic])
            if ic >= 2:
                t.wait_ge(sem["sc"], SC + (ic - 2) + 1)
                t.wait_ge(sem["ve"], VE + (ic - 2) + 1)
            else:
                t.wait_ge(sem["sc"], SC)
                t.wait_ge(sem["ve"], VE)
            for dc in range(ND):
                bi = t.matmul(psg[ic % 2][:],
                              wg[ic % 2][:, dc * 128:(dc + 1) * 128],
                              apx(uT)[:, dc * 512:dc * 512 + 512],
                              start=(dc == 0), stop=(dc == ND - 1))
            bi.then_inc(sem["te"], 1)
            for dc in range(ND):
                bi = t.matmul(psu[ic % 2][:],
                              wu[ic % 2][:, dc * 128:(dc + 1) * 128],
                              apx(uT)[:, dc * 512:dc * 512 + 512],
                              start=(dc == 0), stop=(dc == ND - 1))
            bi.then_inc(sem["te"], 1)
    p.cnt["te"] += 2 * NI

    @b.scalar
    def _(s):
        for ic in range(NI):
            s.wait_ge(sem["te"], TE + 2 * ic + 1)
            if ic >= 2:
                s.wait_ge(sem["ve"], VE + (ic - 2) + 1)
            else:
                s.wait_ge(sem["ve"], VE)
            s.activation(gsil[ic % 2][:], psg[ic % 2][:], AF.Silu)\
                .then_inc(sem["sc"], 1)
    p.cnt["sc"] += NI

    @b.vector
    def _(v):
        for mk in guard_marks:
            wge(v, mk)
        v.wait_ge(sem["te"], TE)
        for ic in range(NI):
            v.wait_ge(sem["sc"], SC + ic + 1)
            v.wait_ge(sem["te"], TE + 2 * ic + 2)
            v.tensor_mul(apx(h_actT)[:, ic * 512:(ic + 1) * 512],
                         gsil[ic % 2][:], psu[ic % 2][:]).then_inc(sem["ve"], 1)
    p.cnt["ve"] += NI

    return p.mark("ve")


def emit_dq(p, gup, gus, dnp, dns, guT, dnT, consts):
    W = 1536
    pk = p.sb("dq_pk", [128, W], I32)
    iscr = p.sb("dq_i", [128, W], I32)
    kf = p.sb("dq_kf", [128, W], F32)
    sf = p.sb("dq_sf", [128, W], F32)
    r4 = p.sb("dq_r4", [128, W], F32)
    r6 = p.sb("dq_r6", [128, W], F32)
    m1 = p.sb("dq_m1", [128, W], F32)
    sful = p.sb("dq_s", [128, W], F32)
    gusT = p.sb("dq_gusT", [128, 176], F32)
    dnsT = p.sb("dq_dnsT", [128, 96], F32)
    cm2 = p.sb("dq_cm2", [128, 1], F32)
    cm6 = p.sb("dq_cm6", [128, 1], F32)
    idf = p.sb("dq_idf", [128, 128], F32)
    deq_gu = p.sb("dq_dgu", [128, 88 * 256], BF16)
    deq_dn = p.sb("dq_ddn", [128, 16 * 768], BF16)
    gusf = deq_gu.ap().bitcast(F32)[0:2, 0:11264]
    dnsf = deq_dn.ap().bitcast(F32)[0:6, 0:2048]
    stg = [p.sb(f"dq_stg{i}", [128, 512], BF16) for i in range(2)]
    for s_ in ("dqpk", "dqpk2", "dqsc", "dqsc2"):
        p.sem(s_)

    b = p.block
    sem = p.sems
    psf = p.pf[4]
    pb = p.pb

    lm = {}

    @b.gpsimd
    def _(g):
        bi = g.dma_start(
            out=pk[:, 0:1408].rearrange("p (a w) -> p a w", a=44, w=32),
            in_=gup[0:5632, :].rearrange("(a p) w -> p a w", p=128))
        lm["gu0"] = p.inc(bi, "dqpk", 16)
        bi = g.dma_start(out=gusf, in_=gus)
        lm["gus"] = p.inc(bi, "dqsc", 16)
        bi = g.dma_start(out=dnsf, in_=dns)
        lm["dns"] = p.inc(bi, "dqsc2", 16)
        g.memset(cm2[:], -2.0).then_inc(sem["gp"], 1)
        g.memset(cm6[:], -6.0).then_inc(sem["gp"], 1)
        g.memset(idf[:], 0.0).then_inc(sem["gp"], 1)
        g.wait_ge(sem["gp"], p.cnt["gp"] + 3)
        g.affine_select(out=idf[:], in_=idf[:], compare_op=ALU.not_equal,
                        fill=1.0, base=0, pattern=[[-1, 128]],
                        channel_multiplier=1).then_inc(sem["gp"], 1)
    p.cnt["gp"] += 4
    gp_done = p.mark("gp")

    @b.tensor
    def _(t):
        wge(t, gp_done)
        wge(t, lm["gus"])
        for a in range(88):
            bi = t.transpose(psf[:, 2 * a:2 * a + 2],
                             gusf[:, a * 128:(a + 1) * 128], idf[0:2, 0:2])
        bi.then_inc(sem["te"], 1)
    p.cnt["te"] += 1
    tm_gus = p.mark("te")

    @b.vector
    def _(v):
        wge(v, tm_gus)
        v.tensor_copy(gusT[:], psf[:, 0:176]).then_inc(sem["ve"], 1)
    p.cnt["ve"] += 1
    gusT_done = p.mark("ve")

    @b.tensor
    def _(t):
        wge(t, gusT_done)
        wge(t, lm["dns"])
        for a in range(16):
            bi = t.transpose(psf[:, 6 * a:6 * a + 6],
                             dnsf[:, a * 128:(a + 1) * 128], idf[0:6, 0:6])
        bi.then_inc(sem["te"], 1)
    p.cnt["te"] += 1
    tm_dns = p.mark("te")

    @b.vector
    def _(v):
        wge(v, tm_dns)
        v.tensor_copy(dnsT[:], psf[:, 0:96]).then_inc(sem["ve"], 1)
    p.cnt["ve"] += 1

    def emit_dq_planes(nwords, na, wper, scT, spere, deq, hl_of_w, lm_key):
        wsub = wper // spere
        VE0 = p.cnt["ve"]

        @b.vector
        def _(v):
            v.wait_ge(sem["ve"], VE0)
            srcS = apx(scT).rearrange("p (a s) -> p a s", a=na, s=spere)\
                .broadcast_to([128, na, spere, wsub])
            dstS = sful[:, 0:nwords].rearrange("p (a s w) -> p a s w",
                                               a=na, s=spere, w=wsub)
            v.tensor_copy(dstS, srcS).then_inc(sem["ve"], 1)
        p.cnt["ve"] += 1

        for j in range(8):
            VE = p.cnt["ve"]
            SC = p.cnt["sc"]

            @b.vector
            def _(v):
                wge(v, lm[lm_key])
                v.wait_ge(sem["ve"], VE)
                v.tensor_scalar(iscr[:, 0:nwords], pk[:, 0:nwords], 4 * j, 7,
                                ALU.logical_shift_right, ALU.bitwise_and)\
                    .then_inc(sem["ve"], 1)
                v.wait_ge(sem["ve"], VE + 1)
                v.tensor_copy(kf[:, 0:nwords], iscr[:, 0:nwords])\
                    .then_inc(sem["ve"], 1)
                v.wait_ge(sem["ve"], VE + 2)
                v.tensor_scalar(iscr[:, 0:nwords], pk[:, 0:nwords], 4 * j, 8,
                                ALU.logical_shift_right, ALU.bitwise_and)\
                    .then_inc(sem["ve"], 1)
                v.wait_ge(sem["ve"], VE + 3)
                v.tensor_copy(sf[:, 0:nwords], iscr[:, 0:nwords])\
                    .then_inc(sem["ve"], 1)
                v.wait_ge(sem["ve"], VE + 4)
                v.tensor_scalar(m1[:, 0:nwords], sf[:, 0:nwords], -0.25, 1.0,
                                ALU.mult, ALU.add).then_inc(sem["ve"], 1)
                v.wait_ge(sem["ve"], VE + 5)
                v.wait_ge(sem["sc"], SC + 2)
                v.scalar_tensor_tensor(sf[:, 0:nwords], kf[:, 0:nwords], 0.5,
                                       r4[:, 0:nwords], op0=ALU.mult,
                                       op1=ALU.add).then_inc(sem["ve"], 1)
                v.wait_ge(sem["ve"], VE + 6)
                v.tensor_add(kf[:, 0:nwords], sf[:, 0:nwords], r6[:, 0:nwords])\
                    .then_inc(sem["ve"], 1)
                v.wait_ge(sem["ve"], VE + 7)
                v.tensor_mul(r4[:, 0:nwords], kf[:, 0:nwords], m1[:, 0:nwords])\
                    .then_inc(sem["ve"], 1)
                v.wait_ge(sem["ve"], VE + 8)
                dstp = apx(deq).rearrange("p (a w) -> p a w", a=na, w=8 * wper)[
                    :, :, hl_of_w(j)::8]
                srcp = r4[:, 0:nwords].rearrange("p (a w) -> p a w", a=na, w=wper)
                srcs = sful[:, 0:nwords].rearrange("p (a w) -> p a w", a=na, w=wper)
                v.tensor_tensor(dstp, srcp, srcs, op=ALU.mult)\
                    .then_inc(sem["ve"], 1)
            p.cnt["ve"] += 9

            @b.scalar
            def _(s):
                s.wait_ge(sem["ve"], VE + 2)
                if j > 0:
                    s.wait_ge(sem["ve"], VE)
                wge(s, gp_done)
                s.activation(r4[:, 0:nwords], kf[:, 0:nwords], AF.Relu,
                             scale=0.5, bias=cm2[:]).then_inc(sem["sc"], 1)
                s.activation(r6[:, 0:nwords], kf[:, 0:nwords], AF.Relu,
                             bias=cm6[:]).then_inc(sem["sc"], 1)
            p.cnt["sc"] += 2

    emit_dq_planes(1408, 44, 32, gusT[:, 0:88],
                   2, deq_gu[:, 0:44 * 256], lambda j: j, "gu0")
    gu0_done = p.mark("ve")

    @b.gpsimd
    def _(g):
        wge(g, gu0_done)
        bi = g.dma_start(
            out=pk[:, 0:1408].rearrange("p (a w) -> p a w", a=44, w=32),
            in_=gup[5632:11264, :].rearrange("(a p) w -> p a w", p=128))
        lm["gu1"] = p.inc(bi, "dqpk", 16)
    emit_dq_planes(1408, 44, 32, gusT[:, 88:176],
                   2, deq_gu[:, 44 * 256:88 * 256], lambda j: j, "gu1")
    guplanes_done = p.mark("ve")

    @b.gpsimd
    def _(g):
        wge(g, guplanes_done)
        bi = g.dma_start(out=pk[:, 0:1536].rearrange("p (a w) -> p a w",
                                                     a=16, w=96),
                         in_=dnp.rearrange("(a p) w -> p a w", p=128))
        lm["dn"] = p.inc(bi, "dqpk2", 16)
    emit_dq_planes(1536, 16, 96, dnsT, 6, deq_dn, lambda j: j, "dn")

    def emit_store_T(deq, ntile_a, per_a, nsub, out_hbm, stsem, base_mark,
                     guard_marks=()):
        TE = p.cnt["te"]
        VE = p.cnt["ve"]
        ssl = [f"{stsem}a", f"{stsem}b"]
        for s_ in ssl:
            p.sem(s_)
        ST0 = [p.cnt[s_] for s_ in ssl]
        nbatch = ntile_a // 4
        ops = []
        for sub in range(nsub):
            for ba in range(nbatch):
                ops.append((sub, ba))

        @b.tensor
        def _(t):
            wge(t, base_mark)
            for n_, (sub, ba) in enumerate(ops):
                if n_ >= 2:
                    t.wait_ge(sem["ve"], VE + (n_ - 2) + 1)
                for q in range(4):
                    a = 4 * ba + q
                    bi = t.transpose(
                        pb[n_ % 2][:, q * 128:(q + 1) * 128],
                        apx(deq)[:, a * per_a + sub * 128: a * per_a + sub * 128 + 128],
                        consts[0][:])
                bi.then_inc(sem["te"], 1)
        p.cnt["te"] += len(ops)

        @b.vector
        def _(v):
            for n_, (sub, ba) in enumerate(ops):
                v.wait_ge(sem["te"], TE + n_ + 1)
                if n_ >= 2:
                    v.wait_ge(p.sems[ssl[n_ % 2]], ST0[n_ % 2] + 16 * (n_ // 2))
                else:
                    for mk in guard_marks:
                        wge(v, mk)
                v.tensor_copy(stg[n_ % 2][:], pb[n_ % 2][:]).then_inc(sem["ve"], 1)
        p.cnt["ve"] += len(ops)

        @b.sync
        def _(sync):
            for n_, (sub, ba) in enumerate(ops):
                sync.wait_ge(sem["ve"], VE + n_ + 1)
                bi = sync.dma_start(
                    out=out_hbm[sub * 128:(sub + 1) * 128,
                                ba * 512:(ba + 1) * 512],
                    in_=stg[n_ % 2][:])
                p.inc(bi, ssl[n_ % 2], 16)

    emit_store_T(deq_gu, 88, 256, 2, guT, "dqst0", p.mark("ve"))
    emit_store_T(deq_dn, 16, 768, 6, dnT, "dqst1", p.mark("ve"),
                 guard_marks=[p.mark("dqst0a"), p.mark("dqst0b")])
    return [p.mark("dqst0a"), p.mark("dqst0b"),
            p.mark("dqst1a"), p.mark("dqst1b")]




def build_dq():
    p = Prog("dq")
    nc = p.nc
    gup = nc.dram_tensor("gup_sh", [11264, 32], I32, kind="ExternalInput").ap()
    gus = nc.dram_tensor("gus_sh", [2, 11264], F16, kind="ExternalInput").ap()
    dnp = nc.dram_tensor("dnp_sh", [2048, 96], I32, kind="ExternalInput").ap()
    dns = nc.dram_tensor("dns_sh", [6, 2048], F16, kind="ExternalInput").ap()
    guT = nc.dram_tensor("guT_sh", [256, 11264], BF16, kind="ExternalOutput").ap()
    dnT = nc.dram_tensor("dnT_sh", [768, 2048], BF16, kind="ExternalOutput").ap()
    p.start()
    consts = emit_consts(p)
    mks = emit_dq(p, gup, gus, dnp, dns, guT, dnT, consts)
    emit_final_wait(p, mks)
    return p.finish()


def build_ffn():
    p = Prog("ffn")
    nc = p.nc
    x = nc.dram_tensor("x", [T, D], F32, kind="ExternalInput").ap()
    w_guT = nc.dram_tensor("w_guT", [D, 11264], BF16, kind="ExternalInput").ap()
    w_downT = nc.dram_tensor("w_downT", [5632, D], BF16, kind="ExternalInput").ap()
    x2 = nc.dram_tensor("x2", [T, D], F32, kind="ExternalOutput").ap()

    uT = p.sb("uTf", [128, ND * 512], BF16)
    h_actT = p.sb("hactT", [128, 44 * 512], BF16)
    p.start()
    consts = emit_consts(p)
    for tg in range(4):
        ln = emit_ln_uT(p, x, uT, 4 * tg, 4, consts, 512, name="f")
        gum = emit_gu(p, uT, w_guT, h_actT, consts, ln.all_done, name="f")
        emit_mm_ln_resid(
            p, lambda ck, tt: h_actT[:, ck * 512 + tt * 128: ck * 512 + tt * 128 + 128],
            44, w_downT, False, None, x, x2, consts, [gum], 4 * tg, 4, name="d")
    emit_final_wait(p, [p.mark("mlstd0"), p.mark("mlstd1")])
    return p.finish()


def emit_mm_ln_resid(p, lhsT_ap, nck, w_hbm, cast_w, b_hbm, x_hbm, out_hbm,
                     consts, ready_marks, t0, ntl, name="", guard_marks=(),
                     extra_mms=None, pre_marks=()):
    _, ones_row, _, cready = consts
    b = p.block
    sem = p.sems
    ngr = ntl // 2
    NWS = 4
    wtile = [p.sb(f"mlw{name}{i}", [128, 512], BF16) for i in range(NWS)]
    y_s = [p.sb(f"mly{name}{i}", [128, D], F32) for i in range(2)]
    o_s = [p.sb(f"mlo{name}{i}", [128, D], F32) for i in range(2)]
    xt = [p.sb(f"mlx{name}{i}", [128, D], F32) for i in range(2)]
    sums4 = [p.sb(f"mls4{name}{i}", [128, 4], F32) for i in range(2)]
    tot = [p.sb(f"mltot{name}{i}", [128, 1], F32) for i in range(2)]
    ssq = [p.sb(f"mlssq{name}{i}", [128, 1], F32) for i in range(2)]
    mean = [p.sb(f"mlmean{name}{i}", [128, 1], F32) for i in range(2)]
    msq = [p.sb(f"mlmsq{name}{i}", [128, 1], F32) for i in range(2)]
    var = [p.sb(f"mlvar{name}{i}", [128, 1], F32) for i in range(2)]
    std = [p.sb(f"mlstd{name}{i}", [128, 1], F32) for i in range(2)]
    rstd = [p.sb(f"mlrstd{name}{i}", [128, 1], F32) for i in range(2)]
    negmr = [p.sb(f"mlnegmr{name}{i}", [128, 1], F32) for i in range(2)]
    inst = getattr(p, "_ml_inst", 0)
    p._ml_inst = inst + 1
    wl = [f"mll{name}{inst}{i}" for i in range(NWS)]
    for i in range(NWS):
        p.sem(wl[i])
    for i in range(2):
        p.sem(f"mlxl{name}{i}")
        p.sem(f"mlst{name}{i}")
    p.sem(f"mlb{name}")
    brow = p.sb(f"mlb{name}", [1, D], BF16) if b_hbm is not None else None
    pt = p.pf[0:2]

    TE = p.cnt["te"]
    SC = p.cnt["sc"]
    VE = p.cnt["ve"]
    STB = [p.cnt[f"mlst{name}{i}"] for i in range(2)]
    XLB = [p.cnt[f"mlxl{name}{i}"] for i in range(2)]

    def evac_sc(ttg, fg, i):
        return SC + 12 * ttg + 2 * fg + i + 1

    def sqrt_sc(ttg, i):
        return SC + 12 * ttg + 8 + 2 * i + 1

    def norm_sc(ttg, i):
        return SC + 12 * ttg + 8 + 2 * i + 2

    def ve_tile(ttg, i, k):
        return VE + 18 * ttg + 9 * i + k

    wmark = []
    bmark = []
    xmark = []

    @b.gpsimd
    def _(g):
        for mk in pre_marks:
            wge(g, mk)
        if b_hbm is not None:
            g.wait_ge(sem["te"], TE)
            bi = g.dma_start(out=brow[:], in_=b_hbm[:, 0:D])
            bmark.append(p.inc(bi, f"mlb{name}", 16))
        for ttg in range(ngr):
            for fg in range(4):
                for ck in range(nck):
                    L = (ttg * 4 + fg) * nck + ck
                    if L >= NWS:
                        g.wait_ge(sem["te"], TE + L - (NWS - 1))
                    else:
                        g.wait_ge(sem["te"], TE)
                    bi = g.dma_start(out=wtile[L % NWS][:],
                                     in_=w_hbm[ck * 128:(ck + 1) * 128,
                                               fg * 512:(fg + 1) * 512])
                    wmark.append(p.inc(bi, wl[L % NWS], 16))

    @b.tensor
    def _(t):
        wge(t, cready)
        for mk in ready_marks:
            wge(t, mk)
        if bmark:
            wge(t, bmark[0])
        for ttg in range(ngr):
            for fg in range(4):
                for ck in range(nck):
                    L = (ttg * 4 + fg) * nck + ck
                    wge(t, wmark[L])
                    for i in range(2):
                        tt = 2 * ttg + i
                        if ck == 0:
                            if fg > 0:
                                t.wait_ge(sem["sc"], evac_sc(ttg, fg - 1, i))
                            elif ttg > 0:
                                t.wait_ge(sem["sc"], evac_sc(ttg - 1, 3, i))
                            else:
                                t.wait_ge(sem["sc"], SC)
                            if b_hbm is not None:
                                t.matmul(pt[i][:], ones_row[0:1, 0:128],
                                         brow[0:1, fg * 512:(fg + 1) * 512],
                                         start=True, stop=False)
                        bi = t.matmul(pt[i][:], lhsT_ap(ck, tt), wtile[L % NWS][:],
                                      start=(ck == 0 and b_hbm is None),
                                      stop=(ck == nck - 1 and extra_mms is None))
                        if ck == nck - 1 and extra_mms is not None:
                            bi = extra_mms(t, pt[i], tt, fg)
                    bi.then_inc(sem["te"], 1)
    p.cnt["te"] += ngr * 4 * nck

    @b.scalar
    def _(s):
        for ttg in range(ngr):
            for fg in range(4):
                s.wait_ge(sem["te"], TE + (ttg * 4 + fg + 1) * nck)
                for i in range(2):
                    if ttg >= 1 and fg == 0:
                        s.wait_ge(sem["ve"], ve_tile(ttg - 1, i, 9))
                    elif fg == 0:
                        s.wait_ge(sem["ve"], VE)
                    s.activation(y_s[i][:, fg * 512:(fg + 1) * 512], pt[i][:],
                                 AF.Identity,
                                 accum_out=sums4[i][:, fg:fg + 1])\
                        .then_inc(sem["sc"], 1)
            for i in range(2):
                s.wait_ge(sem["ve"], ve_tile(ttg, i, 6))
                s.activation(std[i][:], var[i][:], AF.Sqrt).then_inc(sem["sc"], 1)
                s.wait_ge(sem["ve"], ve_tile(ttg, i, 8))
                if ttg >= 1:
                    s.wait_ge(p.sems[f"mlst{name}{i}"], STB[i] + 16 * ttg)
                else:
                    s.wait_ge(p.sems[f"mlst{name}{i}"], STB[i])
                s.activation(o_s[i][:], y_s[i][:], AF.Identity,
                             bias=negmr[i][:], scale=rstd[i][:])\
                    .then_inc(sem["sc"], 1)
    p.cnt["sc"] += ngr * 12

    @b.vector
    def _(v):
        for ttg in range(ngr):
            for i in range(2):
                tt = 2 * ttg + i
                v.wait_ge(sem["sc"], evac_sc(ttg, 3, i))
                v.tensor_reduce(tot[i][:], sums4[i][:], axis=AX.X, op=ALU.add)\
                    .then_inc(sem["ve"], 1)
                if ttg >= 1:
                    v.wait_ge(p.sems[f"mlst{name}{i}"], STB[i] + 16 * ttg)
                else:
                    v.wait_ge(p.sems[f"mlst{name}{i}"], STB[i])
                v.scalar_tensor_tensor(o_s[i][:], y_s[i][:], 0.0, y_s[i][:],
                                       op0=ALU.add, op1=ALU.mult,
                                       accum_out=ssq[i][:]).then_inc(sem["ve"], 1)
                v.wait_ge(sem["ve"], ve_tile(ttg, i, 2))
                v.tensor_scalar_mul(mean[i][:], tot[i][:], 1.0 / D)\
                    .then_inc(sem["ve"], 1)
                v.wait_ge(sem["ve"], ve_tile(ttg, i, 3))
                v.scalar_tensor_tensor(msq[i][:], mean[i][:], 0.0, mean[i][:],
                                       op0=ALU.add, op1=ALU.mult)\
                    .then_inc(sem["ve"], 1)
                v.wait_ge(sem["ve"], ve_tile(ttg, i, 4))
                v.scalar_tensor_tensor(var[i][:], ssq[i][:], 1.0 / D, msq[i][:],
                                       op0=ALU.mult, op1=ALU.subtract)\
                    .then_inc(sem["ve"], 1)
                v.wait_ge(sem["ve"], ve_tile(ttg, i, 5))
                v.tensor_scalar_add(var[i][:], var[i][:], EPS)\
                    .then_inc(sem["ve"], 1)
                v.wait_ge(sem["sc"], sqrt_sc(ttg, i))
                v.reciprocal(rstd[i][:], std[i][:]).then_inc(sem["ve"], 1)
                v.wait_ge(sem["ve"], ve_tile(ttg, i, 7))
                v.scalar_tensor_tensor(negmr[i][:], mean[i][:], -1.0, rstd[i][:],
                                       op0=ALU.mult, op1=ALU.mult)\
                    .then_inc(sem["ve"], 1)
                v.wait_ge(sem["sc"], norm_sc(ttg, i))
                v.wait_ge(p.sems[f"mlxl{name}{i}"], XLB[i] + 16 * (ttg + 1))
                v.tensor_add(o_s[i][:], o_s[i][:], xt[i][:]).then_inc(sem["ve"], 1)
    p.cnt["ve"] += ngr * 18

    @b.sync
    def _(sync):
        for mk in guard_marks:
            wge(sync, mk)

        def store(ttg):
            for i in range(2):
                tt = 2 * ttg + i
                sync.wait_ge(sem["ve"], ve_tile(ttg, i, 9))
                bi = sync.dma_start(
                    out=out_hbm[(t0 + tt) * 128:(t0 + tt + 1) * 128, :],
                    in_=o_s[i][:])
                p.inc(bi, f"mlst{name}{i}", 16)

        for ttg in range(ngr):
            for i in range(2):
                tt = 2 * ttg + i
                if ttg >= 1:
                    sync.wait_ge(sem["ve"], ve_tile(ttg - 1, i, 9))
                else:
                    sync.wait_ge(sem["ve"], VE)
                bi = sync.dma_start(
                    out=xt[i][:],
                    in_=x_hbm[(t0 + tt) * 128:(t0 + tt + 1) * 128, :])
                xmark.append(p.inc(bi, f"mlxl{name}{i}", 16))
            if ttg >= 1:
                store(ttg - 1)
        store(ngr - 1)
    return [p.mark(f"mlst{name}0"), p.mark(f"mlst{name}1")]


def build_moe():
    p = Prog("moe")
    nc = p.nc
    x = nc.dram_tensor("x", [T, D], F32, kind="ExternalInput").ap()
    Wb = nc.dram_tensor("W_base", [D, D], F32, kind="ExternalInput").ap()
    bb = nc.dram_tensor("b_base", [1, D], F32, kind="ExternalInput").ap()
    As = nc.dram_tensor("A_s", [D, 16], F32, kind="ExternalInput").ap()
    Bs = nc.dram_tensor("B_s", [16, D], F32, kind="ExternalInput").ap()
    Ar = nc.dram_tensor("A_r_flat", [D, 96], F32, kind="ExternalInput").ap()
    Br = nc.dram_tensor("B_r_flat", [96, D], F32, kind="ExternalInput").ap()
    Wr = nc.dram_tensor("W_router", [D, 6], F32, kind="ExternalInput").ap()
    br = nc.dram_tensor("b_router", [1, 6], F32, kind="ExternalInput").ap()
    x4 = nc.dram_tensor("x4", [T, D], F32, kind="ExternalOutput").ap()

    HT = 8
    TSP = HT * 128
    uT = p.sb("uTm", [128, ND * TSP], BF16)
    Wb_bf = p.sb("m_wb", [128, ND * D], BF16)
    Ar_bf = p.sb("m_ar", [128, ND * 96], BF16)
    As_bf = p.sb("m_as", [128, ND * 16], BF16)
    Wr_bf = p.sb("m_wr", [128, ND * 6], BF16)
    Bs_bf = p.sb("m_bs", [16, D], BF16)
    Br_bf = p.sb("m_br", [96, D], BF16)
    bb_bf = p.sb("m_bb", [1, D], BF16)
    br_bf = p.sb("m_brr", [1, 6], BF16)
    E16 = p.sb("m_e16", [6, 96], BF16)
    xaT = p.sb("m_xa", [16, 128], BF16)
    hT = p.sb("m_h", [96, 128], BF16)
    hwT = p.sb("m_hw", [96, 128], BF16)
    wtT_s = p.sb("m_wtt", [6, 128], BF16)
    e_s = p.sb("m_es", [128, 6], F32)
    g_s = p.sb("m_gs", [128, 6], F32)
    g2_s = p.sb("m_g2", [128, 6], F32)
    kp_s = p.sb("m_kp", [128, 6], F32)
    wt_s = p.sb("m_wt", [128, 6], F32)
    wt_bf = p.sb("m_wtb", [128, 6], BF16)
    gmax = p.sb("m_gm", [128, 1], F32)
    negmax = p.sb("m_ngm", [128, 1], F32)
    esum = p.sb("m_esm", [128, 1], F32)
    er = p.sb("m_er", [128, 1], F32)
    m1s = p.sb("m_m1", [128, 1], F32)
    m2s = p.sb("m_m2", [128, 1], F32)
    y_s = [p.sb(f"m_y{i}", [128, D], F32) for i in range(2)]
    o_s = [p.sb(f"m_o{i}", [128, D], F32) for i in range(2)]
    xt = [p.sb(f"m_x{i}", [128, D], F32) for i in range(2)]
    sums4 = [p.sb(f"m_s4{i}", [128, 4], F32) for i in range(2)]
    tot = [p.sb(f"m_tot{i}", [128, 1], F32) for i in range(2)]
    ssq = [p.sb(f"m_ssq{i}", [128, 1], F32) for i in range(2)]
    meanb = [p.sb(f"m_mean{i}", [128, 1], F32) for i in range(2)]
    msqb = [p.sb(f"m_msq{i}", [128, 1], F32) for i in range(2)]
    varb = [p.sb(f"m_var{i}", [128, 1], F32) for i in range(2)]
    stdb = [p.sb(f"m_std{i}", [128, 1], F32) for i in range(2)]
    rstdb = [p.sb(f"m_rstd{i}", [128, 1], F32) for i in range(2)]
    negmrb = [p.sb(f"m_negmr{i}", [128, 1], F32) for i in range(2)]
    for s_ in ("mwl", "mxl0", "mxl1", "mst0", "mst1"):
        p.sem(s_)

    p.start()
    consts = emit_consts(p)
    ident, ones_row, ones128, cready = consts
    b = p.block
    sem = p.sems
    GP = p.cnt["gp"]

    lm = {}

    @b.gpsimd
    def _(g):
        def ld(dst3, src3, key):
            bi = g.dma_start(out=dst3, in_=src3)
            lm[key] = p.inc(bi, "mwl", 16)
        ld(Wb_bf.ap().rearrange("p (dc f) -> p dc f", dc=ND, f=D),
           Wb.rearrange("(dc p) f -> p dc f", p=128), "wb")
        ld(Ar_bf.ap().rearrange("p (dc f) -> p dc f", dc=ND, f=96),
           Ar.rearrange("(dc p) f -> p dc f", p=128), "ar")
        ld(As_bf.ap().rearrange("p (dc f) -> p dc f", dc=ND, f=16),
           As.rearrange("(dc p) f -> p dc f", p=128), "as")
        ld(Wr_bf.ap().rearrange("p (dc f) -> p dc f", dc=ND, f=6),
           Wr.rearrange("(dc p) f -> p dc f", p=128), "wr")
        ld(Bs_bf[:], Bs, "bs")
        ld(Br_bf[:], Br, "br")
        ld(bb_bf[:], bb, "bb")
        ld(br_bf[:], br, "brr")
        g.memset(E16[:], 1.0).then_inc(sem["gp"], 1)
        g.wait_ge(sem["gp"], GP + 1)
        g.affine_select(out=E16[:], in_=E16[:], compare_op=ALU.is_ge, fill=0.0,
                        base=0, pattern=[[1, 96]], channel_multiplier=-16)\
            .then_inc(sem["gp"], 1)
        g.wait_ge(sem["gp"], GP + 2)
        g.affine_select(out=E16[:], in_=E16[:], compare_op=ALU.is_ge, fill=0.0,
                        base=15, pattern=[[-1, 96]], channel_multiplier=16)\
            .then_inc(sem["gp"], 1)
    p.cnt["gp"] += 3
    wready = p.mark("mwl")
    e16_done = p.mark("gp")

    ps_xa = p.pf[4][0:16, 0:128]
    ps_h = p.pf[5][0:96, 0:128]
    ps_g = p.pf[2][0:128, 0:6]
    ps_wb = p.pf[3][0:96, 0:128]
    ps_wt = p.pb[0][0:6, 0:128]
    ps_y = [p.pf[0], p.pf[1]]

    def emit_half(half):
        ln = emit_ln_uT(p, x, uT, HT * half, HT, consts, TSP, name="m")
        TE = p.cnt["te"]
        SC = p.cnt["sc"]
        VE = p.cnt["ve"]
        MSTB = [p.cnt[f"mst{i}"] for i in range(2)]
        MXLB = [p.cnt[f"mxl{i}"] for i in range(2)]

        def te_m(tt, k):
            return TE + 9 * tt + k

        def sc_m(tt, k):
            return SC + 10 * tt + k

        def ve_m(tt, k):
            return VE + 21 * tt + k

        @b.tensor
        def _(t):
            wge(t, cready)
            wge(t, wready)
            wge(t, e16_done)
            for tt in range(HT):
                wge(t, ln.tile_done(tt))
                ut = lambda dc: uT[:, dc * TSP + tt * 128: dc * TSP + tt * 128 + 128]
                if tt >= 1:
                    t.wait_ge(sem["sc"], sc_m(tt - 1, 1))
                else:
                    t.wait_ge(sem["sc"], SC)
                for dc in range(ND):
                    bi = t.matmul(ps_xa, As_bf[:, dc * 16:(dc + 1) * 16], ut(dc),
                                  start=(dc == 0), stop=(dc == ND - 1))
                bi.then_inc(sem["te"], 1)
                if tt >= 1:
                    t.wait_ge(sem["sc"], sc_m(tt - 1, 2))
                for dc in range(ND):
                    bi = t.matmul(ps_h, Ar_bf[:, dc * 96:(dc + 1) * 96], ut(dc),
                                  start=(dc == 0), stop=(dc == ND - 1))
                bi.then_inc(sem["te"], 1)
                if tt >= 1:
                    t.wait_ge(sem["sc"], sc_m(tt - 1, 3))
                    t.wait_ge(sem["ve"], ve_m(tt - 1, 4))
                else:
                    t.wait_ge(sem["ve"], VE)
                t.matmul(ps_g, ones_row[0:1, 0:128], br_bf[0:1, :],
                         start=True, stop=False)
                for dc in range(ND):
                    bi = t.matmul(ps_g, ut(dc), Wr_bf[:, dc * 6:(dc + 1) * 6],
                                  start=False, stop=(dc == ND - 1))
                bi.then_inc(sem["te"], 1)
                t.wait_ge(sem["ve"], ve_m(tt, 11))
                if tt >= 1:
                    t.wait_ge(sem["sc"], sc_m(tt - 1, 4))
                t.transpose(ps_wt, wt_bf[:], ident[:]).then_inc(sem["te"], 1)
                t.wait_ge(sem["sc"], sc_m(tt, 4))
                if tt >= 1:
                    t.wait_ge(sem["ve"], ve_m(tt - 1, 12))
                t.matmul(ps_wb, E16[:], wtT_s[:], start=True, stop=True)\
                    .then_inc(sem["te"], 1)
                t.wait_ge(sem["sc"], sc_m(tt, 1))
                t.wait_ge(sem["ve"], ve_m(tt, 12))
                for fg in range(4):
                    if fg >= 1:
                        t.wait_ge(sem["sc"], sc_m(tt, 4 + fg))
                    elif tt >= 2:
                        t.wait_ge(sem["sc"], sc_m(tt - 2, 8))
                    else:
                        t.wait_ge(sem["sc"], SC)
                    t.matmul(ps_y[tt % 2][:], ones_row[0:1, 0:128],
                             bb_bf[0:1, fg * 512:(fg + 1) * 512],
                             start=True, stop=False)
                    for dc in range(ND):
                        t.matmul(ps_y[tt % 2][:], ut(dc),
                                 Wb_bf[:, dc * D + fg * 512: dc * D + fg * 512 + 512],
                                 start=False, stop=False)
                    t.matmul(ps_y[tt % 2][:], xaT[:],
                             Bs_bf[:, fg * 512:(fg + 1) * 512],
                             start=False, stop=False)
                    bi = t.matmul(ps_y[tt % 2][:], hwT[:],
                                  Br_bf[:, fg * 512:(fg + 1) * 512],
                                  start=False, stop=True)
                    bi.then_inc(sem["te"], 1)
        p.cnt["te"] += 9 * HT

        @b.scalar
        def _(s):
            for tt in range(HT):
                i = tt % 2
                s.wait_ge(sem["te"], te_m(tt, 1))
                if tt >= 1:
                    s.wait_ge(sem["te"], te_m(tt - 1, 9))
                s.activation(xaT[:], ps_xa, AF.Identity).then_inc(sem["sc"], 1)
                s.wait_ge(sem["te"], te_m(tt, 2))
                if tt >= 1:
                    s.wait_ge(sem["ve"], ve_m(tt - 1, 12))
                s.activation(hT[:], ps_h, AF.Identity).then_inc(sem["sc"], 1)
                s.wait_ge(sem["te"], te_m(tt, 3))
                s.wait_ge(sem["ve"], ve_m(tt, 2))
                if tt >= 1:
                    s.wait_ge(sem["ve"], ve_m(tt - 1, 4))
                s.activation(e_s[:], ps_g, AF.Exp, bias=negmax[:],
                             accum_out=esum[:]).then_inc(sem["sc"], 1)
                s.wait_ge(sem["te"], te_m(tt, 4))
                if tt >= 1:
                    s.wait_ge(sem["te"], te_m(tt - 1, 5))
                s.activation(wtT_s[:], ps_wt, AF.Identity).then_inc(sem["sc"], 1)
                for fg in range(4):
                    s.wait_ge(sem["te"], te_m(tt, 6 + fg))
                    if tt >= 2 and fg == 0:
                        s.wait_ge(sem["ve"], ve_m(tt - 2, 14))
                    elif fg == 0:
                        s.wait_ge(sem["ve"], VE)
                    s.activation(y_s[i][:, fg * 512:(fg + 1) * 512], ps_y[i][:],
                                 AF.Identity, accum_out=sums4[i][:, fg:fg + 1])\
                        .then_inc(sem["sc"], 1)
                s.wait_ge(sem["ve"], ve_m(tt, 18))
                s.activation(stdb[i][:], varb[i][:], AF.Sqrt).then_inc(sem["sc"], 1)
                s.wait_ge(sem["ve"], ve_m(tt, 20))
                if tt >= 2:
                    s.wait_ge(p.sems[f"mst{i}"], MSTB[i] + 16 * (tt // 2))
                else:
                    s.wait_ge(p.sems[f"mst{i}"], MSTB[i])
                s.activation(o_s[i][:], y_s[i][:], AF.Identity,
                             bias=negmrb[i][:], scale=rstdb[i][:])\
                    .then_inc(sem["sc"], 1)
        p.cnt["sc"] += 10 * HT

        @b.vector
        def _(v):
            for tt in range(HT):
                i = tt % 2
                v.wait_ge(sem["te"], te_m(tt, 3))
                if tt >= 1:
                    v.wait_ge(sem["ve"], ve_m(tt - 1, 2))
                else:
                    v.wait_ge(sem["ve"], VE)
                v.reduce_max(gmax[:], ps_g, axis=AX.X).then_inc(sem["ve"], 1)
                v.wait_ge(sem["ve"], ve_m(tt, 1))
                if tt >= 1:
                    v.wait_ge(sem["sc"], sc_m(tt - 1, 3))
                v.tensor_scalar_mul(negmax[:], gmax[:], -1.0).then_inc(sem["ve"], 1)
                v.wait_ge(sem["sc"], sc_m(tt, 3))
                v.reciprocal(er[:], esum[:]).then_inc(sem["ve"], 1)
                v.wait_ge(sem["ve"], ve_m(tt, 3))
                v.tensor_scalar_mul(g_s[:], e_s[:], er[:, 0:1]).then_inc(sem["ve"], 1)
                v.wait_ge(sem["ve"], ve_m(tt, 4))
                v.reduce_max(m1s[:], g_s[:], axis=AX.X).then_inc(sem["ve"], 1)
                v.wait_ge(sem["ve"], ve_m(tt, 5))
                v.tensor_single_scalar(g2_s[:], g_s[:], m1s[:, 0:1], ALU.is_equal)\
                    .then_inc(sem["ve"], 1)
                v.wait_ge(sem["ve"], ve_m(tt, 6))
                v.scalar_tensor_tensor(g2_s[:], g2_s[:], -1e30, g_s[:],
                                       op0=ALU.mult, op1=ALU.add)\
                    .then_inc(sem["ve"], 1)
                v.wait_ge(sem["ve"], ve_m(tt, 7))
                v.reduce_max(m2s[:], g2_s[:], axis=AX.X).then_inc(sem["ve"], 1)
                v.wait_ge(sem["ve"], ve_m(tt, 8))
                v.tensor_single_scalar(kp_s[:], g_s[:], m2s[:, 0:1], ALU.is_ge)\
                    .then_inc(sem["ve"], 1)
                v.wait_ge(sem["ve"], ve_m(tt, 9))
                v.tensor_mul(wt_s[:], g_s[:], kp_s[:]).then_inc(sem["ve"], 1)
                v.wait_ge(sem["ve"], ve_m(tt, 10))
                if tt >= 1:
                    v.wait_ge(sem["te"], te_m(tt - 1, 4))
                else:
                    v.wait_ge(sem["te"], TE)
                v.tensor_copy(wt_bf[:], wt_s[:]).then_inc(sem["ve"], 1)
                v.wait_ge(sem["te"], te_m(tt, 5))
                v.wait_ge(sem["sc"], sc_m(tt, 2))
                if tt >= 1:
                    v.wait_ge(sem["te"], te_m(tt - 1, 9))
                v.tensor_mul(hwT[:], hT[:], ps_wb).then_inc(sem["ve"], 1)
                v.wait_ge(sem["sc"], sc_m(tt, 8))
                v.tensor_reduce(tot[i][:], sums4[i][:], axis=AX.X, op=ALU.add)\
                    .then_inc(sem["ve"], 1)
                if tt >= 2:
                    v.wait_ge(p.sems[f"mst{i}"], MSTB[i] + 16 * (tt // 2))
                else:
                    v.wait_ge(p.sems[f"mst{i}"], MSTB[i])
                v.scalar_tensor_tensor(o_s[i][:], y_s[i][:], 0.0, y_s[i][:],
                                       op0=ALU.add, op1=ALU.mult,
                                       accum_out=ssq[i][:]).then_inc(sem["ve"], 1)
                v.wait_ge(sem["ve"], ve_m(tt, 14))
                v.tensor_scalar_mul(meanb[i][:], tot[i][:], 1.0 / D)\
                    .then_inc(sem["ve"], 1)
                v.wait_ge(sem["ve"], ve_m(tt, 15))
                v.scalar_tensor_tensor(msqb[i][:], meanb[i][:], 0.0, meanb[i][:],
                                       op0=ALU.add, op1=ALU.mult)\
                    .then_inc(sem["ve"], 1)
                v.wait_ge(sem["ve"], ve_m(tt, 16))
                v.scalar_tensor_tensor(varb[i][:], ssq[i][:], 1.0 / D, msqb[i][:],
                                       op0=ALU.mult, op1=ALU.subtract)\
                    .then_inc(sem["ve"], 1)
                v.wait_ge(sem["ve"], ve_m(tt, 17))
                v.tensor_scalar_add(varb[i][:], varb[i][:], EPS)\
                    .then_inc(sem["ve"], 1)
                v.wait_ge(sem["sc"], sc_m(tt, 9))
                v.reciprocal(rstdb[i][:], stdb[i][:]).then_inc(sem["ve"], 1)
                v.wait_ge(sem["ve"], ve_m(tt, 19))
                v.scalar_tensor_tensor(negmrb[i][:], meanb[i][:], -1.0,
                                       rstdb[i][:], op0=ALU.mult, op1=ALU.mult)\
                    .then_inc(sem["ve"], 1)
                v.wait_ge(sem["sc"], sc_m(tt, 10))
                v.wait_ge(p.sems[f"mxl{i}"], MXLB[i] + 16 * (tt // 2 + 1))
                v.tensor_add(o_s[i][:], o_s[i][:], xt[i][:]).then_inc(sem["ve"], 1)
        p.cnt["ve"] += 21 * HT

        @b.sync
        def _(sync):
            for tt in range(HT):
                i = tt % 2
                gt = HT * half + tt
                if tt >= 2:
                    sync.wait_ge(sem["ve"], ve_m(tt - 2, 21))
                else:
                    sync.wait_ge(sem["ve"], VE)
                bi = sync.dma_start(out=xt[i][:],
                                    in_=x[gt * 128:(gt + 1) * 128, :])
                p.inc(bi, f"mxl{i}", 16)
                if tt >= 1:
                    j = (tt - 1) % 2
                    gtp = HT * half + tt - 1
                    sync.wait_ge(sem["ve"], ve_m(tt - 1, 21))
                    bi = sync.dma_start(out=x4[gtp * 128:(gtp + 1) * 128, :],
                                        in_=o_s[j][:])
                    p.inc(bi, f"mst{j}", 16)
            j = (HT - 1) % 2
            gtp = HT * half + HT - 1
            sync.wait_ge(sem["ve"], ve_m(HT - 1, 21))
            bi = sync.dma_start(out=x4[gtp * 128:(gtp + 1) * 128, :],
                                in_=o_s[j][:])
            p.inc(bi, f"mst{j}", 16)

    emit_half(0)
    emit_half(1)
    emit_final_wait(p, [p.mark("mst0"), p.mark("mst1")])
    return p.finish()


def emit_final_wait(p, store_marks):
    b = p.block

    @b.sync
    def _(sync):
        for mk in store_marks:
            wge(sync, mk)



def emit_moe_pre(p, uT, As, Ar, Wr, br, consts, ln, name=""):
    ident, ones_row, ones128, cready = consts
    b = p.block
    sem = p.sems
    As_bf = p.sb(f"mp_as{name}", [128, ND * 16], BF16)
    Ar_bf = p.sb(f"mp_ar{name}", [128, ND * 96], BF16)
    Wr_bf = p.sb(f"mp_wr{name}", [128, ND * 6], BF16)
    br_bf = p.sb(f"mp_brr{name}", [1, 6], BF16)
    E16 = p.sb(f"mp_e16{name}", [6, 96], BF16)
    xaT_all = p.sb(f"mp_xa{name}", [16, T], BF16)
    hwT_all = p.sb(f"mp_hw{name}", [96, T], BF16)
    hT = p.sb(f"mp_h{name}", [96, 128], BF16)
    wtT_s = p.sb(f"mp_wtt{name}", [6, 128], BF16)
    e_s = p.sb(f"mp_es{name}", [128, 6], F32)
    g_s = p.sb(f"mp_gs{name}", [128, 6], F32)
    g2_s = p.sb(f"mp_g2{name}", [128, 6], F32)
    kp_s = p.sb(f"mp_kp{name}", [128, 6], F32)
    wt_s = p.sb(f"mp_wt{name}", [128, 6], F32)
    gmax = p.sb(f"mp_gm{name}", [128, 1], F32)
    negmax = p.sb(f"mp_ngm{name}", [128, 1], F32)
    esum = p.sb(f"mp_esm{name}", [128, 1], F32)
    er = p.sb(f"mp_er{name}", [128, 1], F32)
    m1s = p.sb(f"mp_m1{name}", [128, 1], F32)
    m2s = p.sb(f"mp_m2{name}", [128, 1], F32)
    p.sem(f"mpw{name}")

    GP = p.cnt["gp"]
    lm = []

    @b.gpsimd
    def _(g):
        g.wait_ge(sem["te"], p.cnt["te"])
        def ld(dst3, src3):
            bi = g.dma_start(out=dst3, in_=src3)
            lm.append(p.inc(bi, f"mpw{name}", 16))
        ld(As_bf.ap().rearrange("p (dc f) -> p dc f", dc=ND, f=16),
           As.rearrange("(dc p) f -> p dc f", p=128))
        ld(Ar_bf.ap().rearrange("p (dc f) -> p dc f", dc=ND, f=96),
           Ar.rearrange("(dc p) f -> p dc f", p=128))
        ld(Wr_bf.ap().rearrange("p (dc f) -> p dc f", dc=ND, f=6),
           Wr.rearrange("(dc p) f -> p dc f", p=128))
        ld(br_bf[:], br)
        g.memset(E16[:], 1.0).then_inc(sem["gp"], 1)
        g.wait_ge(sem["gp"], GP + 1)
        g.affine_select(out=E16[:], in_=E16[:], compare_op=ALU.is_ge, fill=0.0,
                        base=0, pattern=[[1, 96]], channel_multiplier=-16)\
            .then_inc(sem["gp"], 1)
        g.wait_ge(sem["gp"], GP + 2)
        g.affine_select(out=E16[:], in_=E16[:], compare_op=ALU.is_ge, fill=0.0,
                        base=15, pattern=[[-1, 96]], channel_multiplier=16)\
            .then_inc(sem["gp"], 1)
    p.cnt["gp"] += 3
    e16_done = p.mark("gp")

    TE = p.cnt["te"]
    SC = p.cnt["sc"]
    VE = p.cnt["ve"]

    def te_m(tt, k):
        return TE + 5 * tt + k

    def sc_m(tt, k):
        return SC + 4 * tt + k

    def ve_m(tt, k):
        return VE + 11 * tt + k

    ps_xa = p.pf[4][0:16, 0:128]
    ps_h = p.pf[5][0:96, 0:128]
    ps_g = p.pf[2][0:128, 0:6]
    ps_wb = p.pf[3][0:96, 0:128]
    ps_wt = p.pf[3][0:6, 0:128]
    wtT_f = p.sb(f"mp_wttf{name}", [6, 128], F32)

    @b.tensor
    def _(t):
        wge(t, cready)
        wge(t, e16_done)
        for mk in lm:
            wge(t, mk)
        for tt in range(NT):
            wge(t, ln.tile_done(tt))
            ut = lambda dc: apx(uT)[:, dc * T + tt * 128: dc * T + tt * 128 + 128]
            if tt >= 1:
                t.wait_ge(sem["sc"], sc_m(tt - 1, 1))
            else:
                t.wait_ge(sem["sc"], SC)
                t.wait_ge(sem["ve"], VE)
            for dc in range(ND):
                bi = t.matmul(ps_xa, As_bf[:, dc * 16:(dc + 1) * 16], ut(dc),
                              start=(dc == 0), stop=(dc == ND - 1))
            bi.then_inc(sem["te"], 1)
            if tt >= 1:
                t.wait_ge(sem["sc"], sc_m(tt - 1, 2))
            for dc in range(ND):
                bi = t.matmul(ps_h, Ar_bf[:, dc * 96:(dc + 1) * 96], ut(dc),
                              start=(dc == 0), stop=(dc == ND - 1))
            bi.then_inc(sem["te"], 1)
            if tt >= 1:
                t.wait_ge(sem["sc"], sc_m(tt - 1, 3))
                t.wait_ge(sem["ve"], ve_m(tt - 1, 4))
            t.matmul(ps_g, ones_row[0:1, 0:128], br_bf[0:1, :],
                     start=True, stop=False)
            for dc in range(ND):
                bi = t.matmul(ps_g, ut(dc), Wr_bf[:, dc * 6:(dc + 1) * 6],
                              start=False, stop=(dc == ND - 1))
            bi.then_inc(sem["te"], 1)
            t.wait_ge(sem["ve"], ve_m(tt, 10))
            if tt >= 1:
                t.wait_ge(sem["sc"], sc_m(tt - 1, 4))
                t.wait_ge(sem["ve"], ve_m(tt - 1, 11))
            t.transpose(ps_wt, wt_s[:], p._identf[:]).then_inc(sem["te"], 1)
            t.wait_ge(sem["sc"], sc_m(tt, 4))
            t.matmul(ps_wb, E16[:], wtT_s[:], start=True, stop=True)\
                .then_inc(sem["te"], 1)
    p.cnt["te"] += 5 * NT

    @b.scalar
    def _(s):
        for tt in range(NT):
            s.wait_ge(sem["te"], te_m(tt, 1))
            s.activation(xaT_all[:, tt * 128:(tt + 1) * 128], ps_xa, AF.Identity)\
                .then_inc(sem["sc"], 1)
            s.wait_ge(sem["te"], te_m(tt, 2))
            if tt >= 1:
                s.wait_ge(sem["ve"], ve_m(tt - 1, 11))
            s.activation(hT[:], ps_h, AF.Identity).then_inc(sem["sc"], 1)
            s.wait_ge(sem["te"], te_m(tt, 3))
            s.wait_ge(sem["ve"], ve_m(tt, 2))
            if tt >= 1:
                s.wait_ge(sem["ve"], ve_m(tt - 1, 4))
            s.activation(e_s[:], ps_g, AF.Exp, bias=negmax[:],
                         accum_out=esum[:]).then_inc(sem["sc"], 1)
            s.wait_ge(sem["te"], te_m(tt, 4))
            if tt >= 1:
                s.wait_ge(sem["te"], te_m(tt - 1, 5))
            s.activation(wtT_s[:], ps_wt, AF.Identity).then_inc(sem["sc"], 1)
    p.cnt["sc"] += 4 * NT

    @b.vector
    def _(v):
        for tt in range(NT):
            v.wait_ge(sem["te"], te_m(tt, 3))
            if tt >= 1:
                v.wait_ge(sem["ve"], ve_m(tt - 1, 2))
            else:
                v.wait_ge(sem["ve"], VE)
            v.reduce_max(gmax[:], ps_g, axis=AX.X).then_inc(sem["ve"], 1)
            v.wait_ge(sem["ve"], ve_m(tt, 1))
            if tt >= 1:
                v.wait_ge(sem["sc"], sc_m(tt - 1, 3))
            v.tensor_scalar_mul(negmax[:], gmax[:], -1.0).then_inc(sem["ve"], 1)
            v.wait_ge(sem["sc"], sc_m(tt, 3))
            v.reciprocal(er[:], esum[:]).then_inc(sem["ve"], 1)
            v.wait_ge(sem["ve"], ve_m(tt, 3))
            v.tensor_scalar_mul(g_s[:], e_s[:], er[:, 0:1]).then_inc(sem["ve"], 1)
            v.wait_ge(sem["ve"], ve_m(tt, 4))
            v.reduce_max(m1s[:], g_s[:], axis=AX.X).then_inc(sem["ve"], 1)
            v.wait_ge(sem["ve"], ve_m(tt, 5))
            v.tensor_single_scalar(g2_s[:], g_s[:], m1s[:, 0:1], ALU.is_equal)\
                .then_inc(sem["ve"], 1)
            v.wait_ge(sem["ve"], ve_m(tt, 6))
            v.scalar_tensor_tensor(g2_s[:], g2_s[:], -1e30, g_s[:],
                                   op0=ALU.mult, op1=ALU.add).then_inc(sem["ve"], 1)
            v.wait_ge(sem["ve"], ve_m(tt, 7))
            v.reduce_max(m2s[:], g2_s[:], axis=AX.X).then_inc(sem["ve"], 1)
            v.wait_ge(sem["ve"], ve_m(tt, 8))
            v.tensor_single_scalar(kp_s[:], g_s[:], m2s[:, 0:1], ALU.is_ge)\
                .then_inc(sem["ve"], 1)
            v.wait_ge(sem["ve"], ve_m(tt, 9))
            if tt >= 1:
                v.wait_ge(sem["te"], te_m(tt - 1, 4))
            v.tensor_mul(wt_s[:], g_s[:], kp_s[:]).then_inc(sem["ve"], 1)
            v.wait_ge(sem["te"], te_m(tt, 5))
            v.wait_ge(sem["sc"], sc_m(tt, 2))
            v.tensor_mul(hwT_all[:, tt * 128:(tt + 1) * 128], hT[:], ps_wb)\
                .then_inc(sem["ve"], 1)
    p.cnt["ve"] += 11 * NT

    def xa_mark(tt):
        return (sem["sc"], sc_m(tt, 1))

    def hw_mark(tt):
        return (sem["ve"], ve_m(tt, 11))

    return xaT_all, hwT_all, xa_mark, hw_mark


def build_qkv(sep_weights):
    p = Prog("qkv")
    nc = p.nc
    x = nc.dram_tensor("x", [T, D], F32, kind="ExternalInput").ap()
    if sep_weights:
        wq = nc.dram_tensor("wq", [D, D], F32, kind="ExternalInput").ap()
        wk = nc.dram_tensor("wk", [D, D], F32, kind="ExternalInput").ap()
        wv = nc.dram_tensor("wv", [D, D], F32, kind="ExternalInput").ap()
        bq = None
    else:
        wq = nc.dram_tensor("w_qkv", [D, 3 * D], F32, kind="ExternalInput").ap()
        wk = wv = wq
        bq = nc.dram_tensor("b_qkv", [1, 3 * D], F32, kind="ExternalInput").ap()
    qT = nc.dram_tensor("qT", [D, T], BF16, kind="ExternalOutput").ap()
    kT = nc.dram_tensor("kT", [D, T], BF16, kind="ExternalOutput").ap()
    v = nc.dram_tensor("v", [T, D], BF16, kind="ExternalOutput").ap()

    uT = p.sb("uT", [128, ND * T], BF16)
    p.start()
    consts = emit_consts(p)
    ln = emit_ln_uT(p, x, uT, 0, NT, consts, T)
    koff = 0 if sep_weights else D
    voff = 0 if sep_weights else 2 * D
    lnm_fm = lambda tg: ln.tile_done(4 * tg + 3)
    mk = []
    mk += emit_proj_fm(p, uT, wq, 0, bq, 0, qT, consts, lnm_fm, ND, T, name="q")
    mk += emit_proj_fm(p, uT, wk, koff, bq, D, kT, consts, lnm_fm, ND, T, name="k")
    mk += emit_proj_tm(p, uT, wv, voff, bq, 2 * D, v, consts, ln.tile_done, T, name="v")
    emit_final_wait(p, mk)
    return p.finish()


def build_fused(debug=False, upto="all", skip_dq=False):
    p = Prog("mk")
    nc = p.nc
    f = lambda n, s, d=F32: nc.dram_tensor(n, s, d, kind="ExternalInput").ap()
    x0 = f("x", [T, D])
    w_qkv = f("w_qkv", [D, 3 * D])
    b_qkv = f("b_qkv", [1, 3 * D])
    w_out0 = f("w_out0", [D, D])
    b_out0 = f("b_out0", [1, D])
    gup = f("gup_sh", [11264, 32], I32)
    gus = f("gus_sh", [2, 11264], F16)
    dnp = f("dnp_sh", [2048, 96], I32)
    dns = f("dns_sh", [6, 2048], F16)
    wq1 = f("wq1", [D, D])
    wk1 = f("wk1", [D, D])
    wv1 = f("wv1", [D, D])
    wo1 = f("wo1", [D, D])
    Wb = f("W_base", [D, D])
    bb = f("b_base", [1, D])
    As = f("A_s", [D, 16])
    Bs = f("B_s", [16, D])
    Ar = f("A_r_flat", [D, 96])
    Br = f("B_r_flat", [96, D])
    Wr = f("W_router", [D, 6])
    br = f("b_router", [1, 6])
    alpha = f("alpha", [1, 2], BF16)
    out = nc.dram_tensor("out", [T, D], F32, kind="ExternalOutput").ap()

    ik = "ExternalOutput" if debug else "Internal"
    g = lambda n, s, d: nc.dram_tensor(n, s, d, kind="Internal").ap()
    gd = lambda n, s, d: nc.dram_tensor(n, s, d, kind=ik).ap()
    x1 = gd("x1d", [T, D], F32)
    x2 = gd("x2d", [T, D], F32)
    x3 = gd("x3d", [T, D], F32)
    qTd = gd("qTd", [D, T], BF16)
    kv_own = g("kv_own", [2 * T, D], BF16)
    kv_all = g("kv_all", [4 * T, D], BF16)
    guT_sh = g("guT_shd", [256, 11264], BF16)
    guT_all = g("guT_alld", [2048, 11264], BF16)
    dnT_sh = g("dnT_shd", [768, 2048], BF16)
    dnT_all = g("dnT_alld", [6144, 2048], BF16)

    uT = p.sb("uT", [128, ND * T], BF16)
    atk = [p.sb(f"at_k{i}", [128, S], BF16) for i in range(2)]
    atv = [p.sb(f"at_v{i}", [128, 32 * 128], BF16) for i in range(2)]
    atq = [p.sb(f"at_q{i}", [128, T], BF16) for i in range(2)]
    ate = p.sb("at_e", [128, 16 * 512], BF16)
    mly = [p.sb(f"mlyml{i}", [128, D], F32) for i in range(2)]
    mlo = [p.sb(f"mloml{i}", [128, D], F32) for i in range(2)]
    mlx = [p.sb(f"mlxml{i}", [128, D], F32) for i in range(2)]
    uT_a = p.addr("uT")
    atk_a = p.addr("at_k0")
    assert p.addr("at_k1") == atk_a + 8192
    assert p.addr("at_v0") == atk_a + 16384
    assert p.addr("at_v1") == atk_a + 24576
    assert p.addr("at_q0") == atk_a + 32768
    mly_a = p.addr("mlyml0")
    assert p.addr("mlxml1") == mly_a + 5 * 8192
    ate_a = p.addr("at_e")
    p.sb("dq_kf", [128, 1536], F32, at=atk_a)
    p.sb("dq_sf", [128, 1536], F32, at=atk_a + 6144)
    p.sb("dq_r4", [128, 1536], F32, at=atk_a + 12288)
    p.sb("dq_r6", [128, 1536], F32, at=atk_a + 18432)
    p.sb("dq_m1", [128, 1536], F32, at=atk_a + 24576)
    p.sb("dq_s", [128, 1536], F32, at=atk_a + 30720)
    p.sb("dq_pk", [128, 1536], I32, at=ate_a)
    p.sb("dq_i", [128, 1536], I32, at=ate_a + 6144)
    p.sb("dq_dgu", [128, 88 * 256], BF16, at=uT_a)
    p.sb("dq_ddn", [128, 16 * 768], BF16, at=mly_a)
    p.sb("lnxtg0", [128, D], F32, at=mly_a)
    p.sb("lnxtg1", [128, D], F32, at=mly_a + 8192)
    p.sb("lnsqg", [128, D], F32, at=mly_a + 16384)
    p.sb("lntrashg", [128, D], BF16, at=mly_a + 24576)
    p.sb("lnug0", [128, D], BF16, at=mly_a + 28672)
    p.sb("lnug1", [128, D], BF16, at=mly_a + 32768)
    p.sb("pfwfm0", [128, ND * 128], BF16, at=ate_a)
    p.sb("pfwfm1", [128, ND * 128], BF16, at=ate_a + 4096)
    p.sb("pfsfm0", [128, 512], BF16, at=ate_a + 8192)
    p.sb("pfsfm1", [128, 512], BF16, at=ate_a + 9216)
    p.sb("ptwtm0", [128, 512], BF16, at=ate_a + 10240)
    p.sb("ptwtm1", [128, 512], BF16, at=ate_a + 11264)
    p.sb("ptstm0", [128, 512], BF16, at=ate_a + 12288)
    p.sb("ptstm1", [128, 512], BF16, at=ate_a + 13312)
    p.sb("pfbfm", [1, ND * 128], BF16, at=atk_a + 32768)
    p.sb("ptbtm", [1, 4 * 512], BF16, at=atk_a + 36864)
    p.sb("guwgf0", [128, ND * 128], BF16, at=atk_a)
    p.sb("guwgf1", [128, ND * 128], BF16, at=atk_a + 4096)
    p.sb("guwuf0", [128, ND * 128], BF16, at=atk_a + 8192)
    p.sb("guwuf1", [128, ND * 128], BF16, at=atk_a + 12288)
    p.sb("gusilf0", [128, 512], BF16, at=atk_a + 16384)
    p.sb("gusilf1", [128, 512], BF16, at=atk_a + 17408)
    hactT = p.sb("hactTf", [128, 44 * 512], BF16, at=uT_a + ND * 512 * 2)
    p.sb("mp_as", [128, ND * 16], BF16, at=atk_a)
    p.sb("mp_ar", [128, ND * 96], BF16, at=atk_a + 1024)
    p.sb("mp_wr", [128, ND * 6], BF16, at=atk_a + 4096)
    p.sb("mp_xa", [16, T], BF16, at=atk_a + 8192)
    p.sb("mp_hw", [96, T], BF16, at=atk_a + 12288)

    p.start()
    consts = emit_consts(p)
    p.sem("cc")

    if skip_dq:
        dq_mks = []
    else:
        dq_mks = emit_dq(p, gup, gus, dnp, dns, guT_sh, dnT_sh, consts)

    ln0 = emit_ln_uT(p, x0, uT, 0, NT, consts, T, name="g",
                     guard_marks=dq_mks)
    lnm_fm = lambda tg: ln0.tile_done(4 * tg + 3)
    mkk = emit_proj_fm(p, uT, w_qkv, D, b_qkv, D, kv_own[0:T, :], consts,
                       lnm_fm, ND, T, name="fm")
    mkv = emit_proj_tm(p, uT, w_qkv, 2 * D, b_qkv, 2 * D, kv_own[T:2 * T, :],
                       consts, ln0.tile_done, T, name="tm")
    mkq = emit_proj_fm(p, uT, w_qkv, 0, b_qkv, 0, qTd, consts, lnm_fm,
                       ND, T, name="fm")

    @p.block.gpsimd
    def _(gq):
        for mk in mkk + mkv:
            wge(gq, mk)
        gq.collective_compute(
            "AllGather", ALU.bypass,
            replica_groups=[[0, 1], [2, 3], [4, 5], [6, 7]],
            ins=[kv_own.opt()], outs=[kv_all.opt()]).then_inc(p.sems["cc"], 1)
    p.cnt["cc"] += 1
    cc0 = p.mark("cc")

    if not skip_dq:
        @p.block.gpsimd
        def _(gq):
            for mk in dq_mks:
                wge(gq, mk)
            gq.collective_compute(
                "AllGather", ALU.bypass, replica_groups=[list(range(8))],
                ins=[guT_sh.opt()], outs=[guT_all.opt()]).then_inc(p.sems["cc"], 1)
            gq.collective_compute(
                "AllGather", ALU.bypass, replica_groups=[list(range(8))],
                ins=[dnT_sh.opt()], outs=[dnT_all.opt()]).then_inc(p.sems["cc"], 1)
        p.cnt["cc"] += 2
    ccdq = p.mark("cc")

    w_out0_bf = g("w_out0_bf", [D, D], BF16)
    wo1_bf = g("wo1_bf", [D, D], BF16)
    Wb_bf = g("Wb_bf", [D, D], BF16)
    p.sem("wcast")
    wcm = []

    @p.block.gpsimd
    def _(gq):
        for src, dst in ((w_out0, w_out0_bf), (wo1, wo1_bf), (Wb, Wb_bf)):
            bi = gq.dma_start(out=dst, in_=src)
            wcm.append(p.inc(bi, "wcast", 16))

    if upto == "qkv":
        emit_final_wait(p, mkq + mkk + mkv)
        return p.finish()

    avm0 = emit_attn_core(p, qTd, kv_own, kv_all, alpha, consts, L0_SCALE,
                          False, uT, guard_marks=[cc0] + mkq)
    mko0 = emit_mm_ln_resid(
        p, lambda ck, tt: apx(uT)[:, ck * T + tt * 128: ck * T + tt * 128 + 128],
        ND, w_out0_bf, False, b_out0, x0, x1, consts, [avm0], 0, NT, name="ml",
        guard_marks=[avm0], pre_marks=[wcm[0]])
    if upto == "l0attn":
        emit_final_wait(p, mko0)
        return p.finish()

    for tg in range(4):
        lnf = emit_ln_uT(p, x1, uT[:, 0:ND * 512], 4 * tg, 4, consts, 512,
                         name="g", guard_marks=mko0 if tg == 0 else ())
        gum = emit_gu(p, uT[:, 0:ND * 512], guT_all, hactT, consts,
                      lnf.all_done, name="f",
                      guard_marks=[ccdq] + (mko0 if tg == 0 else []))
        mkf = emit_mm_ln_resid(
            p, lambda ck, tt: apx(hactT)[:, ck * 512 + tt * 128:
                                         ck * 512 + tt * 128 + 128],
            44, dnT_all[0:5632, :], False, None, x1, x2, consts, [gum],
            4 * tg, 4, name="ml", guard_marks=[gum])
    if upto == "ffn":
        emit_final_wait(p, mkf)
        return p.finish()

    ln1 = emit_ln_uT(p, x2, uT, 0, NT, consts, T, name="g", guard_marks=mkf)
    lnm1 = lambda tg: ln1.tile_done(4 * tg + 3)
    mkk1 = emit_proj_fm(p, uT, wk1, 0, None, 0, kv_own[0:T, :], consts,
                        lnm1, ND, T, name="fm")
    mkv1 = emit_proj_tm(p, uT, wv1, 0, None, 0, kv_own[T:2 * T, :],
                        consts, ln1.tile_done, T, name="tm")
    mkq1 = emit_proj_fm(p, uT, wq1, 0, None, 0, qTd, consts, lnm1,
                        ND, T, name="fm")

    @p.block.gpsimd
    def _(gq):
        for mk in mkk1 + mkv1:
            wge(gq, mk)
        gq.collective_compute(
            "AllGather", ALU.bypass,
            replica_groups=[[0, 1], [2, 3], [4, 5], [6, 7]],
            ins=[kv_own.opt()], outs=[kv_all.opt()]).then_inc(p.sems["cc"], 1)
    p.cnt["cc"] += 1
    cc1 = p.mark("cc")

    if upto == "l1qkv":
        emit_final_wait(p, mkq1 + mkk1 + mkv1)
        return p.finish()

    avm1 = emit_attn_core(p, qTd, kv_own, kv_all, alpha, consts, L1_SCALE,
                          True, uT, guard_marks=[cc1] + mkq1)
    mko1 = emit_mm_ln_resid(
        p, lambda ck, tt: apx(uT)[:, ck * T + tt * 128: ck * T + tt * 128 + 128],
        ND, wo1_bf, False, None, x2, x3, consts, [avm1], 0, NT, name="ml",
        guard_marks=[avm1], pre_marks=[wcm[1]])

    if upto == "l1attn":
        emit_final_wait(p, mko1)
        return p.finish()

    lnm_ = emit_ln_uT(p, x3, uT, 0, NT, consts, T, name="g", guard_marks=mko1)
    xaT_all, hwT_all, xa_mark, hw_mark = emit_moe_pre(
        p, uT, As, Ar, Wr, br, consts, lnm_, name="")
    Bs_bf = p.sb("fz_bs", [16, D], BF16, at=p.addr("at_v0"))
    Br_bf = p.sb("fz_br", [96, D], BF16, at=p.addr("at_v0") + 4096)
    p.sem("fzw")
    fzm = []

    @p.block.gpsimd
    def _(gq):
        gq.wait_ge(p.sems["te"], p.cnt["te"])
        bi = gq.dma_start(out=Bs_bf[:], in_=Bs)
        fzm.append(p.inc(bi, "fzw", 16))
        bi = gq.dma_start(out=Br_bf[:], in_=Br)
        fzm.append(p.inc(bi, "fzw", 16))

    def extra(t, psum, tt, fg):
        for mk in fzm:
            wge(t, mk)
        wge(t, xa_mark(tt))
        wge(t, hw_mark(tt))
        t.matmul(psum[:], xaT_all[:, tt * 128:(tt + 1) * 128],
                 Bs_bf[:, fg * 512:(fg + 1) * 512], start=False, stop=False)
        return t.matmul(psum[:], hwT_all[:, tt * 128:(tt + 1) * 128],
                        Br_bf[:, fg * 512:(fg + 1) * 512],
                        start=False, stop=True)

    mkfin = emit_mm_ln_resid(
        p, lambda ck, tt: apx(uT)[:, ck * T + tt * 128: ck * T + tt * 128 + 128],
        ND, Wb_bf, False, bb, x3, out, consts, [hw_mark(NT - 1)], 0, NT,
        name="ml", guard_marks=[hw_mark(0)], extra_mms=extra, pre_marks=[wcm[2]])

    emit_final_wait(p, mkfin)
    return p.finish()


_PROGS = {}


def _prog(key, builder, *args):
    if key not in _PROGS:
        _PROGS[key] = builder(*args)
    return _PROGS[key]


def kernel(**inputs):
    f32 = np.float32
    x = np.ascontiguousarray(inputs["L0_attn_x"], dtype=f32)
    nc = _prog("fused", build_fused)
    gup = inputs["L0_ffn_gate_up_packed"]
    gus = inputs["L0_ffn_gate_up_scales"]
    dnp = inputs["L0_ffn_down_packed"]
    dns = inputs["L0_ffn_down_scales"]
    Ar_flat = np.ascontiguousarray(
        inputs["L1_ffn_A_r"].transpose(1, 0, 2).reshape(D, 96))
    Br_flat = np.ascontiguousarray(inputs["L1_ffn_B_r"].reshape(96, D))
    row = lambda a: np.ascontiguousarray(a.reshape(1, -1), f32)
    maps = []
    for c in range(8):
        dnp_sh = np.zeros((2048, 96), np.int32)
        dns_sh = np.zeros((6, 2048), np.float16)
        w0, w1 = 96 * c, min(96 * (c + 1), 704)
        dnp_sh[:, 0:w1 - w0] = dnp[:, w0:w1]
        g0, g1 = 6 * c, min(6 * (c + 1), 44)
        dns_sh[0:g1 - g0, :] = dns[g0:g1, :]
        maps.append({
            "x": np.ascontiguousarray(x[c // 2, (c % 2) * T:(c % 2 + 1) * T]),
            "w_qkv": inputs["L0_attn_w_qkv"],
            "b_qkv": row(inputs["L0_attn_b_qkv"]),
            "w_out0": inputs["L0_attn_w_out"],
            "b_out0": row(inputs["L0_attn_b_out"]),
            "gup_sh": np.ascontiguousarray(gup[:, 32 * c:32 * (c + 1)]),
            "gus_sh": np.ascontiguousarray(gus[2 * c:2 * c + 2, :]),
            "dnp_sh": dnp_sh, "dns_sh": dns_sh,
            "wq1": inputs["L1_attn_Wq"], "wk1": inputs["L1_attn_Wk"],
            "wv1": inputs["L1_attn_Wv"], "wo1": inputs["L1_attn_Wo"],
            "W_base": inputs["L1_ffn_W_base"],
            "b_base": row(inputs["L1_ffn_b_base"]),
            "A_s": inputs["L1_ffn_A_s"], "B_s": inputs["L1_ffn_B_s"],
            "A_r_flat": Ar_flat, "B_r_flat": Br_flat,
            "W_router": inputs["L1_ffn_W_router"],
            "b_router": row(inputs["L1_ffn_b_router"]),
            "alpha": np.array([[-1e30, 1.0]] if c % 2 == 0 else [[0.0, 0.0]],
                              np.float32).astype(BFNP),
        })
    res = run_bass_kernel_spmd(nc, maps, core_ids=list(range(8))).results
    out = np.empty((4, S, D), np.float32)
    for c in range(8):
        out[c // 2, (c % 2) * T:(c % 2 + 1) * T] = res[c]["out"]
    return out


def kernel_multilaunch(**inputs):
    f32 = np.float32
    x = np.ascontiguousarray(inputs["L0_attn_x"], dtype=f32)
    xs = [np.ascontiguousarray(x[c // 2, (c % 2) * T:(c % 2 + 1) * T])
          for c in range(8)]

    def run(nc, maps, cores=None):
        cores = list(range(8)) if cores is None else cores
        return run_bass_kernel_spmd(nc, maps, core_ids=cores).results

    w_qkv = inputs["L0_attn_w_qkv"]
    b_qkv = np.ascontiguousarray(inputs["L0_attn_b_qkv"].reshape(1, -1), f32)
    r1 = run(_prog("qkv0", build_qkv, False),
             [{"x": xs[c], "w_qkv": w_qkv, "b_qkv": b_qkv} for c in range(8)])

    w_out = inputs["L0_attn_w_out"]
    b_out = np.ascontiguousarray(inputs["L0_attn_b_out"].reshape(1, -1), f32)
    kv_own = [np.concatenate([r1[c]["kT"], r1[c]["v"]], axis=0) for c in range(8)]
    kv_all = [np.concatenate([kv_own[2 * b], kv_own[2 * b + 1]], axis=0)
              for b in range(4)]
    al0 = np.zeros((1, 2), np.float32).astype(BFNP)
    r2 = run(_prog("attn0", build_attn, False),
             [{"qT": r1[c]["qT"], "kv_own": kv_own[c], "kv_all": kv_all[c // 2],
               "alpha": al0, "x": xs[c], "w_out": w_out, "b_out": b_out}
              for c in range(8)])
    x1 = [r2[c]["x1"] for c in range(8)]

    gup = inputs["L0_ffn_gate_up_packed"]
    gus = inputs["L0_ffn_gate_up_scales"]
    dnp = inputs["L0_ffn_down_packed"]
    dns = inputs["L0_ffn_down_scales"]
    dq_maps = []
    for c in range(8):
        dnp_sh = np.zeros((2048, 96), np.int32)
        dns_sh = np.zeros((6, 2048), np.float16)
        w0, w1 = 96 * c, min(96 * (c + 1), 704)
        dnp_sh[:, 0:w1 - w0] = dnp[:, w0:w1]
        g0, g1 = 6 * c, min(6 * (c + 1), 44)
        dns_sh[0:g1 - g0, :] = dns[g0:g1, :]
        dq_maps.append({
            "gup_sh": np.ascontiguousarray(gup[:, 32 * c:32 * (c + 1)]),
            "gus_sh": np.ascontiguousarray(gus[2 * c:2 * c + 2, :]),
            "dnp_sh": dnp_sh, "dns_sh": dns_sh})
    r3 = run(_prog("dq", build_dq), dq_maps)
    w_guT = np.concatenate([r3[c]["guT_sh"] for c in range(8)], axis=0)
    w_downT = np.ascontiguousarray(
        np.concatenate([r3[c]["dnT_sh"] for c in range(8)], axis=0)[0:5632])

    r4 = run(_prog("ffn", build_ffn),
             [{"x": x1[c], "w_guT": w_guT, "w_downT": w_downT}
              for c in range(8)])
    x2 = [r4[c]["x2"] for c in range(8)]

    r5 = run(_prog("qkv1", build_qkv, True),
             [{"x": x2[c], "wq": inputs["L1_attn_Wq"],
               "wk": inputs["L1_attn_Wk"], "wv": inputs["L1_attn_Wv"]}
              for c in range(8)])
    kv_own1 = [np.concatenate([r5[c]["kT"], r5[c]["v"]], axis=0) for c in range(8)]
    kv_all1 = [np.concatenate([kv_own1[2 * b], kv_own1[2 * b + 1]], axis=0)
               for b in range(4)]

    Wo = inputs["L1_attn_Wo"]
    alph = [np.array([[-1e30, 1.0]] if c % 2 == 0 else [[0.0, 0.0]],
                     np.float32).astype(BFNP) for c in range(8)]
    r6 = run(_prog("attn1", build_attn, True),
             [{"qT": r5[c]["qT"], "kv_own": kv_own1[c],
               "kv_all": kv_all1[c // 2], "alpha": alph[c],
               "x": x2[c], "w_out": Wo} for c in range(8)])
    x3 = [r6[c]["x1"] for c in range(8)]

    Ar_flat = np.ascontiguousarray(
        inputs["L1_ffn_A_r"].transpose(1, 0, 2).reshape(D, 96))
    Br_flat = np.ascontiguousarray(inputs["L1_ffn_B_r"].reshape(96, D))
    r8 = run(_prog("moe", build_moe),
             [{"x": x3[c], "W_base": inputs["L1_ffn_W_base"],
               "b_base": np.ascontiguousarray(
                   inputs["L1_ffn_b_base"].reshape(1, -1), f32),
               "A_s": inputs["L1_ffn_A_s"], "B_s": inputs["L1_ffn_B_s"],
               "A_r_flat": Ar_flat, "B_r_flat": Br_flat,
               "W_router": inputs["L1_ffn_W_router"],
               "b_router": np.ascontiguousarray(
                   inputs["L1_ffn_b_router"].reshape(1, -1), f32)}
              for c in range(8)])

    out = np.empty((4, S, D), np.float32)
    for c in range(8):
        out[c // 2, (c % 2) * T:(c % 2 + 1) * T] = r8[c]["x4"]
    return out


# version 11
# speedup vs baseline: 1.5191x; 1.0234x over previous
import numpy as np
import ml_dtypes
from contextlib import ExitStack

import concourse.bass as bass
import concourse.mybir as mybir
from concourse.bass_utils import run_bass_kernel_spmd

F32 = mybir.dt.float32
BF16 = mybir.dt.bfloat16
F16 = mybir.dt.float16
I32 = mybir.dt.int32
AF = mybir.ActivationFunctionType
ALU = mybir.AluOpType
AX = mybir.AxisListType
BFNP = ml_dtypes.bfloat16

T = 2048
D = 2048
NT = T // 128
ND = D // 128
NH = 16
HD = 128
S = 4096
EPS = 1e-5

L0_SCALE = float(1.0 / np.sqrt(np.float32(128.0)).astype(np.float32))
L1_SCALE = 1.0 / 64.0


class Prog:

    def __init__(self, name="prog"):
        self.nc = bass.Bass("TRN2", target_bir_lowering=False, name=name)
        self.ctx = ExitStack()
        self.sems = {}
        self.cnt = {}
        self.block = None

    def sb(self, name, shape, dtype, at=None):
        key = ("sb", name)
        if not hasattr(self, "_bufs"):
            self._bufs = {}
        if key not in self._bufs:
            if at is None:
                self._bufs[key] = self.nc.alloc_sbuf_tensor(name, shape, dtype)
            else:
                self._bufs[key] = self.nc.alloc_sbuf_tensor_at(
                    name, shape, dtype, offset=at)
        return self._bufs[key]

    def addr(self, name):
        return self.nc.lookup_mloc(self._bufs[("sb", name)]).addr

    def ps(self, name, shape, dtype=F32):
        return self.ctx.enter_context(self.nc.psum_tensor(name, shape, dtype))

    def sem(self, name):
        if name not in self.sems:
            self.sems[name] = self.ctx.enter_context(self.nc.semaphore(name))
            self.cnt[name] = 0
        return self.sems[name]

    def start(self):
        for s in ("te", "sc", "ve", "gp", "ev"):
            self.sem(s)
        self.pfd = [self.ps(f"pfd{i}", [128, 1024], F32) for i in range(2)]
        self.pf = [self.pfd[0][:, 0:512], self.pfd[0][:, 512:1024],
                   self.pfd[1][:, 0:512], self.pfd[1][:, 512:1024],
                   self.ps("pf4", [128, 512], F32),
                   self.ps("pf5", [128, 512], F32)]
        self.pb = [self.ps(f"pbh{i}", [128, 1024], BF16)[:, 0:512]
                   for i in range(2)]
        self.block = self.ctx.enter_context(self.nc.Block())
        return self.block

    def inc(self, bi, name, k=1):
        self.cnt[name] += k
        bi.then_inc(self.sems[name], k)
        return (self.sems[name], self.cnt[name])

    def mark(self, name):
        return (self.sems[name], self.cnt[name])

    def finish(self):
        self.ctx.close()
        return self.nc


def wge(eng, mk):
    eng.wait_ge(mk[0], mk[1])


def apx(x):
    a = getattr(x, "ap", None)
    return x.ap() if callable(a) else x


def emit_consts(p):
    ident = p.sb("ident", [128, 128], BF16)
    ones_row = p.sb("ones_row", [1, 512], BF16)
    ones128 = p.sb("ones128", [128, 128], BF16)
    sixteens = p.sb("sixteens", [128, 128], BF16)
    identf = p.sb("identf", [128, 128], F32)
    b = p.block
    sem = p.sems
    GP = p.cnt["gp"]

    @b.gpsimd
    def _(g):
        g.memset(ident[:], 0.0).then_inc(sem["gp"], 1)
        g.wait_ge(sem["gp"], GP + 1)
        g.affine_select(out=ident[:], in_=ident[:], compare_op=ALU.not_equal,
                        fill=1.0, base=0, pattern=[[-1, 128]],
                        channel_multiplier=1).then_inc(sem["gp"], 1)
        g.memset(ones_row[:], 1.0).then_inc(sem["gp"], 1)
        g.memset(ones128[:], 1.0).then_inc(sem["gp"], 1)
        g.memset(sixteens[:], 16.0).then_inc(sem["gp"], 1)
        g.memset(identf[:], 0.0).then_inc(sem["gp"], 1)
        g.wait_ge(sem["gp"], GP + 6)
        g.affine_select(out=identf[:], in_=identf[:], compare_op=ALU.not_equal,
                        fill=1.0, base=0, pattern=[[-1, 128]],
                        channel_multiplier=1).then_inc(sem["gp"], 1)
    p.cnt["gp"] += 7
    ready = (sem["gp"], p.cnt["gp"])
    p._identf = identf
    p._sixteens = sixteens
    return ident, ones_row, ones128, ready


class LNOut:
    pass


def emit_ln_uT(p, x_hbm, uT, t0, n_tiles, consts, tspan, name="",
               guard_marks=()):
    ident = consts[0]
    cready = consts[3]
    b = p.block
    sem = p.sems
    xt = [p.sb(f"lnxt{name}{i}", [128, D], F32) for i in range(2)]
    for i in range(2):
        p.sem(f"lx{name}{i}")
    trash_bf = p.sb(f"lntrash{name}", [128, D], BF16)
    trash_sq = p.sb(f"lnsq{name}", [128, D], F32)
    sums = p.sb(f"lnsums{name}", [128, 1], F32)
    ssq = p.sb(f"lnssq{name}", [128, 1], F32)
    mean = p.sb(f"lnmean{name}", [128, 1], F32)
    msq = p.sb(f"lnmsq{name}", [128, 1], F32)
    var = p.sb(f"lnvar{name}", [128, 1], F32)
    std = p.sb(f"lnstd{name}", [128, 1], F32)
    rstd = p.sb(f"lnrstd{name}", [128, 1], F32)
    negmr = p.sb(f"lnnegmr{name}", [128, 1], F32)
    u_t = [p.sb(f"lnu{name}{i}", [128, D], BF16) for i in range(2)]
    pb = p.pb

    SC = p.cnt["sc"]
    VE = p.cnt["ve"]
    TE = p.cnt["te"]
    EV = p.cnt["ev"]
    xmark = []

    @b.sync
    def _(sync):
        for mk in guard_marks:
            wge(sync, mk)
        for tt in range(n_tiles):
            if tt >= 2:
                sync.wait_ge(sem["sc"], SC + 3 * (tt - 2) + 3)
                sync.wait_ge(sem["ve"], VE + 7 * (tt - 2) + 1)
            else:
                sync.wait_ge(sem["sc"], SC)
                sync.wait_ge(sem["ve"], VE)
            bi = sync.dma_start(out=xt[tt % 2][:],
                                in_=x_hbm[(t0 + tt) * 128:(t0 + tt + 1) * 128, :])
            xmark.append(p.inc(bi, f"lx{name}{tt % 2}", 16))
    assert len(xmark) == n_tiles

    @b.scalar
    def _(s):
        for tt in range(n_tiles):
            wge(s, xmark[tt])
            if tt >= 1:
                s.wait_ge(sem["ve"], VE + 7 * (tt - 1) + 2)
            s.activation(trash_bf[:], xt[tt % 2][:], AF.Identity,
                         accum_out=sums[:]).then_inc(sem["sc"], 1)
            s.wait_ge(sem["ve"], VE + 7 * tt + 5)
            s.activation(std[:], var[:], AF.Sqrt).then_inc(sem["sc"], 1)
            s.wait_ge(sem["ve"], VE + 7 * tt + 7)
            if tt >= 2:
                s.wait_ge(sem["te"], TE + 4 * (tt - 2) + 4)
            else:
                s.wait_ge(sem["te"], TE)
            s.activation(u_t[tt % 2][:], xt[tt % 2][:], AF.Identity,
                         bias=negmr[:], scale=rstd[:]).then_inc(sem["sc"], 1)
    p.cnt["sc"] += 3 * n_tiles

    def emit_evac(v, tt):
        for g in range(4):
            G = 4 * tt + g
            v.wait_ge(sem["te"], TE + G + 1)
            dst = apx(uT).rearrange("p (dc t) -> p dc t", dc=ND, t=tspan)[
                :, 4 * g:4 * g + 4, tt * 128:(tt + 1) * 128]
            src = pb[G % 2].rearrange("p (j f) -> p j f", j=4, f=128)
            v.tensor_copy(dst, src).then_inc(sem["ev"], 1)

    @b.vector
    def _(v):
        for mk in guard_marks:
            wge(v, mk)
        for tt in range(n_tiles):
            wge(v, xmark[tt])
            v.scalar_tensor_tensor(trash_sq[:], xt[tt % 2][:], 0.0, xt[tt % 2][:],
                                   op0=ALU.add, op1=ALU.mult,
                                   accum_out=ssq[:]).then_inc(sem["ve"], 1)
            v.wait_ge(sem["sc"], SC + 3 * tt + 1)
            v.tensor_scalar_mul(mean[:], sums[:], 1.0 / D).then_inc(sem["ve"], 1)
            v.wait_ge(sem["ve"], VE + 7 * tt + 2)
            v.scalar_tensor_tensor(msq[:], mean[:], 0.0, mean[:],
                                   op0=ALU.add, op1=ALU.mult).then_inc(sem["ve"], 1)
            v.wait_ge(sem["ve"], VE + 7 * tt + 3)
            v.scalar_tensor_tensor(var[:], ssq[:], 1.0 / D, msq[:],
                                   op0=ALU.mult, op1=ALU.subtract).then_inc(sem["ve"], 1)
            v.wait_ge(sem["ve"], VE + 7 * tt + 4)
            v.tensor_scalar_add(var[:], var[:], EPS).then_inc(sem["ve"], 1)
            v.wait_ge(sem["sc"], SC + 3 * tt + 2)
            v.reciprocal(rstd[:], std[:]).then_inc(sem["ve"], 1)
            v.wait_ge(sem["ve"], VE + 7 * tt + 6)
            v.scalar_tensor_tensor(negmr[:], mean[:], -1.0, rstd[:],
                                   op0=ALU.mult, op1=ALU.mult).then_inc(sem["ve"], 1)
            if tt >= 1:
                emit_evac(v, tt - 1)
        emit_evac(v, n_tiles - 1)
    p.cnt["ve"] += 7 * n_tiles
    p.cnt["ev"] += 4 * n_tiles

    @b.tensor
    def _(t):
        wge(t, cready)
        for tt in range(n_tiles):
            t.wait_ge(sem["sc"], SC + 3 * tt + 3)
            for g in range(4):
                G = 4 * tt + g
                if G >= 2:
                    t.wait_ge(sem["ev"], EV + (G - 2) + 1)
                else:
                    t.wait_ge(sem["ev"], EV)
                for j in range(4):
                    dc = 4 * g + j
                    bi = t.transpose(pb[G % 2][:, j * 128:(j + 1) * 128],
                                     u_t[tt % 2][:, dc * 128:(dc + 1) * 128],
                                     ident[:])
                bi.then_inc(sem["te"], 1)
    p.cnt["te"] += 4 * n_tiles


    out = LNOut()
    out.tile_done = lambda tt: (sem["ev"], EV + 4 * (tt + 1))
    out.all_done = (sem["ev"], EV + 4 * n_tiles)
    return out


def emit_proj_fm(p, uT, w_hbm, w_cols, b_hbm, b_off, out_hbm, consts, ln_mark,
                 nfc, tspan, name="", pre_marks=()):
    _, ones_row, _, cready = consts
    b = p.block
    sem = p.sems
    ntg = tspan // 512
    wbuf = [p.sb(f"pfw{name}{i}", [128, ND * 128], BF16) for i in range(2)]
    stage = [p.sb(f"pfs{name}{i}", [128, 512], BF16) for i in range(2)]
    for i in range(2):
        p.sem(f"pfl{name}{i}")
        p.sem(f"pfo{name}{i}")
    p.sem(f"pflb{name}")
    pf = p.pf[0:2]
    brow = p.sb(f"pfb{name}", [1, nfc * 128], BF16) if b_hbm is not None else None

    TE = p.cnt["te"]
    SC = p.cnt["sc"]
    SO = [p.cnt[f"pfo{name}{i}"] for i in range(2)]
    wmark = []
    bmark = []

    @b.gpsimd
    def _(g):
        for mk in pre_marks:
            wge(g, mk)
        if b_hbm is not None:
            g.wait_ge(sem["te"], TE)
            bi = g.dma_start(out=brow[:], in_=b_hbm[:, b_off:b_off + nfc * 128])
            bmark.append(p.inc(bi, f"pflb{name}", 16))
        for fc in range(nfc):
            if fc >= 2:
                g.wait_ge(sem["te"], TE + ntg * (fc - 1))
            else:
                g.wait_ge(sem["te"], TE)
            src = w_hbm[:, w_cols + fc * 128:w_cols + (fc + 1) * 128]\
                .rearrange("(dc p) f -> p dc f", p=128)
            dst = wbuf[fc % 2].ap().rearrange("p (dc f) -> p dc f", dc=ND, f=128)
            bi = g.dma_start(out=dst, in_=src)
            wmark.append(p.inc(bi, f"pfl{name}{fc % 2}", 16))

    @b.tensor
    def _(t):
        wge(t, cready)
        for fc in range(nfc):
            wge(t, wmark[fc])
            if fc == 0 and bmark:
                wge(t, bmark[0])
            for tg in range(ntg):
                K = fc * ntg + tg
                if K >= 2:
                    t.wait_ge(sem["sc"], SC + (K - 2) + 1)
                else:
                    t.wait_ge(sem["sc"], SC)
                wge(t, ln_mark(tg))
                first = True
                if b_hbm is not None:
                    t.matmul(pf[K % 2][:], brow[0:1, fc * 128:(fc + 1) * 128],
                             ones_row[0:1, :], start=True, stop=False)
                    first = False
                for dc in range(ND):
                    bi = t.matmul(
                        pf[K % 2][:], wbuf[fc % 2][:, dc * 128:(dc + 1) * 128],
                        apx(uT)[:, dc * tspan + tg * 512: dc * tspan + tg * 512 + 512],
                        start=first, stop=(dc == ND - 1))
                    first = False
                bi.then_inc(sem["te"], 1)
    p.cnt["te"] += nfc * ntg

    @b.scalar
    def _(s):
        for fc in range(nfc):
            for tg in range(ntg):
                K = fc * ntg + tg
                s.wait_ge(sem["te"], TE + K + 1)
                if K >= 2:
                    s.wait_ge(p.sems[f"pfo{name}{K % 2}"],
                              SO[K % 2] + 16 * (K // 2))
                s.activation(stage[K % 2][:], pf[K % 2][:], AF.Identity)\
                    .then_inc(sem["sc"], 1)
    p.cnt["sc"] += nfc * ntg

    @b.sync
    def _(sync):
        for fc in range(nfc):
            for tg in range(ntg):
                K = fc * ntg + tg
                sync.wait_ge(sem["sc"], SC + K + 1)
                bi = sync.dma_start(out=out_hbm[fc * 128:(fc + 1) * 128,
                                                tg * 512:tg * 512 + 512],
                                    in_=stage[K % 2][:])
                p.inc(bi, f"pfo{name}{K % 2}", 16)
    return [p.mark(f"pfo{name}0"), p.mark(f"pfo{name}1")]


def emit_proj_tm(p, uT, w_hbm, w_cols, b_hbm, b_off, out_hbm, consts, ln_mark,
                 tspan, t0=0, nfg=4, name="", pre_marks=()):
    _, ones_row, _, cready = consts
    b = p.block
    sem = p.sems
    ntl = tspan // 128
    GR = 4
    ngr = ntl // GR
    NWS = 4
    wtile = [p.sb(f"ptw{name}{i}", [128, 512], BF16) for i in range(NWS)]
    stage = [p.sb(f"pts{name}{i}", [128, 512], BF16) for i in range(2)]
    for i in range(NWS):
        p.sem(f"ptl{name}{i}")
    for i in range(2):
        p.sem(f"pto{name}{i}")
    p.sem(f"ptlb{name}")
    pt = p.pf[0:4]
    brow = p.sb(f"ptb{name}", [1, nfg * 512], BF16) if b_hbm is not None else None

    TE = p.cnt["te"]
    SC = p.cnt["sc"]
    SO = [p.cnt[f"pto{name}{i}"] for i in range(2)]
    wmark = []
    bmark = []

    @b.gpsimd
    def _(g):
        for mk in pre_marks:
            wge(g, mk)
        if b_hbm is not None:
            bi = g.dma_start(out=brow[:], in_=b_hbm[:, b_off:b_off + nfg * 512])
            bmark.append(p.inc(bi, f"ptlb{name}", 16))
        for fg in range(nfg):
            for ttg in range(ngr):
                for dc in range(ND):
                    L = (fg * ngr + ttg) * ND + dc
                    if L >= NWS:
                        g.wait_ge(sem["te"], TE + L - (NWS - 1))
                    else:
                        g.wait_ge(sem["te"], TE)
                    bi = g.dma_start(
                        out=wtile[L % NWS][:],
                        in_=w_hbm[dc * 128:(dc + 1) * 128,
                                  w_cols + fg * 512:w_cols + (fg + 1) * 512])
                    wmark.append(p.inc(bi, f"ptl{name}{L % NWS}", 16))

    @b.tensor
    def _(t):
        wge(t, cready)
        for fg in range(nfg):
            for ttg in range(ngr):
                grp = fg * ngr + ttg
                for dc in range(ND):
                    L = grp * ND + dc
                    wge(t, wmark[L])
                    if L == 0 and bmark:
                        wge(t, bmark[0])
                    for i in range(GR):
                        tt = ttg * GR + i
                        if dc == 0:
                            if grp >= 1:
                                t.wait_ge(sem["sc"], SC + (grp - 1) * GR + i + 1)
                            else:
                                t.wait_ge(sem["sc"], SC)
                            wge(t, ln_mark(tt))
                            if b_hbm is not None:
                                t.matmul(pt[i][:], ones_row[0:1, 0:128],
                                         brow[0:1, fg * 512:(fg + 1) * 512],
                                         start=True, stop=False)
                        bi = t.matmul(
                            pt[i][:],
                            apx(uT)[:, dc * tspan + tt * 128: dc * tspan + tt * 128 + 128],
                            wtile[L % NWS][:],
                            start=(dc == 0 and b_hbm is None),
                            stop=(dc == ND - 1))
                    bi.then_inc(sem["te"], 1)
    p.cnt["te"] += nfg * ngr * ND

    @b.scalar
    def _(s):
        for fg in range(nfg):
            for ttg in range(ngr):
                grp = fg * ngr + ttg
                s.wait_ge(sem["te"], TE + (grp + 1) * ND)
                for i in range(GR):
                    E = grp * GR + i
                    if E >= 2:
                        s.wait_ge(p.sems[f"pto{name}{E % 2}"],
                                  SO[E % 2] + 16 * (E // 2))
                    s.activation(stage[E % 2][:], pt[i][:], AF.Identity)\
                        .then_inc(sem["sc"], 1)
    p.cnt["sc"] += nfg * ngr * GR

    @b.sync
    def _(sync):
        for fg in range(nfg):
            for ttg in range(ngr):
                grp = fg * ngr + ttg
                for i in range(GR):
                    tt = ttg * GR + i
                    E = grp * GR + i
                    sync.wait_ge(sem["sc"], SC + E + 1)
                    bi = sync.dma_start(
                        out=out_hbm[(t0 + tt) * 128:(t0 + tt + 1) * 128,
                                    fg * 512:(fg + 1) * 512],
                        in_=stage[E % 2][:])
                    p.inc(bi, f"pto{name}{E % 2}", 16)
    return [p.mark(f"pto{name}0"), p.mark(f"pto{name}1")]


def emit_attn_core(p, qTh, kv_own, kv_all, alpha_in, consts, scale, l1,
                   AVTn, name="", guard_marks=()):
    ident, ones_row, ones128, cready = consts
    b = p.block
    sem = p.sems
    kT_h = [p.sb(f"at_k{name}{i}", [128, S], BF16) for i in range(2)]
    v_h = [p.sb(f"at_v{name}{i}", [128, 32 * 128], BF16) for i in range(2)]
    qT_h = [p.sb(f"at_q{name}{i}", [128, T], BF16) for i in range(2)]
    pA = p.sb(f"at_pa{name}", [128, T], BF16)
    pC = p.sb(f"at_pc{name}", [128, T], BF16)
    expT = p.sb(f"at_e{name}", [128, 16 * 512], BF16)
    r_s = p.sb(f"at_r{name}", [128, 512], F32)
    a_sb = p.sb(f"at_al{name}", [1, 2], BF16)
    lnA = p.sb(f"at_lna{name}", [128, 1], F32)
    betac = p.sb(f"at_bc{name}", [128, 1], F32)
    B_t = p.sb(f"at_bt{name}", [128, 512], BF16)
    for i in range(2):
        p.sem(f"athd{name}{i}")
    p.sem(f"atal{name}")
    ps_sd = p.pfd
    ps_l = p.pf[4]
    ps_av = p.pf[5]
    ps_b = p.pf[4]

    NQG = T // 512

    def classify(h, qg, ktc):
        if ktc >= 16:
            return "blend" if l1 else "full"
        if not l1:
            return "full"
        lo = 128 * ktc
        if lo >= 512 * (qg + 1):
            return "skip"
        if lo + 127 <= 512 * qg:
            return "full"
        return "diag"

    TE0 = p.cnt["te"]
    SC0 = p.cnt["sc"]
    GP0 = p.cnt["gp"]
    VE0 = p.cnt["ve"]
    def pdone(h):
        return VE0 + 1 + 4 * h + 8 * max(0, h - 1) + 4

    def grp_base(H):
        if H <= NH - 3:
            return VE0 + 9 + 12 * H
        return VE0 + 1 + 4 * NH + 8 * H

    rc_mark = {}
    mu_mark = {}
    for H in range(NH):
        for qg in range(NH * 0 + 4):
            rc_mark[(H, qg)] = grp_base(H) + 2 * qg + 1
            mu_mark[(H, qg)] = grp_base(H) + 2 * qg + 2
    VE_END = VE0 + 1 + 4 * NH + 8 * NH
    TE1 = TE0 + 1
    SC1 = SC0 + 2

    te_n, sc_n = TE1, SC1
    gp_n = GP0 + (4 if l1 else 0)
    score_mark = {}
    exp_mark = {}
    gp_mark = {}
    lp0_mark = {}
    avp0_mark = {}
    l_mark = {}
    av_mark = {}
    exp_done_own = {}
    exp_done = {}
    n_own_of = {}
    for h in range(NH):
        for qg in range(NQG):
            own = [k for k in range(16) if classify(h, qg, k) != "skip"]
            assert own == list(range(len(own))) and len(own) % 2 == 0
            n_own_of[(h, qg)] = len(own)
            for jp in range(len(own) // 2):
                te_n += 2
                score_mark[(h, qg, 2 * jp)] = te_n - 1
                score_mark[(h, qg, 2 * jp + 1)] = te_n
                sc_n += 1
                exp_mark[(h, qg, 2 * jp)] = sc_n
                exp_mark[(h, qg, 2 * jp + 1)] = sc_n
            exp_done_own[(h, qg)] = sc_n
            te_n += 1
            lp0_mark[(h, qg)] = te_n
            te_n += 1
            avp0_mark[(h, qg)] = te_n
            for jp in range(8):
                te_n += 2
                score_mark[(h, qg, 16 + 2 * jp)] = te_n - 1
                score_mark[(h, qg, 17 + 2 * jp)] = te_n
                sc_n += 1
                exp_mark[(h, qg, 16 + 2 * jp)] = sc_n
                exp_mark[(h, qg, 17 + 2 * jp)] = sc_n
            if l1:
                k0 = 4 * qg
                if k0 + 4 <= 15:
                    gp_n += 1
                    for k in range(k0 + 4, 16):
                        gp_mark[(h, qg, k)] = gp_n
                gp_n += 2
                for k in range(k0, k0 + 4):
                    gp_mark[(h, qg, k)] = gp_n
            exp_done[(h, qg)] = sc_n
            te_n += 1
            l_mark[(h, qg)] = te_n
            te_n += 1
            av_mark[(h, qg)] = te_n

    def prev_group(h, qg):
        if qg > 0:
            return (h, qg - 1)
        if h > 0:
            return (h - 1, NQG - 1)
        return None

    amark = []
    hmark = []

    @b.sync
    def _(sync):
        for mk in guard_marks:
            wge(sync, mk)
        sync.wait_ge(sem["te"], TE0)
        sync.wait_ge(sem["ve"], VE0)
        bi = sync.dma_start(out=a_sb[:], in_=alpha_in)
        amark.append(p.inc(bi, f"atal{name}", 16))
        for h in range(NH):
            if h >= 2:
                sync.wait_ge(sem["te"], av_mark[(h - 2, NQG - 1)])
                sync.wait_ge(sem["ve"], pdone(h - 2))
            if h >= 1:
                sync.wait_ge(sem["ve"], pdone(h - 1))
            sn = f"athd{name}{h % 2}"
            bi = sync.dma_start(out=kT_h[h % 2][:, 0:T],
                                in_=kv_own[h * 128:(h + 1) * 128, :])
            p.inc(bi, sn, 16)
            bi = sync.dma_start(out=pA[:],
                                in_=kv_all[h * 128:(h + 1) * 128, :])
            p.inc(bi, sn, 16)
            bi = sync.dma_start(
                out=kT_h[h % 2][:, T:2 * T],
                in_=kv_all[2 * T + h * 128:2 * T + (h + 1) * 128, :])
            p.inc(bi, sn, 16)
            src = kv_own[T:2 * T, h * 128:(h + 1) * 128]\
                .rearrange("(c p) f -> p c f", p=128)
            dst = v_h[h % 2].ap().rearrange("p (c f) -> p c f",
                                            c=32, f=128)[:, 0:16, :]
            bi = sync.dma_start(out=dst, in_=src)
            p.inc(bi, sn, 16)
            src = kv_all[T:2 * T, h * 128:(h + 1) * 128]\
                .rearrange("(c p) f -> p c f", p=128)
            bi = sync.dma_start(
                out=pC.ap().rearrange("p (c f) -> p c f", c=16, f=128),
                in_=src)
            p.inc(bi, sn, 16)
            src = kv_all[3 * T:4 * T, h * 128:(h + 1) * 128]\
                .rearrange("(c p) f -> p c f", p=128)
            bi = sync.dma_start(
                out=v_h[h % 2].ap().rearrange("p (c f) -> p c f",
                                              c=32, f=128)[:, 16:32, :],
                in_=src)
            p.inc(bi, sn, 16)
            bi = sync.dma_start(out=qT_h[h % 2][:],
                                in_=qTh[h * 128:(h + 1) * 128, :])
            hmark.append(p.inc(bi, sn, 16))

    def emit_partner(v, h):
        wge(v, hmark[h])
        if h >= 2:
            v.wait_ge(sem["te"], av_mark[(h - 2, NQG - 1)])
        ve_b = pdone(h) - 4
        v.tensor_add(pA[:], pA[:], kT_h[h % 2][:, T:2 * T])\
            .then_inc(sem["ve"], 1)
        v.wait_ge(sem["ve"], ve_b + 1)
        v.tensor_sub(kT_h[h % 2][:, T:2 * T], pA[:],
                     kT_h[h % 2][:, 0:T]).then_inc(sem["ve"], 1)
        v.tensor_add(pC[:], pC[:],
                     v_h[h % 2][:, 16 * 128:32 * 128]).then_inc(sem["ve"], 1)
        v.wait_ge(sem["ve"], ve_b + 3)
        v.tensor_sub(v_h[h % 2][:, 16 * 128:32 * 128], pC[:],
                     v_h[h % 2][:, 0:16 * 128]).then_inc(sem["ve"], 1)

    def emit_grp(v, H):
        for qg in range(NQG):
            v.wait_ge(sem["te"], l_mark[(H, qg)])
            if not (H == 0 and qg == 0):
                pm = mu_mark[(H, qg - 1)] if qg > 0 else mu_mark[(H - 1, 3)]
                v.wait_ge(sem["ve"], pm)
            v.reciprocal(r_s[:], ps_l[:]).then_inc(sem["ve"], 1)
            v.wait_ge(sem["ve"], rc_mark[(H, qg)])
            v.wait_ge(sem["te"], av_mark[(H, qg)])
            v.tensor_mul(apx(AVTn)[:, H * T + qg * 512: H * T + qg * 512 + 512],
                         ps_av[:], r_s[:]).then_inc(sem["ve"], 1)

    @b.vector
    def _(v):
        wge(v, (sem["sc"], SC1))
        v.tensor_copy(B_t.ap().rearrange("p (a f) -> p a f", a=1, f=512),
                      betac.ap().broadcast_to([128, 1, 512]))\
            .then_inc(sem["ve"], 1)
        emit_partner(v, 0)
        emit_partner(v, 1)
        for h in range(2, NH):
            emit_grp(v, h - 2)
            emit_partner(v, h)
        emit_grp(v, NH - 2)
        emit_grp(v, NH - 1)
    p.cnt["ve"] = VE_END

    @b.tensor
    def _(t):
        wge(t, cready)
        wge(t, amark[0])
        t.wait_ge(sem["sc"], SC0)
        t.matmul(ps_b[:, 0:2], ones_row[0:1, 0:128], a_sb[0:1, :],
                 start=True, stop=True).then_inc(sem["te"], 1)
    p.cnt["te"] = TE1

    @b.scalar
    def _(s):
        s.wait_ge(sem["te"], TE1)
        s.activation(lnA[:], ps_b[:, 0:1], AF.Identity).then_inc(sem["sc"], 1)
        s.activation(betac[:], ps_b[:, 1:2], AF.Identity).then_inc(sem["sc"], 1)
    p.cnt["sc"] = SC1

    @b.tensor
    def _(t):
        w_sc = [0]
        w_gp = [0]

        def wsc(val):
            if val > w_sc[0]:
                t.wait_ge(sem["sc"], val)
                w_sc[0] = val

        def wgp(val):
            if val > w_gp[0]:
                t.wait_ge(sem["gp"], val)
                w_gp[0] = val

        for h in range(NH):
            wge(t, hmark[h])
            t.wait_ge(sem["ve"], pdone(h))
            for qg in range(NQG):
                pg = prev_group(h, qg)
                n_own = n_own_of[(h, qg)]
                for jp in range(n_own // 2):
                    if jp >= 2:
                        wsc(exp_mark[(h, qg, 2 * (jp - 2))])
                    elif pg is not None:
                        wsc(exp_done[pg])
                    else:
                        wsc(SC1)
                    for kk in (2 * jp, 2 * jp + 1):
                        bi = t.matmul(
                            ps_sd[jp % 2][:, (kk % 2) * 512:(kk % 2) * 512 + 512],
                            kT_h[h % 2][:, kk * 128:(kk + 1) * 128],
                            qT_h[h % 2][:, qg * 512:(qg + 1) * 512],
                            start=True, stop=True)
                        bi.then_inc(sem["te"], 1)
                for ktc in range(16):
                    c = classify(h, qg, ktc)
                    if c == "full":
                        wsc(exp_mark[(h, qg, ktc)])
                    else:
                        wgp(gp_mark[(h, qg, ktc)])
                    if ktc == 0 and pg is not None:
                        t.wait_ge(sem["ve"], rc_mark[pg])
                    bi = t.matmul(ps_l[:], ones128[:],
                                  expT[:, ktc * 512:(ktc + 1) * 512],
                                  start=(ktc == 0), stop=False)
                bi.then_inc(sem["te"], 1)
                for ktc in range(16):
                    if ktc == 0 and pg is not None:
                        t.wait_ge(sem["ve"], mu_mark[pg])
                    bi = t.matmul(ps_av[:],
                                  v_h[h % 2][:, ktc * 128:(ktc + 1) * 128],
                                  expT[:, ktc * 512:(ktc + 1) * 512],
                                  start=(ktc == 0), stop=False)
                bi.then_inc(sem["te"], 1)
                for jp in range(8):
                    if jp >= 2:
                        wsc(exp_mark[(h, qg, 16 + 2 * (jp - 2))])
                    else:
                        wsc(exp_done_own[(h, qg)])
                    for kk in (16 + 2 * jp, 17 + 2 * jp):
                        bi = t.matmul(
                            ps_sd[jp % 2][:, (kk % 2) * 512:(kk % 2) * 512 + 512],
                            kT_h[h % 2][:, kk * 128:(kk + 1) * 128],
                            qT_h[h % 2][:, qg * 512:(qg + 1) * 512],
                            start=True, stop=True)
                        bi.then_inc(sem["te"], 1)
                for ktc in range(16, 32):
                    wsc(exp_mark[(h, qg, ktc)])
                    bi = t.matmul(ps_l[:], ones128[:],
                                  expT[:, (ktc - 16) * 512:(ktc - 15) * 512],
                                  start=False, stop=(not l1 and ktc == 31))
                if l1:
                    bi = t.matmul(ps_l[:], p._sixteens[:], B_t[:],
                                  start=False, stop=True)
                bi.then_inc(sem["te"], 1)
                for ktc in range(16, 32):
                    bi = t.matmul(ps_av[:],
                                  v_h[h % 2][:, ktc * 128:(ktc + 1) * 128],
                                  expT[:, (ktc - 16) * 512:(ktc - 15) * 512],
                                  start=False, stop=(not l1 and ktc == 31))
                if l1:
                    for ktc in range(16, 32):
                        bi = t.matmul(ps_av[:],
                                      v_h[h % 2][:, ktc * 128:(ktc + 1) * 128],
                                      B_t[:], start=False, stop=(ktc == 31))
                bi.then_inc(sem["te"], 1)
    p.cnt["te"] = te_n

    @b.scalar
    def _(s):
        for h in range(NH):
            for qg in range(NQG):
                pg = prev_group(h, qg)
                n_own = n_own_of[(h, qg)]
                for jp in range(n_own // 2):
                    if jp == 0 and pg is not None:
                        s.wait_ge(sem["te"], av_mark[pg])
                    s.wait_ge(sem["te"], score_mark[(h, qg, 2 * jp + 1)])
                    s.activation(expT[:, 2 * jp * 512:(2 * jp + 2) * 512],
                                 ps_sd[jp % 2][:], AF.Exp, scale=scale)\
                        .then_inc(sem["sc"], 1)
                for jp in range(8):
                    if jp == 0:
                        s.wait_ge(sem["te"], avp0_mark[(h, qg)])
                    s.wait_ge(sem["te"], score_mark[(h, qg, 17 + 2 * jp)])
                    if l1:
                        s.activation(expT[:, 2 * jp * 512:(2 * jp + 2) * 512],
                                     ps_sd[jp % 2][:], AF.Exp, scale=scale,
                                     bias=lnA[:]).then_inc(sem["sc"], 1)
                    else:
                        s.activation(expT[:, 2 * jp * 512:(2 * jp + 2) * 512],
                                     ps_sd[jp % 2][:], AF.Exp, scale=scale)\
                            .then_inc(sem["sc"], 1)
    p.cnt["sc"] = sc_n

    if l1:
        msk1 = p.sb(f"at_m1{name}", [128, 4 * 512], BF16)
        msk0 = p.sb(f"at_m0{name}", [128, 4 * 512], BF16)

        @b.gpsimd
        def _(g):
            g.memset(msk1[:], 1.0).then_inc(sem["gp"], 1)
            g.wait_ge(sem["gp"], GP0 + 1)
            m3 = msk1.ap().rearrange("p (c f) -> p c f", c=4, f=512)
            g.affine_select(out=m3, in_=m3, compare_op=ALU.is_ge, fill=0.0,
                            base=0, pattern=[[-128, 4], [1, 512]],
                            channel_multiplier=-1).then_inc(sem["gp"], 1)
            g.memset(msk0[:], 0.0).then_inc(sem["gp"], 1)
            g.wait_ge(sem["gp"], GP0 + 3)
            m3 = msk0.ap().rearrange("p (c f) -> p c f", c=4, f=512)
            g.affine_select(out=m3, in_=m3, compare_op=ALU.is_ge, fill=1.0,
                            base=0, pattern=[[-128, 4], [1, 512]],
                            channel_multiplier=-1).then_inc(sem["gp"], 1)
            gpc = GP0 + 4
            g.wait_ge(sem["ve"], VE0 + 1)
            for h in range(NH):
                for qg in range(NQG):
                    pg = prev_group(h, qg)
                    k0 = 4 * qg
                    if pg is not None:
                        g.wait_ge(sem["te"], av_mark[pg])
                    if k0 + 4 <= 15:
                        g.memset(expT[:, (k0 + 4) * 512:16 * 512], 1.0)\
                            .then_inc(sem["gp"], 1)
                        gpc += 1
                    g.wait_ge(sem["sc"], exp_mark[(h, qg, k0 + 3)])
                    dg = expT[:, k0 * 512:(k0 + 4) * 512]
                    g.tensor_mul(dg, dg, msk1[:]).then_inc(sem["gp"], 1)
                    gpc += 1
                    g.wait_ge(sem["gp"], gpc)
                    g.tensor_add(dg, dg, msk0[:]).then_inc(sem["gp"], 1)
                    gpc += 1
            assert gpc == gp_n, (gpc, gp_n)
    p.cnt["gp"] = gp_n

    return p.mark("ve")


def build_attn(l1):
    p = Prog("attn")
    nc = p.nc
    qTh = nc.dram_tensor("qT", [D, T], BF16, kind="ExternalInput").ap()
    kv_own = nc.dram_tensor("kv_own", [2 * T, D], BF16, kind="ExternalInput").ap()
    kv_all = nc.dram_tensor("kv_all", [4 * T, D], BF16, kind="ExternalInput").ap()
    alpha = nc.dram_tensor("alpha", [1, 2], BF16, kind="ExternalInput").ap()
    x = nc.dram_tensor("x", [T, D], F32, kind="ExternalInput").ap()
    w_out = nc.dram_tensor("w_out", [D, D], F32, kind="ExternalInput").ap()
    b_out = None if l1 else \
        nc.dram_tensor("b_out", [1, D], F32, kind="ExternalInput").ap()
    x1 = nc.dram_tensor("x1", [T, D], F32, kind="ExternalOutput").ap()

    AVTn = p.sb("AVTn", [128, NH * T], BF16)
    p.start()
    consts = emit_consts(p)
    scale = L1_SCALE if l1 else L0_SCALE
    avm = emit_attn_core(p, qTh, kv_own, kv_all, alpha, consts, scale, l1, AVTn)
    mk = emit_mm_ln_resid(
        p, lambda ck, tt: AVTn[:, ck * T + tt * 128: ck * T + tt * 128 + 128],
        ND, w_out, True, b_out, x, x1, consts, [avm], 0, NT, name="o")
    emit_final_wait(p, mk)
    return p.finish()


def emit_gu(p, uT, w_guT, h_actT, consts, ready_mark, name="", guard_marks=()):
    _, ones_row, _, cready = consts
    b = p.block
    sem = p.sems
    NI = 44
    wg = [p.sb(f"guwg{name}{i}", [128, ND * 128], BF16) for i in range(2)]
    wu = [p.sb(f"guwu{name}{i}", [128, ND * 128], BF16) for i in range(2)]
    gsil = [p.sb(f"gusil{name}{i}", [128, 512], BF16) for i in range(2)]
    for i in range(2):
        p.sem(f"gug{name}{i}")
        p.sem(f"guu{name}{i}")
    psg = [p.pf[0], p.pf[1]]
    psu = [p.pf[2], p.pf[3]]

    TE = p.cnt["te"]
    SC = p.cnt["sc"]
    VE = p.cnt["ve"]
    gmark, umark = [], []

    @b.gpsimd
    def _(g):
        for mk in guard_marks:
            wge(g, mk)
        for ic in range(NI):
            if ic >= 2:
                g.wait_ge(sem["te"], TE + 2 * (ic - 1))
            else:
                g.wait_ge(sem["te"], TE)
            for (wbuf, marks, semn, col) in (
                    (wg, gmark, f"gug{name}{ic % 2}", ic * 128),
                    (wu, umark, f"guu{name}{ic % 2}", 5632 + ic * 128)):
                srcw = w_guT[:, col:col + 128]\
                    .rearrange("(dc p) f -> p dc f", p=128)
                dstw = wbuf[ic % 2].ap().rearrange("p (dc f) -> p dc f",
                                                   dc=ND, f=128)
                bi = g.dma_start(out=dstw, in_=srcw)
                marks.append(p.inc(bi, semn, 16))

    @b.tensor
    def _(t):
        wge(t, cready)
        wge(t, ready_mark)
        for ic in range(NI):
            wge(t, gmark[ic])
            wge(t, umark[ic])
            if ic >= 2:
                t.wait_ge(sem["sc"], SC + (ic - 2) + 1)
                t.wait_ge(sem["ve"], VE + (ic - 2) + 1)
            else:
                t.wait_ge(sem["sc"], SC)
                t.wait_ge(sem["ve"], VE)
            for dc in range(ND):
                bi = t.matmul(psg[ic % 2][:],
                              wg[ic % 2][:, dc * 128:(dc + 1) * 128],
                              apx(uT)[:, dc * 512:dc * 512 + 512],
                              start=(dc == 0), stop=(dc == ND - 1))
            bi.then_inc(sem["te"], 1)
            for dc in range(ND):
                bi = t.matmul(psu[ic % 2][:],
                              wu[ic % 2][:, dc * 128:(dc + 1) * 128],
                              apx(uT)[:, dc * 512:dc * 512 + 512],
                              start=(dc == 0), stop=(dc == ND - 1))
            bi.then_inc(sem["te"], 1)
    p.cnt["te"] += 2 * NI

    @b.scalar
    def _(s):
        for ic in range(NI):
            s.wait_ge(sem["te"], TE + 2 * ic + 1)
            if ic >= 2:
                s.wait_ge(sem["ve"], VE + (ic - 2) + 1)
            else:
                s.wait_ge(sem["ve"], VE)
            s.activation(gsil[ic % 2][:], psg[ic % 2][:], AF.Silu)\
                .then_inc(sem["sc"], 1)
    p.cnt["sc"] += NI

    @b.vector
    def _(v):
        for mk in guard_marks:
            wge(v, mk)
        v.wait_ge(sem["te"], TE)
        for ic in range(NI):
            v.wait_ge(sem["sc"], SC + ic + 1)
            v.wait_ge(sem["te"], TE + 2 * ic + 2)
            v.tensor_mul(apx(h_actT)[:, ic * 512:(ic + 1) * 512],
                         gsil[ic % 2][:], psu[ic % 2][:]).then_inc(sem["ve"], 1)
    p.cnt["ve"] += NI

    return p.mark("ve")


def emit_dq(p, gup, gus, dnp, dns, guT, dnT, consts):
    W = 1536
    pk = p.sb("dq_pk", [128, W], I32)
    iscr = p.sb("dq_i", [128, W], I32)
    kf = p.sb("dq_kf", [128, W], F32)
    sf = p.sb("dq_sf", [128, W], F32)
    r4 = p.sb("dq_r4", [128, W], F32)
    r6 = p.sb("dq_r6", [128, W], F32)
    m1 = p.sb("dq_m1", [128, W], F32)
    sful = p.sb("dq_s", [128, W], F32)
    gusT = p.sb("dq_gusT", [128, 176], F32)
    dnsT = p.sb("dq_dnsT", [128, 96], F32)
    cm2 = p.sb("dq_cm2", [128, 1], F32)
    cm6 = p.sb("dq_cm6", [128, 1], F32)
    idf = p.sb("dq_idf", [128, 128], F32)
    deq_gu = p.sb("dq_dgu", [128, 88 * 256], BF16)
    deq_dn = p.sb("dq_ddn", [128, 16 * 768], BF16)
    gusf = deq_gu.ap().bitcast(F32)[0:2, 0:11264]
    dnsf = deq_dn.ap().bitcast(F32)[0:6, 0:2048]
    stg = [p.sb(f"dq_stg{i}", [128, 512], BF16) for i in range(2)]
    for s_ in ("dqpk", "dqpk2", "dqsc", "dqsc2"):
        p.sem(s_)

    b = p.block
    sem = p.sems
    psf = p.pf[4]
    pb = p.pb

    lm = {}

    @b.gpsimd
    def _(g):
        bi = g.dma_start(
            out=pk[:, 0:1408].rearrange("p (a w) -> p a w", a=44, w=32),
            in_=gup[0:5632, :].rearrange("(a p) w -> p a w", p=128))
        lm["gu0"] = p.inc(bi, "dqpk", 16)
        bi = g.dma_start(out=gusf, in_=gus)
        lm["gus"] = p.inc(bi, "dqsc", 16)
        bi = g.dma_start(out=dnsf, in_=dns)
        lm["dns"] = p.inc(bi, "dqsc2", 16)
        g.memset(cm2[:], -2.0).then_inc(sem["gp"], 1)
        g.memset(cm6[:], -6.0).then_inc(sem["gp"], 1)
        g.memset(idf[:], 0.0).then_inc(sem["gp"], 1)
        g.wait_ge(sem["gp"], p.cnt["gp"] + 3)
        g.affine_select(out=idf[:], in_=idf[:], compare_op=ALU.not_equal,
                        fill=1.0, base=0, pattern=[[-1, 128]],
                        channel_multiplier=1).then_inc(sem["gp"], 1)
    p.cnt["gp"] += 4
    gp_done = p.mark("gp")

    @b.tensor
    def _(t):
        wge(t, gp_done)
        wge(t, lm["gus"])
        for a in range(88):
            bi = t.transpose(psf[:, 2 * a:2 * a + 2],
                             gusf[:, a * 128:(a + 1) * 128], idf[0:2, 0:2])
        bi.then_inc(sem["te"], 1)
    p.cnt["te"] += 1
    tm_gus = p.mark("te")

    @b.vector
    def _(v):
        wge(v, tm_gus)
        v.tensor_copy(gusT[:], psf[:, 0:176]).then_inc(sem["ve"], 1)
    p.cnt["ve"] += 1
    gusT_done = p.mark("ve")

    @b.tensor
    def _(t):
        wge(t, gusT_done)
        wge(t, lm["dns"])
        for a in range(16):
            bi = t.transpose(psf[:, 6 * a:6 * a + 6],
                             dnsf[:, a * 128:(a + 1) * 128], idf[0:6, 0:6])
        bi.then_inc(sem["te"], 1)
    p.cnt["te"] += 1
    tm_dns = p.mark("te")

    @b.vector
    def _(v):
        wge(v, tm_dns)
        v.tensor_copy(dnsT[:], psf[:, 0:96]).then_inc(sem["ve"], 1)
    p.cnt["ve"] += 1

    def emit_dq_planes(nwords, na, wper, scT, spere, deq, hl_of_w, lm_key):
        wsub = wper // spere
        VE0 = p.cnt["ve"]

        @b.vector
        def _(v):
            v.wait_ge(sem["ve"], VE0)
            srcS = apx(scT).rearrange("p (a s) -> p a s", a=na, s=spere)\
                .broadcast_to([128, na, spere, wsub])
            dstS = sful[:, 0:nwords].rearrange("p (a s w) -> p a s w",
                                               a=na, s=spere, w=wsub)
            v.tensor_copy(dstS, srcS).then_inc(sem["ve"], 1)
        p.cnt["ve"] += 1

        for j in range(8):
            VE = p.cnt["ve"]
            SC = p.cnt["sc"]

            @b.vector
            def _(v):
                wge(v, lm[lm_key])
                v.wait_ge(sem["ve"], VE)
                v.tensor_scalar(iscr[:, 0:nwords], pk[:, 0:nwords], 4 * j, 7,
                                ALU.logical_shift_right, ALU.bitwise_and)\
                    .then_inc(sem["ve"], 1)
                v.wait_ge(sem["ve"], VE + 1)
                v.tensor_copy(kf[:, 0:nwords], iscr[:, 0:nwords])\
                    .then_inc(sem["ve"], 1)
                v.wait_ge(sem["ve"], VE + 2)
                v.tensor_scalar(iscr[:, 0:nwords], pk[:, 0:nwords], 4 * j, 8,
                                ALU.logical_shift_right, ALU.bitwise_and)\
                    .then_inc(sem["ve"], 1)
                v.wait_ge(sem["ve"], VE + 3)
                v.tensor_copy(sf[:, 0:nwords], iscr[:, 0:nwords])\
                    .then_inc(sem["ve"], 1)
                v.wait_ge(sem["ve"], VE + 4)
                v.tensor_scalar(m1[:, 0:nwords], sf[:, 0:nwords], -0.25, 1.0,
                                ALU.mult, ALU.add).then_inc(sem["ve"], 1)
                v.wait_ge(sem["ve"], VE + 5)
                v.wait_ge(sem["sc"], SC + 2)
                v.scalar_tensor_tensor(sf[:, 0:nwords], kf[:, 0:nwords], 0.5,
                                       r4[:, 0:nwords], op0=ALU.mult,
                                       op1=ALU.add).then_inc(sem["ve"], 1)
                v.wait_ge(sem["ve"], VE + 6)
                v.tensor_add(kf[:, 0:nwords], sf[:, 0:nwords], r6[:, 0:nwords])\
                    .then_inc(sem["ve"], 1)
                v.wait_ge(sem["ve"], VE + 7)
                v.tensor_mul(r4[:, 0:nwords], kf[:, 0:nwords], m1[:, 0:nwords])\
                    .then_inc(sem["ve"], 1)
                v.wait_ge(sem["ve"], VE + 8)
                dstp = apx(deq).rearrange("p (a w) -> p a w", a=na, w=8 * wper)[
                    :, :, hl_of_w(j)::8]
                srcp = r4[:, 0:nwords].rearrange("p (a w) -> p a w", a=na, w=wper)
                srcs = sful[:, 0:nwords].rearrange("p (a w) -> p a w", a=na, w=wper)
                v.tensor_tensor(dstp, srcp, srcs, op=ALU.mult)\
                    .then_inc(sem["ve"], 1)
            p.cnt["ve"] += 9

            @b.scalar
            def _(s):
                s.wait_ge(sem["ve"], VE + 2)
                if j > 0:
                    s.wait_ge(sem["ve"], VE)
                wge(s, gp_done)
                s.activation(r4[:, 0:nwords], kf[:, 0:nwords], AF.Relu,
                             scale=0.5, bias=cm2[:]).then_inc(sem["sc"], 1)
                s.activation(r6[:, 0:nwords], kf[:, 0:nwords], AF.Relu,
                             bias=cm6[:]).then_inc(sem["sc"], 1)
            p.cnt["sc"] += 2

    emit_dq_planes(1408, 44, 32, gusT[:, 0:88],
                   2, deq_gu[:, 0:44 * 256], lambda j: j, "gu0")
    gu0_done = p.mark("ve")

    @b.gpsimd
    def _(g):
        wge(g, gu0_done)
        bi = g.dma_start(
            out=pk[:, 0:1408].rearrange("p (a w) -> p a w", a=44, w=32),
            in_=gup[5632:11264, :].rearrange("(a p) w -> p a w", p=128))
        lm["gu1"] = p.inc(bi, "dqpk", 16)
    emit_dq_planes(1408, 44, 32, gusT[:, 88:176],
                   2, deq_gu[:, 44 * 256:88 * 256], lambda j: j, "gu1")
    guplanes_done = p.mark("ve")

    @b.gpsimd
    def _(g):
        wge(g, guplanes_done)
        bi = g.dma_start(out=pk[:, 0:1536].rearrange("p (a w) -> p a w",
                                                     a=16, w=96),
                         in_=dnp.rearrange("(a p) w -> p a w", p=128))
        lm["dn"] = p.inc(bi, "dqpk2", 16)
    emit_dq_planes(1536, 16, 96, dnsT, 6, deq_dn, lambda j: j, "dn")

    def emit_store_T(deq, ntile_a, per_a, nsub, out_hbm, stsem, base_mark,
                     guard_marks=()):
        TE = p.cnt["te"]
        VE = p.cnt["ve"]
        ssl = [f"{stsem}a", f"{stsem}b"]
        for s_ in ssl:
            p.sem(s_)
        ST0 = [p.cnt[s_] for s_ in ssl]
        nbatch = ntile_a // 4
        ops = []
        for sub in range(nsub):
            for ba in range(nbatch):
                ops.append((sub, ba))

        @b.tensor
        def _(t):
            wge(t, base_mark)
            for n_, (sub, ba) in enumerate(ops):
                if n_ >= 2:
                    t.wait_ge(sem["ve"], VE + (n_ - 2) + 1)
                for q in range(4):
                    a = 4 * ba + q
                    bi = t.transpose(
                        pb[n_ % 2][:, q * 128:(q + 1) * 128],
                        apx(deq)[:, a * per_a + sub * 128: a * per_a + sub * 128 + 128],
                        consts[0][:])
                bi.then_inc(sem["te"], 1)
        p.cnt["te"] += len(ops)

        @b.vector
        def _(v):
            for n_, (sub, ba) in enumerate(ops):
                v.wait_ge(sem["te"], TE + n_ + 1)
                if n_ >= 2:
                    v.wait_ge(p.sems[ssl[n_ % 2]], ST0[n_ % 2] + 16 * (n_ // 2))
                else:
                    for mk in guard_marks:
                        wge(v, mk)
                v.tensor_copy(stg[n_ % 2][:], pb[n_ % 2][:]).then_inc(sem["ve"], 1)
        p.cnt["ve"] += len(ops)

        @b.sync
        def _(sync):
            for n_, (sub, ba) in enumerate(ops):
                sync.wait_ge(sem["ve"], VE + n_ + 1)
                bi = sync.dma_start(
                    out=out_hbm[sub * 128:(sub + 1) * 128,
                                ba * 512:(ba + 1) * 512],
                    in_=stg[n_ % 2][:])
                p.inc(bi, ssl[n_ % 2], 16)

    emit_store_T(deq_gu, 88, 256, 2, guT, "dqst0", p.mark("ve"))
    emit_store_T(deq_dn, 16, 768, 6, dnT, "dqst1", p.mark("ve"),
                 guard_marks=[p.mark("dqst0a"), p.mark("dqst0b")])
    return [p.mark("dqst0a"), p.mark("dqst0b"),
            p.mark("dqst1a"), p.mark("dqst1b")]




def build_dq():
    p = Prog("dq")
    nc = p.nc
    gup = nc.dram_tensor("gup_sh", [11264, 32], I32, kind="ExternalInput").ap()
    gus = nc.dram_tensor("gus_sh", [2, 11264], F16, kind="ExternalInput").ap()
    dnp = nc.dram_tensor("dnp_sh", [2048, 96], I32, kind="ExternalInput").ap()
    dns = nc.dram_tensor("dns_sh", [6, 2048], F16, kind="ExternalInput").ap()
    guT = nc.dram_tensor("guT_sh", [256, 11264], BF16, kind="ExternalOutput").ap()
    dnT = nc.dram_tensor("dnT_sh", [768, 2048], BF16, kind="ExternalOutput").ap()
    p.start()
    consts = emit_consts(p)
    mks = emit_dq(p, gup, gus, dnp, dns, guT, dnT, consts)
    emit_final_wait(p, mks)
    return p.finish()


def build_ffn():
    p = Prog("ffn")
    nc = p.nc
    x = nc.dram_tensor("x", [T, D], F32, kind="ExternalInput").ap()
    w_guT = nc.dram_tensor("w_guT", [D, 11264], BF16, kind="ExternalInput").ap()
    w_downT = nc.dram_tensor("w_downT", [5632, D], BF16, kind="ExternalInput").ap()
    x2 = nc.dram_tensor("x2", [T, D], F32, kind="ExternalOutput").ap()

    uT = p.sb("uTf", [128, ND * 512], BF16)
    h_actT = p.sb("hactT", [128, 44 * 512], BF16)
    p.start()
    consts = emit_consts(p)
    for tg in range(4):
        ln = emit_ln_uT(p, x, uT, 4 * tg, 4, consts, 512, name="f")
        gum = emit_gu(p, uT, w_guT, h_actT, consts, ln.all_done, name="f")
        emit_mm_ln_resid(
            p, lambda ck, tt: h_actT[:, ck * 512 + tt * 128: ck * 512 + tt * 128 + 128],
            44, w_downT, False, None, x, x2, consts, [gum], 4 * tg, 4, name="d")
    emit_final_wait(p, [p.mark("mlstd0"), p.mark("mlstd1")])
    return p.finish()


def emit_mm_ln_resid(p, lhsT_ap, nck, w_hbm, cast_w, b_hbm, x_hbm, out_hbm,
                     consts, ready_marks, t0, ntl, name="", guard_marks=(),
                     extra_mms=None, pre_marks=()):
    _, ones_row, _, cready = consts
    b = p.block
    sem = p.sems
    ngr = ntl // 2
    NWS = 4
    wtile = [p.sb(f"mlw{name}{i}", [128, 512], BF16) for i in range(NWS)]
    y_s = [p.sb(f"mly{name}{i}", [128, D], F32) for i in range(2)]
    o_s = [p.sb(f"mlo{name}{i}", [128, D], F32) for i in range(2)]
    xt = [p.sb(f"mlx{name}{i}", [128, D], F32) for i in range(2)]
    sums4 = [p.sb(f"mls4{name}{i}", [128, 4], F32) for i in range(2)]
    tot = [p.sb(f"mltot{name}{i}", [128, 1], F32) for i in range(2)]
    ssq = [p.sb(f"mlssq{name}{i}", [128, 1], F32) for i in range(2)]
    mean = [p.sb(f"mlmean{name}{i}", [128, 1], F32) for i in range(2)]
    msq = [p.sb(f"mlmsq{name}{i}", [128, 1], F32) for i in range(2)]
    var = [p.sb(f"mlvar{name}{i}", [128, 1], F32) for i in range(2)]
    std = [p.sb(f"mlstd{name}{i}", [128, 1], F32) for i in range(2)]
    rstd = [p.sb(f"mlrstd{name}{i}", [128, 1], F32) for i in range(2)]
    negmr = [p.sb(f"mlnegmr{name}{i}", [128, 1], F32) for i in range(2)]
    inst = getattr(p, "_ml_inst", 0)
    p._ml_inst = inst + 1
    wl = [f"mll{name}{inst}{i}" for i in range(NWS)]
    for i in range(NWS):
        p.sem(wl[i])
    for i in range(2):
        p.sem(f"mlxl{name}{i}")
        p.sem(f"mlst{name}{i}")
    p.sem(f"mlb{name}")
    brow = p.sb(f"mlb{name}", [1, D], BF16) if b_hbm is not None else None
    pt = p.pf[0:2]

    TE = p.cnt["te"]
    SC = p.cnt["sc"]
    VE = p.cnt["ve"]
    STB = [p.cnt[f"mlst{name}{i}"] for i in range(2)]
    XLB = [p.cnt[f"mlxl{name}{i}"] for i in range(2)]

    def evac_sc(ttg, fg, i):
        return SC + 12 * ttg + 2 * fg + i + 1

    def sqrt_sc(ttg, i):
        return SC + 12 * ttg + 8 + 2 * i + 1

    def norm_sc(ttg, i):
        return SC + 12 * ttg + 8 + 2 * i + 2

    def ve_tile(ttg, i, k):
        return VE + 18 * ttg + 9 * i + k

    wmark = []
    bmark = []
    xmark = []

    @b.gpsimd
    def _(g):
        for mk in pre_marks:
            wge(g, mk)
        if b_hbm is not None:
            g.wait_ge(sem["te"], TE)
            bi = g.dma_start(out=brow[:], in_=b_hbm[:, 0:D])
            bmark.append(p.inc(bi, f"mlb{name}", 16))
        for ttg in range(ngr):
            for fg in range(4):
                for ck in range(nck):
                    L = (ttg * 4 + fg) * nck + ck
                    if L >= NWS:
                        g.wait_ge(sem["te"], TE + L - (NWS - 1))
                    else:
                        g.wait_ge(sem["te"], TE)
                    bi = g.dma_start(out=wtile[L % NWS][:],
                                     in_=w_hbm[ck * 128:(ck + 1) * 128,
                                               fg * 512:(fg + 1) * 512])
                    wmark.append(p.inc(bi, wl[L % NWS], 16))

    @b.tensor
    def _(t):
        wge(t, cready)
        for mk in ready_marks:
            wge(t, mk)
        if bmark:
            wge(t, bmark[0])
        for ttg in range(ngr):
            for fg in range(4):
                for ck in range(nck):
                    L = (ttg * 4 + fg) * nck + ck
                    wge(t, wmark[L])
                    for i in range(2):
                        tt = 2 * ttg + i
                        if ck == 0:
                            if fg > 0:
                                t.wait_ge(sem["sc"], evac_sc(ttg, fg - 1, i))
                            elif ttg > 0:
                                t.wait_ge(sem["sc"], evac_sc(ttg - 1, 3, i))
                            else:
                                t.wait_ge(sem["sc"], SC)
                            if b_hbm is not None:
                                t.matmul(pt[i][:], ones_row[0:1, 0:128],
                                         brow[0:1, fg * 512:(fg + 1) * 512],
                                         start=True, stop=False)
                        bi = t.matmul(pt[i][:], lhsT_ap(ck, tt), wtile[L % NWS][:],
                                      start=(ck == 0 and b_hbm is None),
                                      stop=(ck == nck - 1 and extra_mms is None))
                        if ck == nck - 1 and extra_mms is not None:
                            bi = extra_mms(t, pt[i], tt, fg)
                    bi.then_inc(sem["te"], 1)
    p.cnt["te"] += ngr * 4 * nck

    @b.scalar
    def _(s):
        for ttg in range(ngr):
            for fg in range(4):
                s.wait_ge(sem["te"], TE + (ttg * 4 + fg + 1) * nck)
                for i in range(2):
                    if ttg >= 1 and fg == 0:
                        s.wait_ge(sem["ve"], ve_tile(ttg - 1, i, 9))
                    elif fg == 0:
                        s.wait_ge(sem["ve"], VE)
                    s.activation(y_s[i][:, fg * 512:(fg + 1) * 512], pt[i][:],
                                 AF.Identity,
                                 accum_out=sums4[i][:, fg:fg + 1])\
                        .then_inc(sem["sc"], 1)
            for i in range(2):
                s.wait_ge(sem["ve"], ve_tile(ttg, i, 6))
                s.activation(std[i][:], var[i][:], AF.Sqrt).then_inc(sem["sc"], 1)
                s.wait_ge(sem["ve"], ve_tile(ttg, i, 8))
                if ttg >= 1:
                    s.wait_ge(p.sems[f"mlst{name}{i}"], STB[i] + 16 * ttg)
                else:
                    s.wait_ge(p.sems[f"mlst{name}{i}"], STB[i])
                s.activation(o_s[i][:], y_s[i][:], AF.Identity,
                             bias=negmr[i][:], scale=rstd[i][:])\
                    .then_inc(sem["sc"], 1)
    p.cnt["sc"] += ngr * 12

    @b.vector
    def _(v):
        for ttg in range(ngr):
            for i in range(2):
                tt = 2 * ttg + i
                v.wait_ge(sem["sc"], evac_sc(ttg, 3, i))
                v.tensor_reduce(tot[i][:], sums4[i][:], axis=AX.X, op=ALU.add)\
                    .then_inc(sem["ve"], 1)
                if ttg >= 1:
                    v.wait_ge(p.sems[f"mlst{name}{i}"], STB[i] + 16 * ttg)
                else:
                    v.wait_ge(p.sems[f"mlst{name}{i}"], STB[i])
                v.scalar_tensor_tensor(o_s[i][:], y_s[i][:], 0.0, y_s[i][:],
                                       op0=ALU.add, op1=ALU.mult,
                                       accum_out=ssq[i][:]).then_inc(sem["ve"], 1)
                v.wait_ge(sem["ve"], ve_tile(ttg, i, 2))
                v.tensor_scalar_mul(mean[i][:], tot[i][:], 1.0 / D)\
                    .then_inc(sem["ve"], 1)
                v.wait_ge(sem["ve"], ve_tile(ttg, i, 3))
                v.scalar_tensor_tensor(msq[i][:], mean[i][:], 0.0, mean[i][:],
                                       op0=ALU.add, op1=ALU.mult)\
                    .then_inc(sem["ve"], 1)
                v.wait_ge(sem["ve"], ve_tile(ttg, i, 4))
                v.scalar_tensor_tensor(var[i][:], ssq[i][:], 1.0 / D, msq[i][:],
                                       op0=ALU.mult, op1=ALU.subtract)\
                    .then_inc(sem["ve"], 1)
                v.wait_ge(sem["ve"], ve_tile(ttg, i, 5))
                v.tensor_scalar_add(var[i][:], var[i][:], EPS)\
                    .then_inc(sem["ve"], 1)
                v.wait_ge(sem["sc"], sqrt_sc(ttg, i))
                v.reciprocal(rstd[i][:], std[i][:]).then_inc(sem["ve"], 1)
                v.wait_ge(sem["ve"], ve_tile(ttg, i, 7))
                v.scalar_tensor_tensor(negmr[i][:], mean[i][:], -1.0, rstd[i][:],
                                       op0=ALU.mult, op1=ALU.mult)\
                    .then_inc(sem["ve"], 1)
                v.wait_ge(sem["sc"], norm_sc(ttg, i))
                v.wait_ge(p.sems[f"mlxl{name}{i}"], XLB[i] + 16 * (ttg + 1))
                v.tensor_add(o_s[i][:], o_s[i][:], xt[i][:]).then_inc(sem["ve"], 1)
    p.cnt["ve"] += ngr * 18

    @b.sync
    def _(sync):
        for mk in guard_marks:
            wge(sync, mk)

        def store(ttg):
            for i in range(2):
                tt = 2 * ttg + i
                sync.wait_ge(sem["ve"], ve_tile(ttg, i, 9))
                bi = sync.dma_start(
                    out=out_hbm[(t0 + tt) * 128:(t0 + tt + 1) * 128, :],
                    in_=o_s[i][:])
                p.inc(bi, f"mlst{name}{i}", 16)

        for ttg in range(ngr):
            for i in range(2):
                tt = 2 * ttg + i
                if ttg >= 1:
                    sync.wait_ge(sem["ve"], ve_tile(ttg - 1, i, 9))
                else:
                    sync.wait_ge(sem["ve"], VE)
                bi = sync.dma_start(
                    out=xt[i][:],
                    in_=x_hbm[(t0 + tt) * 128:(t0 + tt + 1) * 128, :])
                xmark.append(p.inc(bi, f"mlxl{name}{i}", 16))
            if ttg >= 1:
                store(ttg - 1)
        store(ngr - 1)
    return [p.mark(f"mlst{name}0"), p.mark(f"mlst{name}1")]


def build_moe():
    p = Prog("moe")
    nc = p.nc
    x = nc.dram_tensor("x", [T, D], F32, kind="ExternalInput").ap()
    Wb = nc.dram_tensor("W_base", [D, D], F32, kind="ExternalInput").ap()
    bb = nc.dram_tensor("b_base", [1, D], F32, kind="ExternalInput").ap()
    As = nc.dram_tensor("A_s", [D, 16], F32, kind="ExternalInput").ap()
    Bs = nc.dram_tensor("B_s", [16, D], F32, kind="ExternalInput").ap()
    Ar = nc.dram_tensor("A_r_flat", [D, 96], F32, kind="ExternalInput").ap()
    Br = nc.dram_tensor("B_r_flat", [96, D], F32, kind="ExternalInput").ap()
    Wr = nc.dram_tensor("W_router", [D, 6], F32, kind="ExternalInput").ap()
    br = nc.dram_tensor("b_router", [1, 6], F32, kind="ExternalInput").ap()
    x4 = nc.dram_tensor("x4", [T, D], F32, kind="ExternalOutput").ap()

    HT = 8
    TSP = HT * 128
    uT = p.sb("uTm", [128, ND * TSP], BF16)
    Wb_bf = p.sb("m_wb", [128, ND * D], BF16)
    Ar_bf = p.sb("m_ar", [128, ND * 96], BF16)
    As_bf = p.sb("m_as", [128, ND * 16], BF16)
    Wr_bf = p.sb("m_wr", [128, ND * 6], BF16)
    Bs_bf = p.sb("m_bs", [16, D], BF16)
    Br_bf = p.sb("m_br", [96, D], BF16)
    bb_bf = p.sb("m_bb", [1, D], BF16)
    br_bf = p.sb("m_brr", [1, 6], BF16)
    E16 = p.sb("m_e16", [6, 96], BF16)
    xaT = p.sb("m_xa", [16, 128], BF16)
    hT = p.sb("m_h", [96, 128], BF16)
    hwT = p.sb("m_hw", [96, 128], BF16)
    wtT_s = p.sb("m_wtt", [6, 128], BF16)
    e_s = p.sb("m_es", [128, 6], F32)
    g_s = p.sb("m_gs", [128, 6], F32)
    g2_s = p.sb("m_g2", [128, 6], F32)
    kp_s = p.sb("m_kp", [128, 6], F32)
    wt_s = p.sb("m_wt", [128, 6], F32)
    wt_bf = p.sb("m_wtb", [128, 6], BF16)
    gmax = p.sb("m_gm", [128, 1], F32)
    negmax = p.sb("m_ngm", [128, 1], F32)
    esum = p.sb("m_esm", [128, 1], F32)
    er = p.sb("m_er", [128, 1], F32)
    m1s = p.sb("m_m1", [128, 1], F32)
    m2s = p.sb("m_m2", [128, 1], F32)
    y_s = [p.sb(f"m_y{i}", [128, D], F32) for i in range(2)]
    o_s = [p.sb(f"m_o{i}", [128, D], F32) for i in range(2)]
    xt = [p.sb(f"m_x{i}", [128, D], F32) for i in range(2)]
    sums4 = [p.sb(f"m_s4{i}", [128, 4], F32) for i in range(2)]
    tot = [p.sb(f"m_tot{i}", [128, 1], F32) for i in range(2)]
    ssq = [p.sb(f"m_ssq{i}", [128, 1], F32) for i in range(2)]
    meanb = [p.sb(f"m_mean{i}", [128, 1], F32) for i in range(2)]
    msqb = [p.sb(f"m_msq{i}", [128, 1], F32) for i in range(2)]
    varb = [p.sb(f"m_var{i}", [128, 1], F32) for i in range(2)]
    stdb = [p.sb(f"m_std{i}", [128, 1], F32) for i in range(2)]
    rstdb = [p.sb(f"m_rstd{i}", [128, 1], F32) for i in range(2)]
    negmrb = [p.sb(f"m_negmr{i}", [128, 1], F32) for i in range(2)]
    for s_ in ("mwl", "mxl0", "mxl1", "mst0", "mst1"):
        p.sem(s_)

    p.start()
    consts = emit_consts(p)
    ident, ones_row, ones128, cready = consts
    b = p.block
    sem = p.sems
    GP = p.cnt["gp"]

    lm = {}

    @b.gpsimd
    def _(g):
        def ld(dst3, src3, key):
            bi = g.dma_start(out=dst3, in_=src3)
            lm[key] = p.inc(bi, "mwl", 16)
        ld(Wb_bf.ap().rearrange("p (dc f) -> p dc f", dc=ND, f=D),
           Wb.rearrange("(dc p) f -> p dc f", p=128), "wb")
        ld(Ar_bf.ap().rearrange("p (dc f) -> p dc f", dc=ND, f=96),
           Ar.rearrange("(dc p) f -> p dc f", p=128), "ar")
        ld(As_bf.ap().rearrange("p (dc f) -> p dc f", dc=ND, f=16),
           As.rearrange("(dc p) f -> p dc f", p=128), "as")
        ld(Wr_bf.ap().rearrange("p (dc f) -> p dc f", dc=ND, f=6),
           Wr.rearrange("(dc p) f -> p dc f", p=128), "wr")
        ld(Bs_bf[:], Bs, "bs")
        ld(Br_bf[:], Br, "br")
        ld(bb_bf[:], bb, "bb")
        ld(br_bf[:], br, "brr")
        g.memset(E16[:], 1.0).then_inc(sem["gp"], 1)
        g.wait_ge(sem["gp"], GP + 1)
        g.affine_select(out=E16[:], in_=E16[:], compare_op=ALU.is_ge, fill=0.0,
                        base=0, pattern=[[1, 96]], channel_multiplier=-16)\
            .then_inc(sem["gp"], 1)
        g.wait_ge(sem["gp"], GP + 2)
        g.affine_select(out=E16[:], in_=E16[:], compare_op=ALU.is_ge, fill=0.0,
                        base=15, pattern=[[-1, 96]], channel_multiplier=16)\
            .then_inc(sem["gp"], 1)
    p.cnt["gp"] += 3
    wready = p.mark("mwl")
    e16_done = p.mark("gp")

    ps_xa = p.pf[4][0:16, 0:128]
    ps_h = p.pf[5][0:96, 0:128]
    ps_g = p.pf[2][0:128, 0:6]
    ps_wb = p.pf[3][0:96, 0:128]
    ps_wt = p.pb[0][0:6, 0:128]
    ps_y = [p.pf[0], p.pf[1]]

    def emit_half(half):
        ln = emit_ln_uT(p, x, uT, HT * half, HT, consts, TSP, name="m")
        TE = p.cnt["te"]
        SC = p.cnt["sc"]
        VE = p.cnt["ve"]
        MSTB = [p.cnt[f"mst{i}"] for i in range(2)]
        MXLB = [p.cnt[f"mxl{i}"] for i in range(2)]

        def te_m(tt, k):
            return TE + 9 * tt + k

        def sc_m(tt, k):
            return SC + 10 * tt + k

        def ve_m(tt, k):
            return VE + 21 * tt + k

        @b.tensor
        def _(t):
            wge(t, cready)
            wge(t, wready)
            wge(t, e16_done)
            for tt in range(HT):
                wge(t, ln.tile_done(tt))
                ut = lambda dc: uT[:, dc * TSP + tt * 128: dc * TSP + tt * 128 + 128]
                if tt >= 1:
                    t.wait_ge(sem["sc"], sc_m(tt - 1, 1))
                else:
                    t.wait_ge(sem["sc"], SC)
                for dc in range(ND):
                    bi = t.matmul(ps_xa, As_bf[:, dc * 16:(dc + 1) * 16], ut(dc),
                                  start=(dc == 0), stop=(dc == ND - 1))
                bi.then_inc(sem["te"], 1)
                if tt >= 1:
                    t.wait_ge(sem["sc"], sc_m(tt - 1, 2))
                for dc in range(ND):
                    bi = t.matmul(ps_h, Ar_bf[:, dc * 96:(dc + 1) * 96], ut(dc),
                                  start=(dc == 0), stop=(dc == ND - 1))
                bi.then_inc(sem["te"], 1)
                if tt >= 1:
                    t.wait_ge(sem["sc"], sc_m(tt - 1, 3))
                    t.wait_ge(sem["ve"], ve_m(tt - 1, 4))
                else:
                    t.wait_ge(sem["ve"], VE)
                t.matmul(ps_g, ones_row[0:1, 0:128], br_bf[0:1, :],
                         start=True, stop=False)
                for dc in range(ND):
                    bi = t.matmul(ps_g, ut(dc), Wr_bf[:, dc * 6:(dc + 1) * 6],
                                  start=False, stop=(dc == ND - 1))
                bi.then_inc(sem["te"], 1)
                t.wait_ge(sem["ve"], ve_m(tt, 11))
                if tt >= 1:
                    t.wait_ge(sem["sc"], sc_m(tt - 1, 4))
                t.transpose(ps_wt, wt_bf[:], ident[:]).then_inc(sem["te"], 1)
                t.wait_ge(sem["sc"], sc_m(tt, 4))
                if tt >= 1:
                    t.wait_ge(sem["ve"], ve_m(tt - 1, 12))
                t.matmul(ps_wb, E16[:], wtT_s[:], start=True, stop=True)\
                    .then_inc(sem["te"], 1)
                t.wait_ge(sem["sc"], sc_m(tt, 1))
                t.wait_ge(sem["ve"], ve_m(tt, 12))
                for fg in range(4):
                    if fg >= 1:
                        t.wait_ge(sem["sc"], sc_m(tt, 4 + fg))
                    elif tt >= 2:
                        t.wait_ge(sem["sc"], sc_m(tt - 2, 8))
                    else:
                        t.wait_ge(sem["sc"], SC)
                    t.matmul(ps_y[tt % 2][:], ones_row[0:1, 0:128],
                             bb_bf[0:1, fg * 512:(fg + 1) * 512],
                             start=True, stop=False)
                    for dc in range(ND):
                        t.matmul(ps_y[tt % 2][:], ut(dc),
                                 Wb_bf[:, dc * D + fg * 512: dc * D + fg * 512 + 512],
                                 start=False, stop=False)
                    t.matmul(ps_y[tt % 2][:], xaT[:],
                             Bs_bf[:, fg * 512:(fg + 1) * 512],
                             start=False, stop=False)
                    bi = t.matmul(ps_y[tt % 2][:], hwT[:],
                                  Br_bf[:, fg * 512:(fg + 1) * 512],
                                  start=False, stop=True)
                    bi.then_inc(sem["te"], 1)
        p.cnt["te"] += 9 * HT

        @b.scalar
        def _(s):
            for tt in range(HT):
                i = tt % 2
                s.wait_ge(sem["te"], te_m(tt, 1))
                if tt >= 1:
                    s.wait_ge(sem["te"], te_m(tt - 1, 9))
                s.activation(xaT[:], ps_xa, AF.Identity).then_inc(sem["sc"], 1)
                s.wait_ge(sem["te"], te_m(tt, 2))
                if tt >= 1:
                    s.wait_ge(sem["ve"], ve_m(tt - 1, 12))
                s.activation(hT[:], ps_h, AF.Identity).then_inc(sem["sc"], 1)
                s.wait_ge(sem["te"], te_m(tt, 3))
                s.wait_ge(sem["ve"], ve_m(tt, 2))
                if tt >= 1:
                    s.wait_ge(sem["ve"], ve_m(tt - 1, 4))
                s.activation(e_s[:], ps_g, AF.Exp, bias=negmax[:],
                             accum_out=esum[:]).then_inc(sem["sc"], 1)
                s.wait_ge(sem["te"], te_m(tt, 4))
                if tt >= 1:
                    s.wait_ge(sem["te"], te_m(tt - 1, 5))
                s.activation(wtT_s[:], ps_wt, AF.Identity).then_inc(sem["sc"], 1)
                for fg in range(4):
                    s.wait_ge(sem["te"], te_m(tt, 6 + fg))
                    if tt >= 2 and fg == 0:
                        s.wait_ge(sem["ve"], ve_m(tt - 2, 14))
                    elif fg == 0:
                        s.wait_ge(sem["ve"], VE)
                    s.activation(y_s[i][:, fg * 512:(fg + 1) * 512], ps_y[i][:],
                                 AF.Identity, accum_out=sums4[i][:, fg:fg + 1])\
                        .then_inc(sem["sc"], 1)
                s.wait_ge(sem["ve"], ve_m(tt, 18))
                s.activation(stdb[i][:], varb[i][:], AF.Sqrt).then_inc(sem["sc"], 1)
                s.wait_ge(sem["ve"], ve_m(tt, 20))
                if tt >= 2:
                    s.wait_ge(p.sems[f"mst{i}"], MSTB[i] + 16 * (tt // 2))
                else:
                    s.wait_ge(p.sems[f"mst{i}"], MSTB[i])
                s.activation(o_s[i][:], y_s[i][:], AF.Identity,
                             bias=negmrb[i][:], scale=rstdb[i][:])\
                    .then_inc(sem["sc"], 1)
        p.cnt["sc"] += 10 * HT

        @b.vector
        def _(v):
            for tt in range(HT):
                i = tt % 2
                v.wait_ge(sem["te"], te_m(tt, 3))
                if tt >= 1:
                    v.wait_ge(sem["ve"], ve_m(tt - 1, 2))
                else:
                    v.wait_ge(sem["ve"], VE)
                v.reduce_max(gmax[:], ps_g, axis=AX.X).then_inc(sem["ve"], 1)
                v.wait_ge(sem["ve"], ve_m(tt, 1))
                if tt >= 1:
                    v.wait_ge(sem["sc"], sc_m(tt - 1, 3))
                v.tensor_scalar_mul(negmax[:], gmax[:], -1.0).then_inc(sem["ve"], 1)
                v.wait_ge(sem["sc"], sc_m(tt, 3))
                v.reciprocal(er[:], esum[:]).then_inc(sem["ve"], 1)
                v.wait_ge(sem["ve"], ve_m(tt, 3))
                v.tensor_scalar_mul(g_s[:], e_s[:], er[:, 0:1]).then_inc(sem["ve"], 1)
                v.wait_ge(sem["ve"], ve_m(tt, 4))
                v.reduce_max(m1s[:], g_s[:], axis=AX.X).then_inc(sem["ve"], 1)
                v.wait_ge(sem["ve"], ve_m(tt, 5))
                v.tensor_single_scalar(g2_s[:], g_s[:], m1s[:, 0:1], ALU.is_equal)\
                    .then_inc(sem["ve"], 1)
                v.wait_ge(sem["ve"], ve_m(tt, 6))
                v.scalar_tensor_tensor(g2_s[:], g2_s[:], -1e30, g_s[:],
                                       op0=ALU.mult, op1=ALU.add)\
                    .then_inc(sem["ve"], 1)
                v.wait_ge(sem["ve"], ve_m(tt, 7))
                v.reduce_max(m2s[:], g2_s[:], axis=AX.X).then_inc(sem["ve"], 1)
                v.wait_ge(sem["ve"], ve_m(tt, 8))
                v.tensor_single_scalar(kp_s[:], g_s[:], m2s[:, 0:1], ALU.is_ge)\
                    .then_inc(sem["ve"], 1)
                v.wait_ge(sem["ve"], ve_m(tt, 9))
                v.tensor_mul(wt_s[:], g_s[:], kp_s[:]).then_inc(sem["ve"], 1)
                v.wait_ge(sem["ve"], ve_m(tt, 10))
                if tt >= 1:
                    v.wait_ge(sem["te"], te_m(tt - 1, 4))
                else:
                    v.wait_ge(sem["te"], TE)
                v.tensor_copy(wt_bf[:], wt_s[:]).then_inc(sem["ve"], 1)
                v.wait_ge(sem["te"], te_m(tt, 5))
                v.wait_ge(sem["sc"], sc_m(tt, 2))
                if tt >= 1:
                    v.wait_ge(sem["te"], te_m(tt - 1, 9))
                v.tensor_mul(hwT[:], hT[:], ps_wb).then_inc(sem["ve"], 1)
                v.wait_ge(sem["sc"], sc_m(tt, 8))
                v.tensor_reduce(tot[i][:], sums4[i][:], axis=AX.X, op=ALU.add)\
                    .then_inc(sem["ve"], 1)
                if tt >= 2:
                    v.wait_ge(p.sems[f"mst{i}"], MSTB[i] + 16 * (tt // 2))
                else:
                    v.wait_ge(p.sems[f"mst{i}"], MSTB[i])
                v.scalar_tensor_tensor(o_s[i][:], y_s[i][:], 0.0, y_s[i][:],
                                       op0=ALU.add, op1=ALU.mult,
                                       accum_out=ssq[i][:]).then_inc(sem["ve"], 1)
                v.wait_ge(sem["ve"], ve_m(tt, 14))
                v.tensor_scalar_mul(meanb[i][:], tot[i][:], 1.0 / D)\
                    .then_inc(sem["ve"], 1)
                v.wait_ge(sem["ve"], ve_m(tt, 15))
                v.scalar_tensor_tensor(msqb[i][:], meanb[i][:], 0.0, meanb[i][:],
                                       op0=ALU.add, op1=ALU.mult)\
                    .then_inc(sem["ve"], 1)
                v.wait_ge(sem["ve"], ve_m(tt, 16))
                v.scalar_tensor_tensor(varb[i][:], ssq[i][:], 1.0 / D, msqb[i][:],
                                       op0=ALU.mult, op1=ALU.subtract)\
                    .then_inc(sem["ve"], 1)
                v.wait_ge(sem["ve"], ve_m(tt, 17))
                v.tensor_scalar_add(varb[i][:], varb[i][:], EPS)\
                    .then_inc(sem["ve"], 1)
                v.wait_ge(sem["sc"], sc_m(tt, 9))
                v.reciprocal(rstdb[i][:], stdb[i][:]).then_inc(sem["ve"], 1)
                v.wait_ge(sem["ve"], ve_m(tt, 19))
                v.scalar_tensor_tensor(negmrb[i][:], meanb[i][:], -1.0,
                                       rstdb[i][:], op0=ALU.mult, op1=ALU.mult)\
                    .then_inc(sem["ve"], 1)
                v.wait_ge(sem["sc"], sc_m(tt, 10))
                v.wait_ge(p.sems[f"mxl{i}"], MXLB[i] + 16 * (tt // 2 + 1))
                v.tensor_add(o_s[i][:], o_s[i][:], xt[i][:]).then_inc(sem["ve"], 1)
        p.cnt["ve"] += 21 * HT

        @b.sync
        def _(sync):
            for tt in range(HT):
                i = tt % 2
                gt = HT * half + tt
                if tt >= 2:
                    sync.wait_ge(sem["ve"], ve_m(tt - 2, 21))
                else:
                    sync.wait_ge(sem["ve"], VE)
                bi = sync.dma_start(out=xt[i][:],
                                    in_=x[gt * 128:(gt + 1) * 128, :])
                p.inc(bi, f"mxl{i}", 16)
                if tt >= 1:
                    j = (tt - 1) % 2
                    gtp = HT * half + tt - 1
                    sync.wait_ge(sem["ve"], ve_m(tt - 1, 21))
                    bi = sync.dma_start(out=x4[gtp * 128:(gtp + 1) * 128, :],
                                        in_=o_s[j][:])
                    p.inc(bi, f"mst{j}", 16)
            j = (HT - 1) % 2
            gtp = HT * half + HT - 1
            sync.wait_ge(sem["ve"], ve_m(HT - 1, 21))
            bi = sync.dma_start(out=x4[gtp * 128:(gtp + 1) * 128, :],
                                in_=o_s[j][:])
            p.inc(bi, f"mst{j}", 16)

    emit_half(0)
    emit_half(1)
    emit_final_wait(p, [p.mark("mst0"), p.mark("mst1")])
    return p.finish()


def emit_final_wait(p, store_marks):
    b = p.block

    @b.sync
    def _(sync):
        for mk in store_marks:
            wge(sync, mk)



def emit_moe_pre(p, uT, As, Ar, Wr, br, consts, ln, name=""):
    ident, ones_row, ones128, cready = consts
    b = p.block
    sem = p.sems
    As_bf = p.sb(f"mp_as{name}", [128, ND * 16], BF16)
    Ar_bf = p.sb(f"mp_ar{name}", [128, ND * 96], BF16)
    Wr_bf = p.sb(f"mp_wr{name}", [128, ND * 6], BF16)
    br_bf = p.sb(f"mp_brr{name}", [1, 6], BF16)
    E16 = p.sb(f"mp_e16{name}", [6, 96], BF16)
    xaT_all = p.sb(f"mp_xa{name}", [16, T], BF16)
    hwT_all = p.sb(f"mp_hw{name}", [96, T], BF16)
    hT = p.sb(f"mp_h{name}", [96, 128], BF16)
    wtT_s = p.sb(f"mp_wtt{name}", [6, 128], BF16)
    e_s = p.sb(f"mp_es{name}", [128, 6], F32)
    g_s = p.sb(f"mp_gs{name}", [128, 6], F32)
    g2_s = p.sb(f"mp_g2{name}", [128, 6], F32)
    kp_s = p.sb(f"mp_kp{name}", [128, 6], F32)
    wt_s = p.sb(f"mp_wt{name}", [128, 6], F32)
    gmax = p.sb(f"mp_gm{name}", [128, 1], F32)
    negmax = p.sb(f"mp_ngm{name}", [128, 1], F32)
    esum = p.sb(f"mp_esm{name}", [128, 1], F32)
    er = p.sb(f"mp_er{name}", [128, 1], F32)
    m1s = p.sb(f"mp_m1{name}", [128, 1], F32)
    m2s = p.sb(f"mp_m2{name}", [128, 1], F32)
    p.sem(f"mpw{name}")

    GP = p.cnt["gp"]
    lm = []

    @b.gpsimd
    def _(g):
        g.wait_ge(sem["te"], p.cnt["te"])
        def ld(dst3, src3):
            bi = g.dma_start(out=dst3, in_=src3)
            lm.append(p.inc(bi, f"mpw{name}", 16))
        ld(As_bf.ap().rearrange("p (dc f) -> p dc f", dc=ND, f=16),
           As.rearrange("(dc p) f -> p dc f", p=128))
        ld(Ar_bf.ap().rearrange("p (dc f) -> p dc f", dc=ND, f=96),
           Ar.rearrange("(dc p) f -> p dc f", p=128))
        ld(Wr_bf.ap().rearrange("p (dc f) -> p dc f", dc=ND, f=6),
           Wr.rearrange("(dc p) f -> p dc f", p=128))
        ld(br_bf[:], br)
        g.memset(E16[:], 1.0).then_inc(sem["gp"], 1)
        g.wait_ge(sem["gp"], GP + 1)
        g.affine_select(out=E16[:], in_=E16[:], compare_op=ALU.is_ge, fill=0.0,
                        base=0, pattern=[[1, 96]], channel_multiplier=-16)\
            .then_inc(sem["gp"], 1)
        g.wait_ge(sem["gp"], GP + 2)
        g.affine_select(out=E16[:], in_=E16[:], compare_op=ALU.is_ge, fill=0.0,
                        base=15, pattern=[[-1, 96]], channel_multiplier=16)\
            .then_inc(sem["gp"], 1)
    p.cnt["gp"] += 3
    e16_done = p.mark("gp")

    TE = p.cnt["te"]
    SC = p.cnt["sc"]
    VE = p.cnt["ve"]

    def te_m(tt, k):
        return TE + 5 * tt + k

    def sc_m(tt, k):
        return SC + 4 * tt + k

    def ve_m(tt, k):
        return VE + 11 * tt + k

    ps_xa = p.pf[4][0:16, 0:128]
    ps_h = p.pf[5][0:96, 0:128]
    ps_g = p.pf[2][0:128, 0:6]
    ps_wb = p.pf[3][0:96, 0:128]
    ps_wt = p.pf[3][0:6, 0:128]
    wtT_f = p.sb(f"mp_wttf{name}", [6, 128], F32)

    @b.tensor
    def _(t):
        wge(t, cready)
        wge(t, e16_done)
        for mk in lm:
            wge(t, mk)
        for tt in range(NT):
            wge(t, ln.tile_done(tt))
            ut = lambda dc: apx(uT)[:, dc * T + tt * 128: dc * T + tt * 128 + 128]
            if tt >= 1:
                t.wait_ge(sem["sc"], sc_m(tt - 1, 1))
            else:
                t.wait_ge(sem["sc"], SC)
                t.wait_ge(sem["ve"], VE)
            for dc in range(ND):
                bi = t.matmul(ps_xa, As_bf[:, dc * 16:(dc + 1) * 16], ut(dc),
                              start=(dc == 0), stop=(dc == ND - 1))
            bi.then_inc(sem["te"], 1)
            if tt >= 1:
                t.wait_ge(sem["sc"], sc_m(tt - 1, 2))
            for dc in range(ND):
                bi = t.matmul(ps_h, Ar_bf[:, dc * 96:(dc + 1) * 96], ut(dc),
                              start=(dc == 0), stop=(dc == ND - 1))
            bi.then_inc(sem["te"], 1)
            if tt >= 1:
                t.wait_ge(sem["sc"], sc_m(tt - 1, 3))
                t.wait_ge(sem["ve"], ve_m(tt - 1, 4))
            t.matmul(ps_g, ones_row[0:1, 0:128], br_bf[0:1, :],
                     start=True, stop=False)
            for dc in range(ND):
                bi = t.matmul(ps_g, ut(dc), Wr_bf[:, dc * 6:(dc + 1) * 6],
                              start=False, stop=(dc == ND - 1))
            bi.then_inc(sem["te"], 1)
            t.wait_ge(sem["ve"], ve_m(tt, 10))
            if tt >= 1:
                t.wait_ge(sem["sc"], sc_m(tt - 1, 4))
                t.wait_ge(sem["ve"], ve_m(tt - 1, 11))
            t.transpose(ps_wt, wt_s[:], p._identf[:]).then_inc(sem["te"], 1)
            t.wait_ge(sem["sc"], sc_m(tt, 4))
            t.matmul(ps_wb, E16[:], wtT_s[:], start=True, stop=True)\
                .then_inc(sem["te"], 1)
    p.cnt["te"] += 5 * NT

    @b.scalar
    def _(s):
        for tt in range(NT):
            s.wait_ge(sem["te"], te_m(tt, 1))
            s.activation(xaT_all[:, tt * 128:(tt + 1) * 128], ps_xa, AF.Identity)\
                .then_inc(sem["sc"], 1)
            s.wait_ge(sem["te"], te_m(tt, 2))
            if tt >= 1:
                s.wait_ge(sem["ve"], ve_m(tt - 1, 11))
            s.activation(hT[:], ps_h, AF.Identity).then_inc(sem["sc"], 1)
            s.wait_ge(sem["te"], te_m(tt, 3))
            s.wait_ge(sem["ve"], ve_m(tt, 2))
            if tt >= 1:
                s.wait_ge(sem["ve"], ve_m(tt - 1, 4))
            s.activation(e_s[:], ps_g, AF.Exp, bias=negmax[:],
                         accum_out=esum[:]).then_inc(sem["sc"], 1)
            s.wait_ge(sem["te"], te_m(tt, 4))
            if tt >= 1:
                s.wait_ge(sem["te"], te_m(tt - 1, 5))
            s.activation(wtT_s[:], ps_wt, AF.Identity).then_inc(sem["sc"], 1)
    p.cnt["sc"] += 4 * NT

    @b.vector
    def _(v):
        for tt in range(NT):
            v.wait_ge(sem["te"], te_m(tt, 3))
            if tt >= 1:
                v.wait_ge(sem["ve"], ve_m(tt - 1, 2))
            else:
                v.wait_ge(sem["ve"], VE)
            v.reduce_max(gmax[:], ps_g, axis=AX.X).then_inc(sem["ve"], 1)
            v.wait_ge(sem["ve"], ve_m(tt, 1))
            if tt >= 1:
                v.wait_ge(sem["sc"], sc_m(tt - 1, 3))
            v.tensor_scalar_mul(negmax[:], gmax[:], -1.0).then_inc(sem["ve"], 1)
            v.wait_ge(sem["sc"], sc_m(tt, 3))
            v.reciprocal(er[:], esum[:]).then_inc(sem["ve"], 1)
            v.wait_ge(sem["ve"], ve_m(tt, 3))
            v.tensor_scalar_mul(g_s[:], e_s[:], er[:, 0:1]).then_inc(sem["ve"], 1)
            v.wait_ge(sem["ve"], ve_m(tt, 4))
            v.reduce_max(m1s[:], g_s[:], axis=AX.X).then_inc(sem["ve"], 1)
            v.wait_ge(sem["ve"], ve_m(tt, 5))
            v.tensor_single_scalar(g2_s[:], g_s[:], m1s[:, 0:1], ALU.is_equal)\
                .then_inc(sem["ve"], 1)
            v.wait_ge(sem["ve"], ve_m(tt, 6))
            v.scalar_tensor_tensor(g2_s[:], g2_s[:], -1e30, g_s[:],
                                   op0=ALU.mult, op1=ALU.add).then_inc(sem["ve"], 1)
            v.wait_ge(sem["ve"], ve_m(tt, 7))
            v.reduce_max(m2s[:], g2_s[:], axis=AX.X).then_inc(sem["ve"], 1)
            v.wait_ge(sem["ve"], ve_m(tt, 8))
            v.tensor_single_scalar(kp_s[:], g_s[:], m2s[:, 0:1], ALU.is_ge)\
                .then_inc(sem["ve"], 1)
            v.wait_ge(sem["ve"], ve_m(tt, 9))
            if tt >= 1:
                v.wait_ge(sem["te"], te_m(tt - 1, 4))
            v.tensor_mul(wt_s[:], g_s[:], kp_s[:]).then_inc(sem["ve"], 1)
            v.wait_ge(sem["te"], te_m(tt, 5))
            v.wait_ge(sem["sc"], sc_m(tt, 2))
            v.tensor_mul(hwT_all[:, tt * 128:(tt + 1) * 128], hT[:], ps_wb)\
                .then_inc(sem["ve"], 1)
    p.cnt["ve"] += 11 * NT

    def xa_mark(tt):
        return (sem["sc"], sc_m(tt, 1))

    def hw_mark(tt):
        return (sem["ve"], ve_m(tt, 11))

    return xaT_all, hwT_all, xa_mark, hw_mark


def build_qkv(sep_weights):
    p = Prog("qkv")
    nc = p.nc
    x = nc.dram_tensor("x", [T, D], F32, kind="ExternalInput").ap()
    if sep_weights:
        wq = nc.dram_tensor("wq", [D, D], F32, kind="ExternalInput").ap()
        wk = nc.dram_tensor("wk", [D, D], F32, kind="ExternalInput").ap()
        wv = nc.dram_tensor("wv", [D, D], F32, kind="ExternalInput").ap()
        bq = None
    else:
        wq = nc.dram_tensor("w_qkv", [D, 3 * D], F32, kind="ExternalInput").ap()
        wk = wv = wq
        bq = nc.dram_tensor("b_qkv", [1, 3 * D], F32, kind="ExternalInput").ap()
    qT = nc.dram_tensor("qT", [D, T], BF16, kind="ExternalOutput").ap()
    kT = nc.dram_tensor("kT", [D, T], BF16, kind="ExternalOutput").ap()
    v = nc.dram_tensor("v", [T, D], BF16, kind="ExternalOutput").ap()

    uT = p.sb("uT", [128, ND * T], BF16)
    p.start()
    consts = emit_consts(p)
    ln = emit_ln_uT(p, x, uT, 0, NT, consts, T)
    koff = 0 if sep_weights else D
    voff = 0 if sep_weights else 2 * D
    lnm_fm = lambda tg: ln.tile_done(4 * tg + 3)
    mk = []
    mk += emit_proj_fm(p, uT, wq, 0, bq, 0, qT, consts, lnm_fm, ND, T, name="q")
    mk += emit_proj_fm(p, uT, wk, koff, bq, D, kT, consts, lnm_fm, ND, T, name="k")
    mk += emit_proj_tm(p, uT, wv, voff, bq, 2 * D, v, consts, ln.tile_done, T, name="v")
    emit_final_wait(p, mk)
    return p.finish()


def build_fused(debug=False, upto="all", skip_dq=False):
    p = Prog("mk")
    nc = p.nc
    f = lambda n, s, d=F32: nc.dram_tensor(n, s, d, kind="ExternalInput").ap()
    x0 = f("x", [T, D])
    w_qkv = f("w_qkv", [D, 3 * D])
    b_qkv = f("b_qkv", [1, 3 * D])
    w_out0 = f("w_out0", [D, D])
    b_out0 = f("b_out0", [1, D])
    gup = f("gup_sh", [11264, 32], I32)
    gus = f("gus_sh", [2, 11264], F16)
    dnp = f("dnp_sh", [2048, 96], I32)
    dns = f("dns_sh", [6, 2048], F16)
    wq1 = f("wq1", [D, D])
    wk1 = f("wk1", [D, D])
    wv1 = f("wv1", [D, D])
    wo1 = f("wo1", [D, D])
    Wb = f("W_base", [D, D])
    bb = f("b_base", [1, D])
    As = f("A_s", [D, 16])
    Bs = f("B_s", [16, D])
    Ar = f("A_r_flat", [D, 96])
    Br = f("B_r_flat", [96, D])
    Wr = f("W_router", [D, 6])
    br = f("b_router", [1, 6])
    alpha = f("alpha", [1, 2], BF16)
    out = nc.dram_tensor("out", [T, D], F32, kind="ExternalOutput").ap()

    ik = "ExternalOutput" if debug else "Internal"
    g = lambda n, s, d: nc.dram_tensor(n, s, d, kind="Internal").ap()
    gd = lambda n, s, d: nc.dram_tensor(n, s, d, kind=ik).ap()
    x1 = gd("x1d", [T, D], F32)
    x2 = gd("x2d", [T, D], F32)
    x3 = gd("x3d", [T, D], F32)
    qTd = gd("qTd", [D, T], BF16)
    kv_own = g("kv_own", [2 * T, D], BF16)
    kv_all = g("kv_all", [4 * T, D], BF16)
    guT_sh = g("guT_shd", [256, 11264], BF16)
    guT_all = g("guT_alld", [2048, 11264], BF16)
    dnT_sh = g("dnT_shd", [768, 2048], BF16)
    dnT_all = g("dnT_alld", [6144, 2048], BF16)

    uT = p.sb("uT", [128, ND * T], BF16)
    atk = [p.sb(f"at_k{i}", [128, S], BF16) for i in range(2)]
    atv = [p.sb(f"at_v{i}", [128, 32 * 128], BF16) for i in range(2)]
    atq = [p.sb(f"at_q{i}", [128, T], BF16) for i in range(2)]
    ate = p.sb("at_e", [128, 16 * 512], BF16)
    mly = [p.sb(f"mlyml{i}", [128, D], F32) for i in range(2)]
    mlo = [p.sb(f"mloml{i}", [128, D], F32) for i in range(2)]
    mlx = [p.sb(f"mlxml{i}", [128, D], F32) for i in range(2)]
    uT_a = p.addr("uT")
    atk_a = p.addr("at_k0")
    assert p.addr("at_k1") == atk_a + 8192
    assert p.addr("at_v0") == atk_a + 16384
    assert p.addr("at_v1") == atk_a + 24576
    assert p.addr("at_q0") == atk_a + 32768
    mly_a = p.addr("mlyml0")
    assert p.addr("mlxml1") == mly_a + 5 * 8192
    ate_a = p.addr("at_e")
    p.sb("dq_kf", [128, 1536], F32, at=atk_a)
    p.sb("dq_sf", [128, 1536], F32, at=atk_a + 6144)
    p.sb("dq_r4", [128, 1536], F32, at=atk_a + 12288)
    p.sb("dq_r6", [128, 1536], F32, at=atk_a + 18432)
    p.sb("dq_m1", [128, 1536], F32, at=atk_a + 24576)
    p.sb("dq_s", [128, 1536], F32, at=atk_a + 30720)
    p.sb("dq_pk", [128, 1536], I32, at=ate_a)
    p.sb("dq_i", [128, 1536], I32, at=ate_a + 6144)
    p.sb("dq_dgu", [128, 88 * 256], BF16, at=uT_a)
    p.sb("dq_ddn", [128, 16 * 768], BF16, at=mly_a)
    p.sb("lnxtg0", [128, D], F32, at=mly_a)
    p.sb("lnxtg1", [128, D], F32, at=mly_a + 8192)
    p.sb("lnsqg", [128, D], F32, at=mly_a + 16384)
    p.sb("lntrashg", [128, D], BF16, at=mly_a + 24576)
    p.sb("lnug0", [128, D], BF16, at=mly_a + 28672)
    p.sb("lnug1", [128, D], BF16, at=mly_a + 32768)
    p.sb("pfwfm0", [128, ND * 128], BF16, at=ate_a)
    p.sb("pfwfm1", [128, ND * 128], BF16, at=ate_a + 4096)
    p.sb("pfsfm0", [128, 512], BF16, at=ate_a + 8192)
    p.sb("pfsfm1", [128, 512], BF16, at=ate_a + 9216)
    p.sb("ptwtm0", [128, 512], BF16, at=ate_a + 10240)
    p.sb("ptwtm1", [128, 512], BF16, at=ate_a + 11264)
    p.sb("ptstm0", [128, 512], BF16, at=ate_a + 12288)
    p.sb("ptstm1", [128, 512], BF16, at=ate_a + 13312)
    p.sb("ptwtm2", [128, 512], BF16, at=ate_a + 14336)
    p.sb("ptwtm3", [128, 512], BF16, at=ate_a + 15360)
    p.sb("pfbfm", [1, ND * 128], BF16, at=atk_a + 32768)
    p.sb("ptbtm", [1, 4 * 512], BF16, at=atk_a + 36864)
    p.sb("guwgf0", [128, ND * 128], BF16, at=atk_a)
    p.sb("guwgf1", [128, ND * 128], BF16, at=atk_a + 4096)
    p.sb("guwuf0", [128, ND * 128], BF16, at=atk_a + 8192)
    p.sb("guwuf1", [128, ND * 128], BF16, at=atk_a + 12288)
    p.sb("gusilf0", [128, 512], BF16, at=atk_a + 16384)
    p.sb("gusilf1", [128, 512], BF16, at=atk_a + 17408)
    hactT = p.sb("hactTf", [128, 44 * 512], BF16, at=uT_a + ND * 512 * 2)
    p.sb("mp_as", [128, ND * 16], BF16, at=atk_a)
    p.sb("mp_ar", [128, ND * 96], BF16, at=atk_a + 1024)
    p.sb("mp_wr", [128, ND * 6], BF16, at=atk_a + 4096)
    p.sb("mp_xa", [16, T], BF16, at=atk_a + 8192)
    p.sb("mp_hw", [96, T], BF16, at=atk_a + 12288)

    p.start()
    consts = emit_consts(p)
    p.sem("cc")

    if skip_dq:
        dq_mks = []
    else:
        dq_mks = emit_dq(p, gup, gus, dnp, dns, guT_sh, dnT_sh, consts)

    ln0 = emit_ln_uT(p, x0, uT, 0, NT, consts, T, name="g",
                     guard_marks=dq_mks)
    lnm_fm = lambda tg: ln0.tile_done(4 * tg + 3)
    mkk = emit_proj_fm(p, uT, w_qkv, D, b_qkv, D, kv_own[0:T, :], consts,
                       lnm_fm, ND, T, name="fm")
    mkv = emit_proj_tm(p, uT, w_qkv, 2 * D, b_qkv, 2 * D, kv_own[T:2 * T, :],
                       consts, ln0.tile_done, T, name="tm")
    mkq = emit_proj_fm(p, uT, w_qkv, 0, b_qkv, 0, qTd, consts, lnm_fm,
                       ND, T, name="fm")

    @p.block.gpsimd
    def _(gq):
        for mk in mkk + mkv:
            wge(gq, mk)
        gq.collective_compute(
            "AllGather", ALU.bypass,
            replica_groups=[[0, 1], [2, 3], [4, 5], [6, 7]],
            ins=[kv_own.opt()], outs=[kv_all.opt()]).then_inc(p.sems["cc"], 1)
    p.cnt["cc"] += 1
    cc0 = p.mark("cc")

    if not skip_dq:
        @p.block.gpsimd
        def _(gq):
            for mk in dq_mks:
                wge(gq, mk)
            gq.collective_compute(
                "AllGather", ALU.bypass, replica_groups=[list(range(8))],
                ins=[guT_sh.opt()], outs=[guT_all.opt()]).then_inc(p.sems["cc"], 1)
            gq.collective_compute(
                "AllGather", ALU.bypass, replica_groups=[list(range(8))],
                ins=[dnT_sh.opt()], outs=[dnT_all.opt()]).then_inc(p.sems["cc"], 1)
        p.cnt["cc"] += 2
    ccdq = p.mark("cc")

    w_out0_bf = g("w_out0_bf", [D, D], BF16)
    wq1_bf = g("wq1_bf", [D, D], BF16)
    wk1_bf = g("wk1_bf", [D, D], BF16)
    wv1_bf = g("wv1_bf", [D, D], BF16)
    wo1_bf = g("wo1_bf", [D, D], BF16)
    Wb_bf = g("Wb_bf", [D, D], BF16)
    p.sem("wcast")
    wcm = []

    @p.block.gpsimd
    def _(gq):
        for src, dst in ((w_out0, w_out0_bf), (wq1, wq1_bf), (wk1, wk1_bf),
                         (wv1, wv1_bf), (wo1, wo1_bf), (Wb, Wb_bf)):
            bi = gq.dma_start(out=dst, in_=src)
            wcm.append(p.inc(bi, "wcast", 16))

    if upto == "qkv":
        emit_final_wait(p, mkq + mkk + mkv)
        return p.finish()

    avm0 = emit_attn_core(p, qTd, kv_own, kv_all, alpha, consts, L0_SCALE,
                          False, uT, guard_marks=[cc0] + mkq)
    mko0 = emit_mm_ln_resid(
        p, lambda ck, tt: apx(uT)[:, ck * T + tt * 128: ck * T + tt * 128 + 128],
        ND, w_out0_bf, False, b_out0, x0, x1, consts, [avm0], 0, NT, name="ml",
        guard_marks=[avm0], pre_marks=[wcm[0]])
    if upto == "l0attn":
        emit_final_wait(p, mko0)
        return p.finish()

    for tg in range(4):
        lnf = emit_ln_uT(p, x1, uT[:, 0:ND * 512], 4 * tg, 4, consts, 512,
                         name="g", guard_marks=mko0 if tg == 0 else ())
        gum = emit_gu(p, uT[:, 0:ND * 512], guT_all, hactT, consts,
                      lnf.all_done, name="f",
                      guard_marks=[ccdq] + (mko0 if tg == 0 else []))
        mkf = emit_mm_ln_resid(
            p, lambda ck, tt: apx(hactT)[:, ck * 512 + tt * 128:
                                         ck * 512 + tt * 128 + 128],
            44, dnT_all[0:5632, :], False, None, x1, x2, consts, [gum],
            4 * tg, 4, name="ml", guard_marks=[gum])
    if upto == "ffn":
        emit_final_wait(p, mkf)
        return p.finish()

    ln1 = emit_ln_uT(p, x2, uT, 0, NT, consts, T, name="g", guard_marks=mkf)
    lnm1 = lambda tg: ln1.tile_done(4 * tg + 3)
    mkk1 = emit_proj_fm(p, uT, wk1_bf, 0, None, 0, kv_own[0:T, :], consts,
                        lnm1, ND, T, name="fm", pre_marks=[wcm[2]])
    mkv1 = emit_proj_tm(p, uT, wv1_bf, 0, None, 0, kv_own[T:2 * T, :],
                        consts, ln1.tile_done, T, name="tm", pre_marks=[wcm[3]])
    mkq1 = emit_proj_fm(p, uT, wq1_bf, 0, None, 0, qTd, consts, lnm1,
                        ND, T, name="fm", pre_marks=[wcm[1]])

    @p.block.gpsimd
    def _(gq):
        for mk in mkk1 + mkv1:
            wge(gq, mk)
        gq.collective_compute(
            "AllGather", ALU.bypass,
            replica_groups=[[0, 1], [2, 3], [4, 5], [6, 7]],
            ins=[kv_own.opt()], outs=[kv_all.opt()]).then_inc(p.sems["cc"], 1)
    p.cnt["cc"] += 1
    cc1 = p.mark("cc")

    if upto == "l1qkv":
        emit_final_wait(p, mkq1 + mkk1 + mkv1)
        return p.finish()

    avm1 = emit_attn_core(p, qTd, kv_own, kv_all, alpha, consts, L1_SCALE,
                          True, uT, guard_marks=[cc1] + mkq1)
    mko1 = emit_mm_ln_resid(
        p, lambda ck, tt: apx(uT)[:, ck * T + tt * 128: ck * T + tt * 128 + 128],
        ND, wo1_bf, False, None, x2, x3, consts, [avm1], 0, NT, name="ml",
        guard_marks=[avm1], pre_marks=[wcm[4]])

    if upto == "l1attn":
        emit_final_wait(p, mko1)
        return p.finish()

    lnm_ = emit_ln_uT(p, x3, uT, 0, NT, consts, T, name="g", guard_marks=mko1)
    xaT_all, hwT_all, xa_mark, hw_mark = emit_moe_pre(
        p, uT, As, Ar, Wr, br, consts, lnm_, name="")
    Bs_bf = p.sb("fz_bs", [16, D], BF16, at=p.addr("at_v0"))
    Br_bf = p.sb("fz_br", [96, D], BF16, at=p.addr("at_v0") + 4096)
    p.sem("fzw")
    fzm = []

    @p.block.gpsimd
    def _(gq):
        gq.wait_ge(p.sems["te"], p.cnt["te"])
        bi = gq.dma_start(out=Bs_bf[:], in_=Bs)
        fzm.append(p.inc(bi, "fzw", 16))
        bi = gq.dma_start(out=Br_bf[:], in_=Br)
        fzm.append(p.inc(bi, "fzw", 16))

    def extra(t, psum, tt, fg):
        for mk in fzm:
            wge(t, mk)
        wge(t, xa_mark(tt))
        wge(t, hw_mark(tt))
        t.matmul(psum[:], xaT_all[:, tt * 128:(tt + 1) * 128],
                 Bs_bf[:, fg * 512:(fg + 1) * 512], start=False, stop=False)
        return t.matmul(psum[:], hwT_all[:, tt * 128:(tt + 1) * 128],
                        Br_bf[:, fg * 512:(fg + 1) * 512],
                        start=False, stop=True)

    mkfin = emit_mm_ln_resid(
        p, lambda ck, tt: apx(uT)[:, ck * T + tt * 128: ck * T + tt * 128 + 128],
        ND, Wb_bf, False, bb, x3, out, consts, [hw_mark(NT - 1)], 0, NT,
        name="ml", guard_marks=[hw_mark(0)], extra_mms=extra, pre_marks=[wcm[5]])

    emit_final_wait(p, mkfin)
    return p.finish()


_PROGS = {}


def _prog(key, builder, *args):
    if key not in _PROGS:
        _PROGS[key] = builder(*args)
    return _PROGS[key]


def kernel(**inputs):
    f32 = np.float32
    x = np.ascontiguousarray(inputs["L0_attn_x"], dtype=f32)
    nc = _prog("fused", build_fused)
    gup = inputs["L0_ffn_gate_up_packed"]
    gus = inputs["L0_ffn_gate_up_scales"]
    dnp = inputs["L0_ffn_down_packed"]
    dns = inputs["L0_ffn_down_scales"]
    Ar_flat = np.ascontiguousarray(
        inputs["L1_ffn_A_r"].transpose(1, 0, 2).reshape(D, 96))
    Br_flat = np.ascontiguousarray(inputs["L1_ffn_B_r"].reshape(96, D))
    row = lambda a: np.ascontiguousarray(a.reshape(1, -1), f32)
    maps = []
    for c in range(8):
        dnp_sh = np.zeros((2048, 96), np.int32)
        dns_sh = np.zeros((6, 2048), np.float16)
        w0, w1 = 96 * c, min(96 * (c + 1), 704)
        dnp_sh[:, 0:w1 - w0] = dnp[:, w0:w1]
        g0, g1 = 6 * c, min(6 * (c + 1), 44)
        dns_sh[0:g1 - g0, :] = dns[g0:g1, :]
        maps.append({
            "x": np.ascontiguousarray(x[c // 2, (c % 2) * T:(c % 2 + 1) * T]),
            "w_qkv": inputs["L0_attn_w_qkv"],
            "b_qkv": row(inputs["L0_attn_b_qkv"]),
            "w_out0": inputs["L0_attn_w_out"],
            "b_out0": row(inputs["L0_attn_b_out"]),
            "gup_sh": np.ascontiguousarray(gup[:, 32 * c:32 * (c + 1)]),
            "gus_sh": np.ascontiguousarray(gus[2 * c:2 * c + 2, :]),
            "dnp_sh": dnp_sh, "dns_sh": dns_sh,
            "wq1": inputs["L1_attn_Wq"], "wk1": inputs["L1_attn_Wk"],
            "wv1": inputs["L1_attn_Wv"], "wo1": inputs["L1_attn_Wo"],
            "W_base": inputs["L1_ffn_W_base"],
            "b_base": row(inputs["L1_ffn_b_base"]),
            "A_s": inputs["L1_ffn_A_s"], "B_s": inputs["L1_ffn_B_s"],
            "A_r_flat": Ar_flat, "B_r_flat": Br_flat,
            "W_router": inputs["L1_ffn_W_router"],
            "b_router": row(inputs["L1_ffn_b_router"]),
            "alpha": np.array([[-1e30, 1.0]] if c % 2 == 0 else [[0.0, 0.0]],
                              np.float32).astype(BFNP),
        })
    res = run_bass_kernel_spmd(nc, maps, core_ids=list(range(8))).results
    out = np.empty((4, S, D), np.float32)
    for c in range(8):
        out[c // 2, (c % 2) * T:(c % 2 + 1) * T] = res[c]["out"]
    return out


def kernel_multilaunch(**inputs):
    f32 = np.float32
    x = np.ascontiguousarray(inputs["L0_attn_x"], dtype=f32)
    xs = [np.ascontiguousarray(x[c // 2, (c % 2) * T:(c % 2 + 1) * T])
          for c in range(8)]

    def run(nc, maps, cores=None):
        cores = list(range(8)) if cores is None else cores
        return run_bass_kernel_spmd(nc, maps, core_ids=cores).results

    w_qkv = inputs["L0_attn_w_qkv"]
    b_qkv = np.ascontiguousarray(inputs["L0_attn_b_qkv"].reshape(1, -1), f32)
    r1 = run(_prog("qkv0", build_qkv, False),
             [{"x": xs[c], "w_qkv": w_qkv, "b_qkv": b_qkv} for c in range(8)])

    w_out = inputs["L0_attn_w_out"]
    b_out = np.ascontiguousarray(inputs["L0_attn_b_out"].reshape(1, -1), f32)
    kv_own = [np.concatenate([r1[c]["kT"], r1[c]["v"]], axis=0) for c in range(8)]
    kv_all = [np.concatenate([kv_own[2 * b], kv_own[2 * b + 1]], axis=0)
              for b in range(4)]
    al0 = np.zeros((1, 2), np.float32).astype(BFNP)
    r2 = run(_prog("attn0", build_attn, False),
             [{"qT": r1[c]["qT"], "kv_own": kv_own[c], "kv_all": kv_all[c // 2],
               "alpha": al0, "x": xs[c], "w_out": w_out, "b_out": b_out}
              for c in range(8)])
    x1 = [r2[c]["x1"] for c in range(8)]

    gup = inputs["L0_ffn_gate_up_packed"]
    gus = inputs["L0_ffn_gate_up_scales"]
    dnp = inputs["L0_ffn_down_packed"]
    dns = inputs["L0_ffn_down_scales"]
    dq_maps = []
    for c in range(8):
        dnp_sh = np.zeros((2048, 96), np.int32)
        dns_sh = np.zeros((6, 2048), np.float16)
        w0, w1 = 96 * c, min(96 * (c + 1), 704)
        dnp_sh[:, 0:w1 - w0] = dnp[:, w0:w1]
        g0, g1 = 6 * c, min(6 * (c + 1), 44)
        dns_sh[0:g1 - g0, :] = dns[g0:g1, :]
        dq_maps.append({
            "gup_sh": np.ascontiguousarray(gup[:, 32 * c:32 * (c + 1)]),
            "gus_sh": np.ascontiguousarray(gus[2 * c:2 * c + 2, :]),
            "dnp_sh": dnp_sh, "dns_sh": dns_sh})
    r3 = run(_prog("dq", build_dq), dq_maps)
    w_guT = np.concatenate([r3[c]["guT_sh"] for c in range(8)], axis=0)
    w_downT = np.ascontiguousarray(
        np.concatenate([r3[c]["dnT_sh"] for c in range(8)], axis=0)[0:5632])

    r4 = run(_prog("ffn", build_ffn),
             [{"x": x1[c], "w_guT": w_guT, "w_downT": w_downT}
              for c in range(8)])
    x2 = [r4[c]["x2"] for c in range(8)]

    r5 = run(_prog("qkv1", build_qkv, True),
             [{"x": x2[c], "wq": inputs["L1_attn_Wq"],
               "wk": inputs["L1_attn_Wk"], "wv": inputs["L1_attn_Wv"]}
              for c in range(8)])
    kv_own1 = [np.concatenate([r5[c]["kT"], r5[c]["v"]], axis=0) for c in range(8)]
    kv_all1 = [np.concatenate([kv_own1[2 * b], kv_own1[2 * b + 1]], axis=0)
               for b in range(4)]

    Wo = inputs["L1_attn_Wo"]
    alph = [np.array([[-1e30, 1.0]] if c % 2 == 0 else [[0.0, 0.0]],
                     np.float32).astype(BFNP) for c in range(8)]
    r6 = run(_prog("attn1", build_attn, True),
             [{"qT": r5[c]["qT"], "kv_own": kv_own1[c],
               "kv_all": kv_all1[c // 2], "alpha": alph[c],
               "x": x2[c], "w_out": Wo} for c in range(8)])
    x3 = [r6[c]["x1"] for c in range(8)]

    Ar_flat = np.ascontiguousarray(
        inputs["L1_ffn_A_r"].transpose(1, 0, 2).reshape(D, 96))
    Br_flat = np.ascontiguousarray(inputs["L1_ffn_B_r"].reshape(96, D))
    r8 = run(_prog("moe", build_moe),
             [{"x": x3[c], "W_base": inputs["L1_ffn_W_base"],
               "b_base": np.ascontiguousarray(
                   inputs["L1_ffn_b_base"].reshape(1, -1), f32),
               "A_s": inputs["L1_ffn_A_s"], "B_s": inputs["L1_ffn_B_s"],
               "A_r_flat": Ar_flat, "B_r_flat": Br_flat,
               "W_router": inputs["L1_ffn_W_router"],
               "b_router": np.ascontiguousarray(
                   inputs["L1_ffn_b_router"].reshape(1, -1), f32)}
              for c in range(8)])

    out = np.empty((4, S, D), np.float32)
    for c in range(8):
        out[c // 2, (c % 2) * T:(c % 2 + 1) * T] = r8[c]["x4"]
    return out
